# version 06
# baseline (speedup 1.0000x reference)
import contextlib
import numpy as np
import concourse.bass as bass
import concourse.mybir as mybir
from concourse.bass_utils import run_bass_kernel_spmd

F32 = mybir.dt.float32
BF16 = mybir.dt.bfloat16
F16 = mybir.dt.float16
AF = mybir.ActivationFunctionType
ALU = mybir.AluOpType
AX = mybir.AxisListType

ENGS = ("sync", "gpsimd", "vector", "scalar", "tensor")

B, T, D = 8, 2048, 1024
DIN = 2048
NST = 16
DTR = 64
NH, NKV, HD = 16, 4, 64
NE, HF = 8, 4096
TC = 512
NCH = T // TC
KD = D // 128
NDI = DIN // 128


class Prog:

    def __init__(self, name):
        self.nc = bass.Bass("TRN2", target_bir_lowering=False)
        self.name = name
        self.queues = {e: [] for e in ENGS}
        self.tot = {e: 0 for e in ENGS}
        self.tot_dma = 0
        self.pending = {e: None for e in ENGS}
        self.cur_snapshot = ({e: 0 for e in ENGS}, 0, {}, [])
        self.waited = {e: None for e in ENGS}
        self.stack = contextlib.ExitStack()
        self._stacks = [self.stack]
        self._names = {}
        self._dma_hist = [0]
        self._tot_hist = [dict(self.tot)]
        self.eng_lag = {}

    def _uniq(self, name):
        n = self._names.get(name, 0)
        self._names[name] = n + 1
        return name if n == 0 else f"{name}_{n}"

    def push_region(self):
        self._stacks.append(contextlib.ExitStack())

    def pop_region(self):
        self.barrier()
        self._stacks.pop().close()

    def dram(self, name, shape, dtype, kind):
        return self.nc.dram_tensor(name, list(shape), dtype, kind=kind).ap()

    def sb(self, name, shape, dtype):
        return self._stacks[-1].enter_context(
            self.nc.sbuf_tensor(self._uniq(name), list(shape), dtype))

    def ps(self, name, shape, dtype=F32):
        return self._stacks[-1].enter_context(
            self.nc.psum_tensor(self._uniq(name), list(shape), dtype))

    def _q(self, eng, item):
        snap = self.cur_snapshot
        if self.waited[eng] is not snap:
            self.queues[eng].append(("wait", snap, eng))
            self.waited[eng] = snap
        self.queues[eng].append(item)

    def op(self, eng, fn):
        cell = [0]
        self._q(eng, ("call", fn, cell))
        self.pending[eng] = cell

    def dma(self, eng, out, in_):
        cell = ["dma"]
        self._q(eng, ("call", lambda e: e.dma_start(out=out, in_=in_), cell))
        self.tot_dma += 16

    def barrier(self, lazy=False):
        for e in ENGS:
            c = self.pending[e]
            if c is not None:
                c[0] = 1
                self.tot[e] += 1
                self.pending[e] = None
        dval = self._dma_hist[-1] if lazy else self.tot_dma
        self._dma_hist.append(self.tot_dma)
        self._tot_hist.append(dict(self.tot))
        self.cur_snapshot = (dict(self.tot), dval, dict(self.eng_lag),
                             [dict(h) for h in self._tot_hist[-4:]])

    def build(self):
        self.barrier()
        self.queues["sync"].append(("wait", self.cur_snapshot, "__final__"))
        nc = self.nc
        bars = {e: self.stack.enter_context(nc.semaphore(f"bar_{e}")) for e in ENGS}
        bar_dma = self.stack.enter_context(nc.semaphore("bar_dma"))
        with nc.Block() as block:
            def emit_stream(eng_name):
                def _(e):
                    for item in self.queues[eng_name]:
                        if item[0] == "wait":
                            snap, me = item[1], item[2]
                            counts, dmas, lags, hist = snap
                            mylag = lags.get(me, {})
                            for oe in ENGS:
                                v = counts[oe]
                                lag = mylag.get(oe, 0)
                                if lag > 0 and len(hist) > lag:
                                    v = hist[-1 - lag][oe]
                                if v > 0:
                                    e.wait_ge(bars[oe], v)
                            if dmas > 0:
                                e.wait_ge(bar_dma, dmas)
                        else:
                            _, fn, cell = item
                            inst = fn(e)
                            if cell[0] == "dma":
                                inst.then_inc(bar_dma, 16)
                            elif cell[0] == 1:
                                inst.then_inc(bars[eng_name], 1)
                return _
            block.sync(emit_stream("sync"))
            block.gpsimd(emit_stream("gpsimd"))
            block.vector(emit_stream("vector"))
            block.scalar(emit_stream("scalar"))
            block.tensor(emit_stream("tensor"))
        self.stack.close()
        return nc


def _mk_dram(p, dr):
    def _D(name, shape, dtype, kind):
        if dr is not None:
            return dr[name]
        return p.dram(name, shape, dtype, kind)
    return _D


def build_stageA(p=None, dr=None):
    standalone = p is None
    if standalone:
        p = Prog("A")
    _D = _mk_dram(p, dr)
    p.push_region()
    xT = _D("xT", [D, T], F32, "ExternalInput")
    w_inT = _D("w_inT", [D, 2 * DIN], F32, "ExternalInput")
    conv_w = _D("conv_w", [DIN, 4], F32, "ExternalInput")
    conv_b = _D("conv_b", [128, NDI], F32, "ExternalInput")
    u_out = _D("u_bf", [DIN, T], BF16, "ExternalOutput")
    sz_out = _D("sz_bf", [DIN, T], BF16, "ExternalOutput")
    xn_out = _D("xn_bf", [D, T], BF16, "ExternalOutput")

    w_sb = p.sb("w_sb", [128, KD * 2 * DIN], BF16)
    cw_sb = p.sb("cw_sb", [128, NDI * 4], F32)
    cb_sb = p.sb("cb_sb", [128, NDI], F32)
    ones_col = p.sb("ones_col", [128, 1], BF16)
    ones_row = p.sb("ones_row", [1, 128], BF16)
    eps_sb = p.sb("eps_sb", [1, 1], F32)

    x_sb = p.sb("x_sb", [128, KD * TC], F32)
    xb_sb = p.sb("xb_sb", [128, KD * TC], BF16)
    xsq_sb = p.sb("xsq_sb", [128, KD * TC], BF16)
    xn_sb = p.sb("xn_sb", [128, KD * TC], BF16)
    tmp_sb = p.sb("tmp_sb", [128, TC], BF16)
    mrow = p.sb("mrow", [1, TC], F32)
    vrow = p.sb("vrow", [1, TC], F32)
    msq = p.sb("msq", [1, TC], F32)
    srow = p.sb("srow", [1, TC], F32)
    rrow = p.sb("rrow", [1, TC], F32)
    m_bf = p.sb("m_bf", [1, TC], BF16)
    r_bf = p.sb("r_bf", [1, TC], BF16)
    mb_sb = p.sb("mb_sb", [128, TC], BF16)
    rb_sb = p.sb("rb_sb", [128, TC], BF16)
    xc_sb = p.sb("xc_sb", [128, NDI * (TC + 3)], F32)
    cacc = p.sb("cacc", [128, 4 * TC], F32)
    u_sb = p.sb("u_sb", [128, NDI * TC], BF16)
    sz_sb = p.sb("sz_sb", [128, NDI * TC], BF16)

    ps_sx = p.ps("ps_sx", [1, TC])
    ps_sxx = p.ps("ps_sxx", [1, TC])
    ps_mb = p.ps("ps_mb", [128, TC])
    ps_rb = p.ps("ps_rb", [128, TC])
    ps_xz = [p.ps(f"ps_xz{i}", [128, TC]) for i in range(4)]

    for k in range(KD):
        p.dma("gpsimd", w_sb[:, k * 2 * DIN:(k + 1) * 2 * DIN],
              w_inT[k * 128:(k + 1) * 128, :])
    p.dma("sync", cw_sb.ap().rearrange("p (o c) -> p o c", c=4),
          conv_w.rearrange("(o p) c -> p o c", p=128))
    p.dma("sync", cb_sb[:], conv_b)
    p.op("vector", lambda e: e.memset(eps_sb[:], 1e-5))
    p.op("vector", lambda e: e.memset(ones_col[:], 1.0))
    p.op("vector", lambda e: e.memset(ones_row[:], 1.0))
    for o in range(NDI):
        p.op("vector", lambda e, o=o: e.memset(
            xc_sb[:, o * (TC + 3):o * (TC + 3) + 3], 0.0))
    p.barrier()

    for c in range(NCH):
        csl = slice(c * TC, (c + 1) * TC)
        src = xT[:, csl].rearrange("(k p) t -> p k t", p=128)
        p.dma("sync", x_sb.ap().rearrange("p (k t) -> p k t", k=KD), src)
        p.dma("gpsimd", xb_sb.ap().rearrange("p (k t) -> p k t", k=KD), src)
        p.barrier()
        p.op("scalar", lambda e: e.activation(out=xsq_sb[:], in_=x_sb[:],
                                              func=AF.Square))
        p.barrier()
        for k in range(KD):
            p.op("tensor", lambda e, k=k: e.matmul(
                ps_sx[:], ones_col[:], xb_sb[:, k * TC:(k + 1) * TC],
                start=(k == 0), stop=(k == KD - 1)))
        for k in range(KD):
            p.op("tensor", lambda e, k=k: e.matmul(
                ps_sxx[:], ones_col[:], xsq_sb[:, k * TC:(k + 1) * TC],
                start=(k == 0), stop=(k == KD - 1)))
        p.barrier()
        p.op("vector", lambda e: e.tensor_scalar_mul(out=mrow[:], in0=ps_sx[:],
                                                     scalar1=1.0 / D))
        p.op("vector", lambda e: e.tensor_scalar_mul(out=vrow[:], in0=ps_sxx[:],
                                                     scalar1=1.0 / D))
        p.op("vector", lambda e: e.tensor_mul(msq[:], mrow[:], mrow[:]))
        p.op("vector", lambda e: e.tensor_sub(vrow[:], vrow[:], msq[:]))
        p.barrier()
        p.op("scalar", lambda e: e.activation(out=srow[:], in_=vrow[:],
                                              func=AF.Sqrt, bias=eps_sb[:]))
        p.barrier()
        p.op("vector", lambda e: e.reciprocal(rrow[:], srow[:]))
        p.barrier()
        p.op("vector", lambda e: e.tensor_copy(m_bf[:], mrow[:]))
        p.op("vector", lambda e: e.tensor_copy(r_bf[:], rrow[:]))
        p.barrier()
        p.op("tensor", lambda e: e.matmul(ps_mb[:], ones_row[:], m_bf[:],
                                          start=True, stop=True))
        p.op("tensor", lambda e: e.matmul(ps_rb[:], ones_row[:], r_bf[:],
                                          start=True, stop=True))
        p.barrier()
        p.op("vector", lambda e: e.tensor_copy(mb_sb[:], ps_mb[:]))
        p.op("vector", lambda e: e.tensor_copy(rb_sb[:], ps_rb[:]))
        p.barrier()
        for k in range(KD):
            p.op("vector", lambda e, k=k: e.tensor_sub(
                tmp_sb[:], xb_sb[:, k * TC:(k + 1) * TC], mb_sb[:]))
            p.op("vector", lambda e, k=k: e.tensor_mul(
                xn_sb[:, k * TC:(k + 1) * TC], tmp_sb[:], rb_sb[:]))
        p.barrier()
        p.dma("gpsimd", xn_out[:, csl].rearrange("(k p) t -> p k t", p=128),
              xn_sb.ap().rearrange("p (k t) -> p k t", k=KD))
        p.eng_lag = {"tensor": {"vector": 2, "scalar": 2}}
        for g in range(35):
            if g < 32:
                o, pbx = g, g % 4
                for k in range(KD):
                    p.op("tensor", lambda e, o=o, pbx=pbx, k=k: e.matmul(
                        ps_xz[pbx][:],
                        w_sb[:, k * 2 * DIN + o * 128:k * 2 * DIN + (o + 1) * 128],
                        xn_sb[:, k * TC:(k + 1) * TC],
                        start=(k == 0), stop=(k == KD - 1)))
            gm = g - 1
            if 0 <= gm < 32:
                o, pbx = gm, gm % 4
                if o < 16:
                    base = o * (TC + 3)
                    p.op("vector", lambda e, pbx=pbx, base=base: e.tensor_copy(
                        xc_sb[:, base + 3:base + 3 + TC], ps_xz[pbx][:]))
                else:
                    zi = o - 16
                    p.op("scalar", lambda e, zi=zi, pbx=pbx: e.activation(
                        out=sz_sb[:, zi * TC:(zi + 1) * TC],
                        in_=ps_xz[pbx][:], func=AF.Silu))
            gm2 = g - 2
            if 0 <= gm2 < 16:
                o = gm2
                base = o * (TC + 3)
                w4 = cw_sb[:, o * 4:o * 4 + 4]
                cs = cacc[:, (gm2 % 4) * TC:(gm2 % 4 + 1) * TC]
                p.op("vector", lambda e, base=base, w4=w4, cs=cs: e.tensor_scalar_mul(
                    out=cs, in0=xc_sb[:, base + 3:base + 3 + TC],
                    scalar1=w4[:, 3:4]))
                for j in (2, 1, 0):
                    p.op("vector", lambda e, base=base, w4=w4, cs=cs, j=j:
                         e.scalar_tensor_tensor(
                             out=cs, in0=xc_sb[:, base + j:base + j + TC],
                             scalar=w4[:, j:j + 1], in1=cs,
                             op0=ALU.mult, op1=ALU.add))
                p.op("vector", lambda e, base=base: e.tensor_copy(
                    xc_sb[:, base:base + 3],
                    xc_sb[:, base + TC:base + TC + 3]))
            gm3 = g - 3
            if 0 <= gm3 < 16:
                o = gm3
                p.op("scalar", lambda e, o=o, gm3=gm3: e.activation(
                    out=u_sb[:, o * TC:(o + 1) * TC],
                    in_=cacc[:, (gm3 % 4) * TC:(gm3 % 4 + 1) * TC],
                    func=AF.Silu, bias=cb_sb[:, o:o + 1]))
            p.barrier(lazy=True)
        p.eng_lag = {}
        p.dma("gpsimd", u_out[:, csl].rearrange("(k p) t -> p k t", p=128),
              u_sb.ap().rearrange("p (k t) -> p k t", k=NDI))
        p.dma("gpsimd", sz_out[:, csl].rearrange("(k p) t -> p k t", p=128),
              sz_sb.ap().rearrange("p (k t) -> p k t", k=NDI))
        p.barrier()
    p.pop_region()
    if standalone:
        return p.build()


def build_stageB(p=None, dr=None):
    standalone = p is None
    if standalone:
        p = Prog("B")
    _D = _mk_dram(p, dr)
    p.push_region()
    u_in = _D("u_bf", [DIN, T], BF16, "ExternalInput")
    xp_wT = _D("xp_wT", [DIN, 96], F32, "ExternalInput")
    dt_wT = _D("dt_wT", [DTR, DIN], F32, "ExternalInput")
    dt_b = _D("dt_b", [128, NDI], F32, "ExternalInput")
    delta_out = _D("delta", [DIN, T], F32, "ExternalOutput")
    du_out = _D("du", [DIN, T], F16, "ExternalOutput")
    bc_out = _D("bc", [2 * NST, T], F32, "ExternalOutput")

    u_sb = p.sb("u_sb", [128, NDI * T], BF16)
    xpw_sb = p.sb("xpw_sb", [128, NDI * 96], BF16)
    dtw_sb = p.sb("dtw_sb", [DTR, DIN], BF16)
    dtb_sb = p.sb("dtb_sb", [128, NDI], F32)
    dt_sb = p.sb("dt_sb", [DTR, T], BF16)
    bc_sb = p.sb("bc_sb", [96, T], F32)
    delta_sb = p.sb("delta_sb", [128, 2 * TC], F32)
    du_sb = p.sb("du_sb", [128, 2 * TC], F16)

    ps_xd = p.ps("ps_xd", [96, TC])
    ps_dt = [p.ps(f"ps_dt{i}", [128, TC]) for i in range(2)]

    for k in range(NDI):
        p.dma("gpsimd", u_sb[:, k * T:(k + 1) * T], u_in[k * 128:(k + 1) * 128, :])
    p.dma("gpsimd", xpw_sb.ap().rearrange("p (k j) -> p k j", k=NDI),
          xp_wT.rearrange("(k p) j -> p k j", p=128))
    p.dma("gpsimd", dtw_sb[:], dt_wT)
    p.dma("sync", dtb_sb[:], dt_b)
    p.barrier()

    for c in range(NCH):
        csl = slice(c * TC, (c + 1) * TC)
        for k in range(NDI):
            p.op("tensor", lambda e, k=k, c=c: e.matmul(
                ps_xd[:], xpw_sb[:, k * 96:(k + 1) * 96],
                u_sb[:, k * T + c * TC:k * T + (c + 1) * TC],
                start=(k == 0), stop=(k == NDI - 1)))
        p.barrier()
        p.op("vector", lambda e, csl=csl: e.tensor_copy(dt_sb[:, csl], ps_xd[0:64, :]))
        p.op("vector", lambda e, csl=csl: e.tensor_copy(bc_sb[64:96, csl], ps_xd[64:96, :]))
        p.barrier()
        for mm in range(NDI):
            pb = mm % 2
            p.op("tensor", lambda e, mm=mm, pb=pb, csl=csl: e.matmul(
                ps_dt[pb][:], dtw_sb[:, mm * 128:(mm + 1) * 128],
                dt_sb[:, csl], start=True, stop=True))
            p.barrier()
            p.op("scalar", lambda e, mm=mm, pb=pb: e.activation(
                out=delta_sb[:, pb * TC:(pb + 1) * TC], in_=ps_dt[pb][:],
                func=AF.Exp, bias=dtb_sb[:, mm:mm + 1]))
            p.op("scalar", lambda e, pb=pb: e.activation(
                out=delta_sb[:, pb * TC:(pb + 1) * TC],
                in_=delta_sb[:, pb * TC:(pb + 1) * TC], func=AF.Ln, bias=1.0))
            p.barrier()
            p.op("vector", lambda e, mm=mm, pb=pb, c=c: e.tensor_mul(
                du_sb[:, pb * TC:(pb + 1) * TC],
                delta_sb[:, pb * TC:(pb + 1) * TC],
                u_sb[:, mm * T + c * TC:mm * T + (c + 1) * TC]))
            p.barrier()
            p.dma("sync", delta_out[mm * 128:(mm + 1) * 128, csl],
                  delta_sb[:, pb * TC:(pb + 1) * TC])
            p.dma("gpsimd", du_out[mm * 128:(mm + 1) * 128, csl],
                  du_sb[:, pb * TC:(pb + 1) * TC])
            p.barrier()
    p.dma("sync", bc_out, bc_sb[64:96, :])
    p.barrier()
    p.pop_region()
    if standalone:
        return p.build()




def build_stageC(p=None, dr=None):
    CT = 1024
    NCC = T // CT
    standalone = p is None
    if standalone:
        p = Prog("C")
    _D = _mk_dram(p, dr)
    p.push_region()
    delta_in = _D("delta", [DIN, T], F32, "ExternalInput")
    du_in = _D("du", [DIN, T], F16, "ExternalInput")
    u_in = _D("u_bf", [DIN, T], BF16, "ExternalInput")
    sz_in = _D("sz_bf", [DIN, T], BF16, "ExternalInput")
    bc_in = _D("bc", [2 * NST, T], F32, "ExternalInput")
    A_in = _D("A_dev", [128, NDI * NST], F32, "ExternalInput")
    D_in = _D("D_dev", [128, NDI], F32, "ExternalInput")
    outw_in = _D("outw_T", [DIN, D], F32, "ExternalInput")
    mamba_out = _D("mamba", [D, T], F32, "ExternalOutput")

    Bb = p.sb("Bb", [128, NST * CT], F16)
    Cb = p.sb("Cb", [128, NST * CT], F16)
    delta_c = p.sb("delta_c", [128, 2 * CT], F32)
    du_c = p.sb("du_c", [128, 2 * CT], F16)
    u_c = p.sb("u_c", [128, 2 * CT], BF16)
    sz_c = p.sb("sz_c", [128, 2 * CT], BF16)
    dA = p.sb("dA", [128, 3 * CT], F16)
    dBu = p.sb("dBu", [128, 3 * CT], F16)
    h_b = p.sb("h_b", [128, 3 * CT], F16)
    ym = p.sb("ym", [128, 3 * CT], F16)
    y_b = p.sb("y_b", [128, 2 * CT], F16)
    a_gate = p.sb("a_gate", [128, 2 * CT], F16)
    b_gate = p.sb("b_gate", [128, 2 * CT], F16)
    yg = p.sb("yg", [128, NDI * CT], BF16)
    w_out = p.sb("w_out", [128, NDI * D], BF16)
    carry = p.sb("carry", [128, NDI * NST], F32)
    A_sb = p.sb("A_sb", [128, NDI * NST], F32)
    D_sb = p.sb("D_sb", [128, NDI], F32)
    mb_ev = p.sb("mb_ev", [128, 3 * CT], F32)
    ps_op = [p.ps(f"ps_op{i}", [128, CT]) for i in range(4)]

    for k in range(NDI):
        p.dma("gpsimd", w_out[:, k * D:(k + 1) * D],
              outw_in[k * 128:(k + 1) * 128, :])
    p.dma("sync", A_sb[:], A_in)
    p.dma("sync", D_sb[:], D_in)
    p.barrier()
    p.op("scalar", lambda e: e.activation(out=A_sb[:], in_=A_sb[:], func=AF.Exp))
    p.barrier()
    p.op("vector", lambda e: e.tensor_scalar_mul(out=A_sb[:], in0=A_sb[:],
                                                 scalar1=-1.0))
    p.barrier()

    def tile_loads(m, c):
        csl = slice(c * CT, (c + 1) * CT)
        buf = slice((m % 2) * CT, (m % 2 + 1) * CT)
        rows = slice(m * 128, (m + 1) * 128)
        p.dma("sync", delta_c[:, buf], delta_in[rows, csl])
        p.dma("sync", du_c[:, buf], du_in[rows, csl])
        p.dma("sync", u_c[:, buf], u_in[rows, csl])
        p.dma("sync", sz_c[:, buf], sz_in[rows, csl])

    NPH = NDI * NST + 8
    for c in range(NCC):
        csl = slice(c * CT, (c + 1) * CT)
        p.dma("gpsimd", Bb.ap().rearrange("p (n t) -> p n t", n=NST),
              bc_in[0:NST, csl].unsqueeze(0).to_broadcast((128, NST, CT)))
        p.dma("gpsimd", Cb.ap().rearrange("p (n t) -> p n t", n=NST),
              bc_in[NST:2 * NST, csl].unsqueeze(0).to_broadcast((128, NST, CT)))
        tile_loads(0, c)
        p.op("vector", lambda e: e.memset(y_b[:, 0:CT], 0.0))
        p.barrier()
        for t in range(NPH):
            if t < NDI * NST:
                i = t
                m, n = i // NST, i % NST
                j = t % 3
                p.op("scalar", lambda e, j=j, m=m, i=i: e.activation(
                    out=dA[:, j * CT:(j + 1) * CT],
                    in_=delta_c[:, (m % 2) * CT:(m % 2 + 1) * CT],
                    func=AF.Exp, scale=A_sb[:, i:i + 1]))
                p.op("vector", lambda e, j=j, m=m, n=n: e.tensor_tensor(
                    out=dBu[:, j * CT:(j + 1) * CT],
                    in0=du_c[:, (m % 2) * CT:(m % 2 + 1) * CT],
                    in1=Bb[:, n * CT:(n + 1) * CT], op=ALU.mult))
            if 0 <= t - 1 < NDI * NST:
                i = t - 1
                j = i % 3
                init = 0.0 if c == 0 else carry[:, i:i + 1]
                p.op("vector", lambda e, j=j, init=init: e.tensor_tensor_scan(
                    out=h_b[:, j * CT:(j + 1) * CT],
                    data0=dA[:, j * CT:(j + 1) * CT],
                    data1=dBu[:, j * CT:(j + 1) * CT],
                    initial=init, op0=ALU.mult, op1=ALU.add))
            if c == 0 and 0 <= t - 2 < NDI * NST:
                i2 = t - 2
                j2 = i2 % 3
                p.op("vector", lambda e, j2=j2, i2=i2: e.tensor_copy(
                    carry[:, i2:i2 + 1], h_b[:, (j2 + 1) * CT - 1:(j2 + 1) * CT]))
            if 0 <= t - 2 < NDI * NST:
                i = t - 2
                j = i % 3
                n = i % NST
                p.op("vector", lambda e, j=j, n=n: e.tensor_tensor(
                    out=ym[:, j * CT:(j + 1) * CT],
                    in0=h_b[:, j * CT:(j + 1) * CT],
                    in1=Cb[:, n * CT:(n + 1) * CT], op=ALU.mult))
            if 0 <= t - 3 < NDI * NST:
                i = t - 3
                j = i % 3
                m = i // NST
                ysl = slice((m % 2) * CT, (m % 2 + 1) * CT)
                p.op("vector", lambda e, j=j, ysl=ysl: e.tensor_tensor(
                    out=y_b[:, ysl], in0=y_b[:, ysl],
                    in1=ym[:, j * CT:(j + 1) * CT], op=ALU.add))
            if t % NST == 0 and 0 < t < NDI * NST:
                mnew = t // NST
                p.op("vector", lambda e, mnew=mnew: e.memset(
                    y_b[:, (mnew % 2) * CT:(mnew % 2 + 1) * CT], 0.0))
            if t % NST == 8 and t // NST < NDI - 1:
                tile_loads(t // NST + 1, c)
            if t >= 19:
                mg = (t - 3) // NST - 1
                off = (t - 3) % NST
                if 0 <= mg < NDI:
                    gb = slice((mg % 2) * CT, (mg % 2 + 1) * CT)
                    if off == 0:
                        p.op("vector", lambda e, gb=gb, mg=mg: e.tensor_scalar_mul(
                            out=a_gate[:, gb], in0=u_c[:, gb],
                            scalar1=D_sb[:, mg:mg + 1]))
                    elif off == 1:
                        p.op("vector", lambda e, gb=gb: e.tensor_tensor(
                            out=b_gate[:, gb], in0=y_b[:, gb],
                            in1=a_gate[:, gb], op=ALU.add))
                    elif off == 2:
                        p.op("vector", lambda e, gb=gb, mg=mg: e.tensor_tensor(
                            out=yg[:, mg * CT:(mg + 1) * CT], in0=b_gate[:, gb],
                            in1=sz_c[:, gb], op=ALU.mult))
            p.barrier(lazy=True)
        p.eng_lag = {"tensor": {"vector": 2, "scalar": 2}}
        for o in range(KD + 2):
            if o < KD:
                pb = o % 4
                for nn in range(CT // 512):
                    for k in range(NDI):
                        p.op("tensor", lambda e, o=o, pb=pb, nn=nn, k=k: e.matmul(
                            ps_op[pb][:, nn * 512:(nn + 1) * 512],
                            w_out[:, k * D + o * 128:k * D + (o + 1) * 128],
                            yg[:, k * CT + nn * 512:k * CT + nn * 512 + 512],
                            start=(k == 0), stop=(k == NDI - 1)))
            if 0 <= o - 1 < KD:
                pe3 = (o - 1) % 3
                pb = (o - 1) % 4
                p.op("scalar", lambda e, pe3=pe3, pb=pb: e.copy(
                    mb_ev[:, pe3 * CT:(pe3 + 1) * CT], ps_op[pb][:]))
            if 0 <= o - 2 < KD:
                pe3 = (o - 2) % 3
                p.dma("sync", mamba_out[(o - 2) * 128:(o - 1) * 128, csl],
                      mb_ev[:, pe3 * CT:(pe3 + 1) * CT])
            p.barrier(lazy=True)
        p.eng_lag = {}
    p.pop_region()
    if standalone:
        return p.build()



def build_stageD(p=None, dr=None):
    standalone = p is None
    if standalone:
        p = Prog("D")
    _D = _mk_dram(p, dr)
    p.push_region()
    xT = _D("xT", [D, T], F32, "ExternalInput")
    xn_in = _D("xn_bf", [D, T], BF16, "ExternalInput")
    mamba_in = _D("mamba", [D, T], F32, "ExternalInput")
    wrT = _D("wrT", [D, D], F32, "ExternalInput")
    wkT = _D("wkT", [D, D], F32, "ExternalInput")
    wvT = _D("wvT", [D, D], F32, "ExternalInput")
    mur_in = _D("mu_r", [128, KD], F32, "ExternalInput")
    muk_in = _D("mu_k", [128, KD], F32, "ExternalInput")
    x1_out = _D("x1", [D, T], F32, "ExternalOutput")
    xn1_out = _D("xn1_bf", [D, T], BF16, "ExternalOutput")

    wr_sb = p.sb("wr_sb", [128, KD * D], BF16)
    wk_sb = p.sb("wk_sb", [128, KD * D], BF16)
    wv_sb = p.sb("wv_sb", [128, KD * D], BF16)
    mur = p.sb("mur", [128, KD], F32)
    muk = p.sb("muk", [128, KD], F32)
    mur1 = p.sb("mur1", [128, KD], F32)
    muk1 = p.sb("muk1", [128, KD], F32)
    ones_col = p.sb("ones_colD", [128, 1], BF16)
    ones_row = p.sb("ones_rowD", [1, 128], BF16)
    eps_sb = p.sb("eps_sbD", [1, 1], F32)

    xs = p.sb("xs", [128, KD * (TC + 1)], BF16)
    xr_sb = p.sb("xr_sb", [128, KD * TC], BF16)
    xk_sb = p.sb("xk_sb", [128, KD * TC], BF16)
    tmpmix = p.sb("tmpmix", [128, TC], BF16)
    r_sb = p.sb("r_sb", [128, KD * TC], BF16)
    kk_sb = p.sb("kk_sb", [128, KD * TC], BF16)
    krelu = p.sb("krelu", [128, 2 * TC], BF16)
    x_sb = p.sb("x_sbD", [128, KD * TC], F32)
    mam_sb = p.sb("mam_sb", [128, KD * TC], F32)
    ffn_sb = p.sb("ffn_sb", [128, 2 * TC], F32)
    x1_sb = p.sb("x1_sb", [128, KD * TC], F32)
    xb1 = p.sb("xb1", [128, KD * TC], BF16)
    xsq1 = p.sb("xsq1", [128, KD * TC], BF16)
    xn1_sb = p.sb("xn1_sb", [128, KD * TC], BF16)
    tmp_ln = p.sb("tmp_lnD", [128, TC], BF16)
    mrow = p.sb("mrowD", [1, TC], F32)
    vrow = p.sb("vrowD", [1, TC], F32)
    msq = p.sb("msqD", [1, TC], F32)
    srow = p.sb("srowD", [1, TC], F32)
    rrow = p.sb("rrowD", [1, TC], F32)
    m_bf = p.sb("m_bfD", [1, TC], BF16)
    r_bf = p.sb("r_bfD", [1, TC], BF16)
    mb_sb = p.sb("mb_sbD", [128, TC], BF16)
    rb_sb = p.sb("rb_sbD", [128, TC], BF16)

    ps_a = [p.ps(f"ps_Da{i}", [128, TC]) for i in range(2)]
    ps_b = [p.ps(f"ps_Db{i}", [128, TC]) for i in range(2)]
    ps_sx = p.ps("ps_sxD", [1, TC])
    ps_sxx = p.ps("ps_sxxD", [1, TC])
    ps_mb = p.ps("ps_mbD", [128, TC])
    ps_rb = p.ps("ps_rbD", [128, TC])

    for k in range(KD):
        p.dma("gpsimd", wr_sb[:, k * D:(k + 1) * D], wrT[k * 128:(k + 1) * 128, :])
        p.dma("gpsimd", wk_sb[:, k * D:(k + 1) * D], wkT[k * 128:(k + 1) * 128, :])
        p.dma("gpsimd", wv_sb[:, k * D:(k + 1) * D], wvT[k * 128:(k + 1) * 128, :])
    p.dma("sync", mur[:], mur_in)
    p.dma("sync", muk[:], muk_in)
    p.op("vector", lambda e: e.memset(eps_sb[:], 1e-5))
    p.op("vector", lambda e: e.memset(ones_col[:], 1.0))
    p.op("vector", lambda e: e.memset(ones_row[:], 1.0))
    for k in range(KD):
        p.op("vector", lambda e, k=k: e.memset(xs[:, k * (TC + 1):k * (TC + 1) + 1], 0.0))
    p.barrier()
    p.op("vector", lambda e: e.tensor_scalar(out=mur1[:], in0=mur[:], scalar1=-1.0,
                                             scalar2=1.0, op0=ALU.mult, op1=ALU.add))
    p.op("vector", lambda e: e.tensor_scalar(out=muk1[:], in0=muk[:], scalar1=-1.0,
                                             scalar2=1.0, op0=ALU.mult, op1=ALU.add))
    p.barrier()

    for c in range(NCH):
        csl = slice(c * TC, (c + 1) * TC)
        p.dma("sync", xs.ap().rearrange("p (k t) -> p k t", k=KD)[:, :, 1:TC + 1],
              xn_in[:, csl].rearrange("(k p) t -> p k t", p=128))
        p.dma("sync", x_sb.ap().rearrange("p (k t) -> p k t", k=KD),
              xT[:, csl].rearrange("(k p) t -> p k t", p=128))
        p.dma("sync", mam_sb.ap().rearrange("p (k t) -> p k t", k=KD),
              mamba_in[:, csl].rearrange("(k p) t -> p k t", p=128))
        p.barrier()
        for k in range(KD):
            base = k * (TC + 1)
            cur = xs[:, base + 1:base + 1 + TC]
            prv = xs[:, base:base + TC]
            p.op("vector", lambda e, prv=prv, k=k: e.tensor_scalar_mul(
                out=tmpmix[:], in0=prv, scalar1=mur1[:, k:k + 1]))
            p.op("vector", lambda e, cur=cur, k=k: e.scalar_tensor_tensor(
                out=xr_sb[:, k * TC:(k + 1) * TC], in0=cur,
                scalar=mur[:, k:k + 1], in1=tmpmix[:], op0=ALU.mult, op1=ALU.add))
            p.op("vector", lambda e, prv=prv, k=k: e.tensor_scalar_mul(
                out=tmpmix[:], in0=prv, scalar1=muk1[:, k:k + 1]))
            p.op("vector", lambda e, cur=cur, k=k: e.scalar_tensor_tensor(
                out=xk_sb[:, k * TC:(k + 1) * TC], in0=cur,
                scalar=muk[:, k:k + 1], in1=tmpmix[:], op0=ALU.mult, op1=ALU.add))
        for k in range(KD):
            base = k * (TC + 1)
            p.op("vector", lambda e, base=base: e.tensor_copy(
                xs[:, base:base + 1], xs[:, base + TC:base + TC + 1]))
        p.barrier()
        for o in range(KD + 2):
            if o < KD:
                pb = o % 2
                for k in range(KD):
                    p.op("tensor", lambda e, o=o, k=k, pb=pb: e.matmul(
                        ps_a[pb][:], wr_sb[:, k * D + o * 128:k * D + (o + 1) * 128],
                        xr_sb[:, k * TC:(k + 1) * TC],
                        start=(k == 0), stop=(k == KD - 1)))
                for k in range(KD):
                    p.op("tensor", lambda e, o=o, k=k, pb=pb: e.matmul(
                        ps_b[pb][:], wk_sb[:, k * D + o * 128:k * D + (o + 1) * 128],
                        xk_sb[:, k * TC:(k + 1) * TC],
                        start=(k == 0), stop=(k == KD - 1)))
            if 0 <= o - 1 < KD:
                om, pb = o - 1, (o - 1) % 2
                p.op("scalar", lambda e, om=om, pb=pb: e.activation(
                    out=r_sb[:, om * TC:(om + 1) * TC], in_=ps_a[pb][:],
                    func=AF.Sigmoid))
                p.op("vector", lambda e, om=om, pb=pb: e.tensor_scalar_max(
                    out=krelu[:, pb * TC:(pb + 1) * TC], in0=ps_b[pb][:], scalar1=0.0))
            if 0 <= o - 2 < KD:
                om, pb = o - 2, (o - 2) % 2
                p.op("scalar", lambda e, om=om, pb=pb: e.activation(
                    out=kk_sb[:, om * TC:(om + 1) * TC],
                    in_=krelu[:, pb * TC:(pb + 1) * TC], func=AF.Square))
            p.barrier(lazy=True)
        for o in range(KD + 1):
            if o < KD:
                pb = o % 2
                for k in range(KD):
                    p.op("tensor", lambda e, o=o, k=k, pb=pb: e.matmul(
                        ps_a[pb][:], wv_sb[:, k * D + o * 128:k * D + (o + 1) * 128],
                        kk_sb[:, k * TC:(k + 1) * TC],
                        start=(k == 0), stop=(k == KD - 1)))
            if 0 <= o - 1 < KD:
                om, pb = o - 1, (o - 1) % 2
                p.op("vector", lambda e, om=om, pb=pb: e.tensor_tensor(
                    out=ffn_sb[:, pb * TC:(pb + 1) * TC], in0=ps_a[pb][:],
                    in1=r_sb[:, om * TC:(om + 1) * TC], op=ALU.mult))
                p.op("vector", lambda e, om=om: e.tensor_add(
                    x1_sb[:, om * TC:(om + 1) * TC], x_sb[:, om * TC:(om + 1) * TC],
                    mam_sb[:, om * TC:(om + 1) * TC]))
                p.op("vector", lambda e, om=om, pb=pb: e.tensor_add(
                    x1_sb[:, om * TC:(om + 1) * TC], x1_sb[:, om * TC:(om + 1) * TC],
                    ffn_sb[:, pb * TC:(pb + 1) * TC]))
            p.barrier(lazy=True)
        p.op("vector", lambda e: e.tensor_copy(xb1[:], x1_sb[:]))
        p.op("scalar", lambda e: e.activation(out=xsq1[:], in_=x1_sb[:],
                                              func=AF.Square))
        p.barrier()
        for k in range(KD):
            p.op("tensor", lambda e, k=k: e.matmul(
                ps_sx[:], ones_col[:], xb1[:, k * TC:(k + 1) * TC],
                start=(k == 0), stop=(k == KD - 1)))
        for k in range(KD):
            p.op("tensor", lambda e, k=k: e.matmul(
                ps_sxx[:], ones_col[:], xsq1[:, k * TC:(k + 1) * TC],
                start=(k == 0), stop=(k == KD - 1)))
        p.barrier()
        p.op("vector", lambda e: e.tensor_scalar_mul(out=mrow[:], in0=ps_sx[:],
                                                     scalar1=1.0 / D))
        p.op("vector", lambda e: e.tensor_scalar_mul(out=vrow[:], in0=ps_sxx[:],
                                                     scalar1=1.0 / D))
        p.op("vector", lambda e: e.tensor_mul(msq[:], mrow[:], mrow[:]))
        p.op("vector", lambda e: e.tensor_sub(vrow[:], vrow[:], msq[:]))
        p.barrier()
        p.op("scalar", lambda e: e.activation(out=srow[:], in_=vrow[:],
                                              func=AF.Sqrt, bias=eps_sb[:]))
        p.barrier()
        p.op("vector", lambda e: e.reciprocal(rrow[:], srow[:]))
        p.barrier()
        p.op("vector", lambda e: e.tensor_copy(m_bf[:], mrow[:]))
        p.op("vector", lambda e: e.tensor_copy(r_bf[:], rrow[:]))
        p.barrier()
        p.op("tensor", lambda e: e.matmul(ps_mb[:], ones_row[:], m_bf[:],
                                          start=True, stop=True))
        p.op("tensor", lambda e: e.matmul(ps_rb[:], ones_row[:], r_bf[:],
                                          start=True, stop=True))
        p.barrier()
        p.op("vector", lambda e: e.tensor_copy(mb_sb[:], ps_mb[:]))
        p.op("vector", lambda e: e.tensor_copy(rb_sb[:], ps_rb[:]))
        p.barrier()
        for k in range(KD):
            p.op("vector", lambda e, k=k: e.tensor_sub(
                tmp_ln[:], xb1[:, k * TC:(k + 1) * TC], mb_sb[:]))
            p.op("vector", lambda e, k=k: e.tensor_mul(
                xn1_sb[:, k * TC:(k + 1) * TC], tmp_ln[:], rb_sb[:]))
        p.barrier()
        p.dma("gpsimd", x1_out[:, csl].rearrange("(k p) t -> p k t", p=128),
              x1_sb.ap().rearrange("p (k t) -> p k t", k=KD))
        p.dma("gpsimd", xn1_out[:, csl].rearrange("(k p) t -> p k t", p=128),
              xn1_sb.ap().rearrange("p (k t) -> p k t", k=KD))
        p.barrier()
    p.pop_region()
    if standalone:
        return p.build()



def build_stageE(p=None, dr=None):
    standalone = p is None
    if standalone:
        p = Prog("E")
    _D = _mk_dram(p, dr)
    p.push_region()
    xn_in = _D("xn1_bf", [D, T], BF16, "ExternalInput")
    wqT = _D("wqT", [D, D], F32, "ExternalInput")
    wkT = _D("wkT", [D, NKV * HD], F32, "ExternalInput")
    wvT = _D("wvT", [D, NKV * HD], F32, "ExternalInput")
    woT = _D("woT", [D, D], F32, "ExternalInput")
    attn_out = _D("attn", [D, T], F32, "ExternalOutput")

    NB = T // 128
    wq_sb = p.sb("wq_sb", [128, KD * D], BF16)
    wk_sb = p.sb("wk_sb", [128, KD * NKV * HD], BF16)
    wv_sb = p.sb("wv_sb", [128, KD * NKV * HD], BF16)
    wo_sb = p.sb("wo_sb", [128, KD * D], BF16)
    xn_sb = p.sb("xn_sbE", [128, KD * T], BF16)
    q_sb = p.sb("q_sbE", [128, KD * T], BF16)
    k_sb = p.sb("k_sbE", [128, 2 * T], BF16)
    kd_sb = p.sb("kd_sbE", [128, NKV * T], BF16)
    v_sb = p.sb("v_sbE", [128, 2 * T], BF16)
    vtok = p.sb("vtok", [128, NB * NKV * HD], BF16)
    io = p.sb("ioE", [128, 128], F32)
    pcol = p.sb("pcolE", [128, 1], F32)
    ident = p.sb("identE", [128, 128], BF16)
    tri = p.sb("triE", [128, 128], BF16)
    eighth = p.sb("eighth", [128, 1], F32)
    p_buf = p.sb("p_buf", [128, 2 * T], BF16)
    pt_buf = p.sb("pt_buf", [128, 2 * T], BF16)
    nm_b = p.sb("nm_b", [128, 2], F32)
    l_b = p.sb("l_b", [128, 2], F32)
    rl_b = p.sb("rl_b", [128, 2], F32)
    ctx_sb = p.sb("ctx_sbE", [128, KD * T], BF16)
    at_ev = p.sb("at_ev", [128, 2 * 512], F32)

    ps_qkv = [p.ps(f"ps_Eq{i}", [128, 512]) for i in range(2)]
    ps_s = p.ps("ps_Es", [128, T])
    ps_t = p.ps("ps_Et", [128, T], BF16)
    ps_ctx = None

    for k in range(KD):
        p.dma("gpsimd", wq_sb[:, k * D:(k + 1) * D], wqT[k * 128:(k + 1) * 128, :])
        p.dma("gpsimd", wo_sb[:, k * D:(k + 1) * D], woT[k * 128:(k + 1) * 128, :])
        p.dma("gpsimd", wk_sb[:, k * NKV * HD:(k + 1) * NKV * HD],
              wkT[k * 128:(k + 1) * 128, :])
        p.dma("gpsimd", wv_sb[:, k * NKV * HD:(k + 1) * NKV * HD],
              wvT[k * 128:(k + 1) * 128, :])
        p.dma("sync", xn_sb[:, k * T:(k + 1) * T], xn_in[k * 128:(k + 1) * 128, :])
    p.op("gpsimd", lambda e: e.iota(io[:], [[1, 128]], channel_multiplier=0,
                                    allow_small_or_imprecise_dtypes=True))
    p.op("gpsimd", lambda e: e.iota(pcol[:], [[0, 1]], channel_multiplier=1,
                                    allow_small_or_imprecise_dtypes=True))
    p.op("vector", lambda e: e.memset(eighth[:], 0.125))
    p.barrier()
    p.op("vector", lambda e: e.tensor_single_scalar(out=ident[:], in_=io[:],
                                                    scalar=pcol[:], op=ALU.is_equal))
    p.op("vector", lambda e: e.tensor_single_scalar(out=tri[:], in_=io[:],
                                                    scalar=pcol[:], op=ALU.is_le))
    p.barrier()

    def proj(w_sb, nout, dst_sb, scale=None):
        for o in range(nout):
            for nn in range(T // 512):
                pb = (o * (T // 512) + nn) % 2
                for k in range(KD):
                    p.op("tensor", lambda e, o=o, nn=nn, k=k, pb=pb,
                         w_sb=w_sb, nout=nout: e.matmul(
                        ps_qkv[pb][:],
                        w_sb[:, k * nout * 128 + o * 128:k * nout * 128 + (o + 1) * 128],
                        xn_sb[:, k * T + nn * 512:k * T + nn * 512 + 512],
                        start=(k == 0), stop=(k == KD - 1)))
                p.barrier()
                if scale is None:
                    p.op("vector", lambda e, o=o, nn=nn, pb=pb, dst_sb=dst_sb: e.tensor_copy(
                        dst_sb[:, o * T + nn * 512:o * T + nn * 512 + 512], ps_qkv[pb][:]))
                else:
                    p.op("vector", lambda e, o=o, nn=nn, pb=pb, dst_sb=dst_sb,
                         scale=scale: e.tensor_scalar_mul(
                        out=dst_sb[:, o * T + nn * 512:o * T + nn * 512 + 512],
                        in0=ps_qkv[pb][:], scalar1=scale))
        p.barrier()
    proj(wq_sb, KD, q_sb, scale=0.125)
    proj(wk_sb, 2, k_sb)
    proj(wv_sb, 2, v_sb)
    p.barrier()
    for g in range(NKV):
        vt = g // 2
        vo = (g % 2) * 64
        for blk in range(NB):
            p.op("tensor", lambda e, vt=vt, vo=vo, blk=blk: e.transpose(
                ps_t[:, 0:64] if False else ps_t[:, (blk % 8) * 64:(blk % 8) * 64 + 64],
                v_sb[vo:vo + 64, vt * T + blk * 128:vt * T + (blk + 1) * 128],
                ident[vo:vo + 64, vo:vo + 64]))
            if blk % 8 == 7 or blk == NB - 1:
                p.barrier()
                for b2 in range(blk - blk % 8, blk + 1):
                    p.op("vector", lambda e, g=g, b2=b2, blk=blk: e.tensor_copy(
                        vtok[:, b2 * NKV * HD + g * HD:b2 * NKV * HD + (g + 1) * HD],
                        ps_t[:, (b2 % 8) * 64:(b2 % 8) * 64 + 64]))
                p.barrier()
    p.barrier()
    for g in range(NKV):
        kt, ko = g // 2, (g % 2) * 64
        p.dma("sync", kd_sb[0:64, g * T:(g + 1) * T],
              k_sb[ko:ko + 64, kt * T:(kt + 1) * T])
        p.dma("gpsimd", kd_sb[64:128, g * T:(g + 1) * T],
              k_sb[ko:ko + 64, kt * T:(kt + 1) * T])
    p.barrier()

    IT = NH * NB
    p.op("tensor", lambda e: e.matmul(
        ps_s[:, 0:128], q_sb[0:64, 0:128], kd_sb[0:64, 0:128],
        start=True, stop=True))
    p.barrier()
    for i in range(IT + 2):
        if 0 <= i - 1 < IT:
            h, qb = divmod(i - 1, NB)
            j = (i - 1) % 2
            for blk in range(qb + 1):
                p.op("tensor", lambda e, j=j, blk=blk: e.transpose(
                    ps_t[:, blk * 128:(blk + 1) * 128],
                    p_buf[:, j * T + blk * 128:j * T + (blk + 1) * 128], ident[:]))
        if i < IT:
            h, qb = divmod(i, NB)
            j = i % 2
            ncol = (qb + 1) * 128
            p.op("vector", lambda e, ncol=ncol, j=j: e.reduce_max(
                nm_b[:, j:j + 1], ps_s[:, 0:ncol], axis=AX.X, negate=True))
        if 0 <= i - 2 < IT:
            h, qb = divmod(i - 2, NB)
            j = (i - 2) % 2
            qt, qo = h // 2, (h % 2) * 64
            p.op("scalar", lambda e, qt=qt, qo=qo, qb=qb, j=j: e.copy(
                ctx_sb[qo:qo + 64, qt * T + qb * 128:qt * T + (qb + 1) * 128],
                ps_qkv[j][0:64, 0:128]))
        p.barrier(lazy=True)
        if i < IT:
            h, qb = divmod(i, NB)
            j = i % 2
            ncol = (qb + 1) * 128
            p.op("scalar", lambda e, ncol=ncol, j=j: e.activation(
                out=p_buf[:, j * T:j * T + ncol], in_=ps_s[:, 0:ncol],
                func=AF.Exp, bias=nm_b[:, j:j + 1]))
        if 0 <= i - 1 < IT:
            h, qb = divmod(i - 1, NB)
            j = (i - 1) % 2
            ncol = (qb + 1) * 128
            p.op("vector", lambda e, ncol=ncol, j=j: e.tensor_copy(
                pt_buf[:, j * T:j * T + ncol], ps_t[:, 0:ncol]))
        p.barrier(lazy=True)
        if 0 <= i - 1 < IT:
            h, qb = divmod(i - 1, NB)
            j = (i - 1) % 2
            g = h // NKV
            for blk in range(qb + 1):
                p.op("tensor", lambda e, j=j, g=g, blk=blk, qb=qb: e.matmul(
                    ps_qkv[j][0:64, 0:128],
                    vtok[:, blk * NKV * HD + g * HD:blk * NKV * HD + (g + 1) * HD],
                    pt_buf[:, j * T + blk * 128:j * T + (blk + 1) * 128],
                    start=(blk == 0), stop=(blk == qb)))
        if i + 1 < IT:
            h, qb = divmod(i + 1, NB)
            qt, qo = h // 2, (h % 2) * 64
            g = h // NKV
            ncol = (qb + 1) * 128
            nmm = (ncol + 511) // 512
            for mmi in range(nmm):
                w = min(512, ncol - mmi * 512)
                p.op("tensor", lambda e, qt=qt, qo=qo, g=g, qb=qb,
                     mmi=mmi, w=w: e.matmul(
                    ps_s[:, mmi * 512:mmi * 512 + w],
                    q_sb[qo:qo + 64, qt * T + qb * 128:qt * T + (qb + 1) * 128],
                    kd_sb[qo:qo + 64, g * T + mmi * 512:g * T + mmi * 512 + w],
                    start=True, stop=True))
        if i < IT:
            h, qb = divmod(i, NB)
            j = i % 2
            ncol = (qb + 1) * 128
            p.op("vector", lambda e, j=j, qb=qb: e.tensor_mul(
                p_buf[:, j * T + qb * 128:j * T + (qb + 1) * 128],
                p_buf[:, j * T + qb * 128:j * T + (qb + 1) * 128], tri[:]))
            p.op("vector", lambda e, ncol=ncol, j=j: e.reduce_sum(
                l_b[:, j:j + 1], p_buf[:, j * T:j * T + ncol], axis=AX.X))
            p.barrier(lazy=True)
            p.op("vector", lambda e, j=j: e.reciprocal(rl_b[:, j:j + 1],
                                                       l_b[:, j:j + 1]))
            p.barrier(lazy=True)
            p.op("vector", lambda e, ncol=ncol, j=j: e.tensor_scalar_mul(
                out=p_buf[:, j * T:j * T + ncol], in0=p_buf[:, j * T:j * T + ncol],
                scalar1=rl_b[:, j:j + 1]))
        p.barrier(lazy=True)

    for o in range(KD):
        for nn in range(T // 512):
            pb = (o * (T // 512) + nn) % 2
            for k in range(KD):
                p.op("tensor", lambda e, o=o, nn=nn, k=k, pb=pb: e.matmul(
                    ps_qkv[pb][:], wo_sb[:, k * D + o * 128:k * D + (o + 1) * 128],
                    ctx_sb[:, k * T + nn * 512:k * T + nn * 512 + 512],
                    start=(k == 0), stop=(k == KD - 1)))
            p.barrier()
            p.op("vector", lambda e, pb=pb: e.tensor_copy(
                at_ev[:, pb * 512:(pb + 1) * 512], ps_qkv[pb][:]))
            p.barrier()
            p.dma("sync", attn_out[o * 128:(o + 1) * 128, nn * 512:(nn + 1) * 512],
                  at_ev[:, pb * 512:(pb + 1) * 512])
    p.barrier()
    p.pop_region()
    if standalone:
        return p.build()



def build_stageF(p=None, dr=None):
    standalone = p is None
    if standalone:
        p = Prog("F")
    _D = _mk_dram(p, dr)
    p.push_region()
    xn_in = _D("xn1_bf", [D, T], BF16, "ExternalInput")
    x1_in = _D("x1", [D, T], F32, "ExternalInput")
    attn_in = _D("attn", [D, T], F32, "ExternalInput")
    Wr_in = _D("Wr", [D, NE], F32, "ExternalInput")
    br_in = _D("br", [NE, 1], F32, "ExternalInput")
    W1_in = _D("W1", [NE, D, HF], F32, "ExternalInput")
    b1_in = _D("b1d", [NE, 128, HF // 128], F32, "ExternalInput")
    W2_in = _D("W2", [NE, HF, D], F32, "ExternalInput")
    b2_in = _D("b2d", [NE, 128, D // 128], F32, "ExternalInput")
    out_T = _D("outT", [D, T], F32, "ExternalOutput")
    w_dram = p.dram("w_scratch", [NE, T], F32, "Internal")
    acc_dram = p.dram("acc_scratch", [D, T], F32, "Internal")

    NB = T // 128
    NO1 = HF // 128
    xn_sb = p.sb("xn_sbF", [128, KD * T], BF16)
    wr_sb = p.sb("wr_sbF", [128, KD * NE], BF16)
    br_sb = p.sb("br_sbF", [NE, 1], F32)
    w1_sb = p.sb("w1_sbF", [128, KD * HF], BF16)
    w2_sb = p.sb("w2_sbF", [128, NO1 * D], BF16)
    b1_sb = p.sb("b1_sbF", [128, NE * NO1], F32)
    b2_sb = p.sb("b2_sbF", [128, NE * KD], F32)
    io = p.sb("ioF", [128, 128], F32)
    pcol = p.sb("pcolF", [128, 1], F32)
    id32 = p.sb("id32F", [128, 128], F32)
    lg_sb = p.sb("lg_sbF", [128, T], F32)
    lgt = p.sb("lgtF", [128, 2 * NE], F32)
    nm1 = p.sb("nm1F", [128, 1], F32)
    e1 = p.sb("e1F", [128, NE], F32)
    t1 = p.sb("t1F", [128, NE], F32)
    m2 = p.sb("m2F", [128, 1], F32)
    pe = p.sb("peF", [128, NE], F32)
    msk = p.sb("mskF", [128, NE], F32)
    zz = p.sb("zzF", [128, 1], F32)
    rz = p.sb("rzF", [128, 1], F32)
    w_tok = p.sb("w_tokF", [128, NE], F32)
    wb_e = p.sb("wb_eF", [128, TC], F32)
    h_sb = p.sb("h_sbF", [128, NO1 * 512], BF16)


    p.push_region()
    ps_r = p.ps("ps_rF", [NE, 512])
    ps_t8 = p.ps("ps_t8F", [128, NE])
    ps_w8 = p.ps("ps_w8F", [NE, 128])

    for k in range(KD):
        p.dma("sync", xn_sb[:, k * T:(k + 1) * T], xn_in[k * 128:(k + 1) * 128, :])
    p.dma("gpsimd", wr_sb.ap().rearrange("p (k j) -> p k j", k=KD),
          Wr_in.rearrange("(k p) j -> p k j", p=128))
    p.dma("sync", br_sb[:], br_in)
    p.dma("sync", b1_sb.ap().rearrange("p (e o) -> p e o", e=NE),
          b1_in.transpose((1, 0, 2)))
    p.dma("sync", b2_sb.ap().rearrange("p (e o) -> p e o", e=NE),
          b2_in.transpose((1, 0, 2)))
    p.op("gpsimd", lambda e: e.iota(io[:], [[1, 128]], channel_multiplier=0,
                                    allow_small_or_imprecise_dtypes=True))
    p.op("gpsimd", lambda e: e.iota(pcol[:], [[0, 1]], channel_multiplier=1,
                                    allow_small_or_imprecise_dtypes=True))
    p.barrier()
    p.op("vector", lambda e: e.tensor_single_scalar(out=id32[:], in_=io[:],
                                                    scalar=pcol[:], op=ALU.is_equal))
    p.barrier()

    for c in range(NCH):
        for k in range(KD):
            p.op("tensor", lambda e, k=k, c=c: e.matmul(
                ps_r[:], wr_sb[:, k * NE:(k + 1) * NE],
                xn_sb[:, k * T + c * TC:k * T + (c + 1) * TC],
                start=(k == 0), stop=(k == KD - 1)))
        p.barrier()
        p.op("vector", lambda e, c=c: e.tensor_scalar_add(
            out=lg_sb[0:NE, c * TC:(c + 1) * TC], in0=ps_r[:], scalar1=br_sb[:]))
        p.barrier()
    for blk in range(NB):
        pb = blk % 2
        p.op("tensor", lambda e, blk=blk: e.transpose(
            ps_t8[:], lg_sb[0:NE, blk * 128:(blk + 1) * 128], id32[0:NE, 0:NE]))
        p.barrier()
        p.op("vector", lambda e, pb=pb: e.tensor_copy(lgt[:, pb * NE:(pb + 1) * NE],
                                                      ps_t8[:]))
        p.barrier()
        sl = slice(pb * NE, (pb + 1) * NE)
        p.op("vector", lambda e, sl=sl: e.reduce_max(nm1[:], lgt[:, sl], axis=AX.X,
                                                     negate=True))
        p.barrier()
        p.op("vector", lambda e, sl=sl: e.tensor_scalar(
            out=e1[:], in0=lgt[:, sl], scalar1=nm1[:], scalar2=None,
            op0=ALU.add, op1=ALU.bypass))
        p.barrier()
        p.op("vector", lambda e: e.tensor_single_scalar(out=e1[:], in_=e1[:],
                                                        scalar=0.0, op=ALU.is_equal))
        p.barrier()
        p.op("vector", lambda e, sl=sl: e.scalar_tensor_tensor(
            out=t1[:], in0=e1[:], scalar=-1e9, in1=lgt[:, sl],
            op0=ALU.mult, op1=ALU.add))
        p.barrier()
        p.op("vector", lambda e: e.reduce_max(m2[:], t1[:], axis=AX.X))
        p.op("scalar", lambda e, sl=sl: e.activation(out=pe[:], in_=lgt[:, sl],
                                                     func=AF.Exp, bias=nm1[:]))
        p.barrier()
        p.op("vector", lambda e, sl=sl: e.tensor_single_scalar(
            out=msk[:], in_=lgt[:, sl], scalar=m2[:], op=ALU.is_ge))
        p.barrier()
        p.op("vector", lambda e: e.tensor_mul(pe[:], pe[:], msk[:]))
        p.barrier()
        p.op("vector", lambda e: e.reduce_sum(zz[:], pe[:], axis=AX.X))
        p.barrier()
        p.op("vector", lambda e: e.reciprocal(rz[:], zz[:]))
        p.barrier()
        p.op("vector", lambda e: e.tensor_scalar_mul(out=w_tok[:], in0=pe[:],
                                                     scalar1=rz[:]))
        p.barrier()
        p.op("tensor", lambda e: e.transpose(ps_w8[:], w_tok[:], id32[:]))
        p.barrier()
        p.op("vector", lambda e, blk=blk: e.tensor_copy(
            lg_sb[0:NE, blk * 128:(blk + 1) * 128], ps_w8[:]))
        p.barrier()
    p.dma("sync", w_dram, lg_sb[0:NE, :])
    p.barrier()
    p.pop_region()
    ps_m = [p.ps(f"ps_Fm{i}", [128, 512]) for i in range(8)]

    t2p = lg_sb[:, 0:1536]
    for k in range(KD):
        p.dma("gpsimd", w1_sb[:, k * HF:(k + 1) * HF], W1_in[0, k * 128:(k + 1) * 128, :])
    p.dma("gpsimd", w2_sb.ap().rearrange("p (k dd) -> p k dd", k=NO1),
          W2_in[0].rearrange("(k p) dd -> p k dd", p=128))
    p.barrier()
    p.eng_lag = {}
    NP1 = NO1 // 4
    NG = NP1 + 1 + KD + 2
    for ex in range(NE):
        for c in range(NCH):
            csl = slice(c * TC, (c + 1) * TC)
            p.dma("gpsimd", wb_e[:], w_dram[ex, csl].partition_broadcast(128))
            for g in range(NG):
                if g == NP1 + 1 and c == NCH - 1 and ex + 1 < NE:
                    for k in range(KD):
                        p.dma("gpsimd", w1_sb[:, k * HF:(k + 1) * HF],
                              W1_in[ex + 1, k * 128:(k + 1) * 128, :])
                if g == 0 and c == 0 and ex >= 1:
                    p.dma("gpsimd", w2_sb.ap().rearrange("p (k dd) -> p k dd", k=NO1),
                          W2_in[ex].rearrange("(k p) dd -> p k dd", p=128))
                if g < NP1:
                    for oo in range(4):
                        o = g * 4 + oo
                        pb = (g % 2) * 4 + oo
                        for k in range(KD):
                            p.op("tensor", lambda e, o=o, k=k, pb=pb, c=c: e.matmul(
                                ps_m[pb][:],
                                w1_sb[:, k * HF + o * 128:k * HF + (o + 1) * 128],
                                xn_sb[:, k * T + c * TC:k * T + (c + 1) * TC],
                                start=(k == 0), stop=(k == KD - 1)))
                elif NP1 + 1 <= g < NP1 + 1 + KD:
                    dd = g - NP1 - 1
                    for k in range(NO1):
                        p.op("tensor", lambda e, dd=dd, k=k: e.matmul(
                            ps_m[dd][:], w2_sb[:, k * D + dd * 128:k * D + (dd + 1) * 128],
                            h_sb[:, k * TC:(k + 1) * TC],
                            start=(k == 0), stop=(k == NO1 - 1)))
                gm = g - 1
                if 0 <= gm < NP1:
                    for oo in range(4):
                        o = gm * 4 + oo
                        pb = (gm % 2) * 4 + oo
                        if oo % 2 == 0:
                            p.op("vector", lambda e, o=o, pb=pb, ex=ex: e.tensor_scalar(
                                out=h_sb[:, o * TC:(o + 1) * TC], in0=ps_m[pb][:],
                                scalar1=b1_sb[:, ex * NO1 + o:ex * NO1 + o + 1],
                                scalar2=0.0, op0=ALU.add, op1=ALU.max))
                        else:
                            p.op("scalar", lambda e, o=o, pb=pb, ex=ex: e.activation(
                                out=h_sb[:, o * TC:(o + 1) * TC], in_=ps_m[pb][:],
                                func=AF.Relu, bias=b1_sb[:, ex * NO1 + o:ex * NO1 + o + 1]))
                elif NP1 + 1 <= gm < NP1 + 1 + KD:
                    dd = gm - NP1 - 1
                    p3 = dd % 3
                    p.op("vector", lambda e, dd=dd, p3=p3, ex=ex: e.scalar_tensor_tensor(
                        out=t2p[:, p3 * 512:(p3 + 1) * 512], in0=ps_m[dd][:],
                        scalar=b2_sb[:, ex * KD + dd:ex * KD + dd + 1],
                        in1=wb_e[:], op0=ALU.add, op1=ALU.mult))
                gm2 = g - 2
                if NP1 + 1 <= gm2 < NP1 + 1 + KD:
                    dd = gm2 - NP1 - 1
                    p3 = dd % 3
                    if ex == 0:
                        p.dma("gpsimd", acc_dram[dd * 128:(dd + 1) * 128, csl],
                              t2p[:, p3 * 512:(p3 + 1) * 512])
                    else:
                        def accdma(e, dd=dd, csl=csl, p3=p3):
                            return e.dma_start(
                                out=acc_dram[dd * 128:(dd + 1) * 128, csl],
                                in_=t2p[:, p3 * 512:(p3 + 1) * 512], accum_op=ALU.add)
                        cell = ["dma"]
                        p._q("gpsimd", ("call", accdma, cell))
                        p.tot_dma += 16
                p.barrier(lazy=True)
    p.eng_lag = {}
    res_sb = h_sb[:, 0:2048].bitcast(F32)
    out_sb = h_sb[:, 2048:4096].bitcast(F32)
    acc_sb = h_sb[:, 4096:6144].bitcast(F32)
    for dd in range(KD):
        for c in range(NCH):
            pb = c % 2
            csl = slice(c * TC, (c + 1) * TC)
            rows = slice(dd * 128, (dd + 1) * 128)
            rsl = res_sb[:, pb * TC:(pb + 1) * TC]
            osl = out_sb[:, pb * TC:(pb + 1) * TC]
            asl = acc_sb[:, pb * TC:(pb + 1) * TC]
            p.dma("sync", rsl, x1_in[rows, csl])
            p.dma("sync", osl, attn_in[rows, csl])
            p.dma("sync", asl, acc_dram[rows, csl])
            p.barrier()
            p.op("vector", lambda e, rsl=rsl, osl=osl: e.tensor_add(rsl, rsl, osl))
            p.barrier()
            p.op("vector", lambda e, rsl=rsl, osl=osl, asl=asl: e.tensor_add(
                osl, rsl, asl))
            p.barrier()
            p.dma("sync", out_T[rows, csl], osl)
    p.barrier()
    p.pop_region()
    if standalone:
        return p.build()

def _t(a):
    return np.ascontiguousarray(np.asarray(a).T)


def build_fused():
    p = Prog("fused")
    EI = lambda n, s, d: p.dram(n, list(s), d, "ExternalInput")
    IN = lambda n, s, d: p.dram(n, list(s), d, "Internal")
    xT = EI("xT", [D, T], F32)
    w_inT = EI("w_inT", [D, 2 * DIN], F32)
    conv_w = EI("conv_w", [DIN, 4], F32)
    conv_b = EI("conv_b", [128, NDI], F32)
    xp_wT = EI("xp_wT", [DIN, 96], F32)
    dt_wT = EI("dt_wT", [DTR, DIN], F32)
    dt_b = EI("dt_b", [128, NDI], F32)
    A_dev = EI("A_dev", [128, NDI * NST], F32)
    D_dev = EI("D_dev", [128, NDI], F32)
    outw_T = EI("outw_T", [DIN, D], F32)
    wrT = EI("wrT", [D, D], F32)
    wkT_d = EI("wkT_d", [D, D], F32)
    wvT_d = EI("wvT_d", [D, D], F32)
    mu_r = EI("mu_r", [128, KD], F32)
    mu_k = EI("mu_k", [128, KD], F32)
    wqT = EI("wqT", [D, D], F32)
    wkT_e = EI("wkT_e", [D, NKV * HD], F32)
    wvT_e = EI("wvT_e", [D, NKV * HD], F32)
    woT = EI("woT", [D, D], F32)
    Wr = EI("Wr", [D, NE], F32)
    br = EI("br", [NE, 1], F32)
    W1 = EI("W1", [NE, D, HF], F32)
    b1d = EI("b1d", [NE, 128, HF // 128], F32)
    W2 = EI("W2", [NE, HF, D], F32)
    b2d = EI("b2d", [NE, 128, D // 128], F32)
    outT = p.dram("outT", [D, T], F32, "ExternalOutput")

    u_bf = IN("u_bf", [DIN, T], BF16)
    sz_bf = IN("sz_bf", [DIN, T], BF16)
    xn_bf = IN("xn_bf", [D, T], BF16)
    delta = IN("delta", [DIN, T], F32)
    du = IN("du", [DIN, T], F16)
    bc = IN("bc", [2 * NST, T], F32)
    mamba = IN("mamba", [D, T], F32)
    x1 = IN("x1", [D, T], F32)
    xn1_bf = IN("xn1_bf", [D, T], BF16)
    attn = IN("attn", [D, T], F32)

    build_stageA(p, dict(xT=xT, w_inT=w_inT, conv_w=conv_w, conv_b=conv_b,
                         u_bf=u_bf, sz_bf=sz_bf, xn_bf=xn_bf))
    build_stageB(p, dict(u_bf=u_bf, xp_wT=xp_wT, dt_wT=dt_wT, dt_b=dt_b,
                         delta=delta, du=du, bc=bc))
    build_stageC(p, dict(delta=delta, du=du, u_bf=u_bf, sz_bf=sz_bf, bc=bc,
                         A_dev=A_dev, D_dev=D_dev, outw_T=outw_T, mamba=mamba))
    build_stageD(p, dict(xT=xT, xn_bf=xn_bf, mamba=mamba, wrT=wrT, wkT=wkT_d,
                         wvT=wvT_d, mu_r=mu_r, mu_k=mu_k, x1=x1, xn1_bf=xn1_bf))
    build_stageE(p, dict(xn1_bf=xn1_bf, wqT=wqT, wkT=wkT_e, wvT=wvT_e, woT=woT,
                         attn=attn))
    build_stageF(p, dict(xn1_bf=xn1_bf, x1=x1, attn=attn, Wr=Wr, br=br, W1=W1,
                         b1d=b1d, W2=W2, b2d=b2d, outT=outT))
    return p.build()


def kernel(**ins):
    cores = list(range(B))
    shared = {
        "w_inT": _t(ins["L0_attn_in_proj_w"]),
        "conv_w": np.ascontiguousarray(np.asarray(ins["L0_attn_conv_w"])[:, 0, :]),
        "conv_b": np.ascontiguousarray(np.asarray(ins["L0_attn_conv_b"]).reshape(NDI, 128).T),
        "xp_wT": _t(ins["L0_attn_x_proj_w"]),
        "dt_wT": _t(ins["L0_attn_dt_proj_w"]),
        "dt_b": np.ascontiguousarray(np.asarray(ins["L0_attn_dt_proj_b"]).reshape(NDI, 128).T),
        "A_dev": np.ascontiguousarray(np.asarray(ins["L0_attn_A_log"])
                                      .reshape(NDI, 128, NST).transpose(1, 0, 2)
                                      .reshape(128, NDI * NST)),
        "D_dev": np.ascontiguousarray(np.asarray(ins["L0_attn_D"]).reshape(NDI, 128).T),
        "outw_T": _t(ins["L0_attn_out_proj_w"]),
        "wrT": _t(ins["L0_ffn_W_r"]),
        "wkT_d": _t(ins["L0_ffn_W_k"]),
        "wvT_d": _t(ins["L0_ffn_W_v"]),
        "mu_r": np.ascontiguousarray(np.asarray(ins["L0_ffn_mu_r"]).reshape(KD, 128).T),
        "mu_k": np.ascontiguousarray(np.asarray(ins["L0_ffn_mu_k"]).reshape(KD, 128).T),
        "wqT": _t(ins["L1_attn_wq"]),
        "wkT_e": _t(ins["L1_attn_wk"]),
        "wvT_e": _t(ins["L1_attn_wv"]),
        "woT": _t(ins["L1_attn_wo"]),
        "Wr": np.ascontiguousarray(np.asarray(ins["L1_ffn_Wr"], np.float32)),
        "br": np.ascontiguousarray(np.asarray(ins["L1_ffn_br"], np.float32)[:, None]),
        "W1": np.ascontiguousarray(np.asarray(ins["L1_ffn_W1"], np.float32)),
        "b1d": np.ascontiguousarray(np.asarray(ins["L1_ffn_b1"], np.float32)
                                    .reshape(NE, HF // 128, 128).transpose(0, 2, 1)),
        "W2": np.ascontiguousarray(np.asarray(ins["L1_ffn_W2"], np.float32)),
        "b2d": np.ascontiguousarray(np.asarray(ins["L1_ffn_b2"], np.float32)
                                    .reshape(NE, D // 128, 128).transpose(0, 2, 1)),
    }
    maps = [dict(shared, xT=np.ascontiguousarray(
        np.asarray(ins["L0_attn_hidden_states"][b], np.float32).T)) for b in range(B)]
    nc = build_fused()
    res = run_bass_kernel_spmd(nc, maps, core_ids=cores)
    out = np.stack([np.asarray(res.results[b]["outT"]).T for b in range(B)], 0)
    return np.ascontiguousarray(out.astype(np.float32))


STAGE_BUILDERS = {"A": build_stageA, "B": build_stageB, "C": build_stageC, "D": build_stageD, "E": build_stageE, "F": build_stageF}


# version 07
# speedup vs baseline: 1.1352x; 1.1352x over previous
import contextlib
import numpy as np
import concourse.bass as bass
import concourse.mybir as mybir
from concourse.bass_utils import run_bass_kernel_spmd

F32 = mybir.dt.float32
BF16 = mybir.dt.bfloat16
F16 = mybir.dt.float16
AF = mybir.ActivationFunctionType
ALU = mybir.AluOpType
AX = mybir.AxisListType

ENGS = ("sync", "gpsimd", "vector", "scalar", "tensor")

B, T, D = 8, 2048, 1024
DIN = 2048
NST = 16
DTR = 64
NH, NKV, HD = 16, 4, 64
NE, HF = 8, 4096
TC = 512
NCH = T // TC
KD = D // 128
NDI = DIN // 128


class Prog:

    def __init__(self, name):
        self.nc = bass.Bass("TRN2", target_bir_lowering=False)
        self.name = name
        self.queues = {e: [] for e in ENGS}
        self.tot = {e: 0 for e in ENGS}
        self.tot_dma = 0
        self.pending = {e: None for e in ENGS}
        self.cur_snapshot = ({e: 0 for e in ENGS}, 0, {}, [])
        self.waited = {e: None for e in ENGS}
        self.stack = contextlib.ExitStack()
        self._stacks = [self.stack]
        self._names = {}
        self._dma_hist = [0]
        self.tot_dma1 = 0
        self._tot_hist = [dict(self.tot)]
        self.eng_lag = {}

    def _uniq(self, name):
        n = self._names.get(name, 0)
        self._names[name] = n + 1
        return name if n == 0 else f"{name}_{n}"

    def push_region(self):
        self._stacks.append(contextlib.ExitStack())

    def pop_region(self):
        self.barrier()
        self._stacks.pop().close()

    def dram(self, name, shape, dtype, kind):
        return self.nc.dram_tensor(name, list(shape), dtype, kind=kind).ap()

    def sb(self, name, shape, dtype):
        return self._stacks[-1].enter_context(
            self.nc.sbuf_tensor(self._uniq(name), list(shape), dtype))

    def ps(self, name, shape, dtype=F32):
        return self._stacks[-1].enter_context(
            self.nc.psum_tensor(self._uniq(name), list(shape), dtype))

    def _q(self, eng, item):
        snap = self.cur_snapshot
        if self.waited[eng] is not snap:
            self.queues[eng].append(("wait", snap, eng))
            self.waited[eng] = snap
        self.queues[eng].append(item)

    def op(self, eng, fn):
        cell = [0]
        self._q(eng, ("call", fn, cell))
        self.pending[eng] = cell

    def dma(self, eng, out, in_, lane=0):
        if lane == 0:
            cell = ["dma"]
            self._q(eng, ("call", lambda e: e.dma_start(out=out, in_=in_), cell))
            self.tot_dma += 16
        else:
            cell = ["dma1"]
            self._q(eng, ("call", lambda e: e.dma_start(out=out, in_=in_), cell))
            self.tot_dma1 += 16

    def fence_lane1(self, eng):
        v = self.tot_dma1
        if v > 0:
            self._q(eng, ("wait1", v, eng))

    def barrier(self, lazy=False):
        for e in ENGS:
            c = self.pending[e]
            if c is not None:
                c[0] = 1
                self.tot[e] += 1
                self.pending[e] = None
        dval = self._dma_hist[-1] if lazy else self.tot_dma
        self._dma_hist.append(self.tot_dma)
        self._tot_hist.append(dict(self.tot))
        self.cur_snapshot = (dict(self.tot), dval, dict(self.eng_lag),
                             [dict(h) for h in self._tot_hist[-4:]])

    def build(self):
        self.barrier()
        self.queues["sync"].append(("wait", self.cur_snapshot, "__final__"))
        if self.tot_dma1 > 0:
            self.queues["sync"].append(("wait1", self.tot_dma1, "__final__"))
        nc = self.nc
        bars = {e: self.stack.enter_context(nc.semaphore(f"bar_{e}")) for e in ENGS}
        bar_dma = self.stack.enter_context(nc.semaphore("bar_dma"))
        bar_dma1 = self.stack.enter_context(nc.semaphore("bar_dma1"))
        with nc.Block() as block:
            def emit_stream(eng_name):
                def _(e):
                    for item in self.queues[eng_name]:
                        if item[0] == "wait1":
                            e.wait_ge(bar_dma1, item[1])
                            continue
                        if item[0] == "wait":
                            snap, me = item[1], item[2]
                            counts, dmas, lags, hist = snap
                            mylag = lags.get(me, {})
                            for oe in ENGS:
                                v = counts[oe]
                                lag = mylag.get(oe, 0)
                                if lag > 0 and len(hist) > lag:
                                    v = hist[-1 - lag][oe]
                                if v > 0:
                                    e.wait_ge(bars[oe], v)
                            if dmas > 0:
                                e.wait_ge(bar_dma, dmas)
                        else:
                            _, fn, cell = item
                            inst = fn(e)
                            if cell[0] == "dma":
                                inst.then_inc(bar_dma, 16)
                            elif cell[0] == "dma1":
                                inst.then_inc(bar_dma1, 16)
                            elif cell[0] == 1:
                                inst.then_inc(bars[eng_name], 1)
                return _
            block.sync(emit_stream("sync"))
            block.gpsimd(emit_stream("gpsimd"))
            block.vector(emit_stream("vector"))
            block.scalar(emit_stream("scalar"))
            block.tensor(emit_stream("tensor"))
        self.stack.close()
        return nc


def _mk_dram(p, dr):
    def _D(name, shape, dtype, kind):
        if dr is not None:
            return dr[name]
        return p.dram(name, shape, dtype, kind)
    return _D


def build_stageA(p=None, dr=None):
    standalone = p is None
    if standalone:
        p = Prog("A")
    _D = _mk_dram(p, dr)
    p.push_region()
    xT = _D("xT", [D, T], F32, "ExternalInput")
    w_inT = _D("w_inT", [D, 2 * DIN], F32, "ExternalInput")
    conv_w = _D("conv_w", [DIN, 4], F32, "ExternalInput")
    conv_b = _D("conv_b", [128, NDI], F32, "ExternalInput")
    u_out = _D("u_bf", [DIN, T], BF16, "ExternalOutput")
    sz_out = _D("sz_bf", [DIN, T], BF16, "ExternalOutput")
    xn_out = _D("xn_bf", [D, T], BF16, "ExternalOutput")

    w_sb = p.sb("w_sb", [128, KD * 2 * DIN], BF16)
    cw_sb = p.sb("cw_sb", [128, NDI * 4], F32)
    cb_sb = p.sb("cb_sb", [128, NDI], F32)
    ones_col = p.sb("ones_col", [128, 1], BF16)
    ones_row = p.sb("ones_row", [1, 128], BF16)
    eps_sb = p.sb("eps_sb", [1, 1], F32)

    x_sb = p.sb("x_sb", [128, KD * TC], F32)
    xb_sb = p.sb("xb_sb", [128, KD * TC], BF16)
    xsq_sb = p.sb("xsq_sb", [128, KD * TC], BF16)
    xn_sb = p.sb("xn_sb", [128, KD * TC], BF16)
    tmp_sb = p.sb("tmp_sb", [128, TC], BF16)
    mrow = p.sb("mrow", [1, TC], F32)
    vrow = p.sb("vrow", [1, TC], F32)
    msq = p.sb("msq", [1, TC], F32)
    srow = p.sb("srow", [1, TC], F32)
    rrow = p.sb("rrow", [1, TC], F32)
    m_bf = p.sb("m_bf", [1, TC], BF16)
    r_bf = p.sb("r_bf", [1, TC], BF16)
    mb_sb = p.sb("mb_sb", [128, TC], BF16)
    rb_sb = p.sb("rb_sb", [128, TC], BF16)
    xc_sb = p.sb("xc_sb", [128, NDI * (TC + 3)], F32)
    cacc = p.sb("cacc", [128, 4 * TC], F32)
    u_sb = p.sb("u_sb", [128, NDI * TC], BF16)
    sz_sb = p.sb("sz_sb", [128, NDI * TC], BF16)

    ps_sx = p.ps("ps_sx", [1, TC])
    ps_sxx = p.ps("ps_sxx", [1, TC])
    ps_mb = p.ps("ps_mb", [128, TC])
    ps_rb = p.ps("ps_rb", [128, TC])
    ps_xz = [p.ps(f"ps_xz{i}", [128, TC]) for i in range(4)]

    for k in range(KD):
        p.dma("gpsimd", w_sb[:, k * 2 * DIN:(k + 1) * 2 * DIN],
              w_inT[k * 128:(k + 1) * 128, :])
    p.dma("sync", cw_sb.ap().rearrange("p (o c) -> p o c", c=4),
          conv_w.rearrange("(o p) c -> p o c", p=128))
    p.dma("sync", cb_sb[:], conv_b)
    p.op("vector", lambda e: e.memset(eps_sb[:], 1e-5))
    p.op("vector", lambda e: e.memset(ones_col[:], 1.0))
    p.op("vector", lambda e: e.memset(ones_row[:], 1.0))
    for o in range(NDI):
        p.op("vector", lambda e, o=o: e.memset(
            xc_sb[:, o * (TC + 3):o * (TC + 3) + 3], 0.0))
    p.barrier()

    for c in range(NCH):
        csl = slice(c * TC, (c + 1) * TC)
        src = xT[:, csl].rearrange("(k p) t -> p k t", p=128)
        p.dma("sync", x_sb.ap().rearrange("p (k t) -> p k t", k=KD), src)
        p.dma("gpsimd", xb_sb.ap().rearrange("p (k t) -> p k t", k=KD), src)
        p.barrier()
        p.op("scalar", lambda e: e.activation(out=xsq_sb[:], in_=x_sb[:],
                                              func=AF.Square))
        p.barrier()
        for k in range(KD):
            p.op("tensor", lambda e, k=k: e.matmul(
                ps_sx[:], ones_col[:], xb_sb[:, k * TC:(k + 1) * TC],
                start=(k == 0), stop=(k == KD - 1)))
        for k in range(KD):
            p.op("tensor", lambda e, k=k: e.matmul(
                ps_sxx[:], ones_col[:], xsq_sb[:, k * TC:(k + 1) * TC],
                start=(k == 0), stop=(k == KD - 1)))
        p.barrier()
        p.op("vector", lambda e: e.tensor_scalar_mul(out=mrow[:], in0=ps_sx[:],
                                                     scalar1=1.0 / D))
        p.op("vector", lambda e: e.tensor_scalar_mul(out=vrow[:], in0=ps_sxx[:],
                                                     scalar1=1.0 / D))
        p.op("vector", lambda e: e.tensor_mul(msq[:], mrow[:], mrow[:]))
        p.op("vector", lambda e: e.tensor_sub(vrow[:], vrow[:], msq[:]))
        p.barrier()
        p.op("scalar", lambda e: e.activation(out=srow[:], in_=vrow[:],
                                              func=AF.Sqrt, bias=eps_sb[:]))
        p.barrier()
        p.op("vector", lambda e: e.reciprocal(rrow[:], srow[:]))
        p.barrier()
        p.op("vector", lambda e: e.tensor_copy(m_bf[:], mrow[:]))
        p.op("vector", lambda e: e.tensor_copy(r_bf[:], rrow[:]))
        p.barrier()
        p.op("tensor", lambda e: e.matmul(ps_mb[:], ones_row[:], m_bf[:],
                                          start=True, stop=True))
        p.op("tensor", lambda e: e.matmul(ps_rb[:], ones_row[:], r_bf[:],
                                          start=True, stop=True))
        p.barrier()
        p.op("vector", lambda e: e.tensor_copy(mb_sb[:], ps_mb[:]))
        p.op("vector", lambda e: e.tensor_copy(rb_sb[:], ps_rb[:]))
        p.barrier()
        for k in range(KD):
            p.op("vector", lambda e, k=k: e.tensor_sub(
                tmp_sb[:], xb_sb[:, k * TC:(k + 1) * TC], mb_sb[:]))
            p.op("vector", lambda e, k=k: e.tensor_mul(
                xn_sb[:, k * TC:(k + 1) * TC], tmp_sb[:], rb_sb[:]))
        p.barrier()
        p.dma("gpsimd", xn_out[:, csl].rearrange("(k p) t -> p k t", p=128),
              xn_sb.ap().rearrange("p (k t) -> p k t", k=KD))
        p.eng_lag = {"tensor": {"vector": 2, "scalar": 2}}
        for g in range(35):
            if g < 32:
                o, pbx = g, g % 4
                for k in range(KD):
                    p.op("tensor", lambda e, o=o, pbx=pbx, k=k: e.matmul(
                        ps_xz[pbx][:],
                        w_sb[:, k * 2 * DIN + o * 128:k * 2 * DIN + (o + 1) * 128],
                        xn_sb[:, k * TC:(k + 1) * TC],
                        start=(k == 0), stop=(k == KD - 1)))
            gm = g - 1
            if 0 <= gm < 32:
                o, pbx = gm, gm % 4
                if o < 16:
                    base = o * (TC + 3)
                    p.op("vector", lambda e, pbx=pbx, base=base: e.tensor_copy(
                        xc_sb[:, base + 3:base + 3 + TC], ps_xz[pbx][:]))
                else:
                    zi = o - 16
                    p.op("scalar", lambda e, zi=zi, pbx=pbx: e.activation(
                        out=sz_sb[:, zi * TC:(zi + 1) * TC],
                        in_=ps_xz[pbx][:], func=AF.Silu))
            gm2 = g - 2
            if 0 <= gm2 < 16:
                o = gm2
                base = o * (TC + 3)
                w4 = cw_sb[:, o * 4:o * 4 + 4]
                cs = cacc[:, (gm2 % 4) * TC:(gm2 % 4 + 1) * TC]
                p.op("vector", lambda e, base=base, w4=w4, cs=cs: e.tensor_scalar_mul(
                    out=cs, in0=xc_sb[:, base + 3:base + 3 + TC],
                    scalar1=w4[:, 3:4]))
                for j in (2, 1, 0):
                    p.op("vector", lambda e, base=base, w4=w4, cs=cs, j=j:
                         e.scalar_tensor_tensor(
                             out=cs, in0=xc_sb[:, base + j:base + j + TC],
                             scalar=w4[:, j:j + 1], in1=cs,
                             op0=ALU.mult, op1=ALU.add))
                p.op("vector", lambda e, base=base: e.tensor_copy(
                    xc_sb[:, base:base + 3],
                    xc_sb[:, base + TC:base + TC + 3]))
            gm3 = g - 3
            if 0 <= gm3 < 16:
                o = gm3
                p.op("scalar", lambda e, o=o, gm3=gm3: e.activation(
                    out=u_sb[:, o * TC:(o + 1) * TC],
                    in_=cacc[:, (gm3 % 4) * TC:(gm3 % 4 + 1) * TC],
                    func=AF.Silu, bias=cb_sb[:, o:o + 1]))
            p.barrier(lazy=True)
        p.eng_lag = {}
        p.dma("gpsimd", u_out[:, csl].rearrange("(k p) t -> p k t", p=128),
              u_sb.ap().rearrange("p (k t) -> p k t", k=NDI))
        p.dma("gpsimd", sz_out[:, csl].rearrange("(k p) t -> p k t", p=128),
              sz_sb.ap().rearrange("p (k t) -> p k t", k=NDI))
        p.barrier()
    p.pop_region()
    if standalone:
        return p.build()


def build_stageB(p=None, dr=None):
    standalone = p is None
    if standalone:
        p = Prog("B")
    _D = _mk_dram(p, dr)
    p.push_region()
    u_in = _D("u_bf", [DIN, T], BF16, "ExternalInput")
    xp_wT = _D("xp_wT", [DIN, 96], F32, "ExternalInput")
    dt_wT = _D("dt_wT", [DTR, DIN], F32, "ExternalInput")
    dt_b = _D("dt_b", [128, NDI], F32, "ExternalInput")
    delta_out = _D("delta", [DIN, T], F32, "ExternalOutput")
    du_out = _D("du", [DIN, T], F16, "ExternalOutput")
    bc_out = _D("bc", [2 * NST, T], F32, "ExternalOutput")

    u_sb = p.sb("u_sb", [128, NDI * T], BF16)
    xpw_sb = p.sb("xpw_sb", [128, NDI * 96], BF16)
    dtw_sb = p.sb("dtw_sb", [DTR, DIN], BF16)
    dtb_sb = p.sb("dtb_sb", [128, NDI], F32)
    dt_sb = p.sb("dt_sb", [DTR, T], BF16)
    bc_sb = p.sb("bc_sb", [96, T], F32)
    delta_sb = p.sb("delta_sb", [128, 2 * TC], F32)
    du_sb = p.sb("du_sb", [128, 2 * TC], F16)

    ps_xd = p.ps("ps_xd", [96, TC])
    ps_dt = [p.ps(f"ps_dt{i}", [128, TC]) for i in range(2)]

    for k in range(NDI):
        p.dma("gpsimd", u_sb[:, k * T:(k + 1) * T], u_in[k * 128:(k + 1) * 128, :])
    p.dma("gpsimd", xpw_sb.ap().rearrange("p (k j) -> p k j", k=NDI),
          xp_wT.rearrange("(k p) j -> p k j", p=128))
    p.dma("gpsimd", dtw_sb[:], dt_wT)
    p.dma("sync", dtb_sb[:], dt_b)
    p.barrier()

    for c in range(NCH):
        csl = slice(c * TC, (c + 1) * TC)
        for k in range(NDI):
            p.op("tensor", lambda e, k=k, c=c: e.matmul(
                ps_xd[:], xpw_sb[:, k * 96:(k + 1) * 96],
                u_sb[:, k * T + c * TC:k * T + (c + 1) * TC],
                start=(k == 0), stop=(k == NDI - 1)))
        p.barrier()
        p.op("vector", lambda e, csl=csl: e.tensor_copy(dt_sb[:, csl], ps_xd[0:64, :]))
        p.op("vector", lambda e, csl=csl: e.tensor_copy(bc_sb[64:96, csl], ps_xd[64:96, :]))
        p.barrier()
        for mm in range(NDI):
            pb = mm % 2
            p.op("tensor", lambda e, mm=mm, pb=pb, csl=csl: e.matmul(
                ps_dt[pb][:], dtw_sb[:, mm * 128:(mm + 1) * 128],
                dt_sb[:, csl], start=True, stop=True))
            p.barrier()
            p.op("scalar", lambda e, mm=mm, pb=pb: e.activation(
                out=delta_sb[:, pb * TC:(pb + 1) * TC], in_=ps_dt[pb][:],
                func=AF.Exp, bias=dtb_sb[:, mm:mm + 1]))
            p.op("scalar", lambda e, pb=pb: e.activation(
                out=delta_sb[:, pb * TC:(pb + 1) * TC],
                in_=delta_sb[:, pb * TC:(pb + 1) * TC], func=AF.Ln, bias=1.0))
            p.barrier()
            p.op("vector", lambda e, mm=mm, pb=pb, c=c: e.tensor_mul(
                du_sb[:, pb * TC:(pb + 1) * TC],
                delta_sb[:, pb * TC:(pb + 1) * TC],
                u_sb[:, mm * T + c * TC:mm * T + (c + 1) * TC]))
            p.barrier()
            p.dma("sync", delta_out[mm * 128:(mm + 1) * 128, csl],
                  delta_sb[:, pb * TC:(pb + 1) * TC])
            p.dma("gpsimd", du_out[mm * 128:(mm + 1) * 128, csl],
                  du_sb[:, pb * TC:(pb + 1) * TC])
            p.barrier()
    p.dma("sync", bc_out, bc_sb[64:96, :])
    p.barrier()
    p.pop_region()
    if standalone:
        return p.build()




def build_stageC(p=None, dr=None):
    CT = 1024
    NCC = T // CT
    standalone = p is None
    if standalone:
        p = Prog("C")
    _D = _mk_dram(p, dr)
    p.push_region()
    delta_in = _D("delta", [DIN, T], F32, "ExternalInput")
    du_in = _D("du", [DIN, T], F16, "ExternalInput")
    u_in = _D("u_bf", [DIN, T], BF16, "ExternalInput")
    sz_in = _D("sz_bf", [DIN, T], BF16, "ExternalInput")
    bc_in = _D("bc", [2 * NST, T], F32, "ExternalInput")
    A_in = _D("A_dev", [128, NDI * NST], F32, "ExternalInput")
    D_in = _D("D_dev", [128, NDI], F32, "ExternalInput")
    outw_in = _D("outw_T", [DIN, D], F32, "ExternalInput")
    mamba_out = _D("mamba", [D, T], F32, "ExternalOutput")

    Bb = p.sb("Bb", [128, NST * CT], F16)
    Cb = p.sb("Cb", [128, NST * CT], F16)
    delta_c = p.sb("delta_c", [128, 2 * CT], F32)
    du_c = p.sb("du_c", [128, 2 * CT], F16)
    u_c = p.sb("u_c", [128, 2 * CT], BF16)
    sz_c = p.sb("sz_c", [128, 2 * CT], BF16)
    dA = p.sb("dA", [128, 3 * CT], F16)
    dBu = p.sb("dBu", [128, 3 * CT], F16)
    h_b = p.sb("h_b", [128, 3 * CT], F16)
    ym = p.sb("ym", [128, 3 * CT], F16)
    y_b = p.sb("y_b", [128, 2 * CT], F16)
    a_gate = p.sb("a_gate", [128, 2 * CT], F16)
    b_gate = p.sb("b_gate", [128, 2 * CT], F16)
    yg = p.sb("yg", [128, NDI * CT], BF16)
    w_out = p.sb("w_out", [128, NDI * D], BF16)
    carry = p.sb("carry", [128, NDI * NST], F32)
    A_sb = p.sb("A_sb", [128, NDI * NST], F32)
    D_sb = p.sb("D_sb", [128, NDI], F32)
    mb_ev = p.sb("mb_ev", [128, 3 * CT], F32)
    ps_op = [p.ps(f"ps_op{i}", [128, CT]) for i in range(4)]

    for k in range(NDI):
        p.dma("gpsimd", w_out[:, k * D:(k + 1) * D],
              outw_in[k * 128:(k + 1) * 128, :])
    p.dma("sync", A_sb[:], A_in)
    p.dma("sync", D_sb[:], D_in)
    p.barrier()
    p.op("scalar", lambda e: e.activation(out=A_sb[:], in_=A_sb[:], func=AF.Exp))
    p.barrier()
    p.op("vector", lambda e: e.tensor_scalar_mul(out=A_sb[:], in0=A_sb[:],
                                                 scalar1=-1.0))
    p.barrier()

    def tile_loads(m, c):
        csl = slice(c * CT, (c + 1) * CT)
        buf = slice((m % 2) * CT, (m % 2 + 1) * CT)
        rows = slice(m * 128, (m + 1) * 128)
        p.dma("sync", delta_c[:, buf], delta_in[rows, csl])
        p.dma("sync", du_c[:, buf], du_in[rows, csl])
        p.dma("sync", u_c[:, buf], u_in[rows, csl])
        p.dma("sync", sz_c[:, buf], sz_in[rows, csl])

    NPH = NDI * NST + 8
    for c in range(NCC):
        csl = slice(c * CT, (c + 1) * CT)
        p.dma("gpsimd", Bb.ap().rearrange("p (n t) -> p n t", n=NST),
              bc_in[0:NST, csl].unsqueeze(0).to_broadcast((128, NST, CT)))
        p.dma("gpsimd", Cb.ap().rearrange("p (n t) -> p n t", n=NST),
              bc_in[NST:2 * NST, csl].unsqueeze(0).to_broadcast((128, NST, CT)))
        tile_loads(0, c)
        p.op("vector", lambda e: e.memset(y_b[:, 0:CT], 0.0))
        p.barrier()
        for t in range(NPH):
            if t < NDI * NST:
                i = t
                m, n = i // NST, i % NST
                j = t % 3
                p.op("scalar", lambda e, j=j, m=m, i=i: e.activation(
                    out=dA[:, j * CT:(j + 1) * CT],
                    in_=delta_c[:, (m % 2) * CT:(m % 2 + 1) * CT],
                    func=AF.Exp, scale=A_sb[:, i:i + 1]))
                p.op("vector", lambda e, j=j, m=m, n=n: e.tensor_tensor(
                    out=dBu[:, j * CT:(j + 1) * CT],
                    in0=du_c[:, (m % 2) * CT:(m % 2 + 1) * CT],
                    in1=Bb[:, n * CT:(n + 1) * CT], op=ALU.mult))
            if 0 <= t - 1 < NDI * NST:
                i = t - 1
                j = i % 3
                init = 0.0 if c == 0 else carry[:, i:i + 1]
                p.op("vector", lambda e, j=j, init=init: e.tensor_tensor_scan(
                    out=h_b[:, j * CT:(j + 1) * CT],
                    data0=dA[:, j * CT:(j + 1) * CT],
                    data1=dBu[:, j * CT:(j + 1) * CT],
                    initial=init, op0=ALU.mult, op1=ALU.add))
            if c == 0 and 0 <= t - 2 < NDI * NST:
                i2 = t - 2
                j2 = i2 % 3
                p.op("vector", lambda e, j2=j2, i2=i2: e.tensor_copy(
                    carry[:, i2:i2 + 1], h_b[:, (j2 + 1) * CT - 1:(j2 + 1) * CT]))
            if 0 <= t - 2 < NDI * NST:
                i = t - 2
                j = i % 3
                n = i % NST
                p.op("gpsimd", lambda e, j=j, n=n: e.tensor_tensor(
                    out=ym[:, j * CT:(j + 1) * CT],
                    in0=h_b[:, j * CT:(j + 1) * CT],
                    in1=Cb[:, n * CT:(n + 1) * CT], op=ALU.mult))
            if 0 <= t - 3 < NDI * NST:
                i = t - 3
                j = i % 3
                m = i // NST
                ysl = slice((m % 2) * CT, (m % 2 + 1) * CT)
                p.op("vector", lambda e, j=j, ysl=ysl: e.tensor_tensor(
                    out=y_b[:, ysl], in0=y_b[:, ysl],
                    in1=ym[:, j * CT:(j + 1) * CT], op=ALU.add))
            if t % NST == 0 and 0 < t < NDI * NST:
                mnew = t // NST
                p.op("vector", lambda e, mnew=mnew: e.memset(
                    y_b[:, (mnew % 2) * CT:(mnew % 2 + 1) * CT], 0.0))
            if t % NST == 8 and t // NST < NDI - 1:
                tile_loads(t // NST + 1, c)
            if t >= 19:
                mg = (t - 3) // NST - 1
                off = (t - 3) % NST
                if 0 <= mg < NDI:
                    gb = slice((mg % 2) * CT, (mg % 2 + 1) * CT)
                    if off == 0:
                        p.op("vector", lambda e, gb=gb, mg=mg: e.tensor_scalar_mul(
                            out=a_gate[:, gb], in0=u_c[:, gb],
                            scalar1=D_sb[:, mg:mg + 1]))
                    elif off == 1:
                        p.op("vector", lambda e, gb=gb: e.tensor_tensor(
                            out=b_gate[:, gb], in0=y_b[:, gb],
                            in1=a_gate[:, gb], op=ALU.add))
                    elif off == 2:
                        p.op("vector", lambda e, gb=gb, mg=mg: e.tensor_tensor(
                            out=yg[:, mg * CT:(mg + 1) * CT], in0=b_gate[:, gb],
                            in1=sz_c[:, gb], op=ALU.mult))
            p.barrier(lazy=True)
        p.eng_lag = {"tensor": {"vector": 2, "scalar": 2}}
        for o in range(KD + 2):
            if o < KD:
                pb = o % 4
                for nn in range(CT // 512):
                    for k in range(NDI):
                        p.op("tensor", lambda e, o=o, pb=pb, nn=nn, k=k: e.matmul(
                            ps_op[pb][:, nn * 512:(nn + 1) * 512],
                            w_out[:, k * D + o * 128:k * D + (o + 1) * 128],
                            yg[:, k * CT + nn * 512:k * CT + nn * 512 + 512],
                            start=(k == 0), stop=(k == NDI - 1)))
            if 0 <= o - 1 < KD:
                pe3 = (o - 1) % 3
                pb = (o - 1) % 4
                p.op("scalar", lambda e, pe3=pe3, pb=pb: e.copy(
                    mb_ev[:, pe3 * CT:(pe3 + 1) * CT], ps_op[pb][:]))
            if 0 <= o - 2 < KD:
                pe3 = (o - 2) % 3
                p.dma("sync", mamba_out[(o - 2) * 128:(o - 1) * 128, csl],
                      mb_ev[:, pe3 * CT:(pe3 + 1) * CT])
            p.barrier(lazy=True)
        p.eng_lag = {}
    p.pop_region()
    if standalone:
        return p.build()



def build_stageD(p=None, dr=None):
    standalone = p is None
    if standalone:
        p = Prog("D")
    _D = _mk_dram(p, dr)
    p.push_region()
    xT = _D("xT", [D, T], F32, "ExternalInput")
    xn_in = _D("xn_bf", [D, T], BF16, "ExternalInput")
    mamba_in = _D("mamba", [D, T], F32, "ExternalInput")
    wrT = _D("wrT", [D, D], F32, "ExternalInput")
    wkT = _D("wkT", [D, D], F32, "ExternalInput")
    wvT = _D("wvT", [D, D], F32, "ExternalInput")
    mur_in = _D("mu_r", [128, KD], F32, "ExternalInput")
    muk_in = _D("mu_k", [128, KD], F32, "ExternalInput")
    x1_out = _D("x1", [D, T], F32, "ExternalOutput")
    xn1_out = _D("xn1_bf", [D, T], BF16, "ExternalOutput")

    wr_sb = p.sb("wr_sb", [128, KD * D], BF16)
    wk_sb = p.sb("wk_sb", [128, KD * D], BF16)
    wv_sb = p.sb("wv_sb", [128, KD * D], BF16)
    mur = p.sb("mur", [128, KD], F32)
    muk = p.sb("muk", [128, KD], F32)
    mur1 = p.sb("mur1", [128, KD], F32)
    muk1 = p.sb("muk1", [128, KD], F32)
    ones_col = p.sb("ones_colD", [128, 1], BF16)
    ones_row = p.sb("ones_rowD", [1, 128], BF16)
    eps_sb = p.sb("eps_sbD", [1, 1], F32)

    xs = p.sb("xs", [128, KD * (TC + 1)], BF16)
    xr_sb = p.sb("xr_sb", [128, KD * TC], BF16)
    xk_sb = p.sb("xk_sb", [128, KD * TC], BF16)
    tmpmix = p.sb("tmpmix", [128, TC], BF16)
    r_sb = p.sb("r_sb", [128, KD * TC], BF16)
    kk_sb = p.sb("kk_sb", [128, KD * TC], BF16)
    krelu = p.sb("krelu", [128, 2 * TC], BF16)
    x_sb = p.sb("x_sbD", [128, KD * TC], F32)
    mam_sb = p.sb("mam_sb", [128, KD * TC], F32)
    ffn_sb = p.sb("ffn_sb", [128, 2 * TC], F32)
    x1_sb = p.sb("x1_sb", [128, KD * TC], F32)
    xb1 = p.sb("xb1", [128, KD * TC], BF16)
    xsq1 = p.sb("xsq1", [128, KD * TC], BF16)
    xn1_sb = p.sb("xn1_sb", [128, KD * TC], BF16)
    tmp_ln = p.sb("tmp_lnD", [128, TC], BF16)
    mrow = p.sb("mrowD", [1, TC], F32)
    vrow = p.sb("vrowD", [1, TC], F32)
    msq = p.sb("msqD", [1, TC], F32)
    srow = p.sb("srowD", [1, TC], F32)
    rrow = p.sb("rrowD", [1, TC], F32)
    m_bf = p.sb("m_bfD", [1, TC], BF16)
    r_bf = p.sb("r_bfD", [1, TC], BF16)
    mb_sb = p.sb("mb_sbD", [128, TC], BF16)
    rb_sb = p.sb("rb_sbD", [128, TC], BF16)

    ps_a = [p.ps(f"ps_Da{i}", [128, TC]) for i in range(2)]
    ps_b = [p.ps(f"ps_Db{i}", [128, TC]) for i in range(2)]
    ps_sx = p.ps("ps_sxD", [1, TC])
    ps_sxx = p.ps("ps_sxxD", [1, TC])
    ps_mb = p.ps("ps_mbD", [128, TC])
    ps_rb = p.ps("ps_rbD", [128, TC])

    for k in range(KD):
        p.dma("gpsimd", wr_sb[:, k * D:(k + 1) * D], wrT[k * 128:(k + 1) * 128, :])
        p.dma("gpsimd", wk_sb[:, k * D:(k + 1) * D], wkT[k * 128:(k + 1) * 128, :])
        p.dma("gpsimd", wv_sb[:, k * D:(k + 1) * D], wvT[k * 128:(k + 1) * 128, :])
    p.dma("sync", mur[:], mur_in)
    p.dma("sync", muk[:], muk_in)
    p.op("vector", lambda e: e.memset(eps_sb[:], 1e-5))
    p.op("vector", lambda e: e.memset(ones_col[:], 1.0))
    p.op("vector", lambda e: e.memset(ones_row[:], 1.0))
    for k in range(KD):
        p.op("vector", lambda e, k=k: e.memset(xs[:, k * (TC + 1):k * (TC + 1) + 1], 0.0))
    p.barrier()
    p.op("vector", lambda e: e.tensor_scalar(out=mur1[:], in0=mur[:], scalar1=-1.0,
                                             scalar2=1.0, op0=ALU.mult, op1=ALU.add))
    p.op("vector", lambda e: e.tensor_scalar(out=muk1[:], in0=muk[:], scalar1=-1.0,
                                             scalar2=1.0, op0=ALU.mult, op1=ALU.add))
    p.barrier()

    for c in range(NCH):
        csl = slice(c * TC, (c + 1) * TC)
        p.dma("sync", xs.ap().rearrange("p (k t) -> p k t", k=KD)[:, :, 1:TC + 1],
              xn_in[:, csl].rearrange("(k p) t -> p k t", p=128))
        p.dma("sync", x_sb.ap().rearrange("p (k t) -> p k t", k=KD),
              xT[:, csl].rearrange("(k p) t -> p k t", p=128))
        p.dma("sync", mam_sb.ap().rearrange("p (k t) -> p k t", k=KD),
              mamba_in[:, csl].rearrange("(k p) t -> p k t", p=128))
        p.barrier()
        for k in range(KD):
            base = k * (TC + 1)
            cur = xs[:, base + 1:base + 1 + TC]
            prv = xs[:, base:base + TC]
            p.op("vector", lambda e, prv=prv, k=k: e.tensor_scalar_mul(
                out=tmpmix[:], in0=prv, scalar1=mur1[:, k:k + 1]))
            p.op("vector", lambda e, cur=cur, k=k: e.scalar_tensor_tensor(
                out=xr_sb[:, k * TC:(k + 1) * TC], in0=cur,
                scalar=mur[:, k:k + 1], in1=tmpmix[:], op0=ALU.mult, op1=ALU.add))
            p.op("vector", lambda e, prv=prv, k=k: e.tensor_scalar_mul(
                out=tmpmix[:], in0=prv, scalar1=muk1[:, k:k + 1]))
            p.op("vector", lambda e, cur=cur, k=k: e.scalar_tensor_tensor(
                out=xk_sb[:, k * TC:(k + 1) * TC], in0=cur,
                scalar=muk[:, k:k + 1], in1=tmpmix[:], op0=ALU.mult, op1=ALU.add))
        for k in range(KD):
            base = k * (TC + 1)
            p.op("vector", lambda e, base=base: e.tensor_copy(
                xs[:, base:base + 1], xs[:, base + TC:base + TC + 1]))
        p.barrier()
        for o in range(KD + 2):
            if o < KD:
                pb = o % 2
                for k in range(KD):
                    p.op("tensor", lambda e, o=o, k=k, pb=pb: e.matmul(
                        ps_a[pb][:], wr_sb[:, k * D + o * 128:k * D + (o + 1) * 128],
                        xr_sb[:, k * TC:(k + 1) * TC],
                        start=(k == 0), stop=(k == KD - 1)))
                for k in range(KD):
                    p.op("tensor", lambda e, o=o, k=k, pb=pb: e.matmul(
                        ps_b[pb][:], wk_sb[:, k * D + o * 128:k * D + (o + 1) * 128],
                        xk_sb[:, k * TC:(k + 1) * TC],
                        start=(k == 0), stop=(k == KD - 1)))
            if 0 <= o - 1 < KD:
                om, pb = o - 1, (o - 1) % 2
                p.op("scalar", lambda e, om=om, pb=pb: e.activation(
                    out=r_sb[:, om * TC:(om + 1) * TC], in_=ps_a[pb][:],
                    func=AF.Sigmoid))
                p.op("vector", lambda e, om=om, pb=pb: e.tensor_scalar_max(
                    out=krelu[:, pb * TC:(pb + 1) * TC], in0=ps_b[pb][:], scalar1=0.0))
            if 0 <= o - 2 < KD:
                om, pb = o - 2, (o - 2) % 2
                p.op("scalar", lambda e, om=om, pb=pb: e.activation(
                    out=kk_sb[:, om * TC:(om + 1) * TC],
                    in_=krelu[:, pb * TC:(pb + 1) * TC], func=AF.Square))
            p.barrier(lazy=True)
        for o in range(KD + 1):
            if o < KD:
                pb = o % 2
                for k in range(KD):
                    p.op("tensor", lambda e, o=o, k=k, pb=pb: e.matmul(
                        ps_a[pb][:], wv_sb[:, k * D + o * 128:k * D + (o + 1) * 128],
                        kk_sb[:, k * TC:(k + 1) * TC],
                        start=(k == 0), stop=(k == KD - 1)))
            if 0 <= o - 1 < KD:
                om, pb = o - 1, (o - 1) % 2
                p.op("vector", lambda e, om=om, pb=pb: e.tensor_tensor(
                    out=ffn_sb[:, pb * TC:(pb + 1) * TC], in0=ps_a[pb][:],
                    in1=r_sb[:, om * TC:(om + 1) * TC], op=ALU.mult))
                p.op("vector", lambda e, om=om: e.tensor_add(
                    x1_sb[:, om * TC:(om + 1) * TC], x_sb[:, om * TC:(om + 1) * TC],
                    mam_sb[:, om * TC:(om + 1) * TC]))
                p.op("vector", lambda e, om=om, pb=pb: e.tensor_add(
                    x1_sb[:, om * TC:(om + 1) * TC], x1_sb[:, om * TC:(om + 1) * TC],
                    ffn_sb[:, pb * TC:(pb + 1) * TC]))
            p.barrier(lazy=True)
        p.op("vector", lambda e: e.tensor_copy(xb1[:], x1_sb[:]))
        p.op("scalar", lambda e: e.activation(out=xsq1[:], in_=x1_sb[:],
                                              func=AF.Square))
        p.barrier()
        for k in range(KD):
            p.op("tensor", lambda e, k=k: e.matmul(
                ps_sx[:], ones_col[:], xb1[:, k * TC:(k + 1) * TC],
                start=(k == 0), stop=(k == KD - 1)))
        for k in range(KD):
            p.op("tensor", lambda e, k=k: e.matmul(
                ps_sxx[:], ones_col[:], xsq1[:, k * TC:(k + 1) * TC],
                start=(k == 0), stop=(k == KD - 1)))
        p.barrier()
        p.op("vector", lambda e: e.tensor_scalar_mul(out=mrow[:], in0=ps_sx[:],
                                                     scalar1=1.0 / D))
        p.op("vector", lambda e: e.tensor_scalar_mul(out=vrow[:], in0=ps_sxx[:],
                                                     scalar1=1.0 / D))
        p.op("vector", lambda e: e.tensor_mul(msq[:], mrow[:], mrow[:]))
        p.op("vector", lambda e: e.tensor_sub(vrow[:], vrow[:], msq[:]))
        p.barrier()
        p.op("scalar", lambda e: e.activation(out=srow[:], in_=vrow[:],
                                              func=AF.Sqrt, bias=eps_sb[:]))
        p.barrier()
        p.op("vector", lambda e: e.reciprocal(rrow[:], srow[:]))
        p.barrier()
        p.op("vector", lambda e: e.tensor_copy(m_bf[:], mrow[:]))
        p.op("vector", lambda e: e.tensor_copy(r_bf[:], rrow[:]))
        p.barrier()
        p.op("tensor", lambda e: e.matmul(ps_mb[:], ones_row[:], m_bf[:],
                                          start=True, stop=True))
        p.op("tensor", lambda e: e.matmul(ps_rb[:], ones_row[:], r_bf[:],
                                          start=True, stop=True))
        p.barrier()
        p.op("vector", lambda e: e.tensor_copy(mb_sb[:], ps_mb[:]))
        p.op("vector", lambda e: e.tensor_copy(rb_sb[:], ps_rb[:]))
        p.barrier()
        for k in range(KD):
            p.op("vector", lambda e, k=k: e.tensor_sub(
                tmp_ln[:], xb1[:, k * TC:(k + 1) * TC], mb_sb[:]))
            p.op("vector", lambda e, k=k: e.tensor_mul(
                xn1_sb[:, k * TC:(k + 1) * TC], tmp_ln[:], rb_sb[:]))
        p.barrier()
        p.dma("gpsimd", x1_out[:, csl].rearrange("(k p) t -> p k t", p=128),
              x1_sb.ap().rearrange("p (k t) -> p k t", k=KD))
        p.dma("gpsimd", xn1_out[:, csl].rearrange("(k p) t -> p k t", p=128),
              xn1_sb.ap().rearrange("p (k t) -> p k t", k=KD))
        p.barrier()
    p.pop_region()
    if standalone:
        return p.build()



def build_stageE(p=None, dr=None):
    standalone = p is None
    if standalone:
        p = Prog("E")
    _D = _mk_dram(p, dr)
    p.push_region()
    xn_in = _D("xn1_bf", [D, T], BF16, "ExternalInput")
    wqT = _D("wqT", [D, D], F32, "ExternalInput")
    wkT = _D("wkT", [D, NKV * HD], F32, "ExternalInput")
    wvT = _D("wvT", [D, NKV * HD], F32, "ExternalInput")
    woT = _D("woT", [D, D], F32, "ExternalInput")
    attn_out = _D("attn", [D, T], F32, "ExternalOutput")

    NB = T // 128
    wq_sb = p.sb("wq_sb", [128, KD * D], BF16)
    wk_sb = p.sb("wk_sb", [128, KD * NKV * HD], BF16)
    wv_sb = p.sb("wv_sb", [128, KD * NKV * HD], BF16)
    wo_sb = p.sb("wo_sb", [128, KD * D], BF16)
    xn_sb = p.sb("xn_sbE", [128, KD * T], BF16)
    q_sb = p.sb("q_sbE", [128, KD * T], BF16)
    k_sb = p.sb("k_sbE", [128, 2 * T], BF16)
    kd_sb = p.sb("kd_sbE", [128, NKV * T], BF16)
    v_sb = p.sb("v_sbE", [128, 2 * T], BF16)
    vtok = p.sb("vtok", [128, NB * NKV * HD], BF16)
    io = p.sb("ioE", [128, 128], F32)
    pcol = p.sb("pcolE", [128, 1], F32)
    ident = p.sb("identE", [128, 128], BF16)
    tri = p.sb("triE", [128, 128], BF16)
    eighth = p.sb("eighth", [128, 1], F32)
    p_buf = p.sb("p_buf", [128, 2 * T], BF16)
    pt_buf = p.sb("pt_buf", [128, 2 * T], BF16)
    nm_b = p.sb("nm_b", [128, 2], F32)
    l_b = p.sb("l_b", [128, 2], F32)
    rl_b = p.sb("rl_b", [128, 2], F32)
    ctx_sb = p.sb("ctx_sbE", [128, KD * T], BF16)
    at_ev = p.sb("at_ev", [128, 2 * 512], F32)

    ps_qkv = [p.ps(f"ps_Eq{i}", [128, 512]) for i in range(2)]
    ps_s = p.ps("ps_Es", [128, T])
    ps_t = p.ps("ps_Et", [128, T], BF16)
    ps_ctx = None

    for k in range(KD):
        p.dma("gpsimd", wq_sb[:, k * D:(k + 1) * D], wqT[k * 128:(k + 1) * 128, :])
        p.dma("gpsimd", wo_sb[:, k * D:(k + 1) * D], woT[k * 128:(k + 1) * 128, :])
        p.dma("gpsimd", wk_sb[:, k * NKV * HD:(k + 1) * NKV * HD],
              wkT[k * 128:(k + 1) * 128, :])
        p.dma("gpsimd", wv_sb[:, k * NKV * HD:(k + 1) * NKV * HD],
              wvT[k * 128:(k + 1) * 128, :])
        p.dma("sync", xn_sb[:, k * T:(k + 1) * T], xn_in[k * 128:(k + 1) * 128, :])
    p.op("gpsimd", lambda e: e.iota(io[:], [[1, 128]], channel_multiplier=0,
                                    allow_small_or_imprecise_dtypes=True))
    p.op("gpsimd", lambda e: e.iota(pcol[:], [[0, 1]], channel_multiplier=1,
                                    allow_small_or_imprecise_dtypes=True))
    p.op("vector", lambda e: e.memset(eighth[:], 0.125))
    p.barrier()
    p.op("vector", lambda e: e.tensor_single_scalar(out=ident[:], in_=io[:],
                                                    scalar=pcol[:], op=ALU.is_equal))
    p.op("vector", lambda e: e.tensor_single_scalar(out=tri[:], in_=io[:],
                                                    scalar=pcol[:], op=ALU.is_le))
    p.barrier()

    def proj(w_sb, nout, dst_sb, scale=None):
        for o in range(nout):
            for nn in range(T // 512):
                pb = (o * (T // 512) + nn) % 2
                for k in range(KD):
                    p.op("tensor", lambda e, o=o, nn=nn, k=k, pb=pb,
                         w_sb=w_sb, nout=nout: e.matmul(
                        ps_qkv[pb][:],
                        w_sb[:, k * nout * 128 + o * 128:k * nout * 128 + (o + 1) * 128],
                        xn_sb[:, k * T + nn * 512:k * T + nn * 512 + 512],
                        start=(k == 0), stop=(k == KD - 1)))
                p.barrier()
                if scale is None:
                    p.op("vector", lambda e, o=o, nn=nn, pb=pb, dst_sb=dst_sb: e.tensor_copy(
                        dst_sb[:, o * T + nn * 512:o * T + nn * 512 + 512], ps_qkv[pb][:]))
                else:
                    p.op("vector", lambda e, o=o, nn=nn, pb=pb, dst_sb=dst_sb,
                         scale=scale: e.tensor_scalar_mul(
                        out=dst_sb[:, o * T + nn * 512:o * T + nn * 512 + 512],
                        in0=ps_qkv[pb][:], scalar1=scale))
        p.barrier()
    proj(wq_sb, KD, q_sb, scale=0.125)
    proj(wk_sb, 2, k_sb)
    proj(wv_sb, 2, v_sb)
    p.barrier()
    for g in range(NKV):
        vt = g // 2
        vo = (g % 2) * 64
        for blk in range(NB):
            p.op("tensor", lambda e, vt=vt, vo=vo, blk=blk: e.transpose(
                ps_t[:, 0:64] if False else ps_t[:, (blk % 8) * 64:(blk % 8) * 64 + 64],
                v_sb[vo:vo + 64, vt * T + blk * 128:vt * T + (blk + 1) * 128],
                ident[vo:vo + 64, vo:vo + 64]))
            if blk % 8 == 7 or blk == NB - 1:
                p.barrier()
                for b2 in range(blk - blk % 8, blk + 1):
                    p.op("vector", lambda e, g=g, b2=b2, blk=blk: e.tensor_copy(
                        vtok[:, b2 * NKV * HD + g * HD:b2 * NKV * HD + (g + 1) * HD],
                        ps_t[:, (b2 % 8) * 64:(b2 % 8) * 64 + 64]))
                p.barrier()
    p.barrier()
    for g in range(NKV):
        kt, ko = g // 2, (g % 2) * 64
        p.dma("sync", kd_sb[0:64, g * T:(g + 1) * T],
              k_sb[ko:ko + 64, kt * T:(kt + 1) * T])
        p.dma("gpsimd", kd_sb[64:128, g * T:(g + 1) * T],
              k_sb[ko:ko + 64, kt * T:(kt + 1) * T])
    p.barrier()

    IT = NH * NB
    p.op("tensor", lambda e: e.matmul(
        ps_s[:, 0:128], q_sb[0:64, 0:128], kd_sb[0:64, 0:128],
        start=True, stop=True))
    p.barrier()
    for i in range(IT + 2):
        if 0 <= i - 1 < IT:
            h, qb = divmod(i - 1, NB)
            j = (i - 1) % 2
            for blk in range(qb + 1):
                p.op("tensor", lambda e, j=j, blk=blk: e.transpose(
                    ps_t[:, blk * 128:(blk + 1) * 128],
                    p_buf[:, j * T + blk * 128:j * T + (blk + 1) * 128], ident[:]))
        if i < IT:
            h, qb = divmod(i, NB)
            j = i % 2
            ncol = (qb + 1) * 128
            p.op("vector", lambda e, ncol=ncol, j=j: e.reduce_max(
                nm_b[:, j:j + 1], ps_s[:, 0:ncol], axis=AX.X, negate=True))
        if 0 <= i - 2 < IT:
            h, qb = divmod(i - 2, NB)
            j = (i - 2) % 2
            qt, qo = h // 2, (h % 2) * 64
            p.op("scalar", lambda e, qt=qt, qo=qo, qb=qb, j=j: e.copy(
                ctx_sb[qo:qo + 64, qt * T + qb * 128:qt * T + (qb + 1) * 128],
                ps_qkv[j][0:64, 0:128]))
        p.barrier(lazy=True)
        if i < IT:
            h, qb = divmod(i, NB)
            j = i % 2
            ncol = (qb + 1) * 128
            p.op("scalar", lambda e, ncol=ncol, j=j: e.activation(
                out=p_buf[:, j * T:j * T + ncol], in_=ps_s[:, 0:ncol],
                func=AF.Exp, bias=nm_b[:, j:j + 1]))
        if 0 <= i - 1 < IT:
            h, qb = divmod(i - 1, NB)
            j = (i - 1) % 2
            ncol = (qb + 1) * 128
            p.op("vector", lambda e, ncol=ncol, j=j: e.tensor_copy(
                pt_buf[:, j * T:j * T + ncol], ps_t[:, 0:ncol]))
        p.barrier(lazy=True)
        if 0 <= i - 1 < IT:
            h, qb = divmod(i - 1, NB)
            j = (i - 1) % 2
            g = h // NKV
            for blk in range(qb + 1):
                p.op("tensor", lambda e, j=j, g=g, blk=blk, qb=qb: e.matmul(
                    ps_qkv[j][0:64, 0:128],
                    vtok[:, blk * NKV * HD + g * HD:blk * NKV * HD + (g + 1) * HD],
                    pt_buf[:, j * T + blk * 128:j * T + (blk + 1) * 128],
                    start=(blk == 0), stop=(blk == qb)))
        if i + 1 < IT:
            h, qb = divmod(i + 1, NB)
            qt, qo = h // 2, (h % 2) * 64
            g = h // NKV
            ncol = (qb + 1) * 128
            nmm = (ncol + 511) // 512
            for mmi in range(nmm):
                w = min(512, ncol - mmi * 512)
                p.op("tensor", lambda e, qt=qt, qo=qo, g=g, qb=qb,
                     mmi=mmi, w=w: e.matmul(
                    ps_s[:, mmi * 512:mmi * 512 + w],
                    q_sb[qo:qo + 64, qt * T + qb * 128:qt * T + (qb + 1) * 128],
                    kd_sb[qo:qo + 64, g * T + mmi * 512:g * T + mmi * 512 + w],
                    start=True, stop=True))
        if i < IT:
            h, qb = divmod(i, NB)
            j = i % 2
            ncol = (qb + 1) * 128
            p.op("vector", lambda e, j=j, qb=qb: e.tensor_mul(
                p_buf[:, j * T + qb * 128:j * T + (qb + 1) * 128],
                p_buf[:, j * T + qb * 128:j * T + (qb + 1) * 128], tri[:]))
            p.op("vector", lambda e, ncol=ncol, j=j: e.reduce_sum(
                l_b[:, j:j + 1], p_buf[:, j * T:j * T + ncol], axis=AX.X))
            p.barrier(lazy=True)
            p.op("vector", lambda e, j=j: e.reciprocal(rl_b[:, j:j + 1],
                                                       l_b[:, j:j + 1]))
            p.barrier(lazy=True)
            p.op("vector", lambda e, ncol=ncol, j=j: e.tensor_scalar_mul(
                out=p_buf[:, j * T:j * T + ncol], in0=p_buf[:, j * T:j * T + ncol],
                scalar1=rl_b[:, j:j + 1]))
        p.barrier(lazy=True)

    for o in range(KD):
        for nn in range(T // 512):
            pb = (o * (T // 512) + nn) % 2
            for k in range(KD):
                p.op("tensor", lambda e, o=o, nn=nn, k=k, pb=pb: e.matmul(
                    ps_qkv[pb][:], wo_sb[:, k * D + o * 128:k * D + (o + 1) * 128],
                    ctx_sb[:, k * T + nn * 512:k * T + nn * 512 + 512],
                    start=(k == 0), stop=(k == KD - 1)))
            p.barrier()
            p.op("vector", lambda e, pb=pb: e.tensor_copy(
                at_ev[:, pb * 512:(pb + 1) * 512], ps_qkv[pb][:]))
            p.barrier()
            p.dma("sync", attn_out[o * 128:(o + 1) * 128, nn * 512:(nn + 1) * 512],
                  at_ev[:, pb * 512:(pb + 1) * 512])
    p.barrier()
    p.pop_region()
    if standalone:
        return p.build()



def build_stageF(p=None, dr=None):
    standalone = p is None
    if standalone:
        p = Prog("F")
    _D = _mk_dram(p, dr)
    p.push_region()
    xn_in = _D("xn1_bf", [D, T], BF16, "ExternalInput")
    x1_in = _D("x1", [D, T], F32, "ExternalInput")
    attn_in = _D("attn", [D, T], F32, "ExternalInput")
    Wr_in = _D("Wr", [D, NE], F32, "ExternalInput")
    br_in = _D("br", [NE, 1], F32, "ExternalInput")
    W1_in = _D("W1", [NE, D, HF], F32, "ExternalInput")
    b1_in = _D("b1d", [NE, 128, HF // 128], F32, "ExternalInput")
    W2_in = _D("W2", [NE, HF, D], F32, "ExternalInput")
    b2_in = _D("b2d", [NE, 128, D // 128], F32, "ExternalInput")
    out_T = _D("outT", [D, T], F32, "ExternalOutput")
    w_dram = p.dram("w_scratch", [NE, T], F32, "Internal")
    acc_dram = p.dram("acc_scratch", [D, T], F32, "Internal")

    NB = T // 128
    NO1 = HF // 128
    xn_sb = p.sb("xn_sbF", [128, KD * T], BF16)
    wr_sb = p.sb("wr_sbF", [128, KD * NE], BF16)
    br_sb = p.sb("br_sbF", [NE, 1], F32)
    w1_sb = p.sb("w1_sbF", [128, KD * HF], BF16)
    w2_sb = p.sb("w2_sbF", [128, NO1 * D], BF16)
    b1_sb = p.sb("b1_sbF", [128, NE * NO1], F32)
    b2_sb = p.sb("b2_sbF", [128, NE * KD], F32)
    io = p.sb("ioF", [128, 128], F32)
    pcol = p.sb("pcolF", [128, 1], F32)
    id32 = p.sb("id32F", [128, 128], F32)
    lg_sb = p.sb("lg_sbF", [128, T], F32)
    lgt = p.sb("lgtF", [128, 2 * NE], F32)
    nm1 = p.sb("nm1F", [128, 1], F32)
    e1 = p.sb("e1F", [128, NE], F32)
    t1 = p.sb("t1F", [128, NE], F32)
    m2 = p.sb("m2F", [128, 1], F32)
    pe = p.sb("peF", [128, NE], F32)
    msk = p.sb("mskF", [128, NE], F32)
    zz = p.sb("zzF", [128, 1], F32)
    rz = p.sb("rzF", [128, 1], F32)
    w_tok = p.sb("w_tokF", [128, NE], F32)
    wb_e = p.sb("wb_eF", [128, 2 * TC], F32)
    h_sb = p.sb("h_sbF", [128, NO1 * 512], BF16)


    p.push_region()
    ps_r = p.ps("ps_rF", [NE, 512])
    ps_t8 = p.ps("ps_t8F", [128, NE])
    ps_w8 = p.ps("ps_w8F", [NE, 128])

    for k in range(KD):
        p.dma("sync", xn_sb[:, k * T:(k + 1) * T], xn_in[k * 128:(k + 1) * 128, :])
    p.dma("gpsimd", wr_sb.ap().rearrange("p (k j) -> p k j", k=KD),
          Wr_in.rearrange("(k p) j -> p k j", p=128))
    p.dma("sync", br_sb[:], br_in)
    p.dma("sync", b1_sb.ap().rearrange("p (e o) -> p e o", e=NE),
          b1_in.transpose((1, 0, 2)))
    p.dma("sync", b2_sb.ap().rearrange("p (e o) -> p e o", e=NE),
          b2_in.transpose((1, 0, 2)))
    p.op("gpsimd", lambda e: e.iota(io[:], [[1, 128]], channel_multiplier=0,
                                    allow_small_or_imprecise_dtypes=True))
    p.op("gpsimd", lambda e: e.iota(pcol[:], [[0, 1]], channel_multiplier=1,
                                    allow_small_or_imprecise_dtypes=True))
    p.barrier()
    p.op("vector", lambda e: e.tensor_single_scalar(out=id32[:], in_=io[:],
                                                    scalar=pcol[:], op=ALU.is_equal))
    p.barrier()

    for c in range(NCH):
        for k in range(KD):
            p.op("tensor", lambda e, k=k, c=c: e.matmul(
                ps_r[:], wr_sb[:, k * NE:(k + 1) * NE],
                xn_sb[:, k * T + c * TC:k * T + (c + 1) * TC],
                start=(k == 0), stop=(k == KD - 1)))
        p.barrier()
        p.op("vector", lambda e, c=c: e.tensor_scalar_add(
            out=lg_sb[0:NE, c * TC:(c + 1) * TC], in0=ps_r[:], scalar1=br_sb[:]))
        p.barrier()
    for blk in range(NB):
        pb = blk % 2
        p.op("tensor", lambda e, blk=blk: e.transpose(
            ps_t8[:], lg_sb[0:NE, blk * 128:(blk + 1) * 128], id32[0:NE, 0:NE]))
        p.barrier()
        p.op("vector", lambda e, pb=pb: e.tensor_copy(lgt[:, pb * NE:(pb + 1) * NE],
                                                      ps_t8[:]))
        p.barrier()
        sl = slice(pb * NE, (pb + 1) * NE)
        p.op("vector", lambda e, sl=sl: e.reduce_max(nm1[:], lgt[:, sl], axis=AX.X,
                                                     negate=True))
        p.barrier()
        p.op("vector", lambda e, sl=sl: e.tensor_scalar(
            out=e1[:], in0=lgt[:, sl], scalar1=nm1[:], scalar2=None,
            op0=ALU.add, op1=ALU.bypass))
        p.barrier()
        p.op("vector", lambda e: e.tensor_single_scalar(out=e1[:], in_=e1[:],
                                                        scalar=0.0, op=ALU.is_equal))
        p.barrier()
        p.op("vector", lambda e, sl=sl: e.scalar_tensor_tensor(
            out=t1[:], in0=e1[:], scalar=-1e9, in1=lgt[:, sl],
            op0=ALU.mult, op1=ALU.add))
        p.barrier()
        p.op("vector", lambda e: e.reduce_max(m2[:], t1[:], axis=AX.X))
        p.op("scalar", lambda e, sl=sl: e.activation(out=pe[:], in_=lgt[:, sl],
                                                     func=AF.Exp, bias=nm1[:]))
        p.barrier()
        p.op("vector", lambda e, sl=sl: e.tensor_single_scalar(
            out=msk[:], in_=lgt[:, sl], scalar=m2[:], op=ALU.is_ge))
        p.barrier()
        p.op("vector", lambda e: e.tensor_mul(pe[:], pe[:], msk[:]))
        p.barrier()
        p.op("vector", lambda e: e.reduce_sum(zz[:], pe[:], axis=AX.X))
        p.barrier()
        p.op("vector", lambda e: e.reciprocal(rz[:], zz[:]))
        p.barrier()
        p.op("vector", lambda e: e.tensor_scalar_mul(out=w_tok[:], in0=pe[:],
                                                     scalar1=rz[:]))
        p.barrier()
        p.op("tensor", lambda e: e.transpose(ps_w8[:], w_tok[:], id32[:]))
        p.barrier()
        p.op("vector", lambda e, blk=blk: e.tensor_copy(
            lg_sb[0:NE, blk * 128:(blk + 1) * 128], ps_w8[:]))
        p.barrier()
    p.dma("sync", w_dram, lg_sb[0:NE, :])
    p.barrier()
    p.pop_region()
    ps_m = [p.ps(f"ps_Fm{i}", [128, 512]) for i in range(8)]

    t2p = lg_sb[:, 0:1536]
    for k in range(KD):
        p.dma("gpsimd", w1_sb[:, k * HF:(k + 1) * HF], W1_in[0, k * 128:(k + 1) * 128, :],
              lane=1)
    p.dma("gpsimd", w2_sb.ap().rearrange("p (k dd) -> p k dd", k=NO1),
          W2_in[0].rearrange("(k p) dd -> p k dd", p=128), lane=1)
    p.dma("gpsimd", wb_e[:, 0:TC], w_dram[0, 0:TC].partition_broadcast(128), lane=1)
    p.fence_lane1("tensor")
    p.fence_lane1("vector")
    p.barrier()
    p.eng_lag = {}
    NP1 = NO1 // 4
    NG = NP1 + 1 + KD + 2
    for ex in range(NE):
        for c in range(NCH):
            csl = slice(c * TC, (c + 1) * TC)
            par = (ex * NCH + c) % 2
            wb_cur = wb_e[:, par * TC:(par + 1) * TC]
            cn, exn = (c + 1, ex) if c + 1 < NCH else (0, ex + 1)
            for g in range(NG):
                if g == 0 and exn < NE:
                    p.dma("gpsimd", wb_e[:, (1 - par) * TC:(2 - par) * TC],
                          w_dram[exn, cn * TC:(cn + 1) * TC].partition_broadcast(128),
                          lane=1)
                if g == 0 and c == 0 and ex >= 1:
                    p.fence_lane1("tensor")
                    p.dma("gpsimd", w2_sb.ap().rearrange("p (k dd) -> p k dd", k=NO1),
                          W2_in[ex].rearrange("(k p) dd -> p k dd", p=128), lane=1)
                if g == NP1:
                    if c == 0:
                        p.fence_lane1("tensor")
                    p.fence_lane1("vector")
                if g == NP1 + 1 and c == NCH - 1 and ex + 1 < NE:
                    for k in range(KD):
                        p.dma("gpsimd", w1_sb[:, k * HF:(k + 1) * HF],
                              W1_in[ex + 1, k * 128:(k + 1) * 128, :], lane=1)
                if g < NP1:
                    for oo in range(4):
                        o = g * 4 + oo
                        pb = (g % 2) * 4 + oo
                        for k in range(KD):
                            p.op("tensor", lambda e, o=o, k=k, pb=pb, c=c: e.matmul(
                                ps_m[pb][:],
                                w1_sb[:, k * HF + o * 128:k * HF + (o + 1) * 128],
                                xn_sb[:, k * T + c * TC:k * T + (c + 1) * TC],
                                start=(k == 0), stop=(k == KD - 1)))
                elif NP1 + 1 <= g < NP1 + 1 + KD:
                    dd = g - NP1 - 1
                    for k in range(NO1):
                        p.op("tensor", lambda e, dd=dd, k=k: e.matmul(
                            ps_m[dd][:], w2_sb[:, k * D + dd * 128:k * D + (dd + 1) * 128],
                            h_sb[:, k * TC:(k + 1) * TC],
                            start=(k == 0), stop=(k == NO1 - 1)))
                gm = g - 1
                if 0 <= gm < NP1:
                    for oo in range(4):
                        o = gm * 4 + oo
                        pb = (gm % 2) * 4 + oo
                        if oo % 2 == 0:
                            p.op("vector", lambda e, o=o, pb=pb, ex=ex: e.tensor_scalar(
                                out=h_sb[:, o * TC:(o + 1) * TC], in0=ps_m[pb][:],
                                scalar1=b1_sb[:, ex * NO1 + o:ex * NO1 + o + 1],
                                scalar2=0.0, op0=ALU.add, op1=ALU.max))
                        else:
                            p.op("scalar", lambda e, o=o, pb=pb, ex=ex: e.activation(
                                out=h_sb[:, o * TC:(o + 1) * TC], in_=ps_m[pb][:],
                                func=AF.Relu, bias=b1_sb[:, ex * NO1 + o:ex * NO1 + o + 1]))
                elif NP1 + 1 <= gm < NP1 + 1 + KD:
                    dd = gm - NP1 - 1
                    p3 = dd % 3
                    p.op("vector", lambda e, dd=dd, p3=p3, ex=ex, wb_cur=wb_cur:
                         e.scalar_tensor_tensor(
                        out=t2p[:, p3 * 512:(p3 + 1) * 512], in0=ps_m[dd][:],
                        scalar=b2_sb[:, ex * KD + dd:ex * KD + dd + 1],
                        in1=wb_cur, op0=ALU.add, op1=ALU.mult))
                gm2 = g - 2
                if NP1 + 1 <= gm2 < NP1 + 1 + KD:
                    dd = gm2 - NP1 - 1
                    p3 = dd % 3
                    if ex == 0:
                        p.dma("gpsimd", acc_dram[dd * 128:(dd + 1) * 128, csl],
                              t2p[:, p3 * 512:(p3 + 1) * 512])
                    else:
                        def accdma(e, dd=dd, csl=csl, p3=p3):
                            return e.dma_start(
                                out=acc_dram[dd * 128:(dd + 1) * 128, csl],
                                in_=t2p[:, p3 * 512:(p3 + 1) * 512], accum_op=ALU.add)
                        cell = ["dma"]
                        p._q("gpsimd", ("call", accdma, cell))
                        p.tot_dma += 16
                p.barrier(lazy=True)
    p.eng_lag = {}
    res_sb = h_sb[:, 0:2048].bitcast(F32)
    out_sb = h_sb[:, 2048:4096].bitcast(F32)
    acc_sb = h_sb[:, 4096:6144].bitcast(F32)
    for dd in range(KD):
        for c in range(NCH):
            pb = c % 2
            csl = slice(c * TC, (c + 1) * TC)
            rows = slice(dd * 128, (dd + 1) * 128)
            rsl = res_sb[:, pb * TC:(pb + 1) * TC]
            osl = out_sb[:, pb * TC:(pb + 1) * TC]
            asl = acc_sb[:, pb * TC:(pb + 1) * TC]
            p.dma("sync", rsl, x1_in[rows, csl])
            p.dma("sync", osl, attn_in[rows, csl])
            p.dma("sync", asl, acc_dram[rows, csl])
            p.barrier()
            p.op("vector", lambda e, rsl=rsl, osl=osl: e.tensor_add(rsl, rsl, osl))
            p.barrier()
            p.op("vector", lambda e, rsl=rsl, osl=osl, asl=asl: e.tensor_add(
                osl, rsl, asl))
            p.barrier()
            p.dma("sync", out_T[rows, csl], osl)
    p.barrier()
    p.pop_region()
    if standalone:
        return p.build()

def _t(a):
    return np.ascontiguousarray(np.asarray(a).T)


def build_fused():
    p = Prog("fused")
    EI = lambda n, s, d: p.dram(n, list(s), d, "ExternalInput")
    IN = lambda n, s, d: p.dram(n, list(s), d, "Internal")
    xT = EI("xT", [D, T], F32)
    w_inT = EI("w_inT", [D, 2 * DIN], F32)
    conv_w = EI("conv_w", [DIN, 4], F32)
    conv_b = EI("conv_b", [128, NDI], F32)
    xp_wT = EI("xp_wT", [DIN, 96], F32)
    dt_wT = EI("dt_wT", [DTR, DIN], F32)
    dt_b = EI("dt_b", [128, NDI], F32)
    A_dev = EI("A_dev", [128, NDI * NST], F32)
    D_dev = EI("D_dev", [128, NDI], F32)
    outw_T = EI("outw_T", [DIN, D], F32)
    wrT = EI("wrT", [D, D], F32)
    wkT_d = EI("wkT_d", [D, D], F32)
    wvT_d = EI("wvT_d", [D, D], F32)
    mu_r = EI("mu_r", [128, KD], F32)
    mu_k = EI("mu_k", [128, KD], F32)
    wqT = EI("wqT", [D, D], F32)
    wkT_e = EI("wkT_e", [D, NKV * HD], F32)
    wvT_e = EI("wvT_e", [D, NKV * HD], F32)
    woT = EI("woT", [D, D], F32)
    Wr = EI("Wr", [D, NE], F32)
    br = EI("br", [NE, 1], F32)
    W1 = EI("W1", [NE, D, HF], F32)
    b1d = EI("b1d", [NE, 128, HF // 128], F32)
    W2 = EI("W2", [NE, HF, D], F32)
    b2d = EI("b2d", [NE, 128, D // 128], F32)
    outT = p.dram("outT", [D, T], F32, "ExternalOutput")

    u_bf = IN("u_bf", [DIN, T], BF16)
    sz_bf = IN("sz_bf", [DIN, T], BF16)
    xn_bf = IN("xn_bf", [D, T], BF16)
    delta = IN("delta", [DIN, T], F32)
    du = IN("du", [DIN, T], F16)
    bc = IN("bc", [2 * NST, T], F32)
    mamba = IN("mamba", [D, T], F32)
    x1 = IN("x1", [D, T], F32)
    xn1_bf = IN("xn1_bf", [D, T], BF16)
    attn = IN("attn", [D, T], F32)

    build_stageA(p, dict(xT=xT, w_inT=w_inT, conv_w=conv_w, conv_b=conv_b,
                         u_bf=u_bf, sz_bf=sz_bf, xn_bf=xn_bf))
    build_stageB(p, dict(u_bf=u_bf, xp_wT=xp_wT, dt_wT=dt_wT, dt_b=dt_b,
                         delta=delta, du=du, bc=bc))
    build_stageC(p, dict(delta=delta, du=du, u_bf=u_bf, sz_bf=sz_bf, bc=bc,
                         A_dev=A_dev, D_dev=D_dev, outw_T=outw_T, mamba=mamba))
    build_stageD(p, dict(xT=xT, xn_bf=xn_bf, mamba=mamba, wrT=wrT, wkT=wkT_d,
                         wvT=wvT_d, mu_r=mu_r, mu_k=mu_k, x1=x1, xn1_bf=xn1_bf))
    build_stageE(p, dict(xn1_bf=xn1_bf, wqT=wqT, wkT=wkT_e, wvT=wvT_e, woT=woT,
                         attn=attn))
    build_stageF(p, dict(xn1_bf=xn1_bf, x1=x1, attn=attn, Wr=Wr, br=br, W1=W1,
                         b1d=b1d, W2=W2, b2d=b2d, outT=outT))
    return p.build()


def kernel(**ins):
    cores = list(range(B))
    shared = {
        "w_inT": _t(ins["L0_attn_in_proj_w"]),
        "conv_w": np.ascontiguousarray(np.asarray(ins["L0_attn_conv_w"])[:, 0, :]),
        "conv_b": np.ascontiguousarray(np.asarray(ins["L0_attn_conv_b"]).reshape(NDI, 128).T),
        "xp_wT": _t(ins["L0_attn_x_proj_w"]),
        "dt_wT": _t(ins["L0_attn_dt_proj_w"]),
        "dt_b": np.ascontiguousarray(np.asarray(ins["L0_attn_dt_proj_b"]).reshape(NDI, 128).T),
        "A_dev": np.ascontiguousarray(np.asarray(ins["L0_attn_A_log"])
                                      .reshape(NDI, 128, NST).transpose(1, 0, 2)
                                      .reshape(128, NDI * NST)),
        "D_dev": np.ascontiguousarray(np.asarray(ins["L0_attn_D"]).reshape(NDI, 128).T),
        "outw_T": _t(ins["L0_attn_out_proj_w"]),
        "wrT": _t(ins["L0_ffn_W_r"]),
        "wkT_d": _t(ins["L0_ffn_W_k"]),
        "wvT_d": _t(ins["L0_ffn_W_v"]),
        "mu_r": np.ascontiguousarray(np.asarray(ins["L0_ffn_mu_r"]).reshape(KD, 128).T),
        "mu_k": np.ascontiguousarray(np.asarray(ins["L0_ffn_mu_k"]).reshape(KD, 128).T),
        "wqT": _t(ins["L1_attn_wq"]),
        "wkT_e": _t(ins["L1_attn_wk"]),
        "wvT_e": _t(ins["L1_attn_wv"]),
        "woT": _t(ins["L1_attn_wo"]),
        "Wr": np.ascontiguousarray(np.asarray(ins["L1_ffn_Wr"], np.float32)),
        "br": np.ascontiguousarray(np.asarray(ins["L1_ffn_br"], np.float32)[:, None]),
        "W1": np.ascontiguousarray(np.asarray(ins["L1_ffn_W1"], np.float32)),
        "b1d": np.ascontiguousarray(np.asarray(ins["L1_ffn_b1"], np.float32)
                                    .reshape(NE, HF // 128, 128).transpose(0, 2, 1)),
        "W2": np.ascontiguousarray(np.asarray(ins["L1_ffn_W2"], np.float32)),
        "b2d": np.ascontiguousarray(np.asarray(ins["L1_ffn_b2"], np.float32)
                                    .reshape(NE, D // 128, 128).transpose(0, 2, 1)),
    }
    maps = [dict(shared, xT=np.ascontiguousarray(
        np.asarray(ins["L0_attn_hidden_states"][b], np.float32).T)) for b in range(B)]
    nc = build_fused()
    res = run_bass_kernel_spmd(nc, maps, core_ids=cores)
    out = np.stack([np.asarray(res.results[b]["outT"]).T for b in range(B)], 0)
    return np.ascontiguousarray(out.astype(np.float32))


STAGE_BUILDERS = {"A": build_stageA, "B": build_stageB, "C": build_stageC, "D": build_stageD, "E": build_stageE, "F": build_stageF}


# version 08
# speedup vs baseline: 1.2394x; 1.0918x over previous
import contextlib
import numpy as np
import concourse.bass as bass
import concourse.mybir as mybir
from concourse.bass_utils import run_bass_kernel_spmd

F32 = mybir.dt.float32
BF16 = mybir.dt.bfloat16
F16 = mybir.dt.float16
AF = mybir.ActivationFunctionType
ALU = mybir.AluOpType
AX = mybir.AxisListType

ENGS = ("sync", "gpsimd", "vector", "scalar", "tensor")

B, T, D = 8, 2048, 1024
DIN = 2048
NST = 16
DTR = 64
NH, NKV, HD = 16, 4, 64
NE, HF = 8, 4096
TC = 512
NCH = T // TC
KD = D // 128
NDI = DIN // 128


class Prog:

    def __init__(self, name):
        self.nc = bass.Bass("TRN2", target_bir_lowering=False)
        self.name = name
        self.queues = {e: [] for e in ENGS}
        self.tot = {e: 0 for e in ENGS}
        self.tot_dma = 0
        self.pending = {e: None for e in ENGS}
        self.cur_snapshot = ({e: 0 for e in ENGS}, 0, {}, [])
        self.waited = {e: None for e in ENGS}
        self.stack = contextlib.ExitStack()
        self._stacks = [self.stack]
        self._names = {}
        self._dma_hist = [0]
        self.tot_dma1 = 0
        self._tot_hist = [dict(self.tot)]
        self.eng_lag = {}

    def _uniq(self, name):
        n = self._names.get(name, 0)
        self._names[name] = n + 1
        return name if n == 0 else f"{name}_{n}"

    def push_region(self):
        self._stacks.append(contextlib.ExitStack())

    def pop_region(self):
        self.barrier()
        self._stacks.pop().close()

    def dram(self, name, shape, dtype, kind):
        return self.nc.dram_tensor(name, list(shape), dtype, kind=kind).ap()

    def sb(self, name, shape, dtype):
        return self._stacks[-1].enter_context(
            self.nc.sbuf_tensor(self._uniq(name), list(shape), dtype))

    def ps(self, name, shape, dtype=F32):
        return self._stacks[-1].enter_context(
            self.nc.psum_tensor(self._uniq(name), list(shape), dtype))

    def _q(self, eng, item):
        snap = self.cur_snapshot
        if self.waited[eng] is not snap:
            self.queues[eng].append(("wait", snap, eng))
            self.waited[eng] = snap
        self.queues[eng].append(item)

    def op(self, eng, fn):
        cell = [0]
        self._q(eng, ("call", fn, cell))
        self.pending[eng] = cell

    def dma(self, eng, out, in_, lane=0):
        if lane == 0:
            cell = ["dma"]
            self._q(eng, ("call", lambda e: e.dma_start(out=out, in_=in_), cell))
            self.tot_dma += 16
        else:
            cell = ["dma1"]
            self._q(eng, ("call", lambda e: e.dma_start(out=out, in_=in_), cell))
            self.tot_dma1 += 16

    def fence_lane1(self, eng):
        v = self.tot_dma1
        if v > 0:
            self._q(eng, ("wait1", v, eng))

    def barrier(self, lazy=False):
        for e in ENGS:
            c = self.pending[e]
            if c is not None:
                c[0] = 1
                self.tot[e] += 1
                self.pending[e] = None
        dval = self._dma_hist[-1] if lazy else self.tot_dma
        self._dma_hist.append(self.tot_dma)
        self._tot_hist.append(dict(self.tot))
        self.cur_snapshot = (dict(self.tot), dval, dict(self.eng_lag),
                             [dict(h) for h in self._tot_hist[-4:]])

    def build(self):
        self.barrier()
        self.queues["sync"].append(("wait", self.cur_snapshot, "__final__"))
        if self.tot_dma1 > 0:
            self.queues["sync"].append(("wait1", self.tot_dma1, "__final__"))
        nc = self.nc
        bars = {e: self.stack.enter_context(nc.semaphore(f"bar_{e}")) for e in ENGS}
        bar_dma = self.stack.enter_context(nc.semaphore("bar_dma"))
        bar_dma1 = self.stack.enter_context(nc.semaphore("bar_dma1"))
        with nc.Block() as block:
            def emit_stream(eng_name):
                def _(e):
                    for item in self.queues[eng_name]:
                        if item[0] == "wait1":
                            e.wait_ge(bar_dma1, item[1])
                            continue
                        if item[0] == "wait":
                            snap, me = item[1], item[2]
                            counts, dmas, lags, hist = snap
                            mylag = lags.get(me, {})
                            for oe in ENGS:
                                v = counts[oe]
                                lag = mylag.get(oe, 0)
                                if lag > 0 and len(hist) > lag:
                                    v = hist[-1 - lag][oe]
                                if v > 0:
                                    e.wait_ge(bars[oe], v)
                            if dmas > 0:
                                e.wait_ge(bar_dma, dmas)
                        else:
                            _, fn, cell = item
                            inst = fn(e)
                            if cell[0] == "dma":
                                inst.then_inc(bar_dma, 16)
                            elif cell[0] == "dma1":
                                inst.then_inc(bar_dma1, 16)
                            elif cell[0] == 1:
                                inst.then_inc(bars[eng_name], 1)
                return _
            block.sync(emit_stream("sync"))
            block.gpsimd(emit_stream("gpsimd"))
            block.vector(emit_stream("vector"))
            block.scalar(emit_stream("scalar"))
            block.tensor(emit_stream("tensor"))
        self.stack.close()
        return nc


def _mk_dram(p, dr):
    def _D(name, shape, dtype, kind):
        if dr is not None:
            return dr[name]
        return p.dram(name, shape, dtype, kind)
    return _D


def build_stageA(p=None, dr=None):
    standalone = p is None
    if standalone:
        p = Prog("A")
    _D = _mk_dram(p, dr)
    p.push_region()
    xT = _D("xT", [D, T], F32, "ExternalInput")
    w_inT = _D("w_inT", [D, 2 * DIN], F32, "ExternalInput")
    conv_w = _D("conv_w", [DIN, 4], F32, "ExternalInput")
    conv_b = _D("conv_b", [128, NDI], F32, "ExternalInput")
    u_out = _D("u_bf", [DIN, T], BF16, "ExternalOutput")
    sz_out = _D("sz_bf", [DIN, T], BF16, "ExternalOutput")
    xn_out = _D("xn_bf", [D, T], BF16, "ExternalOutput")

    w_sb = p.sb("w_sb", [128, KD * 2 * DIN], BF16)
    cw_sb = p.sb("cw_sb", [128, NDI * 4], F32)
    cb_sb = p.sb("cb_sb", [128, NDI], F32)
    ones_col = p.sb("ones_col", [128, 1], BF16)
    ones_row = p.sb("ones_row", [1, 128], BF16)
    eps_sb = p.sb("eps_sb", [1, 1], F32)

    x_sb = p.sb("x_sb", [128, KD * TC], F32)
    xb_sb = p.sb("xb_sb", [128, KD * TC], BF16)
    xsq_sb = p.sb("xsq_sb", [128, KD * TC], BF16)
    xn_sb = p.sb("xn_sb", [128, KD * TC], BF16)
    tmp_sb = p.sb("tmp_sb", [128, TC], BF16)
    mrow = p.sb("mrow", [1, TC], F32)
    vrow = p.sb("vrow", [1, TC], F32)
    msq = p.sb("msq", [1, TC], F32)
    srow = p.sb("srow", [1, TC], F32)
    rrow = p.sb("rrow", [1, TC], F32)
    m_bf = p.sb("m_bf", [1, TC], BF16)
    r_bf = p.sb("r_bf", [1, TC], BF16)
    mb_sb = p.sb("mb_sb", [128, TC], BF16)
    rb_sb = p.sb("rb_sb", [128, TC], BF16)
    xc_sb = p.sb("xc_sb", [128, NDI * (TC + 3)], F32)
    cacc = p.sb("cacc", [128, 4 * TC], F32)
    u_sb = p.sb("u_sb", [128, NDI * TC], BF16)
    sz_sb = p.sb("sz_sb", [128, NDI * TC], BF16)

    ps_sx = p.ps("ps_sx", [1, TC])
    ps_sxx = p.ps("ps_sxx", [1, TC])
    ps_mb = p.ps("ps_mb", [128, TC])
    ps_rb = p.ps("ps_rb", [128, TC])
    ps_xz = [p.ps(f"ps_xz{i}", [128, TC]) for i in range(4)]

    for k in range(KD):
        p.dma("gpsimd", w_sb[:, k * 2 * DIN:(k + 1) * 2 * DIN],
              w_inT[k * 128:(k + 1) * 128, :])
    p.dma("sync", cw_sb.ap().rearrange("p (o c) -> p o c", c=4),
          conv_w.rearrange("(o p) c -> p o c", p=128))
    p.dma("sync", cb_sb[:], conv_b)
    p.op("vector", lambda e: e.memset(eps_sb[:], 1e-5))
    p.op("vector", lambda e: e.memset(ones_col[:], 1.0))
    p.op("vector", lambda e: e.memset(ones_row[:], 1.0))
    for o in range(NDI):
        p.op("vector", lambda e, o=o: e.memset(
            xc_sb[:, o * (TC + 3):o * (TC + 3) + 3], 0.0))
    p.barrier()

    for c in range(NCH):
        csl = slice(c * TC, (c + 1) * TC)
        src = xT[:, csl].rearrange("(k p) t -> p k t", p=128)
        p.dma("sync", x_sb.ap().rearrange("p (k t) -> p k t", k=KD), src)
        p.dma("gpsimd", xb_sb.ap().rearrange("p (k t) -> p k t", k=KD), src)
        p.barrier()
        p.op("scalar", lambda e: e.activation(out=xsq_sb[:], in_=x_sb[:],
                                              func=AF.Square))
        p.barrier()
        for k in range(KD):
            p.op("tensor", lambda e, k=k: e.matmul(
                ps_sx[:], ones_col[:], xb_sb[:, k * TC:(k + 1) * TC],
                start=(k == 0), stop=(k == KD - 1)))
        for k in range(KD):
            p.op("tensor", lambda e, k=k: e.matmul(
                ps_sxx[:], ones_col[:], xsq_sb[:, k * TC:(k + 1) * TC],
                start=(k == 0), stop=(k == KD - 1)))
        p.barrier()
        p.op("vector", lambda e: e.tensor_scalar_mul(out=mrow[:], in0=ps_sx[:],
                                                     scalar1=1.0 / D))
        p.op("vector", lambda e: e.tensor_scalar_mul(out=vrow[:], in0=ps_sxx[:],
                                                     scalar1=1.0 / D))
        p.op("vector", lambda e: e.tensor_mul(msq[:], mrow[:], mrow[:]))
        p.op("vector", lambda e: e.tensor_sub(vrow[:], vrow[:], msq[:]))
        p.barrier()
        p.op("scalar", lambda e: e.activation(out=srow[:], in_=vrow[:],
                                              func=AF.Sqrt, bias=eps_sb[:]))
        p.barrier()
        p.op("vector", lambda e: e.reciprocal(rrow[:], srow[:]))
        p.barrier()
        p.op("vector", lambda e: e.tensor_copy(m_bf[:], mrow[:]))
        p.op("vector", lambda e: e.tensor_copy(r_bf[:], rrow[:]))
        p.barrier()
        p.op("tensor", lambda e: e.matmul(ps_mb[:], ones_row[:], m_bf[:],
                                          start=True, stop=True))
        p.op("tensor", lambda e: e.matmul(ps_rb[:], ones_row[:], r_bf[:],
                                          start=True, stop=True))
        p.barrier()
        p.op("vector", lambda e: e.tensor_copy(mb_sb[:], ps_mb[:]))
        p.op("vector", lambda e: e.tensor_copy(rb_sb[:], ps_rb[:]))
        p.barrier()
        for k in range(KD):
            p.op("vector", lambda e, k=k: e.tensor_sub(
                tmp_sb[:], xb_sb[:, k * TC:(k + 1) * TC], mb_sb[:]))
            p.op("vector", lambda e, k=k: e.tensor_mul(
                xn_sb[:, k * TC:(k + 1) * TC], tmp_sb[:], rb_sb[:]))
        p.barrier()
        p.dma("gpsimd", xn_out[:, csl].rearrange("(k p) t -> p k t", p=128),
              xn_sb.ap().rearrange("p (k t) -> p k t", k=KD))
        p.eng_lag = {"tensor": {"vector": 2, "scalar": 2}}
        for g in range(35):
            if g < 32:
                o, pbx = g, g % 4
                for k in range(KD):
                    p.op("tensor", lambda e, o=o, pbx=pbx, k=k: e.matmul(
                        ps_xz[pbx][:],
                        w_sb[:, k * 2 * DIN + o * 128:k * 2 * DIN + (o + 1) * 128],
                        xn_sb[:, k * TC:(k + 1) * TC],
                        start=(k == 0), stop=(k == KD - 1)))
            gm = g - 1
            if 0 <= gm < 32:
                o, pbx = gm, gm % 4
                if o < 16:
                    base = o * (TC + 3)
                    p.op("vector", lambda e, pbx=pbx, base=base: e.tensor_copy(
                        xc_sb[:, base + 3:base + 3 + TC], ps_xz[pbx][:]))
                else:
                    zi = o - 16
                    p.op("scalar", lambda e, zi=zi, pbx=pbx: e.activation(
                        out=sz_sb[:, zi * TC:(zi + 1) * TC],
                        in_=ps_xz[pbx][:], func=AF.Silu))
            gm2 = g - 2
            if 0 <= gm2 < 16:
                o = gm2
                base = o * (TC + 3)
                w4 = cw_sb[:, o * 4:o * 4 + 4]
                cs = cacc[:, (gm2 % 4) * TC:(gm2 % 4 + 1) * TC]
                p.op("vector", lambda e, base=base, w4=w4, cs=cs: e.tensor_scalar_mul(
                    out=cs, in0=xc_sb[:, base + 3:base + 3 + TC],
                    scalar1=w4[:, 3:4]))
                for j in (2, 1, 0):
                    p.op("vector", lambda e, base=base, w4=w4, cs=cs, j=j:
                         e.scalar_tensor_tensor(
                             out=cs, in0=xc_sb[:, base + j:base + j + TC],
                             scalar=w4[:, j:j + 1], in1=cs,
                             op0=ALU.mult, op1=ALU.add))
                p.op("vector", lambda e, base=base: e.tensor_copy(
                    xc_sb[:, base:base + 3],
                    xc_sb[:, base + TC:base + TC + 3]))
            gm3 = g - 3
            if 0 <= gm3 < 16:
                o = gm3
                p.op("scalar", lambda e, o=o, gm3=gm3: e.activation(
                    out=u_sb[:, o * TC:(o + 1) * TC],
                    in_=cacc[:, (gm3 % 4) * TC:(gm3 % 4 + 1) * TC],
                    func=AF.Silu, bias=cb_sb[:, o:o + 1]))
            p.barrier(lazy=True)
        p.eng_lag = {}
        p.dma("gpsimd", u_out[:, csl].rearrange("(k p) t -> p k t", p=128),
              u_sb.ap().rearrange("p (k t) -> p k t", k=NDI))
        p.dma("gpsimd", sz_out[:, csl].rearrange("(k p) t -> p k t", p=128),
              sz_sb.ap().rearrange("p (k t) -> p k t", k=NDI))
        p.barrier()
    p.pop_region()
    if standalone:
        return p.build()


def build_stageB(p=None, dr=None):
    standalone = p is None
    if standalone:
        p = Prog("B")
    _D = _mk_dram(p, dr)
    p.push_region()
    u_in = _D("u_bf", [DIN, T], BF16, "ExternalInput")
    xp_wT = _D("xp_wT", [DIN, 96], F32, "ExternalInput")
    dt_wT = _D("dt_wT", [DTR, DIN], F32, "ExternalInput")
    dt_b = _D("dt_b", [128, NDI], F32, "ExternalInput")
    delta_out = _D("delta", [DIN, T], F32, "ExternalOutput")
    du_out = _D("du", [DIN, T], F16, "ExternalOutput")
    bc_out = _D("bc", [2 * NST, T], F32, "ExternalOutput")

    u_sb = p.sb("u_sb", [128, NDI * T], BF16)
    xpw_sb = p.sb("xpw_sb", [128, NDI * 96], BF16)
    dtw_sb = p.sb("dtw_sb", [DTR, DIN], BF16)
    dtb_sb = p.sb("dtb_sb", [128, NDI], F32)
    dt_sb = p.sb("dt_sb", [DTR, T], BF16)
    bc_sb = p.sb("bc_sb", [96, T], F32)
    delta_sb = p.sb("delta_sb", [128, 4 * TC], F32)
    du_sb = p.sb("du_sb", [128, 4 * TC], F16)

    ps_xd = p.ps("ps_xd", [96, TC])
    ps_dt = [p.ps(f"ps_dt{i}", [128, TC]) for i in range(4)]

    for k in range(NDI):
        p.dma("gpsimd", u_sb[:, k * T:(k + 1) * T], u_in[k * 128:(k + 1) * 128, :])
    p.dma("gpsimd", xpw_sb.ap().rearrange("p (k j) -> p k j", k=NDI),
          xp_wT.rearrange("(k p) j -> p k j", p=128))
    p.dma("gpsimd", dtw_sb[:], dt_wT)
    p.dma("sync", dtb_sb[:], dt_b)
    p.barrier()

    for c in range(NCH):
        csl = slice(c * TC, (c + 1) * TC)
        for k in range(NDI):
            p.op("tensor", lambda e, k=k, c=c: e.matmul(
                ps_xd[:], xpw_sb[:, k * 96:(k + 1) * 96],
                u_sb[:, k * T + c * TC:k * T + (c + 1) * TC],
                start=(k == 0), stop=(k == NDI - 1)))
        p.barrier()
        p.op("vector", lambda e, csl=csl: e.tensor_copy(dt_sb[:, csl], ps_xd[0:64, :]))
        p.op("vector", lambda e, csl=csl: e.tensor_copy(bc_sb[64:96, csl], ps_xd[64:96, :]))
        p.barrier()
        for mm in range(NDI + 3):
            if mm < NDI:
                p.op("tensor", lambda e, mm=mm, csl=csl: e.matmul(
                    ps_dt[mm % 4][:], dtw_sb[:, mm * 128:(mm + 1) * 128],
                    dt_sb[:, csl], start=True, stop=True))
            if 0 <= mm - 1 < NDI:
                m1 = mm - 1
                sl = slice((m1 % 4) * TC, (m1 % 4 + 1) * TC)
                p.op("scalar", lambda e, m1=m1, sl=sl: e.activation(
                    out=delta_sb[:, sl], in_=ps_dt[m1 % 4][:],
                    func=AF.Exp, bias=dtb_sb[:, m1:m1 + 1]))
                p.op("scalar", lambda e, sl=sl: e.activation(
                    out=delta_sb[:, sl], in_=delta_sb[:, sl], func=AF.Ln, bias=1.0))
            if 0 <= mm - 2 < NDI:
                m2 = mm - 2
                sl = slice((m2 % 4) * TC, (m2 % 4 + 1) * TC)
                p.op("vector", lambda e, m2=m2, sl=sl, c=c: e.tensor_mul(
                    du_sb[:, sl], delta_sb[:, sl],
                    u_sb[:, m2 * T + c * TC:m2 * T + (c + 1) * TC]))
            if 0 <= mm - 3 < NDI:
                m3 = mm - 3
                sl = slice((m3 % 4) * TC, (m3 % 4 + 1) * TC)
                p.dma("sync", delta_out[m3 * 128:(m3 + 1) * 128, csl],
                      delta_sb[:, sl])
                p.dma("gpsimd", du_out[m3 * 128:(m3 + 1) * 128, csl],
                      du_sb[:, sl])
            p.barrier()
    p.dma("sync", bc_out, bc_sb[64:96, :])
    p.barrier()
    p.pop_region()
    if standalone:
        return p.build()




def build_stageC(p=None, dr=None):
    CT = 1024
    NCC = T // CT
    standalone = p is None
    if standalone:
        p = Prog("C")
    _D = _mk_dram(p, dr)
    p.push_region()
    delta_in = _D("delta", [DIN, T], F32, "ExternalInput")
    du_in = _D("du", [DIN, T], F16, "ExternalInput")
    u_in = _D("u_bf", [DIN, T], BF16, "ExternalInput")
    sz_in = _D("sz_bf", [DIN, T], BF16, "ExternalInput")
    bc_in = _D("bc", [2 * NST, T], F32, "ExternalInput")
    A_in = _D("A_dev", [128, NDI * NST], F32, "ExternalInput")
    D_in = _D("D_dev", [128, NDI], F32, "ExternalInput")
    outw_in = _D("outw_T", [DIN, D], F32, "ExternalInput")
    mamba_out = _D("mamba", [D, T], F32, "ExternalOutput")

    Bb = p.sb("Bb", [128, NST * CT], F16)
    Cb = p.sb("Cb", [128, NST * CT], F16)
    delta_c = p.sb("delta_c", [128, 2 * CT], F32)
    du_c = p.sb("du_c", [128, 2 * CT], F16)
    u_c = p.sb("u_c", [128, 2 * CT], BF16)
    sz_c = p.sb("sz_c", [128, 2 * CT], BF16)
    dA = p.sb("dA", [128, 3 * CT], F16)
    dBu = p.sb("dBu", [128, 3 * CT], F16)
    h_b = p.sb("h_b", [128, 3 * CT], F16)
    ym = p.sb("ym", [128, 3 * CT], F16)
    y_b = p.sb("y_b", [128, 2 * CT], F16)
    a_gate = p.sb("a_gate", [128, 2 * CT], F16)
    b_gate = p.sb("b_gate", [128, 2 * CT], F16)
    yg = p.sb("yg", [128, NDI * CT], BF16)
    w_out = p.sb("w_out", [128, NDI * D], BF16)
    carry = p.sb("carry", [128, NDI * NST], F32)
    A_sb = p.sb("A_sb", [128, NDI * NST], F32)
    D_sb = p.sb("D_sb", [128, NDI], F32)
    mb_ev = p.sb("mb_ev", [128, 3 * CT], F32)
    ps_op = [p.ps(f"ps_op{i}", [128, CT]) for i in range(4)]

    for k in range(NDI):
        p.dma("gpsimd", w_out[:, k * D:(k + 1) * D],
              outw_in[k * 128:(k + 1) * 128, :])
    p.dma("sync", A_sb[:], A_in)
    p.dma("sync", D_sb[:], D_in)
    p.barrier()
    p.op("scalar", lambda e: e.activation(out=A_sb[:], in_=A_sb[:], func=AF.Exp))
    p.barrier()
    p.op("vector", lambda e: e.tensor_scalar_mul(out=A_sb[:], in0=A_sb[:],
                                                 scalar1=-1.0))
    p.barrier()

    def tile_loads(m, c):
        csl = slice(c * CT, (c + 1) * CT)
        buf = slice((m % 2) * CT, (m % 2 + 1) * CT)
        rows = slice(m * 128, (m + 1) * 128)
        p.dma("sync", delta_c[:, buf], delta_in[rows, csl])
        p.dma("sync", du_c[:, buf], du_in[rows, csl])
        p.dma("sync", u_c[:, buf], u_in[rows, csl])
        p.dma("sync", sz_c[:, buf], sz_in[rows, csl])

    NPH = NDI * NST + 8
    for c in range(NCC):
        csl = slice(c * CT, (c + 1) * CT)
        p.dma("gpsimd", Bb.ap().rearrange("p (n t) -> p n t", n=NST),
              bc_in[0:NST, csl].unsqueeze(0).to_broadcast((128, NST, CT)))
        p.dma("gpsimd", Cb.ap().rearrange("p (n t) -> p n t", n=NST),
              bc_in[NST:2 * NST, csl].unsqueeze(0).to_broadcast((128, NST, CT)))
        tile_loads(0, c)
        p.op("vector", lambda e: e.memset(y_b[:, 0:CT], 0.0))
        p.barrier()
        for t in range(NPH):
            if t < NDI * NST:
                i = t
                m, n = i // NST, i % NST
                j = t % 3
                p.op("scalar", lambda e, j=j, m=m, i=i: e.activation(
                    out=dA[:, j * CT:(j + 1) * CT],
                    in_=delta_c[:, (m % 2) * CT:(m % 2 + 1) * CT],
                    func=AF.Exp, scale=A_sb[:, i:i + 1]))
                p.op("vector", lambda e, j=j, m=m, n=n: e.tensor_tensor(
                    out=dBu[:, j * CT:(j + 1) * CT],
                    in0=du_c[:, (m % 2) * CT:(m % 2 + 1) * CT],
                    in1=Bb[:, n * CT:(n + 1) * CT], op=ALU.mult))
            if 0 <= t - 1 < NDI * NST:
                i = t - 1
                j = i % 3
                init = 0.0 if c == 0 else carry[:, i:i + 1]
                p.op("vector", lambda e, j=j, init=init: e.tensor_tensor_scan(
                    out=h_b[:, j * CT:(j + 1) * CT],
                    data0=dA[:, j * CT:(j + 1) * CT],
                    data1=dBu[:, j * CT:(j + 1) * CT],
                    initial=init, op0=ALU.mult, op1=ALU.add))
            if c == 0 and 0 <= t - 2 < NDI * NST:
                i2 = t - 2
                j2 = i2 % 3
                p.op("vector", lambda e, j2=j2, i2=i2: e.tensor_copy(
                    carry[:, i2:i2 + 1], h_b[:, (j2 + 1) * CT - 1:(j2 + 1) * CT]))
            if 0 <= t - 2 < NDI * NST:
                i = t - 2
                j = i % 3
                n = i % NST
                p.op("vector", lambda e, j=j, n=n: e.tensor_tensor(
                    out=ym[:, j * CT:(j + 1) * CT],
                    in0=h_b[:, j * CT:(j + 1) * CT],
                    in1=Cb[:, n * CT:(n + 1) * CT], op=ALU.mult))
            if 0 <= t - 3 < NDI * NST:
                i = t - 3
                j = i % 3
                m = i // NST
                ysl = slice((m % 2) * CT, (m % 2 + 1) * CT)
                p.op("vector", lambda e, j=j, ysl=ysl: e.tensor_tensor(
                    out=y_b[:, ysl], in0=y_b[:, ysl],
                    in1=ym[:, j * CT:(j + 1) * CT], op=ALU.add))
            if t % NST == 0 and 0 < t < NDI * NST:
                mnew = t // NST
                p.op("vector", lambda e, mnew=mnew: e.memset(
                    y_b[:, (mnew % 2) * CT:(mnew % 2 + 1) * CT], 0.0))
            if t % NST == 8 and t // NST < NDI - 1:
                tile_loads(t // NST + 1, c)
            if t >= 19:
                mg = (t - 3) // NST - 1
                off = (t - 3) % NST
                if 0 <= mg < NDI:
                    gb = slice((mg % 2) * CT, (mg % 2 + 1) * CT)
                    if off == 0:
                        p.op("vector", lambda e, gb=gb, mg=mg: e.tensor_scalar_mul(
                            out=a_gate[:, gb], in0=u_c[:, gb],
                            scalar1=D_sb[:, mg:mg + 1]))
                    elif off == 1:
                        p.op("vector", lambda e, gb=gb: e.tensor_tensor(
                            out=b_gate[:, gb], in0=y_b[:, gb],
                            in1=a_gate[:, gb], op=ALU.add))
                    elif off == 2:
                        p.op("vector", lambda e, gb=gb, mg=mg: e.tensor_tensor(
                            out=yg[:, mg * CT:(mg + 1) * CT], in0=b_gate[:, gb],
                            in1=sz_c[:, gb], op=ALU.mult))
            p.barrier(lazy=True)
        p.eng_lag = {"tensor": {"vector": 2, "scalar": 2}}
        for o in range(KD + 2):
            if o < KD:
                pb = o % 4
                for nn in range(CT // 512):
                    for k in range(NDI):
                        p.op("tensor", lambda e, o=o, pb=pb, nn=nn, k=k: e.matmul(
                            ps_op[pb][:, nn * 512:(nn + 1) * 512],
                            w_out[:, k * D + o * 128:k * D + (o + 1) * 128],
                            yg[:, k * CT + nn * 512:k * CT + nn * 512 + 512],
                            start=(k == 0), stop=(k == NDI - 1)))
            if 0 <= o - 1 < KD:
                pe3 = (o - 1) % 3
                pb = (o - 1) % 4
                p.op("scalar", lambda e, pe3=pe3, pb=pb: e.copy(
                    mb_ev[:, pe3 * CT:(pe3 + 1) * CT], ps_op[pb][:]))
            if 0 <= o - 2 < KD:
                pe3 = (o - 2) % 3
                p.dma("sync", mamba_out[(o - 2) * 128:(o - 1) * 128, csl],
                      mb_ev[:, pe3 * CT:(pe3 + 1) * CT])
            p.barrier(lazy=True)
        p.eng_lag = {}
    p.pop_region()
    if standalone:
        return p.build()



def build_stageD(p=None, dr=None):
    standalone = p is None
    if standalone:
        p = Prog("D")
    _D = _mk_dram(p, dr)
    p.push_region()
    xT = _D("xT", [D, T], F32, "ExternalInput")
    xn_in = _D("xn_bf", [D, T], BF16, "ExternalInput")
    mamba_in = _D("mamba", [D, T], F32, "ExternalInput")
    wrT = _D("wrT", [D, D], F32, "ExternalInput")
    wkT = _D("wkT", [D, D], F32, "ExternalInput")
    wvT = _D("wvT", [D, D], F32, "ExternalInput")
    mur_in = _D("mu_r", [128, KD], F32, "ExternalInput")
    muk_in = _D("mu_k", [128, KD], F32, "ExternalInput")
    x1_out = _D("x1", [D, T], F32, "ExternalOutput")
    xn1_out = _D("xn1_bf", [D, T], BF16, "ExternalOutput")

    wr_sb = p.sb("wr_sb", [128, KD * D], BF16)
    wk_sb = p.sb("wk_sb", [128, KD * D], BF16)
    wv_sb = p.sb("wv_sb", [128, KD * D], BF16)
    mur = p.sb("mur", [128, KD], F32)
    muk = p.sb("muk", [128, KD], F32)
    mur1 = p.sb("mur1", [128, KD], F32)
    muk1 = p.sb("muk1", [128, KD], F32)
    ones_col = p.sb("ones_colD", [128, 1], BF16)
    ones_row = p.sb("ones_rowD", [1, 128], BF16)
    eps_sb = p.sb("eps_sbD", [1, 1], F32)

    xs = p.sb("xs", [128, KD * (TC + 1)], BF16)
    xr_sb = p.sb("xr_sb", [128, KD * TC], BF16)
    xk_sb = p.sb("xk_sb", [128, KD * TC], BF16)
    tmpmix = p.sb("tmpmix", [128, TC], BF16)
    r_sb = p.sb("r_sb", [128, KD * TC], BF16)
    kk_sb = p.sb("kk_sb", [128, KD * TC], BF16)
    krelu = p.sb("krelu", [128, 2 * TC], BF16)
    x_sb = p.sb("x_sbD", [128, KD * TC], F32)
    mam_sb = p.sb("mam_sb", [128, KD * TC], F32)
    ffn_sb = p.sb("ffn_sb", [128, 2 * TC], F32)
    x1_sb = p.sb("x1_sb", [128, KD * TC], F32)
    xb1 = p.sb("xb1", [128, KD * TC], BF16)
    xsq1 = p.sb("xsq1", [128, KD * TC], BF16)
    xn1_sb = p.sb("xn1_sb", [128, KD * TC], BF16)
    tmp_ln = p.sb("tmp_lnD", [128, TC], BF16)
    mrow = p.sb("mrowD", [1, TC], F32)
    vrow = p.sb("vrowD", [1, TC], F32)
    msq = p.sb("msqD", [1, TC], F32)
    srow = p.sb("srowD", [1, TC], F32)
    rrow = p.sb("rrowD", [1, TC], F32)
    m_bf = p.sb("m_bfD", [1, TC], BF16)
    r_bf = p.sb("r_bfD", [1, TC], BF16)
    mb_sb = p.sb("mb_sbD", [128, TC], BF16)
    rb_sb = p.sb("rb_sbD", [128, TC], BF16)

    ps_a = [p.ps(f"ps_Da{i}", [128, TC]) for i in range(2)]
    ps_b = [p.ps(f"ps_Db{i}", [128, TC]) for i in range(2)]
    ps_sx = p.ps("ps_sxD", [1, TC])
    ps_sxx = p.ps("ps_sxxD", [1, TC])
    ps_mb = p.ps("ps_mbD", [128, TC])
    ps_rb = p.ps("ps_rbD", [128, TC])

    for k in range(KD):
        p.dma("gpsimd", wr_sb[:, k * D:(k + 1) * D], wrT[k * 128:(k + 1) * 128, :])
        p.dma("gpsimd", wk_sb[:, k * D:(k + 1) * D], wkT[k * 128:(k + 1) * 128, :])
        p.dma("gpsimd", wv_sb[:, k * D:(k + 1) * D], wvT[k * 128:(k + 1) * 128, :])
    p.dma("sync", mur[:], mur_in)
    p.dma("sync", muk[:], muk_in)
    p.op("vector", lambda e: e.memset(eps_sb[:], 1e-5))
    p.op("vector", lambda e: e.memset(ones_col[:], 1.0))
    p.op("vector", lambda e: e.memset(ones_row[:], 1.0))
    for k in range(KD):
        p.op("vector", lambda e, k=k: e.memset(xs[:, k * (TC + 1):k * (TC + 1) + 1], 0.0))
    p.barrier()
    p.op("vector", lambda e: e.tensor_scalar(out=mur1[:], in0=mur[:], scalar1=-1.0,
                                             scalar2=1.0, op0=ALU.mult, op1=ALU.add))
    p.op("vector", lambda e: e.tensor_scalar(out=muk1[:], in0=muk[:], scalar1=-1.0,
                                             scalar2=1.0, op0=ALU.mult, op1=ALU.add))
    p.barrier()

    for c in range(NCH):
        csl = slice(c * TC, (c + 1) * TC)
        p.dma("sync", xs.ap().rearrange("p (k t) -> p k t", k=KD)[:, :, 1:TC + 1],
              xn_in[:, csl].rearrange("(k p) t -> p k t", p=128))
        p.dma("sync", x_sb.ap().rearrange("p (k t) -> p k t", k=KD),
              xT[:, csl].rearrange("(k p) t -> p k t", p=128))
        p.dma("sync", mam_sb.ap().rearrange("p (k t) -> p k t", k=KD),
              mamba_in[:, csl].rearrange("(k p) t -> p k t", p=128))
        p.barrier()
        for k in range(KD):
            base = k * (TC + 1)
            cur = xs[:, base + 1:base + 1 + TC]
            prv = xs[:, base:base + TC]
            p.op("vector", lambda e, prv=prv, k=k: e.tensor_scalar_mul(
                out=tmpmix[:], in0=prv, scalar1=mur1[:, k:k + 1]))
            p.op("vector", lambda e, cur=cur, k=k: e.scalar_tensor_tensor(
                out=xr_sb[:, k * TC:(k + 1) * TC], in0=cur,
                scalar=mur[:, k:k + 1], in1=tmpmix[:], op0=ALU.mult, op1=ALU.add))
            p.op("vector", lambda e, prv=prv, k=k: e.tensor_scalar_mul(
                out=tmpmix[:], in0=prv, scalar1=muk1[:, k:k + 1]))
            p.op("vector", lambda e, cur=cur, k=k: e.scalar_tensor_tensor(
                out=xk_sb[:, k * TC:(k + 1) * TC], in0=cur,
                scalar=muk[:, k:k + 1], in1=tmpmix[:], op0=ALU.mult, op1=ALU.add))
        for k in range(KD):
            base = k * (TC + 1)
            p.op("vector", lambda e, base=base: e.tensor_copy(
                xs[:, base:base + 1], xs[:, base + TC:base + TC + 1]))
        p.barrier()
        for o in range(KD + 2):
            if o < KD:
                pb = o % 2
                for k in range(KD):
                    p.op("tensor", lambda e, o=o, k=k, pb=pb: e.matmul(
                        ps_a[pb][:], wr_sb[:, k * D + o * 128:k * D + (o + 1) * 128],
                        xr_sb[:, k * TC:(k + 1) * TC],
                        start=(k == 0), stop=(k == KD - 1)))
                for k in range(KD):
                    p.op("tensor", lambda e, o=o, k=k, pb=pb: e.matmul(
                        ps_b[pb][:], wk_sb[:, k * D + o * 128:k * D + (o + 1) * 128],
                        xk_sb[:, k * TC:(k + 1) * TC],
                        start=(k == 0), stop=(k == KD - 1)))
            if 0 <= o - 1 < KD:
                om, pb = o - 1, (o - 1) % 2
                p.op("scalar", lambda e, om=om, pb=pb: e.activation(
                    out=r_sb[:, om * TC:(om + 1) * TC], in_=ps_a[pb][:],
                    func=AF.Sigmoid))
                p.op("vector", lambda e, om=om, pb=pb: e.tensor_scalar_max(
                    out=krelu[:, pb * TC:(pb + 1) * TC], in0=ps_b[pb][:], scalar1=0.0))
            if 0 <= o - 2 < KD:
                om, pb = o - 2, (o - 2) % 2
                p.op("scalar", lambda e, om=om, pb=pb: e.activation(
                    out=kk_sb[:, om * TC:(om + 1) * TC],
                    in_=krelu[:, pb * TC:(pb + 1) * TC], func=AF.Square))
            p.barrier(lazy=True)
        for o in range(KD + 1):
            if o < KD:
                pb = o % 2
                for k in range(KD):
                    p.op("tensor", lambda e, o=o, k=k, pb=pb: e.matmul(
                        ps_a[pb][:], wv_sb[:, k * D + o * 128:k * D + (o + 1) * 128],
                        kk_sb[:, k * TC:(k + 1) * TC],
                        start=(k == 0), stop=(k == KD - 1)))
            if 0 <= o - 1 < KD:
                om, pb = o - 1, (o - 1) % 2
                p.op("vector", lambda e, om=om, pb=pb: e.tensor_tensor(
                    out=ffn_sb[:, pb * TC:(pb + 1) * TC], in0=ps_a[pb][:],
                    in1=r_sb[:, om * TC:(om + 1) * TC], op=ALU.mult))
                p.op("vector", lambda e, om=om: e.tensor_add(
                    x1_sb[:, om * TC:(om + 1) * TC], x_sb[:, om * TC:(om + 1) * TC],
                    mam_sb[:, om * TC:(om + 1) * TC]))
                p.op("vector", lambda e, om=om, pb=pb: e.tensor_add(
                    x1_sb[:, om * TC:(om + 1) * TC], x1_sb[:, om * TC:(om + 1) * TC],
                    ffn_sb[:, pb * TC:(pb + 1) * TC]))
            p.barrier(lazy=True)
        p.op("vector", lambda e: e.tensor_copy(xb1[:], x1_sb[:]))
        p.op("scalar", lambda e: e.activation(out=xsq1[:], in_=x1_sb[:],
                                              func=AF.Square))
        p.barrier()
        for k in range(KD):
            p.op("tensor", lambda e, k=k: e.matmul(
                ps_sx[:], ones_col[:], xb1[:, k * TC:(k + 1) * TC],
                start=(k == 0), stop=(k == KD - 1)))
        for k in range(KD):
            p.op("tensor", lambda e, k=k: e.matmul(
                ps_sxx[:], ones_col[:], xsq1[:, k * TC:(k + 1) * TC],
                start=(k == 0), stop=(k == KD - 1)))
        p.barrier()
        p.op("vector", lambda e: e.tensor_scalar_mul(out=mrow[:], in0=ps_sx[:],
                                                     scalar1=1.0 / D))
        p.op("vector", lambda e: e.tensor_scalar_mul(out=vrow[:], in0=ps_sxx[:],
                                                     scalar1=1.0 / D))
        p.op("vector", lambda e: e.tensor_mul(msq[:], mrow[:], mrow[:]))
        p.op("vector", lambda e: e.tensor_sub(vrow[:], vrow[:], msq[:]))
        p.barrier()
        p.op("scalar", lambda e: e.activation(out=srow[:], in_=vrow[:],
                                              func=AF.Sqrt, bias=eps_sb[:]))
        p.barrier()
        p.op("vector", lambda e: e.reciprocal(rrow[:], srow[:]))
        p.barrier()
        p.op("vector", lambda e: e.tensor_copy(m_bf[:], mrow[:]))
        p.op("vector", lambda e: e.tensor_copy(r_bf[:], rrow[:]))
        p.barrier()
        p.op("tensor", lambda e: e.matmul(ps_mb[:], ones_row[:], m_bf[:],
                                          start=True, stop=True))
        p.op("tensor", lambda e: e.matmul(ps_rb[:], ones_row[:], r_bf[:],
                                          start=True, stop=True))
        p.barrier()
        p.op("vector", lambda e: e.tensor_copy(mb_sb[:], ps_mb[:]))
        p.op("vector", lambda e: e.tensor_copy(rb_sb[:], ps_rb[:]))
        p.barrier()
        for k in range(KD):
            p.op("vector", lambda e, k=k: e.tensor_sub(
                tmp_ln[:], xb1[:, k * TC:(k + 1) * TC], mb_sb[:]))
            p.op("vector", lambda e, k=k: e.tensor_mul(
                xn1_sb[:, k * TC:(k + 1) * TC], tmp_ln[:], rb_sb[:]))
        p.barrier()
        p.dma("gpsimd", x1_out[:, csl].rearrange("(k p) t -> p k t", p=128),
              x1_sb.ap().rearrange("p (k t) -> p k t", k=KD))
        p.dma("gpsimd", xn1_out[:, csl].rearrange("(k p) t -> p k t", p=128),
              xn1_sb.ap().rearrange("p (k t) -> p k t", k=KD))
        p.barrier()
    p.pop_region()
    if standalone:
        return p.build()



def build_stageE(p=None, dr=None):
    standalone = p is None
    if standalone:
        p = Prog("E")
    _D = _mk_dram(p, dr)
    p.push_region()
    xn_in = _D("xn1_bf", [D, T], BF16, "ExternalInput")
    wqT = _D("wqT", [D, D], F32, "ExternalInput")
    wkT = _D("wkT", [D, NKV * HD], F32, "ExternalInput")
    wvT = _D("wvT", [D, NKV * HD], F32, "ExternalInput")
    woT = _D("woT", [D, D], F32, "ExternalInput")
    attn_out = _D("attn", [D, T], F32, "ExternalOutput")

    NB = T // 128
    wq_sb = p.sb("wq_sb", [128, KD * D], BF16)
    wk_sb = p.sb("wk_sb", [128, KD * NKV * HD], BF16)
    wv_sb = p.sb("wv_sb", [128, KD * NKV * HD], BF16)
    wo_sb = p.sb("wo_sb", [128, KD * D], BF16)
    xn_sb = p.sb("xn_sbE", [128, KD * T], BF16)
    q_sb = p.sb("q_sbE", [128, KD * T], BF16)
    k_sb = p.sb("k_sbE", [128, 2 * T], BF16)
    kd_sb = p.sb("kd_sbE", [128, NKV * T], BF16)
    v_sb = p.sb("v_sbE", [128, 2 * T], BF16)
    vtok = p.sb("vtok", [128, NB * NKV * HD], BF16)
    io = p.sb("ioE", [128, 128], F32)
    pcol = p.sb("pcolE", [128, 1], F32)
    ident = p.sb("identE", [128, 128], BF16)
    tri = p.sb("triE", [128, 128], BF16)
    eighth = p.sb("eighth", [128, 1], F32)
    p_buf = p.sb("p_buf", [128, 2 * T], BF16)
    pt_buf = p.sb("pt_buf", [128, 2 * T], BF16)
    nm_b = p.sb("nm_b", [128, 2], F32)
    l_b = p.sb("l_b", [128, 2], F32)
    rl_b = p.sb("rl_b", [128, 2], F32)
    ctx_sb = p.sb("ctx_sbE", [128, KD * T], BF16)
    at_ev = p.sb("at_ev", [128, 2 * 512], F32)

    ps_qkv = [p.ps(f"ps_Eq{i}", [128, 512]) for i in range(2)]
    ps_s = p.ps("ps_Es", [128, T])
    ps_t = p.ps("ps_Et", [128, T], BF16)
    ps_ctx = None

    for k in range(KD):
        p.dma("gpsimd", wq_sb[:, k * D:(k + 1) * D], wqT[k * 128:(k + 1) * 128, :])
        p.dma("gpsimd", wo_sb[:, k * D:(k + 1) * D], woT[k * 128:(k + 1) * 128, :])
        p.dma("gpsimd", wk_sb[:, k * NKV * HD:(k + 1) * NKV * HD],
              wkT[k * 128:(k + 1) * 128, :])
        p.dma("gpsimd", wv_sb[:, k * NKV * HD:(k + 1) * NKV * HD],
              wvT[k * 128:(k + 1) * 128, :])
        p.dma("sync", xn_sb[:, k * T:(k + 1) * T], xn_in[k * 128:(k + 1) * 128, :])
    p.op("gpsimd", lambda e: e.iota(io[:], [[1, 128]], channel_multiplier=0,
                                    allow_small_or_imprecise_dtypes=True))
    p.op("gpsimd", lambda e: e.iota(pcol[:], [[0, 1]], channel_multiplier=1,
                                    allow_small_or_imprecise_dtypes=True))
    p.op("vector", lambda e: e.memset(eighth[:], 0.125))
    p.barrier()
    p.op("vector", lambda e: e.tensor_single_scalar(out=ident[:], in_=io[:],
                                                    scalar=pcol[:], op=ALU.is_equal))
    p.op("vector", lambda e: e.tensor_single_scalar(out=tri[:], in_=io[:],
                                                    scalar=pcol[:], op=ALU.is_le))
    p.barrier()

    def proj(w_sb, nout, dst_sb, scale=None):
        for o in range(nout):
            for nn in range(T // 512):
                pb = (o * (T // 512) + nn) % 2
                for k in range(KD):
                    p.op("tensor", lambda e, o=o, nn=nn, k=k, pb=pb,
                         w_sb=w_sb, nout=nout: e.matmul(
                        ps_qkv[pb][:],
                        w_sb[:, k * nout * 128 + o * 128:k * nout * 128 + (o + 1) * 128],
                        xn_sb[:, k * T + nn * 512:k * T + nn * 512 + 512],
                        start=(k == 0), stop=(k == KD - 1)))
                p.barrier()
                if scale is None:
                    p.op("vector", lambda e, o=o, nn=nn, pb=pb, dst_sb=dst_sb: e.tensor_copy(
                        dst_sb[:, o * T + nn * 512:o * T + nn * 512 + 512], ps_qkv[pb][:]))
                else:
                    p.op("vector", lambda e, o=o, nn=nn, pb=pb, dst_sb=dst_sb,
                         scale=scale: e.tensor_scalar_mul(
                        out=dst_sb[:, o * T + nn * 512:o * T + nn * 512 + 512],
                        in0=ps_qkv[pb][:], scalar1=scale))
        p.barrier()
    proj(wq_sb, KD, q_sb, scale=0.125)
    proj(wk_sb, 2, k_sb)
    proj(wv_sb, 2, v_sb)
    p.barrier()
    for g in range(NKV):
        vt = g // 2
        vo = (g % 2) * 64
        for blk in range(NB):
            p.op("tensor", lambda e, vt=vt, vo=vo, blk=blk: e.transpose(
                ps_t[:, 0:64] if False else ps_t[:, (blk % 8) * 64:(blk % 8) * 64 + 64],
                v_sb[vo:vo + 64, vt * T + blk * 128:vt * T + (blk + 1) * 128],
                ident[vo:vo + 64, vo:vo + 64]))
            if blk % 8 == 7 or blk == NB - 1:
                p.barrier()
                for b2 in range(blk - blk % 8, blk + 1):
                    p.op("vector", lambda e, g=g, b2=b2, blk=blk: e.tensor_copy(
                        vtok[:, b2 * NKV * HD + g * HD:b2 * NKV * HD + (g + 1) * HD],
                        ps_t[:, (b2 % 8) * 64:(b2 % 8) * 64 + 64]))
                p.barrier()
    p.barrier()
    for g in range(NKV):
        kt, ko = g // 2, (g % 2) * 64
        p.dma("sync", kd_sb[0:64, g * T:(g + 1) * T],
              k_sb[ko:ko + 64, kt * T:(kt + 1) * T])
        p.dma("gpsimd", kd_sb[64:128, g * T:(g + 1) * T],
              k_sb[ko:ko + 64, kt * T:(kt + 1) * T])
    p.barrier()

    IT = NH * NB
    p.op("tensor", lambda e: e.matmul(
        ps_s[:, 0:128], q_sb[0:64, 0:128], kd_sb[0:64, 0:128],
        start=True, stop=True))
    p.barrier()
    for i in range(IT + 4):
        if i < IT:
            h, qb = divmod(i, NB)
            j = i % 2
            ncol = (qb + 1) * 128
            p.op("vector", lambda e, ncol=ncol, j=j: e.reduce_max(
                nm_b[:, j:j + 1], ps_s[:, 0:ncol], axis=AX.X, negate=True))
        if 0 <= i - 1 < IT:
            j = (i - 1) % 2
            p.op("vector", lambda e, j=j: e.reciprocal(rl_b[:, j:j + 1],
                                                       l_b[:, j:j + 1]))
        if 0 <= i - 2 < IT:
            h, qb = divmod(i - 2, NB)
            j = (i - 2) % 2
            ncol = (qb + 1) * 128
            p.op("vector", lambda e, ncol=ncol, j=j: e.tensor_copy(
                pt_buf[:, j * T:j * T + ncol], ps_t[:, 0:ncol]))
        p.barrier(lazy=True)
        if i < IT:
            h, qb = divmod(i, NB)
            j = i % 2
            ncol = (qb + 1) * 128
            p.op("scalar", lambda e, ncol=ncol, j=j: e.activation(
                out=p_buf[:, j * T:j * T + ncol], in_=ps_s[:, 0:ncol],
                func=AF.Exp, bias=nm_b[:, j:j + 1]))
        if 0 <= i - 1 < IT:
            h, qb = divmod(i - 1, NB)
            j = (i - 1) % 2
            ncol = (qb + 1) * 128
            p.op("vector", lambda e, ncol=ncol, j=j: e.tensor_scalar_mul(
                out=p_buf[:, j * T:j * T + ncol], in0=p_buf[:, j * T:j * T + ncol],
                scalar1=rl_b[:, j:j + 1]))
        if 0 <= i - 2 < IT:
            h, qb = divmod(i - 2, NB)
            j = (i - 2) % 2
            g = h // NKV
            for blk in range(qb + 1):
                p.op("tensor", lambda e, j=j, g=g, blk=blk, qb=qb, i=i: e.matmul(
                    ps_qkv[(i - 2) % 2][0:64, 0:128],
                    vtok[:, blk * NKV * HD + g * HD:blk * NKV * HD + (g + 1) * HD],
                    pt_buf[:, j * T + blk * 128:j * T + (blk + 1) * 128],
                    start=(blk == 0), stop=(blk == qb)))
        p.barrier(lazy=True)
        if i < IT:
            h, qb = divmod(i, NB)
            j = i % 2
            ncol = (qb + 1) * 128
            p.op("vector", lambda e, j=j, qb=qb: e.tensor_mul(
                p_buf[:, j * T + qb * 128:j * T + (qb + 1) * 128],
                p_buf[:, j * T + qb * 128:j * T + (qb + 1) * 128], tri[:]))
            p.op("vector", lambda e, ncol=ncol, j=j: e.reduce_sum(
                l_b[:, j:j + 1], p_buf[:, j * T:j * T + ncol], axis=AX.X))
        if 0 <= i - 1 < IT:
            h, qb = divmod(i - 1, NB)
            j = (i - 1) % 2
            for blk in range(qb + 1):
                p.op("tensor", lambda e, j=j, blk=blk: e.transpose(
                    ps_t[:, blk * 128:(blk + 1) * 128],
                    p_buf[:, j * T + blk * 128:j * T + (blk + 1) * 128], ident[:]))
        if i + 1 < IT:
            h, qb = divmod(i + 1, NB)
            qt, qo = h // 2, (h % 2) * 64
            g = h // NKV
            ncol = (qb + 1) * 128
            nmm = (ncol + 511) // 512
            for mmi in range(nmm):
                w = min(512, ncol - mmi * 512)
                p.op("tensor", lambda e, qt=qt, qo=qo, g=g, qb=qb,
                     mmi=mmi, w=w: e.matmul(
                    ps_s[:, mmi * 512:mmi * 512 + w],
                    q_sb[qo:qo + 64, qt * T + qb * 128:qt * T + (qb + 1) * 128],
                    kd_sb[qo:qo + 64, g * T + mmi * 512:g * T + mmi * 512 + w],
                    start=True, stop=True))
        if 0 <= i - 2 < IT:
            h, qb = divmod(i - 2, NB)
            j = (i - 2) % 2
            qt, qo = h // 2, (h % 2) * 64
            p.op("scalar", lambda e, qt=qt, qo=qo, qb=qb, j=j: e.copy(
                ctx_sb[qo:qo + 64, qt * T + qb * 128:qt * T + (qb + 1) * 128],
                ps_qkv[j][0:64, 0:128]))
        p.barrier(lazy=True)

    for o in range(KD):
        for nn in range(T // 512):
            pb = (o * (T // 512) + nn) % 2
            for k in range(KD):
                p.op("tensor", lambda e, o=o, nn=nn, k=k, pb=pb: e.matmul(
                    ps_qkv[pb][:], wo_sb[:, k * D + o * 128:k * D + (o + 1) * 128],
                    ctx_sb[:, k * T + nn * 512:k * T + nn * 512 + 512],
                    start=(k == 0), stop=(k == KD - 1)))
            p.barrier()
            p.op("vector", lambda e, pb=pb: e.tensor_copy(
                at_ev[:, pb * 512:(pb + 1) * 512], ps_qkv[pb][:]))
            p.barrier()
            p.dma("sync", attn_out[o * 128:(o + 1) * 128, nn * 512:(nn + 1) * 512],
                  at_ev[:, pb * 512:(pb + 1) * 512])
    p.barrier()
    p.pop_region()
    if standalone:
        return p.build()



def build_stageF(p=None, dr=None):
    standalone = p is None
    if standalone:
        p = Prog("F")
    _D = _mk_dram(p, dr)
    p.push_region()
    xn_in = _D("xn1_bf", [D, T], BF16, "ExternalInput")
    x1_in = _D("x1", [D, T], F32, "ExternalInput")
    attn_in = _D("attn", [D, T], F32, "ExternalInput")
    Wr_in = _D("Wr", [D, NE], F32, "ExternalInput")
    br_in = _D("br", [NE, 1], F32, "ExternalInput")
    W1_in = _D("W1", [NE, D, HF], F32, "ExternalInput")
    b1_in = _D("b1d", [NE, 128, HF // 128], F32, "ExternalInput")
    W2_in = _D("W2", [NE, HF, D], F32, "ExternalInput")
    b2_in = _D("b2d", [NE, 128, D // 128], F32, "ExternalInput")
    out_T = _D("outT", [D, T], F32, "ExternalOutput")
    w_dram = p.dram("w_scratch", [NE, T], F32, "Internal")
    acc_dram = p.dram("acc_scratch", [D, T], F32, "Internal")

    NB = T // 128
    NO1 = HF // 128
    xn_sb = p.sb("xn_sbF", [128, KD * T], BF16)
    wr_sb = p.sb("wr_sbF", [128, KD * NE], BF16)
    br_sb = p.sb("br_sbF", [NE, 1], F32)
    w1_sb = p.sb("w1_sbF", [128, KD * HF], BF16)
    w2_sb = p.sb("w2_sbF", [128, NO1 * D], BF16)
    b1_sb = p.sb("b1_sbF", [128, NE * NO1], F32)
    b2_sb = p.sb("b2_sbF", [128, NE * KD], F32)
    io = p.sb("ioF", [128, 128], F32)
    pcol = p.sb("pcolF", [128, 1], F32)
    id32 = p.sb("id32F", [128, 128], F32)
    lg_sb = p.sb("lg_sbF", [128, T], F32)
    lgt = p.sb("lgtF", [128, 2 * NE], F32)
    nm1 = p.sb("nm1F", [128, 1], F32)
    e1 = p.sb("e1F", [128, NE], F32)
    t1 = p.sb("t1F", [128, NE], F32)
    m2 = p.sb("m2F", [128, 1], F32)
    pe = p.sb("peF", [128, NE], F32)
    msk = p.sb("mskF", [128, NE], F32)
    zz = p.sb("zzF", [128, 1], F32)
    rz = p.sb("rzF", [128, 1], F32)
    w_tok = p.sb("w_tokF", [128, NE], F32)
    wb_e = p.sb("wb_eF", [128, 2 * TC], F32)
    h_sb = p.sb("h_sbF", [128, NO1 * 512], BF16)


    p.push_region()
    ps_r = p.ps("ps_rF", [NE, 512])
    ps_t8 = p.ps("ps_t8F", [128, NE])
    ps_w8 = p.ps("ps_w8F", [NE, 128])

    for k in range(KD):
        p.dma("sync", xn_sb[:, k * T:(k + 1) * T], xn_in[k * 128:(k + 1) * 128, :])
    p.dma("gpsimd", wr_sb.ap().rearrange("p (k j) -> p k j", k=KD),
          Wr_in.rearrange("(k p) j -> p k j", p=128))
    p.dma("sync", br_sb[:], br_in)
    p.dma("sync", b1_sb.ap().rearrange("p (e o) -> p e o", e=NE),
          b1_in.transpose((1, 0, 2)))
    p.dma("sync", b2_sb.ap().rearrange("p (e o) -> p e o", e=NE),
          b2_in.transpose((1, 0, 2)))
    p.op("gpsimd", lambda e: e.iota(io[:], [[1, 128]], channel_multiplier=0,
                                    allow_small_or_imprecise_dtypes=True))
    p.op("gpsimd", lambda e: e.iota(pcol[:], [[0, 1]], channel_multiplier=1,
                                    allow_small_or_imprecise_dtypes=True))
    p.barrier()
    p.op("vector", lambda e: e.tensor_single_scalar(out=id32[:], in_=io[:],
                                                    scalar=pcol[:], op=ALU.is_equal))
    p.barrier()

    for c in range(NCH):
        for k in range(KD):
            p.op("tensor", lambda e, k=k, c=c: e.matmul(
                ps_r[:], wr_sb[:, k * NE:(k + 1) * NE],
                xn_sb[:, k * T + c * TC:k * T + (c + 1) * TC],
                start=(k == 0), stop=(k == KD - 1)))
        p.barrier()
        p.op("vector", lambda e, c=c: e.tensor_scalar_add(
            out=lg_sb[0:NE, c * TC:(c + 1) * TC], in0=ps_r[:], scalar1=br_sb[:]))
        p.barrier()
    for blk in range(NB):
        pb = blk % 2
        p.op("tensor", lambda e, blk=blk: e.transpose(
            ps_t8[:], lg_sb[0:NE, blk * 128:(blk + 1) * 128], id32[0:NE, 0:NE]))
        p.barrier()
        p.op("vector", lambda e, pb=pb: e.tensor_copy(lgt[:, pb * NE:(pb + 1) * NE],
                                                      ps_t8[:]))
        p.barrier()
        sl = slice(pb * NE, (pb + 1) * NE)
        p.op("vector", lambda e, sl=sl: e.reduce_max(nm1[:], lgt[:, sl], axis=AX.X,
                                                     negate=True))
        p.barrier()
        p.op("vector", lambda e, sl=sl: e.tensor_scalar(
            out=e1[:], in0=lgt[:, sl], scalar1=nm1[:], scalar2=None,
            op0=ALU.add, op1=ALU.bypass))
        p.barrier()
        p.op("vector", lambda e: e.tensor_single_scalar(out=e1[:], in_=e1[:],
                                                        scalar=0.0, op=ALU.is_equal))
        p.barrier()
        p.op("vector", lambda e, sl=sl: e.scalar_tensor_tensor(
            out=t1[:], in0=e1[:], scalar=-1e9, in1=lgt[:, sl],
            op0=ALU.mult, op1=ALU.add))
        p.barrier()
        p.op("vector", lambda e: e.reduce_max(m2[:], t1[:], axis=AX.X))
        p.op("scalar", lambda e, sl=sl: e.activation(out=pe[:], in_=lgt[:, sl],
                                                     func=AF.Exp, bias=nm1[:]))
        p.barrier()
        p.op("vector", lambda e, sl=sl: e.tensor_single_scalar(
            out=msk[:], in_=lgt[:, sl], scalar=m2[:], op=ALU.is_ge))
        p.barrier()
        p.op("vector", lambda e: e.tensor_mul(pe[:], pe[:], msk[:]))
        p.barrier()
        p.op("vector", lambda e: e.reduce_sum(zz[:], pe[:], axis=AX.X))
        p.barrier()
        p.op("vector", lambda e: e.reciprocal(rz[:], zz[:]))
        p.barrier()
        p.op("vector", lambda e: e.tensor_scalar_mul(out=w_tok[:], in0=pe[:],
                                                     scalar1=rz[:]))
        p.barrier()
        p.op("tensor", lambda e: e.transpose(ps_w8[:], w_tok[:], id32[:]))
        p.barrier()
        p.op("vector", lambda e, blk=blk: e.tensor_copy(
            lg_sb[0:NE, blk * 128:(blk + 1) * 128], ps_w8[:]))
        p.barrier()
    p.dma("sync", w_dram, lg_sb[0:NE, :])
    p.barrier()
    p.pop_region()
    ps_m = [p.ps(f"ps_Fm{i}", [128, 512]) for i in range(8)]

    t2p = lg_sb[:, 0:1536]
    for k in range(KD):
        p.dma("gpsimd", w1_sb[:, k * HF:(k + 1) * HF], W1_in[0, k * 128:(k + 1) * 128, :],
              lane=1)
    p.dma("gpsimd", w2_sb.ap().rearrange("p (k dd) -> p k dd", k=NO1),
          W2_in[0].rearrange("(k p) dd -> p k dd", p=128), lane=1)
    p.dma("gpsimd", wb_e[:, 0:TC], w_dram[0, 0:TC].partition_broadcast(128), lane=1)
    p.fence_lane1("tensor")
    p.fence_lane1("vector")
    p.barrier()
    p.eng_lag = {}
    NP1 = NO1 // 4
    NG = NP1 + 1 + KD + 2
    for ex in range(NE):
        for c in range(NCH):
            csl = slice(c * TC, (c + 1) * TC)
            par = (ex * NCH + c) % 2
            wb_cur = wb_e[:, par * TC:(par + 1) * TC]
            cn, exn = (c + 1, ex) if c + 1 < NCH else (0, ex + 1)
            for g in range(NG):
                if g == 0 and exn < NE:
                    p.dma("gpsimd", wb_e[:, (1 - par) * TC:(2 - par) * TC],
                          w_dram[exn, cn * TC:(cn + 1) * TC].partition_broadcast(128),
                          lane=1)
                if g == 0 and c == 0 and ex >= 1:
                    p.fence_lane1("tensor")
                    p.dma("gpsimd", w2_sb.ap().rearrange("p (k dd) -> p k dd", k=NO1),
                          W2_in[ex].rearrange("(k p) dd -> p k dd", p=128), lane=1)
                if g == NP1:
                    if c == 0:
                        p.fence_lane1("tensor")
                    p.fence_lane1("vector")
                if g == NP1 + 1 and c == NCH - 1 and ex + 1 < NE:
                    for k in range(KD):
                        p.dma("gpsimd", w1_sb[:, k * HF:(k + 1) * HF],
                              W1_in[ex + 1, k * 128:(k + 1) * 128, :], lane=1)
                if g < NP1:
                    for oo in range(4):
                        o = g * 4 + oo
                        pb = (g % 2) * 4 + oo
                        for k in range(KD):
                            p.op("tensor", lambda e, o=o, k=k, pb=pb, c=c: e.matmul(
                                ps_m[pb][:],
                                w1_sb[:, k * HF + o * 128:k * HF + (o + 1) * 128],
                                xn_sb[:, k * T + c * TC:k * T + (c + 1) * TC],
                                start=(k == 0), stop=(k == KD - 1)))
                elif NP1 + 1 <= g < NP1 + 1 + KD:
                    dd = g - NP1 - 1
                    for k in range(NO1):
                        p.op("tensor", lambda e, dd=dd, k=k: e.matmul(
                            ps_m[dd][:], w2_sb[:, k * D + dd * 128:k * D + (dd + 1) * 128],
                            h_sb[:, k * TC:(k + 1) * TC],
                            start=(k == 0), stop=(k == NO1 - 1)))
                gm = g - 1
                if 0 <= gm < NP1:
                    for oo in range(4):
                        o = gm * 4 + oo
                        pb = (gm % 2) * 4 + oo
                        if oo % 2 == 0:
                            p.op("vector", lambda e, o=o, pb=pb, ex=ex: e.tensor_scalar(
                                out=h_sb[:, o * TC:(o + 1) * TC], in0=ps_m[pb][:],
                                scalar1=b1_sb[:, ex * NO1 + o:ex * NO1 + o + 1],
                                scalar2=0.0, op0=ALU.add, op1=ALU.max))
                        else:
                            p.op("scalar", lambda e, o=o, pb=pb, ex=ex: e.activation(
                                out=h_sb[:, o * TC:(o + 1) * TC], in_=ps_m[pb][:],
                                func=AF.Relu, bias=b1_sb[:, ex * NO1 + o:ex * NO1 + o + 1]))
                elif NP1 + 1 <= gm < NP1 + 1 + KD:
                    dd = gm - NP1 - 1
                    p3 = dd % 3
                    p.op("vector", lambda e, dd=dd, p3=p3, ex=ex, wb_cur=wb_cur:
                         e.scalar_tensor_tensor(
                        out=t2p[:, p3 * 512:(p3 + 1) * 512], in0=ps_m[dd][:],
                        scalar=b2_sb[:, ex * KD + dd:ex * KD + dd + 1],
                        in1=wb_cur, op0=ALU.add, op1=ALU.mult))
                gm2 = g - 2
                if NP1 + 1 <= gm2 < NP1 + 1 + KD:
                    dd = gm2 - NP1 - 1
                    p3 = dd % 3
                    if ex == 0:
                        p.dma("gpsimd", acc_dram[dd * 128:(dd + 1) * 128, csl],
                              t2p[:, p3 * 512:(p3 + 1) * 512])
                    else:
                        def accdma(e, dd=dd, csl=csl, p3=p3):
                            return e.dma_start(
                                out=acc_dram[dd * 128:(dd + 1) * 128, csl],
                                in_=t2p[:, p3 * 512:(p3 + 1) * 512], accum_op=ALU.add)
                        cell = ["dma"]
                        p._q("gpsimd", ("call", accdma, cell))
                        p.tot_dma += 16
                p.barrier(lazy=True)
    p.eng_lag = {}
    res_sb = h_sb[:, 0:2048].bitcast(F32)
    out_sb = h_sb[:, 2048:4096].bitcast(F32)
    acc_sb = h_sb[:, 4096:6144].bitcast(F32)
    for dd in range(KD):
        for c in range(NCH):
            pb = c % 2
            csl = slice(c * TC, (c + 1) * TC)
            rows = slice(dd * 128, (dd + 1) * 128)
            rsl = res_sb[:, pb * TC:(pb + 1) * TC]
            osl = out_sb[:, pb * TC:(pb + 1) * TC]
            asl = acc_sb[:, pb * TC:(pb + 1) * TC]
            p.dma("sync", rsl, x1_in[rows, csl])
            p.dma("sync", osl, attn_in[rows, csl])
            p.dma("sync", asl, acc_dram[rows, csl])
            p.barrier()
            p.op("vector", lambda e, rsl=rsl, osl=osl: e.tensor_add(rsl, rsl, osl))
            p.barrier()
            p.op("vector", lambda e, rsl=rsl, osl=osl, asl=asl: e.tensor_add(
                osl, rsl, asl))
            p.barrier()
            p.dma("sync", out_T[rows, csl], osl)
    p.barrier()
    p.pop_region()
    if standalone:
        return p.build()

def _t(a):
    return np.ascontiguousarray(np.asarray(a).T)


def build_fused():
    p = Prog("fused")
    EI = lambda n, s, d: p.dram(n, list(s), d, "ExternalInput")
    IN = lambda n, s, d: p.dram(n, list(s), d, "Internal")
    xT = EI("xT", [D, T], F32)
    w_inT = EI("w_inT", [D, 2 * DIN], F32)
    conv_w = EI("conv_w", [DIN, 4], F32)
    conv_b = EI("conv_b", [128, NDI], F32)
    xp_wT = EI("xp_wT", [DIN, 96], F32)
    dt_wT = EI("dt_wT", [DTR, DIN], F32)
    dt_b = EI("dt_b", [128, NDI], F32)
    A_dev = EI("A_dev", [128, NDI * NST], F32)
    D_dev = EI("D_dev", [128, NDI], F32)
    outw_T = EI("outw_T", [DIN, D], F32)
    wrT = EI("wrT", [D, D], F32)
    wkT_d = EI("wkT_d", [D, D], F32)
    wvT_d = EI("wvT_d", [D, D], F32)
    mu_r = EI("mu_r", [128, KD], F32)
    mu_k = EI("mu_k", [128, KD], F32)
    wqT = EI("wqT", [D, D], F32)
    wkT_e = EI("wkT_e", [D, NKV * HD], F32)
    wvT_e = EI("wvT_e", [D, NKV * HD], F32)
    woT = EI("woT", [D, D], F32)
    Wr = EI("Wr", [D, NE], F32)
    br = EI("br", [NE, 1], F32)
    W1 = EI("W1", [NE, D, HF], F32)
    b1d = EI("b1d", [NE, 128, HF // 128], F32)
    W2 = EI("W2", [NE, HF, D], F32)
    b2d = EI("b2d", [NE, 128, D // 128], F32)
    outT = p.dram("outT", [D, T], F32, "ExternalOutput")

    u_bf = IN("u_bf", [DIN, T], BF16)
    sz_bf = IN("sz_bf", [DIN, T], BF16)
    xn_bf = IN("xn_bf", [D, T], BF16)
    delta = IN("delta", [DIN, T], F32)
    du = IN("du", [DIN, T], F16)
    bc = IN("bc", [2 * NST, T], F32)
    mamba = IN("mamba", [D, T], F32)
    x1 = IN("x1", [D, T], F32)
    xn1_bf = IN("xn1_bf", [D, T], BF16)
    attn = IN("attn", [D, T], F32)

    build_stageA(p, dict(xT=xT, w_inT=w_inT, conv_w=conv_w, conv_b=conv_b,
                         u_bf=u_bf, sz_bf=sz_bf, xn_bf=xn_bf))
    build_stageB(p, dict(u_bf=u_bf, xp_wT=xp_wT, dt_wT=dt_wT, dt_b=dt_b,
                         delta=delta, du=du, bc=bc))
    build_stageC(p, dict(delta=delta, du=du, u_bf=u_bf, sz_bf=sz_bf, bc=bc,
                         A_dev=A_dev, D_dev=D_dev, outw_T=outw_T, mamba=mamba))
    build_stageD(p, dict(xT=xT, xn_bf=xn_bf, mamba=mamba, wrT=wrT, wkT=wkT_d,
                         wvT=wvT_d, mu_r=mu_r, mu_k=mu_k, x1=x1, xn1_bf=xn1_bf))
    build_stageE(p, dict(xn1_bf=xn1_bf, wqT=wqT, wkT=wkT_e, wvT=wvT_e, woT=woT,
                         attn=attn))
    build_stageF(p, dict(xn1_bf=xn1_bf, x1=x1, attn=attn, Wr=Wr, br=br, W1=W1,
                         b1d=b1d, W2=W2, b2d=b2d, outT=outT))
    return p.build()


def kernel(**ins):
    cores = list(range(B))
    shared = {
        "w_inT": _t(ins["L0_attn_in_proj_w"]),
        "conv_w": np.ascontiguousarray(np.asarray(ins["L0_attn_conv_w"])[:, 0, :]),
        "conv_b": np.ascontiguousarray(np.asarray(ins["L0_attn_conv_b"]).reshape(NDI, 128).T),
        "xp_wT": _t(ins["L0_attn_x_proj_w"]),
        "dt_wT": _t(ins["L0_attn_dt_proj_w"]),
        "dt_b": np.ascontiguousarray(np.asarray(ins["L0_attn_dt_proj_b"]).reshape(NDI, 128).T),
        "A_dev": np.ascontiguousarray(np.asarray(ins["L0_attn_A_log"])
                                      .reshape(NDI, 128, NST).transpose(1, 0, 2)
                                      .reshape(128, NDI * NST)),
        "D_dev": np.ascontiguousarray(np.asarray(ins["L0_attn_D"]).reshape(NDI, 128).T),
        "outw_T": _t(ins["L0_attn_out_proj_w"]),
        "wrT": _t(ins["L0_ffn_W_r"]),
        "wkT_d": _t(ins["L0_ffn_W_k"]),
        "wvT_d": _t(ins["L0_ffn_W_v"]),
        "mu_r": np.ascontiguousarray(np.asarray(ins["L0_ffn_mu_r"]).reshape(KD, 128).T),
        "mu_k": np.ascontiguousarray(np.asarray(ins["L0_ffn_mu_k"]).reshape(KD, 128).T),
        "wqT": _t(ins["L1_attn_wq"]),
        "wkT_e": _t(ins["L1_attn_wk"]),
        "wvT_e": _t(ins["L1_attn_wv"]),
        "woT": _t(ins["L1_attn_wo"]),
        "Wr": np.ascontiguousarray(np.asarray(ins["L1_ffn_Wr"], np.float32)),
        "br": np.ascontiguousarray(np.asarray(ins["L1_ffn_br"], np.float32)[:, None]),
        "W1": np.ascontiguousarray(np.asarray(ins["L1_ffn_W1"], np.float32)),
        "b1d": np.ascontiguousarray(np.asarray(ins["L1_ffn_b1"], np.float32)
                                    .reshape(NE, HF // 128, 128).transpose(0, 2, 1)),
        "W2": np.ascontiguousarray(np.asarray(ins["L1_ffn_W2"], np.float32)),
        "b2d": np.ascontiguousarray(np.asarray(ins["L1_ffn_b2"], np.float32)
                                    .reshape(NE, D // 128, 128).transpose(0, 2, 1)),
    }
    maps = [dict(shared, xT=np.ascontiguousarray(
        np.asarray(ins["L0_attn_hidden_states"][b], np.float32).T)) for b in range(B)]
    nc = build_fused()
    res = run_bass_kernel_spmd(nc, maps, core_ids=cores)
    out = np.stack([np.asarray(res.results[b]["outT"]).T for b in range(B)], 0)
    return np.ascontiguousarray(out.astype(np.float32))


STAGE_BUILDERS = {"A": build_stageA, "B": build_stageB, "C": build_stageC, "D": build_stageD, "E": build_stageE, "F": build_stageF}


# version 09
# speedup vs baseline: 1.2433x; 1.0032x over previous
import contextlib
import numpy as np
import concourse.bass as bass
import concourse.mybir as mybir
from concourse.bass_utils import run_bass_kernel_spmd

F32 = mybir.dt.float32
BF16 = mybir.dt.bfloat16
F16 = mybir.dt.float16
AF = mybir.ActivationFunctionType
ALU = mybir.AluOpType
AX = mybir.AxisListType

ENGS = ("sync", "gpsimd", "vector", "scalar", "tensor")

B, T, D = 8, 2048, 1024
DIN = 2048
NST = 16
DTR = 64
NH, NKV, HD = 16, 4, 64
NE, HF = 8, 4096
TC = 512
NCH = T // TC
KD = D // 128
NDI = DIN // 128


class Prog:

    def __init__(self, name):
        self.nc = bass.Bass("TRN2", target_bir_lowering=False)
        self.name = name
        self.queues = {e: [] for e in ENGS}
        self.tot = {e: 0 for e in ENGS}
        self.tot_dma = 0
        self.pending = {e: None for e in ENGS}
        self.cur_snapshot = ({e: 0 for e in ENGS}, 0, {}, [])
        self.waited = {e: None for e in ENGS}
        self.stack = contextlib.ExitStack()
        self._stacks = [self.stack]
        self._names = {}
        self._dma_hist = [0]
        self.tot_dma1 = 0
        self._tot_hist = [dict(self.tot)]
        self.eng_lag = {}

    def _uniq(self, name):
        n = self._names.get(name, 0)
        self._names[name] = n + 1
        return name if n == 0 else f"{name}_{n}"

    def push_region(self):
        self._stacks.append(contextlib.ExitStack())

    def pop_region(self):
        self.barrier()
        self._stacks.pop().close()

    def dram(self, name, shape, dtype, kind):
        return self.nc.dram_tensor(name, list(shape), dtype, kind=kind).ap()

    def sb(self, name, shape, dtype):
        return self._stacks[-1].enter_context(
            self.nc.sbuf_tensor(self._uniq(name), list(shape), dtype))

    def ps(self, name, shape, dtype=F32):
        return self._stacks[-1].enter_context(
            self.nc.psum_tensor(self._uniq(name), list(shape), dtype))

    def _q(self, eng, item):
        snap = self.cur_snapshot
        if self.waited[eng] is not snap:
            self.queues[eng].append(("wait", snap, eng))
            self.waited[eng] = snap
        self.queues[eng].append(item)

    def op(self, eng, fn):
        cell = [0]
        self._q(eng, ("call", fn, cell))
        self.pending[eng] = cell

    def dma(self, eng, out, in_, lane=0):
        if lane == 0:
            cell = ["dma"]
            self._q(eng, ("call", lambda e: e.dma_start(out=out, in_=in_), cell))
            self.tot_dma += 16
        else:
            cell = ["dma1"]
            self._q(eng, ("call", lambda e: e.dma_start(out=out, in_=in_), cell))
            self.tot_dma1 += 16

    def fence_lane1(self, eng):
        v = self.tot_dma1
        if v > 0:
            self._q(eng, ("wait1", v, eng))

    def barrier(self, lazy=False):
        for e in ENGS:
            c = self.pending[e]
            if c is not None:
                c[0] = 1
                self.tot[e] += 1
                self.pending[e] = None
        dval = self._dma_hist[-1] if lazy else self.tot_dma
        self._dma_hist.append(self.tot_dma)
        self._tot_hist.append(dict(self.tot))
        self.cur_snapshot = (dict(self.tot), dval, dict(self.eng_lag),
                             [dict(h) for h in self._tot_hist[-4:]])

    def build(self):
        self.barrier()
        self.queues["sync"].append(("wait", self.cur_snapshot, "__final__"))
        if self.tot_dma1 > 0:
            self.queues["sync"].append(("wait1", self.tot_dma1, "__final__"))
        nc = self.nc
        bars = {e: self.stack.enter_context(nc.semaphore(f"bar_{e}")) for e in ENGS}
        bar_dma = self.stack.enter_context(nc.semaphore("bar_dma"))
        bar_dma1 = self.stack.enter_context(nc.semaphore("bar_dma1"))
        with nc.Block() as block:
            def emit_stream(eng_name):
                def _(e):
                    for item in self.queues[eng_name]:
                        if item[0] == "wait1":
                            e.wait_ge(bar_dma1, item[1])
                            continue
                        if item[0] == "wait":
                            snap, me = item[1], item[2]
                            counts, dmas, lags, hist = snap
                            mylag = lags.get(me, {})
                            for oe in ENGS:
                                v = counts[oe]
                                lag = mylag.get(oe, 0)
                                if lag > 0 and len(hist) > lag:
                                    v = hist[-1 - lag][oe]
                                if v > 0:
                                    e.wait_ge(bars[oe], v)
                            if dmas > 0:
                                e.wait_ge(bar_dma, dmas)
                        else:
                            _, fn, cell = item
                            inst = fn(e)
                            if cell[0] == "dma":
                                inst.then_inc(bar_dma, 16)
                            elif cell[0] == "dma1":
                                inst.then_inc(bar_dma1, 16)
                            elif cell[0] == 1:
                                inst.then_inc(bars[eng_name], 1)
                return _
            block.sync(emit_stream("sync"))
            block.gpsimd(emit_stream("gpsimd"))
            block.vector(emit_stream("vector"))
            block.scalar(emit_stream("scalar"))
            block.tensor(emit_stream("tensor"))
        self.stack.close()
        return nc


def _mk_dram(p, dr):
    def _D(name, shape, dtype, kind):
        if dr is not None:
            return dr[name]
        return p.dram(name, shape, dtype, kind)
    return _D


def build_stageA(p=None, dr=None):
    standalone = p is None
    if standalone:
        p = Prog("A")
    _D = _mk_dram(p, dr)
    p.push_region()
    xT = _D("xT", [D, T], F32, "ExternalInput")
    w_inT = _D("w_inT", [D, 2 * DIN], F32, "ExternalInput")
    conv_w = _D("conv_w", [DIN, 4], F32, "ExternalInput")
    conv_b = _D("conv_b", [128, NDI], F32, "ExternalInput")
    u_out = _D("u_bf", [DIN, T], BF16, "ExternalOutput")
    sz_out = _D("sz_bf", [DIN, T], BF16, "ExternalOutput")
    xn_out = _D("xn_bf", [D, T], BF16, "ExternalOutput")

    w_sb = p.sb("w_sb", [128, KD * 2 * DIN], BF16)
    cw_sb = p.sb("cw_sb", [128, NDI * 4], F32)
    cb_sb = p.sb("cb_sb", [128, NDI], F32)
    ones_col = p.sb("ones_col", [128, 1], BF16)
    ones_row = p.sb("ones_row", [1, 128], BF16)
    eps_sb = p.sb("eps_sb", [1, 1], F32)

    x_sb = p.sb("x_sb", [128, KD * TC], F32)
    xb_sb = p.sb("xb_sb", [128, KD * TC], BF16)
    xsq_sb = p.sb("xsq_sb", [128, KD * TC], BF16)
    xn_sb = p.sb("xn_sb", [128, KD * TC], BF16)
    tmp_sb = p.sb("tmp_sb", [128, TC], BF16)
    mrow = p.sb("mrow", [1, TC], F32)
    vrow = p.sb("vrow", [1, TC], F32)
    msq = p.sb("msq", [1, TC], F32)
    srow = p.sb("srow", [1, TC], F32)
    rrow = p.sb("rrow", [1, TC], F32)
    m_bf = p.sb("m_bf", [1, TC], BF16)
    r_bf = p.sb("r_bf", [1, TC], BF16)
    mb_sb = p.sb("mb_sb", [128, TC], BF16)
    rb_sb = p.sb("rb_sb", [128, TC], BF16)
    xc_sb = p.sb("xc_sb", [128, NDI * (TC + 3)], F32)
    cacc = p.sb("cacc", [128, 4 * TC], F32)
    u_sb = p.sb("u_sb", [128, NDI * TC], BF16)
    sz_sb = p.sb("sz_sb", [128, NDI * TC], BF16)

    ps_sx = p.ps("ps_sx", [1, TC])
    ps_sxx = p.ps("ps_sxx", [1, TC])
    ps_mb = p.ps("ps_mb", [128, TC])
    ps_rb = p.ps("ps_rb", [128, TC])
    ps_xz = [p.ps(f"ps_xz{i}", [128, TC]) for i in range(4)]

    for k in range(KD):
        p.dma("gpsimd", w_sb[:, k * 2 * DIN:(k + 1) * 2 * DIN],
              w_inT[k * 128:(k + 1) * 128, :])
    p.dma("sync", cw_sb.ap().rearrange("p (o c) -> p o c", c=4),
          conv_w.rearrange("(o p) c -> p o c", p=128))
    p.dma("sync", cb_sb[:], conv_b)
    p.op("vector", lambda e: e.memset(eps_sb[:], 1e-5))
    p.op("vector", lambda e: e.memset(ones_col[:], 1.0))
    p.op("vector", lambda e: e.memset(ones_row[:], 1.0))
    for o in range(NDI):
        p.op("vector", lambda e, o=o: e.memset(
            xc_sb[:, o * (TC + 3):o * (TC + 3) + 3], 0.0))
    p.barrier()

    for c in range(NCH):
        csl = slice(c * TC, (c + 1) * TC)
        src = xT[:, csl].rearrange("(k p) t -> p k t", p=128)
        p.dma("sync", x_sb.ap().rearrange("p (k t) -> p k t", k=KD), src)
        p.dma("gpsimd", xb_sb.ap().rearrange("p (k t) -> p k t", k=KD), src)
        p.barrier()
        p.op("scalar", lambda e: e.activation(out=xsq_sb[:], in_=x_sb[:],
                                              func=AF.Square))
        p.barrier()
        for k in range(KD):
            p.op("tensor", lambda e, k=k: e.matmul(
                ps_sx[:], ones_col[:], xb_sb[:, k * TC:(k + 1) * TC],
                start=(k == 0), stop=(k == KD - 1)))
        for k in range(KD):
            p.op("tensor", lambda e, k=k: e.matmul(
                ps_sxx[:], ones_col[:], xsq_sb[:, k * TC:(k + 1) * TC],
                start=(k == 0), stop=(k == KD - 1)))
        p.barrier()
        p.op("vector", lambda e: e.tensor_scalar_mul(out=mrow[:], in0=ps_sx[:],
                                                     scalar1=1.0 / D))
        p.op("vector", lambda e: e.tensor_scalar_mul(out=vrow[:], in0=ps_sxx[:],
                                                     scalar1=1.0 / D))
        p.op("vector", lambda e: e.tensor_mul(msq[:], mrow[:], mrow[:]))
        p.op("vector", lambda e: e.tensor_sub(vrow[:], vrow[:], msq[:]))
        p.barrier()
        p.op("scalar", lambda e: e.activation(out=srow[:], in_=vrow[:],
                                              func=AF.Sqrt, bias=eps_sb[:]))
        p.barrier()
        p.op("vector", lambda e: e.reciprocal(rrow[:], srow[:]))
        p.barrier()
        p.op("vector", lambda e: e.tensor_copy(m_bf[:], mrow[:]))
        p.op("vector", lambda e: e.tensor_copy(r_bf[:], rrow[:]))
        p.barrier()
        p.op("tensor", lambda e: e.matmul(ps_mb[:], ones_row[:], m_bf[:],
                                          start=True, stop=True))
        p.op("tensor", lambda e: e.matmul(ps_rb[:], ones_row[:], r_bf[:],
                                          start=True, stop=True))
        p.barrier()
        p.op("vector", lambda e: e.tensor_copy(mb_sb[:], ps_mb[:]))
        p.op("vector", lambda e: e.tensor_copy(rb_sb[:], ps_rb[:]))
        p.barrier()
        for k in range(KD):
            p.op("vector", lambda e, k=k: e.tensor_sub(
                tmp_sb[:], xb_sb[:, k * TC:(k + 1) * TC], mb_sb[:]))
            p.op("vector", lambda e, k=k: e.tensor_mul(
                xn_sb[:, k * TC:(k + 1) * TC], tmp_sb[:], rb_sb[:]))
        p.barrier()
        p.dma("gpsimd", xn_out[:, csl].rearrange("(k p) t -> p k t", p=128),
              xn_sb.ap().rearrange("p (k t) -> p k t", k=KD))
        p.eng_lag = {"tensor": {"vector": 2, "scalar": 2}}
        for g in range(35):
            if g < 32:
                o, pbx = g, g % 4
                for k in range(KD):
                    p.op("tensor", lambda e, o=o, pbx=pbx, k=k: e.matmul(
                        ps_xz[pbx][:],
                        w_sb[:, k * 2 * DIN + o * 128:k * 2 * DIN + (o + 1) * 128],
                        xn_sb[:, k * TC:(k + 1) * TC],
                        start=(k == 0), stop=(k == KD - 1)))
            gm = g - 1
            if 0 <= gm < 32:
                o, pbx = gm, gm % 4
                if o < 16:
                    base = o * (TC + 3)
                    p.op("vector", lambda e, pbx=pbx, base=base: e.tensor_copy(
                        xc_sb[:, base + 3:base + 3 + TC], ps_xz[pbx][:]))
                else:
                    zi = o - 16
                    p.op("scalar", lambda e, zi=zi, pbx=pbx: e.activation(
                        out=sz_sb[:, zi * TC:(zi + 1) * TC],
                        in_=ps_xz[pbx][:], func=AF.Silu))
            gm2 = g - 2
            if 0 <= gm2 < 16:
                o = gm2
                base = o * (TC + 3)
                w4 = cw_sb[:, o * 4:o * 4 + 4]
                cs = cacc[:, (gm2 % 4) * TC:(gm2 % 4 + 1) * TC]
                p.op("vector", lambda e, base=base, w4=w4, cs=cs: e.tensor_scalar_mul(
                    out=cs, in0=xc_sb[:, base + 3:base + 3 + TC],
                    scalar1=w4[:, 3:4]))
                for j in (2, 1, 0):
                    p.op("vector", lambda e, base=base, w4=w4, cs=cs, j=j:
                         e.scalar_tensor_tensor(
                             out=cs, in0=xc_sb[:, base + j:base + j + TC],
                             scalar=w4[:, j:j + 1], in1=cs,
                             op0=ALU.mult, op1=ALU.add))
                p.op("vector", lambda e, base=base: e.tensor_copy(
                    xc_sb[:, base:base + 3],
                    xc_sb[:, base + TC:base + TC + 3]))
            gm3 = g - 3
            if 0 <= gm3 < 16:
                o = gm3
                p.op("scalar", lambda e, o=o, gm3=gm3: e.activation(
                    out=u_sb[:, o * TC:(o + 1) * TC],
                    in_=cacc[:, (gm3 % 4) * TC:(gm3 % 4 + 1) * TC],
                    func=AF.Silu, bias=cb_sb[:, o:o + 1]))
            p.barrier(lazy=True)
        p.eng_lag = {}
        p.dma("gpsimd", u_out[:, csl].rearrange("(k p) t -> p k t", p=128),
              u_sb.ap().rearrange("p (k t) -> p k t", k=NDI))
        p.dma("gpsimd", sz_out[:, csl].rearrange("(k p) t -> p k t", p=128),
              sz_sb.ap().rearrange("p (k t) -> p k t", k=NDI))
        p.barrier()
    p.pop_region()
    if standalone:
        return p.build()


def build_stageB(p=None, dr=None):
    standalone = p is None
    if standalone:
        p = Prog("B")
    _D = _mk_dram(p, dr)
    p.push_region()
    u_in = _D("u_bf", [DIN, T], BF16, "ExternalInput")
    xp_wT = _D("xp_wT", [DIN, 96], F32, "ExternalInput")
    dt_wT = _D("dt_wT", [DTR, DIN], F32, "ExternalInput")
    dt_b = _D("dt_b", [128, NDI], F32, "ExternalInput")
    delta_out = _D("delta", [DIN, T], F32, "ExternalOutput")
    du_out = _D("du", [DIN, T], F16, "ExternalOutput")
    bc_out = _D("bc", [2 * NST, T], F32, "ExternalOutput")

    u_sb = p.sb("u_sb", [128, NDI * T], BF16)
    xpw_sb = p.sb("xpw_sb", [128, NDI * 96], BF16)
    dtw_sb = p.sb("dtw_sb", [DTR, DIN], BF16)
    dtb_sb = p.sb("dtb_sb", [128, NDI], F32)
    dt_sb = p.sb("dt_sb", [DTR, T], BF16)
    bc_sb = p.sb("bc_sb", [96, T], F32)
    delta_sb = p.sb("delta_sb", [128, 4 * TC], F32)
    du_sb = p.sb("du_sb", [128, 4 * TC], F16)

    ps_xd = p.ps("ps_xd", [96, TC])
    ps_dt = [p.ps(f"ps_dt{i}", [128, TC]) for i in range(4)]

    for k in range(NDI):
        p.dma("gpsimd", u_sb[:, k * T:(k + 1) * T], u_in[k * 128:(k + 1) * 128, :])
    p.dma("gpsimd", xpw_sb.ap().rearrange("p (k j) -> p k j", k=NDI),
          xp_wT.rearrange("(k p) j -> p k j", p=128))
    p.dma("gpsimd", dtw_sb[:], dt_wT)
    p.dma("sync", dtb_sb[:], dt_b)
    p.barrier()

    for c in range(NCH):
        csl = slice(c * TC, (c + 1) * TC)
        for k in range(NDI):
            p.op("tensor", lambda e, k=k, c=c: e.matmul(
                ps_xd[:], xpw_sb[:, k * 96:(k + 1) * 96],
                u_sb[:, k * T + c * TC:k * T + (c + 1) * TC],
                start=(k == 0), stop=(k == NDI - 1)))
        p.barrier()
        p.op("vector", lambda e, csl=csl: e.tensor_copy(dt_sb[:, csl], ps_xd[0:64, :]))
        p.op("vector", lambda e, csl=csl: e.tensor_copy(bc_sb[64:96, csl], ps_xd[64:96, :]))
        p.barrier()
        for mm in range(NDI + 3):
            if mm < NDI:
                p.op("tensor", lambda e, mm=mm, csl=csl: e.matmul(
                    ps_dt[mm % 4][:], dtw_sb[:, mm * 128:(mm + 1) * 128],
                    dt_sb[:, csl], start=True, stop=True))
            if 0 <= mm - 1 < NDI:
                m1 = mm - 1
                sl = slice((m1 % 4) * TC, (m1 % 4 + 1) * TC)
                p.op("scalar", lambda e, m1=m1, sl=sl: e.activation(
                    out=delta_sb[:, sl], in_=ps_dt[m1 % 4][:],
                    func=AF.Exp, bias=dtb_sb[:, m1:m1 + 1]))
                p.op("scalar", lambda e, sl=sl: e.activation(
                    out=delta_sb[:, sl], in_=delta_sb[:, sl], func=AF.Ln, bias=1.0))
            if 0 <= mm - 2 < NDI:
                m2 = mm - 2
                sl = slice((m2 % 4) * TC, (m2 % 4 + 1) * TC)
                p.op("vector", lambda e, m2=m2, sl=sl, c=c: e.tensor_mul(
                    du_sb[:, sl], delta_sb[:, sl],
                    u_sb[:, m2 * T + c * TC:m2 * T + (c + 1) * TC]))
            if 0 <= mm - 3 < NDI:
                m3 = mm - 3
                sl = slice((m3 % 4) * TC, (m3 % 4 + 1) * TC)
                p.dma("sync", delta_out[m3 * 128:(m3 + 1) * 128, csl],
                      delta_sb[:, sl])
                p.dma("gpsimd", du_out[m3 * 128:(m3 + 1) * 128, csl],
                      du_sb[:, sl])
            p.barrier()
    p.dma("sync", bc_out, bc_sb[64:96, :])
    p.barrier()
    p.pop_region()
    if standalone:
        return p.build()




def build_stageC(p=None, dr=None):
    CT = 1024
    NCC = T // CT
    standalone = p is None
    if standalone:
        p = Prog("C")
    _D = _mk_dram(p, dr)
    p.push_region()
    delta_in = _D("delta", [DIN, T], F32, "ExternalInput")
    du_in = _D("du", [DIN, T], F16, "ExternalInput")
    u_in = _D("u_bf", [DIN, T], BF16, "ExternalInput")
    sz_in = _D("sz_bf", [DIN, T], BF16, "ExternalInput")
    bc_in = _D("bc", [2 * NST, T], F32, "ExternalInput")
    A_in = _D("A_dev", [128, NDI * NST], F32, "ExternalInput")
    D_in = _D("D_dev", [128, NDI], F32, "ExternalInput")
    outw_in = _D("outw_T", [DIN, D], F32, "ExternalInput")
    mamba_out = _D("mamba", [D, T], F32, "ExternalOutput")

    Bb = p.sb("Bb", [128, NST * CT], F16)
    Cb = p.sb("Cb", [128, NST * CT], F16)
    delta_c = p.sb("delta_c", [128, 2 * CT], F32)
    du_c = p.sb("du_c", [128, 2 * CT], F16)
    u_c = p.sb("u_c", [128, 2 * CT], BF16)
    sz_c = p.sb("sz_c", [128, 2 * CT], BF16)
    dA = p.sb("dA", [128, 3 * CT], F16)
    dBu = p.sb("dBu", [128, 3 * CT], F16)
    h_b = p.sb("h_b", [128, 3 * CT], F16)
    ym = p.sb("ym", [128, 3 * CT], F16)
    y_b = p.sb("y_b", [128, 2 * CT], F16)
    a_gate = p.sb("a_gate", [128, 2 * CT], F16)
    b_gate = p.sb("b_gate", [128, 2 * CT], F16)
    yg = p.sb("yg", [128, NDI * CT], BF16)
    w_out = p.sb("w_out", [128, NDI * D], BF16)
    carry = p.sb("carry", [128, NDI * NST], F32)
    A_sb = p.sb("A_sb", [128, NDI * NST], F32)
    D_sb = p.sb("D_sb", [128, NDI], F32)
    mb_ev = p.sb("mb_ev", [128, 3 * CT], F32)
    ps_op = [p.ps(f"ps_op{i}", [128, CT]) for i in range(4)]

    for k in range(NDI):
        p.dma("gpsimd", w_out[:, k * D:(k + 1) * D],
              outw_in[k * 128:(k + 1) * 128, :])
    p.dma("sync", A_sb[:], A_in)
    p.dma("sync", D_sb[:], D_in)
    p.barrier()
    p.op("scalar", lambda e: e.activation(out=A_sb[:], in_=A_sb[:], func=AF.Exp))
    p.barrier()
    p.op("vector", lambda e: e.tensor_scalar_mul(out=A_sb[:], in0=A_sb[:],
                                                 scalar1=-1.0))
    p.barrier()

    def tile_loads(m, c):
        csl = slice(c * CT, (c + 1) * CT)
        buf = slice((m % 2) * CT, (m % 2 + 1) * CT)
        rows = slice(m * 128, (m + 1) * 128)
        p.dma("sync", delta_c[:, buf], delta_in[rows, csl])
        p.dma("sync", du_c[:, buf], du_in[rows, csl])
        p.dma("sync", u_c[:, buf], u_in[rows, csl])
        p.dma("sync", sz_c[:, buf], sz_in[rows, csl])

    NPH = NDI * NST + 8
    for c in range(NCC):
        csl = slice(c * CT, (c + 1) * CT)
        p.dma("gpsimd", Bb.ap().rearrange("p (n t) -> p n t", n=NST),
              bc_in[0:NST, csl].unsqueeze(0).to_broadcast((128, NST, CT)))
        p.dma("gpsimd", Cb.ap().rearrange("p (n t) -> p n t", n=NST),
              bc_in[NST:2 * NST, csl].unsqueeze(0).to_broadcast((128, NST, CT)))
        tile_loads(0, c)
        p.op("vector", lambda e: e.memset(y_b[:, 0:CT], 0.0))
        p.barrier()
        for t in range(NPH):
            if t < NDI * NST:
                i = t
                m, n = i // NST, i % NST
                j = t % 3
                p.op("scalar", lambda e, j=j, m=m, i=i: e.activation(
                    out=dA[:, j * CT:(j + 1) * CT],
                    in_=delta_c[:, (m % 2) * CT:(m % 2 + 1) * CT],
                    func=AF.Exp, scale=A_sb[:, i:i + 1]))
                p.op("vector", lambda e, j=j, m=m, n=n: e.tensor_tensor(
                    out=dBu[:, j * CT:(j + 1) * CT],
                    in0=du_c[:, (m % 2) * CT:(m % 2 + 1) * CT],
                    in1=Bb[:, n * CT:(n + 1) * CT], op=ALU.mult))
            if 0 <= t - 1 < NDI * NST:
                i = t - 1
                j = i % 3
                init = 0.0 if c == 0 else carry[:, i:i + 1]
                p.op("vector", lambda e, j=j, init=init: e.tensor_tensor_scan(
                    out=h_b[:, j * CT:(j + 1) * CT],
                    data0=dA[:, j * CT:(j + 1) * CT],
                    data1=dBu[:, j * CT:(j + 1) * CT],
                    initial=init, op0=ALU.mult, op1=ALU.add))
            if c == 0 and 0 <= t - 2 < NDI * NST:
                i2 = t - 2
                j2 = i2 % 3
                p.op("vector", lambda e, j2=j2, i2=i2: e.tensor_copy(
                    carry[:, i2:i2 + 1], h_b[:, (j2 + 1) * CT - 1:(j2 + 1) * CT]))
            if 0 <= t - 2 < NDI * NST:
                i = t - 2
                j = i % 3
                n = i % NST
                p.op("vector", lambda e, j=j, n=n: e.tensor_tensor(
                    out=ym[:, j * CT:(j + 1) * CT],
                    in0=h_b[:, j * CT:(j + 1) * CT],
                    in1=Cb[:, n * CT:(n + 1) * CT], op=ALU.mult))
            if 0 <= t - 3 < NDI * NST:
                i = t - 3
                j = i % 3
                m = i // NST
                ysl = slice((m % 2) * CT, (m % 2 + 1) * CT)
                p.op("vector", lambda e, j=j, ysl=ysl: e.tensor_tensor(
                    out=y_b[:, ysl], in0=y_b[:, ysl],
                    in1=ym[:, j * CT:(j + 1) * CT], op=ALU.add))
            if t % NST == 0 and 0 < t < NDI * NST:
                mnew = t // NST
                p.op("vector", lambda e, mnew=mnew: e.memset(
                    y_b[:, (mnew % 2) * CT:(mnew % 2 + 1) * CT], 0.0))
            if t % NST == 8 and t // NST < NDI - 1:
                tile_loads(t // NST + 1, c)
            if t >= 19:
                mg = (t - 3) // NST - 1
                off = (t - 3) % NST
                if 0 <= mg < NDI:
                    gb = slice((mg % 2) * CT, (mg % 2 + 1) * CT)
                    if off == 0:
                        p.op("vector", lambda e, gb=gb, mg=mg: e.tensor_scalar_mul(
                            out=a_gate[:, gb], in0=u_c[:, gb],
                            scalar1=D_sb[:, mg:mg + 1]))
                    elif off == 1:
                        p.op("vector", lambda e, gb=gb: e.tensor_tensor(
                            out=b_gate[:, gb], in0=y_b[:, gb],
                            in1=a_gate[:, gb], op=ALU.add))
                    elif off == 2:
                        p.op("vector", lambda e, gb=gb, mg=mg: e.tensor_tensor(
                            out=yg[:, mg * CT:(mg + 1) * CT], in0=b_gate[:, gb],
                            in1=sz_c[:, gb], op=ALU.mult))
            p.barrier(lazy=True)
        p.eng_lag = {"tensor": {"vector": 2, "scalar": 2}}
        for o in range(KD + 2):
            if o < KD:
                pb = o % 4
                for nn in range(CT // 512):
                    for k in range(NDI):
                        p.op("tensor", lambda e, o=o, pb=pb, nn=nn, k=k: e.matmul(
                            ps_op[pb][:, nn * 512:(nn + 1) * 512],
                            w_out[:, k * D + o * 128:k * D + (o + 1) * 128],
                            yg[:, k * CT + nn * 512:k * CT + nn * 512 + 512],
                            start=(k == 0), stop=(k == NDI - 1)))
            if 0 <= o - 1 < KD:
                pe3 = (o - 1) % 3
                pb = (o - 1) % 4
                p.op("scalar", lambda e, pe3=pe3, pb=pb: e.copy(
                    mb_ev[:, pe3 * CT:(pe3 + 1) * CT], ps_op[pb][:]))
            if 0 <= o - 2 < KD:
                pe3 = (o - 2) % 3
                p.dma("sync", mamba_out[(o - 2) * 128:(o - 1) * 128, csl],
                      mb_ev[:, pe3 * CT:(pe3 + 1) * CT])
            p.barrier(lazy=True)
        p.eng_lag = {}
    p.pop_region()
    if standalone:
        return p.build()



def build_stageD(p=None, dr=None):
    standalone = p is None
    if standalone:
        p = Prog("D")
    _D = _mk_dram(p, dr)
    p.push_region()
    xT = _D("xT", [D, T], F32, "ExternalInput")
    xn_in = _D("xn_bf", [D, T], BF16, "ExternalInput")
    mamba_in = _D("mamba", [D, T], F32, "ExternalInput")
    wrT = _D("wrT", [D, D], F32, "ExternalInput")
    wkT = _D("wkT", [D, D], F32, "ExternalInput")
    wvT = _D("wvT", [D, D], F32, "ExternalInput")
    mur_in = _D("mu_r", [128, KD], F32, "ExternalInput")
    muk_in = _D("mu_k", [128, KD], F32, "ExternalInput")
    x1_out = _D("x1", [D, T], F32, "ExternalOutput")
    xn1_out = _D("xn1_bf", [D, T], BF16, "ExternalOutput")

    wr_sb = p.sb("wr_sb", [128, KD * D], BF16)
    wk_sb = p.sb("wk_sb", [128, KD * D], BF16)
    wv_sb = p.sb("wv_sb", [128, KD * D], BF16)
    mur = p.sb("mur", [128, KD], F32)
    muk = p.sb("muk", [128, KD], F32)
    mur1 = p.sb("mur1", [128, KD], F32)
    muk1 = p.sb("muk1", [128, KD], F32)
    ones_col = p.sb("ones_colD", [128, 1], BF16)
    ones_row = p.sb("ones_rowD", [1, 128], BF16)
    eps_sb = p.sb("eps_sbD", [1, 1], F32)

    xs = p.sb("xs", [128, KD * (TC + 1)], BF16)
    xr_sb = p.sb("xr_sb", [128, KD * TC], BF16)
    xk_sb = p.sb("xk_sb", [128, KD * TC], BF16)
    tmpmix = p.sb("tmpmix", [128, TC], BF16)
    r_sb = p.sb("r_sb", [128, KD * TC], BF16)
    kk_sb = p.sb("kk_sb", [128, KD * TC], BF16)
    krelu = p.sb("krelu", [128, 2 * TC], BF16)
    x_sb = p.sb("x_sbD", [128, KD * TC], F32)
    mam_sb = p.sb("mam_sb", [128, KD * TC], F32)
    ffn_sb = p.sb("ffn_sb", [128, 2 * TC], F32)
    x1_sb = p.sb("x1_sb", [128, KD * TC], F32)
    xb1 = p.sb("xb1", [128, KD * TC], BF16)
    xsq1 = p.sb("xsq1", [128, KD * TC], BF16)
    xn1_sb = p.sb("xn1_sb", [128, KD * TC], BF16)
    tmp_ln = p.sb("tmp_lnD", [128, TC], BF16)
    mrow = p.sb("mrowD", [1, TC], F32)
    vrow = p.sb("vrowD", [1, TC], F32)
    msq = p.sb("msqD", [1, TC], F32)
    srow = p.sb("srowD", [1, TC], F32)
    rrow = p.sb("rrowD", [1, TC], F32)
    m_bf = p.sb("m_bfD", [1, TC], BF16)
    r_bf = p.sb("r_bfD", [1, TC], BF16)
    mb_sb = p.sb("mb_sbD", [128, TC], BF16)
    rb_sb = p.sb("rb_sbD", [128, TC], BF16)

    ps_a = [p.ps(f"ps_Da{i}", [128, TC]) for i in range(2)]
    ps_b = [p.ps(f"ps_Db{i}", [128, TC]) for i in range(2)]
    ps_sx = p.ps("ps_sxD", [1, TC])
    ps_sxx = p.ps("ps_sxxD", [1, TC])
    ps_mb = p.ps("ps_mbD", [128, TC])
    ps_rb = p.ps("ps_rbD", [128, TC])

    for k in range(KD):
        p.dma("gpsimd", wr_sb[:, k * D:(k + 1) * D], wrT[k * 128:(k + 1) * 128, :])
        p.dma("gpsimd", wk_sb[:, k * D:(k + 1) * D], wkT[k * 128:(k + 1) * 128, :])
        p.dma("gpsimd", wv_sb[:, k * D:(k + 1) * D], wvT[k * 128:(k + 1) * 128, :])
    p.dma("sync", mur[:], mur_in)
    p.dma("sync", muk[:], muk_in)
    p.op("vector", lambda e: e.memset(eps_sb[:], 1e-5))
    p.op("vector", lambda e: e.memset(ones_col[:], 1.0))
    p.op("vector", lambda e: e.memset(ones_row[:], 1.0))
    for k in range(KD):
        p.op("vector", lambda e, k=k: e.memset(xs[:, k * (TC + 1):k * (TC + 1) + 1], 0.0))
    p.barrier()
    p.op("vector", lambda e: e.tensor_scalar(out=mur1[:], in0=mur[:], scalar1=-1.0,
                                             scalar2=1.0, op0=ALU.mult, op1=ALU.add))
    p.op("vector", lambda e: e.tensor_scalar(out=muk1[:], in0=muk[:], scalar1=-1.0,
                                             scalar2=1.0, op0=ALU.mult, op1=ALU.add))
    p.barrier()

    for c in range(NCH):
        csl = slice(c * TC, (c + 1) * TC)
        p.dma("sync", xs.ap().rearrange("p (k t) -> p k t", k=KD)[:, :, 1:TC + 1],
              xn_in[:, csl].rearrange("(k p) t -> p k t", p=128))
        p.dma("sync", x_sb.ap().rearrange("p (k t) -> p k t", k=KD),
              xT[:, csl].rearrange("(k p) t -> p k t", p=128))
        p.dma("sync", mam_sb.ap().rearrange("p (k t) -> p k t", k=KD),
              mamba_in[:, csl].rearrange("(k p) t -> p k t", p=128))
        p.barrier()
        for k in range(KD):
            base = k * (TC + 1)
            cur = xs[:, base + 1:base + 1 + TC]
            prv = xs[:, base:base + TC]
            p.op("vector", lambda e, prv=prv, k=k: e.tensor_scalar_mul(
                out=tmpmix[:], in0=prv, scalar1=mur1[:, k:k + 1]))
            p.op("vector", lambda e, cur=cur, k=k: e.scalar_tensor_tensor(
                out=xr_sb[:, k * TC:(k + 1) * TC], in0=cur,
                scalar=mur[:, k:k + 1], in1=tmpmix[:], op0=ALU.mult, op1=ALU.add))
            p.op("vector", lambda e, prv=prv, k=k: e.tensor_scalar_mul(
                out=tmpmix[:], in0=prv, scalar1=muk1[:, k:k + 1]))
            p.op("vector", lambda e, cur=cur, k=k: e.scalar_tensor_tensor(
                out=xk_sb[:, k * TC:(k + 1) * TC], in0=cur,
                scalar=muk[:, k:k + 1], in1=tmpmix[:], op0=ALU.mult, op1=ALU.add))
        for k in range(KD):
            base = k * (TC + 1)
            p.op("vector", lambda e, base=base: e.tensor_copy(
                xs[:, base:base + 1], xs[:, base + TC:base + TC + 1]))
        p.barrier()
        for o in range(KD + 2):
            if o < KD:
                pb = o % 2
                for k in range(KD):
                    p.op("tensor", lambda e, o=o, k=k, pb=pb: e.matmul(
                        ps_a[pb][:], wr_sb[:, k * D + o * 128:k * D + (o + 1) * 128],
                        xr_sb[:, k * TC:(k + 1) * TC],
                        start=(k == 0), stop=(k == KD - 1)))
                for k in range(KD):
                    p.op("tensor", lambda e, o=o, k=k, pb=pb: e.matmul(
                        ps_b[pb][:], wk_sb[:, k * D + o * 128:k * D + (o + 1) * 128],
                        xk_sb[:, k * TC:(k + 1) * TC],
                        start=(k == 0), stop=(k == KD - 1)))
            if 0 <= o - 1 < KD:
                om, pb = o - 1, (o - 1) % 2
                p.op("scalar", lambda e, om=om, pb=pb: e.activation(
                    out=r_sb[:, om * TC:(om + 1) * TC], in_=ps_a[pb][:],
                    func=AF.Sigmoid))
                p.op("vector", lambda e, om=om, pb=pb: e.tensor_scalar_max(
                    out=krelu[:, pb * TC:(pb + 1) * TC], in0=ps_b[pb][:], scalar1=0.0))
            if 0 <= o - 2 < KD:
                om, pb = o - 2, (o - 2) % 2
                p.op("scalar", lambda e, om=om, pb=pb: e.activation(
                    out=kk_sb[:, om * TC:(om + 1) * TC],
                    in_=krelu[:, pb * TC:(pb + 1) * TC], func=AF.Square))
            p.barrier(lazy=True)
        for o in range(KD + 1):
            if o < KD:
                pb = o % 2
                for k in range(KD):
                    p.op("tensor", lambda e, o=o, k=k, pb=pb: e.matmul(
                        ps_a[pb][:], wv_sb[:, k * D + o * 128:k * D + (o + 1) * 128],
                        kk_sb[:, k * TC:(k + 1) * TC],
                        start=(k == 0), stop=(k == KD - 1)))
            if 0 <= o - 1 < KD:
                om, pb = o - 1, (o - 1) % 2
                p.op("vector", lambda e, om=om, pb=pb: e.tensor_tensor(
                    out=ffn_sb[:, pb * TC:(pb + 1) * TC], in0=ps_a[pb][:],
                    in1=r_sb[:, om * TC:(om + 1) * TC], op=ALU.mult))
                p.op("vector", lambda e, om=om: e.tensor_add(
                    x1_sb[:, om * TC:(om + 1) * TC], x_sb[:, om * TC:(om + 1) * TC],
                    mam_sb[:, om * TC:(om + 1) * TC]))
                p.op("vector", lambda e, om=om, pb=pb: e.tensor_add(
                    x1_sb[:, om * TC:(om + 1) * TC], x1_sb[:, om * TC:(om + 1) * TC],
                    ffn_sb[:, pb * TC:(pb + 1) * TC]))
            p.barrier(lazy=True)
        p.op("vector", lambda e: e.tensor_copy(xb1[:], x1_sb[:]))
        p.op("scalar", lambda e: e.activation(out=xsq1[:], in_=x1_sb[:],
                                              func=AF.Square))
        p.barrier()
        for k in range(KD):
            p.op("tensor", lambda e, k=k: e.matmul(
                ps_sx[:], ones_col[:], xb1[:, k * TC:(k + 1) * TC],
                start=(k == 0), stop=(k == KD - 1)))
        for k in range(KD):
            p.op("tensor", lambda e, k=k: e.matmul(
                ps_sxx[:], ones_col[:], xsq1[:, k * TC:(k + 1) * TC],
                start=(k == 0), stop=(k == KD - 1)))
        p.barrier()
        p.op("vector", lambda e: e.tensor_scalar_mul(out=mrow[:], in0=ps_sx[:],
                                                     scalar1=1.0 / D))
        p.op("vector", lambda e: e.tensor_scalar_mul(out=vrow[:], in0=ps_sxx[:],
                                                     scalar1=1.0 / D))
        p.op("vector", lambda e: e.tensor_mul(msq[:], mrow[:], mrow[:]))
        p.op("vector", lambda e: e.tensor_sub(vrow[:], vrow[:], msq[:]))
        p.barrier()
        p.op("scalar", lambda e: e.activation(out=srow[:], in_=vrow[:],
                                              func=AF.Sqrt, bias=eps_sb[:]))
        p.barrier()
        p.op("vector", lambda e: e.reciprocal(rrow[:], srow[:]))
        p.barrier()
        p.op("vector", lambda e: e.tensor_copy(m_bf[:], mrow[:]))
        p.op("vector", lambda e: e.tensor_copy(r_bf[:], rrow[:]))
        p.barrier()
        p.op("tensor", lambda e: e.matmul(ps_mb[:], ones_row[:], m_bf[:],
                                          start=True, stop=True))
        p.op("tensor", lambda e: e.matmul(ps_rb[:], ones_row[:], r_bf[:],
                                          start=True, stop=True))
        p.barrier()
        p.op("vector", lambda e: e.tensor_copy(mb_sb[:], ps_mb[:]))
        p.op("vector", lambda e: e.tensor_copy(rb_sb[:], ps_rb[:]))
        p.barrier()
        for k in range(KD):
            p.op("vector", lambda e, k=k: e.tensor_sub(
                tmp_ln[:], xb1[:, k * TC:(k + 1) * TC], mb_sb[:]))
            p.op("vector", lambda e, k=k: e.tensor_mul(
                xn1_sb[:, k * TC:(k + 1) * TC], tmp_ln[:], rb_sb[:]))
        p.barrier()
        p.dma("gpsimd", x1_out[:, csl].rearrange("(k p) t -> p k t", p=128),
              x1_sb.ap().rearrange("p (k t) -> p k t", k=KD))
        p.dma("gpsimd", xn1_out[:, csl].rearrange("(k p) t -> p k t", p=128),
              xn1_sb.ap().rearrange("p (k t) -> p k t", k=KD))
        p.barrier()
    p.pop_region()
    if standalone:
        return p.build()



def build_stageE(p=None, dr=None):
    standalone = p is None
    if standalone:
        p = Prog("E")
    _D = _mk_dram(p, dr)
    p.push_region()
    xn_in = _D("xn1_bf", [D, T], BF16, "ExternalInput")
    wqT = _D("wqT", [D, D], F32, "ExternalInput")
    wkT = _D("wkT", [D, NKV * HD], F32, "ExternalInput")
    wvT = _D("wvT", [D, NKV * HD], F32, "ExternalInput")
    woT = _D("woT", [D, D], F32, "ExternalInput")
    attn_out = _D("attn", [D, T], F32, "ExternalOutput")

    NB = T // 128
    wq_sb = p.sb("wq_sb", [128, KD * D], BF16)
    wk_sb = p.sb("wk_sb", [128, KD * NKV * HD], BF16)
    wv_sb = p.sb("wv_sb", [128, KD * NKV * HD], BF16)
    wo_sb = p.sb("wo_sb", [128, KD * D], BF16)
    xn_sb = p.sb("xn_sbE", [128, KD * T], BF16)
    q_sb = p.sb("q_sbE", [128, KD * T], BF16)
    k_sb = p.sb("k_sbE", [128, 2 * T], BF16)
    kd_sb = p.sb("kd_sbE", [128, NKV * T], BF16)
    v_sb = p.sb("v_sbE", [128, 2 * T], BF16)
    vtok = p.sb("vtok", [128, NB * NKV * HD], BF16)
    io = p.sb("ioE", [128, 128], F32)
    pcol = p.sb("pcolE", [128, 1], F32)
    ident = p.sb("identE", [128, 128], BF16)
    tri = p.sb("triE", [128, 128], BF16)
    eighth = p.sb("eighth", [128, 1], F32)
    p_buf = p.sb("p_buf", [128, 2 * T], BF16)
    pt_buf = p.sb("pt_buf", [128, 2 * T], BF16)
    nm_b = p.sb("nm_b", [128, 2], F32)
    l_b = p.sb("l_b", [128, 2], F32)
    rl_b = p.sb("rl_b", [128, 2], F32)
    ctx_sb = p.sb("ctx_sbE", [128, KD * T], BF16)
    at_ev = p.sb("at_ev", [128, 2 * 512], F32)

    ps_qkv = [p.ps(f"ps_Eq{i}", [128, 512]) for i in range(2)]
    ps_s = p.ps("ps_Es", [128, T])
    ps_t = p.ps("ps_Et", [128, T], BF16)
    ps_ctx = None

    for k in range(KD):
        p.dma("gpsimd", wq_sb[:, k * D:(k + 1) * D], wqT[k * 128:(k + 1) * 128, :])
        p.dma("gpsimd", wo_sb[:, k * D:(k + 1) * D], woT[k * 128:(k + 1) * 128, :])
        p.dma("gpsimd", wk_sb[:, k * NKV * HD:(k + 1) * NKV * HD],
              wkT[k * 128:(k + 1) * 128, :])
        p.dma("gpsimd", wv_sb[:, k * NKV * HD:(k + 1) * NKV * HD],
              wvT[k * 128:(k + 1) * 128, :])
        p.dma("sync", xn_sb[:, k * T:(k + 1) * T], xn_in[k * 128:(k + 1) * 128, :])
    p.op("gpsimd", lambda e: e.iota(io[:], [[1, 128]], channel_multiplier=0,
                                    allow_small_or_imprecise_dtypes=True))
    p.op("gpsimd", lambda e: e.iota(pcol[:], [[0, 1]], channel_multiplier=1,
                                    allow_small_or_imprecise_dtypes=True))
    p.op("vector", lambda e: e.memset(eighth[:], 0.125))
    p.barrier()
    p.op("vector", lambda e: e.tensor_single_scalar(out=ident[:], in_=io[:],
                                                    scalar=pcol[:], op=ALU.is_equal))
    p.op("vector", lambda e: e.tensor_single_scalar(out=tri[:], in_=io[:],
                                                    scalar=pcol[:], op=ALU.is_le))
    p.barrier()

    def proj(w_sb, nout, dst_sb, scale=None):
        for o in range(nout):
            for nn in range(T // 512):
                pb = (o * (T // 512) + nn) % 2
                for k in range(KD):
                    p.op("tensor", lambda e, o=o, nn=nn, k=k, pb=pb,
                         w_sb=w_sb, nout=nout: e.matmul(
                        ps_qkv[pb][:],
                        w_sb[:, k * nout * 128 + o * 128:k * nout * 128 + (o + 1) * 128],
                        xn_sb[:, k * T + nn * 512:k * T + nn * 512 + 512],
                        start=(k == 0), stop=(k == KD - 1)))
                p.barrier()
                if scale is None:
                    p.op("vector", lambda e, o=o, nn=nn, pb=pb, dst_sb=dst_sb: e.tensor_copy(
                        dst_sb[:, o * T + nn * 512:o * T + nn * 512 + 512], ps_qkv[pb][:]))
                else:
                    p.op("vector", lambda e, o=o, nn=nn, pb=pb, dst_sb=dst_sb,
                         scale=scale: e.tensor_scalar_mul(
                        out=dst_sb[:, o * T + nn * 512:o * T + nn * 512 + 512],
                        in0=ps_qkv[pb][:], scalar1=scale))
        p.barrier()
    proj(wq_sb, KD, q_sb, scale=0.125)
    proj(wk_sb, 2, k_sb)
    proj(wv_sb, 2, v_sb)
    p.barrier()
    for g in range(NKV):
        vt = g // 2
        vo = (g % 2) * 64
        for blk in range(NB):
            p.op("tensor", lambda e, vt=vt, vo=vo, blk=blk: e.transpose(
                ps_t[:, 0:64] if False else ps_t[:, (blk % 8) * 64:(blk % 8) * 64 + 64],
                v_sb[vo:vo + 64, vt * T + blk * 128:vt * T + (blk + 1) * 128],
                ident[vo:vo + 64, vo:vo + 64]))
            if blk % 8 == 7 or blk == NB - 1:
                p.barrier()
                for b2 in range(blk - blk % 8, blk + 1):
                    p.op("vector", lambda e, g=g, b2=b2, blk=blk: e.tensor_copy(
                        vtok[:, b2 * NKV * HD + g * HD:b2 * NKV * HD + (g + 1) * HD],
                        ps_t[:, (b2 % 8) * 64:(b2 % 8) * 64 + 64]))
                p.barrier()
    p.barrier()
    for g in range(NKV):
        kt, ko = g // 2, (g % 2) * 64
        p.dma("sync", kd_sb[0:64, g * T:(g + 1) * T],
              k_sb[ko:ko + 64, kt * T:(kt + 1) * T])
        p.dma("gpsimd", kd_sb[64:128, g * T:(g + 1) * T],
              k_sb[ko:ko + 64, kt * T:(kt + 1) * T])
    p.barrier()

    IT = NH * NB
    p.op("tensor", lambda e: e.matmul(
        ps_s[:, 0:128], q_sb[0:64, 0:128], kd_sb[0:64, 0:128],
        start=True, stop=True))
    p.barrier()
    for i in range(IT + 4):
        if i < IT:
            h, qb = divmod(i, NB)
            j = i % 2
            ncol = (qb + 1) * 128
            p.op("vector", lambda e, ncol=ncol, j=j: e.reduce_max(
                nm_b[:, j:j + 1], ps_s[:, 0:ncol], axis=AX.X, negate=True))
        if 0 <= i - 1 < IT:
            j = (i - 1) % 2
            p.op("vector", lambda e, j=j: e.reciprocal(rl_b[:, j:j + 1],
                                                       l_b[:, j:j + 1]))
        if 0 <= i - 2 < IT:
            h, qb = divmod(i - 2, NB)
            j = (i - 2) % 2
            ncol = (qb + 1) * 128
            p.op("vector", lambda e, ncol=ncol, j=j: e.tensor_copy(
                pt_buf[:, j * T:j * T + ncol], ps_t[:, 0:ncol]))
        p.barrier(lazy=True)
        if i < IT:
            h, qb = divmod(i, NB)
            j = i % 2
            ncol = (qb + 1) * 128
            p.op("scalar", lambda e, ncol=ncol, j=j: e.activation(
                out=p_buf[:, j * T:j * T + ncol], in_=ps_s[:, 0:ncol],
                func=AF.Exp, bias=nm_b[:, j:j + 1]))
        if 0 <= i - 1 < IT:
            h, qb = divmod(i - 1, NB)
            j = (i - 1) % 2
            ncol = (qb + 1) * 128
            p.op("vector", lambda e, ncol=ncol, j=j: e.tensor_scalar_mul(
                out=p_buf[:, j * T:j * T + ncol], in0=p_buf[:, j * T:j * T + ncol],
                scalar1=rl_b[:, j:j + 1]))
        if 0 <= i - 2 < IT:
            h, qb = divmod(i - 2, NB)
            j = (i - 2) % 2
            g = h // NKV
            for blk in range(qb + 1):
                p.op("tensor", lambda e, j=j, g=g, blk=blk, qb=qb, i=i: e.matmul(
                    ps_qkv[(i - 2) % 2][0:64, 0:128],
                    vtok[:, blk * NKV * HD + g * HD:blk * NKV * HD + (g + 1) * HD],
                    pt_buf[:, j * T + blk * 128:j * T + (blk + 1) * 128],
                    start=(blk == 0), stop=(blk == qb)))
        p.barrier(lazy=True)
        if i < IT:
            h, qb = divmod(i, NB)
            j = i % 2
            ncol = (qb + 1) * 128
            p.op("vector", lambda e, j=j, qb=qb: e.tensor_mul(
                p_buf[:, j * T + qb * 128:j * T + (qb + 1) * 128],
                p_buf[:, j * T + qb * 128:j * T + (qb + 1) * 128], tri[:]))
            p.op("vector", lambda e, ncol=ncol, j=j: e.reduce_sum(
                l_b[:, j:j + 1], p_buf[:, j * T:j * T + ncol], axis=AX.X))
        if 0 <= i - 1 < IT:
            h, qb = divmod(i - 1, NB)
            j = (i - 1) % 2
            for blk in range(qb + 1):
                p.op("tensor", lambda e, j=j, blk=blk: e.transpose(
                    ps_t[:, blk * 128:(blk + 1) * 128],
                    p_buf[:, j * T + blk * 128:j * T + (blk + 1) * 128], ident[:]))
        if i + 1 < IT:
            h, qb = divmod(i + 1, NB)
            qt, qo = h // 2, (h % 2) * 64
            g = h // NKV
            ncol = (qb + 1) * 128
            nmm = (ncol + 511) // 512
            for mmi in range(nmm):
                w = min(512, ncol - mmi * 512)
                p.op("tensor", lambda e, qt=qt, qo=qo, g=g, qb=qb,
                     mmi=mmi, w=w: e.matmul(
                    ps_s[:, mmi * 512:mmi * 512 + w],
                    q_sb[qo:qo + 64, qt * T + qb * 128:qt * T + (qb + 1) * 128],
                    kd_sb[qo:qo + 64, g * T + mmi * 512:g * T + mmi * 512 + w],
                    start=True, stop=True))
        if 0 <= i - 2 < IT:
            h, qb = divmod(i - 2, NB)
            j = (i - 2) % 2
            qt, qo = h // 2, (h % 2) * 64
            p.op("scalar", lambda e, qt=qt, qo=qo, qb=qb, j=j: e.copy(
                ctx_sb[qo:qo + 64, qt * T + qb * 128:qt * T + (qb + 1) * 128],
                ps_qkv[j][0:64, 0:128]))
        p.barrier(lazy=True)

    for o in range(KD):
        for nn in range(T // 512):
            pb = (o * (T // 512) + nn) % 2
            for k in range(KD):
                p.op("tensor", lambda e, o=o, nn=nn, k=k, pb=pb: e.matmul(
                    ps_qkv[pb][:], wo_sb[:, k * D + o * 128:k * D + (o + 1) * 128],
                    ctx_sb[:, k * T + nn * 512:k * T + nn * 512 + 512],
                    start=(k == 0), stop=(k == KD - 1)))
            p.barrier()
            p.op("vector", lambda e, pb=pb: e.tensor_copy(
                at_ev[:, pb * 512:(pb + 1) * 512], ps_qkv[pb][:]))
            p.barrier()
            p.dma("sync", attn_out[o * 128:(o + 1) * 128, nn * 512:(nn + 1) * 512],
                  at_ev[:, pb * 512:(pb + 1) * 512])
    p.barrier()
    p.pop_region()
    if standalone:
        return p.build()



def build_stageF(p=None, dr=None):
    standalone = p is None
    if standalone:
        p = Prog("F")
    _D = _mk_dram(p, dr)
    p.push_region()
    xn_in = _D("xn1_bf", [D, T], BF16, "ExternalInput")
    x1_in = _D("x1", [D, T], F32, "ExternalInput")
    attn_in = _D("attn", [D, T], F32, "ExternalInput")
    Wr_in = _D("Wr", [D, NE], F32, "ExternalInput")
    br_in = _D("br", [NE, 1], F32, "ExternalInput")
    W1_in = _D("W1", [NE, D, HF], F32, "ExternalInput")
    b1_in = _D("b1d", [NE, 128, HF // 128], F32, "ExternalInput")
    W2_in = _D("W2", [NE, HF, D], F32, "ExternalInput")
    b2_in = _D("b2d", [NE, 128, D // 128], F32, "ExternalInput")
    out_T = _D("outT", [D, T], F32, "ExternalOutput")
    w_dram = p.dram("w_scratch", [NE, T], F32, "Internal")
    acc_dram = p.dram("acc_scratch", [D, T], F32, "Internal")

    NB = T // 128
    NO1 = HF // 128
    xn_sb = p.sb("xn_sbF", [128, KD * T], BF16)
    wr_sb = p.sb("wr_sbF", [128, KD * NE], BF16)
    br_sb = p.sb("br_sbF", [NE, 1], F32)
    w1_sb = p.sb("w1_sbF", [128, KD * HF], BF16)
    w2_sb = p.sb("w2_sbF", [128, NO1 * D], BF16)
    b1_sb = p.sb("b1_sbF", [128, NE * NO1], F32)
    b2_sb = p.sb("b2_sbF", [128, NE * KD], F32)
    io = p.sb("ioF", [128, 128], F32)
    pcol = p.sb("pcolF", [128, 1], F32)
    id32 = p.sb("id32F", [128, 128], F32)
    lg_sb = p.sb("lg_sbF", [128, T], F32)
    lgt = p.sb("lgtF", [128, 2 * NE], F32)
    nm1 = p.sb("nm1F", [128, 1], F32)
    e1 = p.sb("e1F", [128, NE], F32)
    t1 = p.sb("t1F", [128, NE], F32)
    m2 = p.sb("m2F", [128, 1], F32)
    pe = p.sb("peF", [128, NE], F32)
    msk = p.sb("mskF", [128, NE], F32)
    zz = p.sb("zzF", [128, 1], F32)
    rz = p.sb("rzF", [128, 1], F32)
    w_tok = p.sb("w_tokF", [128, NE], F32)
    wb_e = p.sb("wb_eF", [128, 2 * TC], F32)
    h_sb = p.sb("h_sbF", [128, NO1 * 512], BF16)


    p.push_region()
    ps_r = p.ps("ps_rF", [NE, 512])
    ps_t8 = p.ps("ps_t8F", [128, NE])
    ps_w8 = p.ps("ps_w8F", [NE, 128])

    for k in range(KD):
        p.dma("sync", xn_sb[:, k * T:(k + 1) * T], xn_in[k * 128:(k + 1) * 128, :])
    p.dma("gpsimd", wr_sb.ap().rearrange("p (k j) -> p k j", k=KD),
          Wr_in.rearrange("(k p) j -> p k j", p=128))
    p.dma("sync", br_sb[:], br_in)
    p.dma("sync", b1_sb.ap().rearrange("p (e o) -> p e o", e=NE),
          b1_in.transpose((1, 0, 2)))
    p.dma("sync", b2_sb.ap().rearrange("p (e o) -> p e o", e=NE),
          b2_in.transpose((1, 0, 2)))
    p.op("gpsimd", lambda e: e.iota(io[:], [[1, 128]], channel_multiplier=0,
                                    allow_small_or_imprecise_dtypes=True))
    p.op("gpsimd", lambda e: e.iota(pcol[:], [[0, 1]], channel_multiplier=1,
                                    allow_small_or_imprecise_dtypes=True))
    p.barrier()
    p.op("vector", lambda e: e.tensor_single_scalar(out=id32[:], in_=io[:],
                                                    scalar=pcol[:], op=ALU.is_equal))
    p.barrier()

    for c in range(NCH):
        for k in range(KD):
            p.op("tensor", lambda e, k=k, c=c: e.matmul(
                ps_r[:], wr_sb[:, k * NE:(k + 1) * NE],
                xn_sb[:, k * T + c * TC:k * T + (c + 1) * TC],
                start=(k == 0), stop=(k == KD - 1)))
        p.barrier()
        p.op("vector", lambda e, c=c: e.tensor_scalar_add(
            out=lg_sb[0:NE, c * TC:(c + 1) * TC], in0=ps_r[:], scalar1=br_sb[:]))
        p.barrier()
    for blk in range(NB):
        pb = blk % 2
        p.op("tensor", lambda e, blk=blk: e.transpose(
            ps_t8[:], lg_sb[0:NE, blk * 128:(blk + 1) * 128], id32[0:NE, 0:NE]))
        p.barrier()
        p.op("vector", lambda e, pb=pb: e.tensor_copy(lgt[:, pb * NE:(pb + 1) * NE],
                                                      ps_t8[:]))
        p.barrier()
        sl = slice(pb * NE, (pb + 1) * NE)
        p.op("vector", lambda e, sl=sl: e.reduce_max(nm1[:], lgt[:, sl], axis=AX.X,
                                                     negate=True))
        p.barrier()
        p.op("vector", lambda e, sl=sl: e.tensor_scalar(
            out=e1[:], in0=lgt[:, sl], scalar1=nm1[:], scalar2=None,
            op0=ALU.add, op1=ALU.bypass))
        p.barrier()
        p.op("vector", lambda e: e.tensor_single_scalar(out=e1[:], in_=e1[:],
                                                        scalar=0.0, op=ALU.is_equal))
        p.barrier()
        p.op("vector", lambda e, sl=sl: e.scalar_tensor_tensor(
            out=t1[:], in0=e1[:], scalar=-1e9, in1=lgt[:, sl],
            op0=ALU.mult, op1=ALU.add))
        p.barrier()
        p.op("vector", lambda e: e.reduce_max(m2[:], t1[:], axis=AX.X))
        p.op("scalar", lambda e, sl=sl: e.activation(out=pe[:], in_=lgt[:, sl],
                                                     func=AF.Exp, bias=nm1[:]))
        p.barrier()
        p.op("vector", lambda e, sl=sl: e.tensor_single_scalar(
            out=msk[:], in_=lgt[:, sl], scalar=m2[:], op=ALU.is_ge))
        p.barrier()
        p.op("vector", lambda e: e.tensor_mul(pe[:], pe[:], msk[:]))
        p.barrier()
        p.op("vector", lambda e: e.reduce_sum(zz[:], pe[:], axis=AX.X))
        p.barrier()
        p.op("vector", lambda e: e.reciprocal(rz[:], zz[:]))
        p.barrier()
        p.op("vector", lambda e: e.tensor_scalar_mul(out=w_tok[:], in0=pe[:],
                                                     scalar1=rz[:]))
        p.barrier()
        p.op("tensor", lambda e: e.transpose(ps_w8[:], w_tok[:], id32[:]))
        p.barrier()
        p.op("vector", lambda e, blk=blk: e.tensor_copy(
            lg_sb[0:NE, blk * 128:(blk + 1) * 128], ps_w8[:]))
        p.barrier()
    p.dma("sync", w_dram, lg_sb[0:NE, :])
    p.barrier()
    p.pop_region()
    ps_m = [p.ps(f"ps_Fm{i}", [128, 512]) for i in range(8)]

    t2p = lg_sb[:, 0:1536]
    for k in range(KD):
        p.dma("gpsimd", w1_sb[:, k * HF:(k + 1) * HF], W1_in[0, k * 128:(k + 1) * 128, :],
              lane=1)
    p.dma("gpsimd", w2_sb.ap().rearrange("p (k dd) -> p k dd", k=NO1),
          W2_in[0].rearrange("(k p) dd -> p k dd", p=128), lane=1)
    p.dma("gpsimd", wb_e[:, 0:TC], w_dram[0, 0:TC].partition_broadcast(128), lane=1)
    p.fence_lane1("tensor")
    p.fence_lane1("vector")
    p.barrier()
    p.eng_lag = {}
    NP1 = NO1 // 4
    NG = NP1 + 1 + KD + 2
    for ex in range(NE):
        for c in range(NCH):
            csl = slice(c * TC, (c + 1) * TC)
            par = (ex * NCH + c) % 2
            wb_cur = wb_e[:, par * TC:(par + 1) * TC]
            cn, exn = (c + 1, ex) if c + 1 < NCH else (0, ex + 1)
            for g in range(NG):
                if g == 0 and c == 0 and ex >= 1:
                    p.fence_lane1("tensor")
                if g == 0 and exn < NE:
                    p.dma("gpsimd", wb_e[:, (1 - par) * TC:(2 - par) * TC],
                          w_dram[exn, cn * TC:(cn + 1) * TC].partition_broadcast(128),
                          lane=1)
                if g == 0 and c == 0 and ex >= 1:
                    p.dma("gpsimd", w2_sb.ap().rearrange("p (k dd) -> p k dd", k=NO1),
                          W2_in[ex].rearrange("(k p) dd -> p k dd", p=128), lane=1)
                if g == NP1:
                    if c == 0:
                        p.fence_lane1("tensor")
                    p.fence_lane1("vector")
                if g == NP1 + 1 and c == NCH - 1 and ex + 1 < NE:
                    for k in range(KD):
                        p.dma("gpsimd", w1_sb[:, k * HF:(k + 1) * HF],
                              W1_in[ex + 1, k * 128:(k + 1) * 128, :], lane=1)
                if g < NP1:
                    for oo in range(4):
                        o = g * 4 + oo
                        pb = (g % 2) * 4 + oo
                        for k in range(KD):
                            p.op("tensor", lambda e, o=o, k=k, pb=pb, c=c: e.matmul(
                                ps_m[pb][:],
                                w1_sb[:, k * HF + o * 128:k * HF + (o + 1) * 128],
                                xn_sb[:, k * T + c * TC:k * T + (c + 1) * TC],
                                start=(k == 0), stop=(k == KD - 1)))
                elif NP1 + 1 <= g < NP1 + 1 + KD:
                    dd = g - NP1 - 1
                    for k in range(NO1):
                        p.op("tensor", lambda e, dd=dd, k=k: e.matmul(
                            ps_m[dd][:], w2_sb[:, k * D + dd * 128:k * D + (dd + 1) * 128],
                            h_sb[:, k * TC:(k + 1) * TC],
                            start=(k == 0), stop=(k == NO1 - 1)))
                gm = g - 1
                if 0 <= gm < NP1:
                    for oo in range(4):
                        o = gm * 4 + oo
                        pb = (gm % 2) * 4 + oo
                        if oo % 2 == 0:
                            p.op("vector", lambda e, o=o, pb=pb, ex=ex: e.tensor_scalar(
                                out=h_sb[:, o * TC:(o + 1) * TC], in0=ps_m[pb][:],
                                scalar1=b1_sb[:, ex * NO1 + o:ex * NO1 + o + 1],
                                scalar2=0.0, op0=ALU.add, op1=ALU.max))
                        else:
                            p.op("scalar", lambda e, o=o, pb=pb, ex=ex: e.activation(
                                out=h_sb[:, o * TC:(o + 1) * TC], in_=ps_m[pb][:],
                                func=AF.Relu, bias=b1_sb[:, ex * NO1 + o:ex * NO1 + o + 1]))
                elif NP1 + 1 <= gm < NP1 + 1 + KD:
                    dd = gm - NP1 - 1
                    p3 = dd % 3
                    p.op("vector", lambda e, dd=dd, p3=p3, ex=ex, wb_cur=wb_cur:
                         e.scalar_tensor_tensor(
                        out=t2p[:, p3 * 512:(p3 + 1) * 512], in0=ps_m[dd][:],
                        scalar=b2_sb[:, ex * KD + dd:ex * KD + dd + 1],
                        in1=wb_cur, op0=ALU.add, op1=ALU.mult))
                gm2 = g - 2
                if NP1 + 1 <= gm2 < NP1 + 1 + KD:
                    dd = gm2 - NP1 - 1
                    p3 = dd % 3
                    if ex == 0:
                        p.dma("gpsimd", acc_dram[dd * 128:(dd + 1) * 128, csl],
                              t2p[:, p3 * 512:(p3 + 1) * 512])
                    else:
                        def accdma(e, dd=dd, csl=csl, p3=p3):
                            return e.dma_start(
                                out=acc_dram[dd * 128:(dd + 1) * 128, csl],
                                in_=t2p[:, p3 * 512:(p3 + 1) * 512], accum_op=ALU.add)
                        cell = ["dma"]
                        p._q("gpsimd", ("call", accdma, cell))
                        p.tot_dma += 16
                p.barrier(lazy=True)
    p.eng_lag = {}
    res_sb = h_sb[:, 0:2048].bitcast(F32)
    out_sb = h_sb[:, 2048:4096].bitcast(F32)
    acc_sb = h_sb[:, 4096:6144].bitcast(F32)
    for dd in range(KD):
        for c in range(NCH):
            pb = c % 2
            csl = slice(c * TC, (c + 1) * TC)
            rows = slice(dd * 128, (dd + 1) * 128)
            rsl = res_sb[:, pb * TC:(pb + 1) * TC]
            osl = out_sb[:, pb * TC:(pb + 1) * TC]
            asl = acc_sb[:, pb * TC:(pb + 1) * TC]
            p.dma("sync", rsl, x1_in[rows, csl])
            p.dma("sync", osl, attn_in[rows, csl])
            p.dma("sync", asl, acc_dram[rows, csl])
            p.barrier()
            p.op("vector", lambda e, rsl=rsl, osl=osl: e.tensor_add(rsl, rsl, osl))
            p.barrier()
            p.op("vector", lambda e, rsl=rsl, osl=osl, asl=asl: e.tensor_add(
                osl, rsl, asl))
            p.barrier()
            p.dma("sync", out_T[rows, csl], osl)
    p.barrier()
    p.pop_region()
    if standalone:
        return p.build()

def _t(a):
    return np.ascontiguousarray(np.asarray(a).T)


def build_fused():
    p = Prog("fused")
    EI = lambda n, s, d: p.dram(n, list(s), d, "ExternalInput")
    IN = lambda n, s, d: p.dram(n, list(s), d, "Internal")
    xT = EI("xT", [D, T], F32)
    w_inT = EI("w_inT", [D, 2 * DIN], F32)
    conv_w = EI("conv_w", [DIN, 4], F32)
    conv_b = EI("conv_b", [128, NDI], F32)
    xp_wT = EI("xp_wT", [DIN, 96], F32)
    dt_wT = EI("dt_wT", [DTR, DIN], F32)
    dt_b = EI("dt_b", [128, NDI], F32)
    A_dev = EI("A_dev", [128, NDI * NST], F32)
    D_dev = EI("D_dev", [128, NDI], F32)
    outw_T = EI("outw_T", [DIN, D], F32)
    wrT = EI("wrT", [D, D], F32)
    wkT_d = EI("wkT_d", [D, D], F32)
    wvT_d = EI("wvT_d", [D, D], F32)
    mu_r = EI("mu_r", [128, KD], F32)
    mu_k = EI("mu_k", [128, KD], F32)
    wqT = EI("wqT", [D, D], F32)
    wkT_e = EI("wkT_e", [D, NKV * HD], F32)
    wvT_e = EI("wvT_e", [D, NKV * HD], F32)
    woT = EI("woT", [D, D], F32)
    Wr = EI("Wr", [D, NE], F32)
    br = EI("br", [NE, 1], F32)
    W1 = EI("W1", [NE, D, HF], F32)
    b1d = EI("b1d", [NE, 128, HF // 128], F32)
    W2 = EI("W2", [NE, HF, D], F32)
    b2d = EI("b2d", [NE, 128, D // 128], F32)
    outT = p.dram("outT", [D, T], F32, "ExternalOutput")

    u_bf = IN("u_bf", [DIN, T], BF16)
    sz_bf = IN("sz_bf", [DIN, T], BF16)
    xn_bf = IN("xn_bf", [D, T], BF16)
    delta = IN("delta", [DIN, T], F32)
    du = IN("du", [DIN, T], F16)
    bc = IN("bc", [2 * NST, T], F32)
    mamba = IN("mamba", [D, T], F32)
    x1 = IN("x1", [D, T], F32)
    xn1_bf = IN("xn1_bf", [D, T], BF16)
    attn = IN("attn", [D, T], F32)

    build_stageA(p, dict(xT=xT, w_inT=w_inT, conv_w=conv_w, conv_b=conv_b,
                         u_bf=u_bf, sz_bf=sz_bf, xn_bf=xn_bf))
    build_stageB(p, dict(u_bf=u_bf, xp_wT=xp_wT, dt_wT=dt_wT, dt_b=dt_b,
                         delta=delta, du=du, bc=bc))
    build_stageC(p, dict(delta=delta, du=du, u_bf=u_bf, sz_bf=sz_bf, bc=bc,
                         A_dev=A_dev, D_dev=D_dev, outw_T=outw_T, mamba=mamba))
    build_stageD(p, dict(xT=xT, xn_bf=xn_bf, mamba=mamba, wrT=wrT, wkT=wkT_d,
                         wvT=wvT_d, mu_r=mu_r, mu_k=mu_k, x1=x1, xn1_bf=xn1_bf))
    build_stageE(p, dict(xn1_bf=xn1_bf, wqT=wqT, wkT=wkT_e, wvT=wvT_e, woT=woT,
                         attn=attn))
    build_stageF(p, dict(xn1_bf=xn1_bf, x1=x1, attn=attn, Wr=Wr, br=br, W1=W1,
                         b1d=b1d, W2=W2, b2d=b2d, outT=outT))
    return p.build()


def kernel(**ins):
    cores = list(range(B))
    shared = {
        "w_inT": _t(ins["L0_attn_in_proj_w"]),
        "conv_w": np.ascontiguousarray(np.asarray(ins["L0_attn_conv_w"])[:, 0, :]),
        "conv_b": np.ascontiguousarray(np.asarray(ins["L0_attn_conv_b"]).reshape(NDI, 128).T),
        "xp_wT": _t(ins["L0_attn_x_proj_w"]),
        "dt_wT": _t(ins["L0_attn_dt_proj_w"]),
        "dt_b": np.ascontiguousarray(np.asarray(ins["L0_attn_dt_proj_b"]).reshape(NDI, 128).T),
        "A_dev": np.ascontiguousarray(np.asarray(ins["L0_attn_A_log"])
                                      .reshape(NDI, 128, NST).transpose(1, 0, 2)
                                      .reshape(128, NDI * NST)),
        "D_dev": np.ascontiguousarray(np.asarray(ins["L0_attn_D"]).reshape(NDI, 128).T),
        "outw_T": _t(ins["L0_attn_out_proj_w"]),
        "wrT": _t(ins["L0_ffn_W_r"]),
        "wkT_d": _t(ins["L0_ffn_W_k"]),
        "wvT_d": _t(ins["L0_ffn_W_v"]),
        "mu_r": np.ascontiguousarray(np.asarray(ins["L0_ffn_mu_r"]).reshape(KD, 128).T),
        "mu_k": np.ascontiguousarray(np.asarray(ins["L0_ffn_mu_k"]).reshape(KD, 128).T),
        "wqT": _t(ins["L1_attn_wq"]),
        "wkT_e": _t(ins["L1_attn_wk"]),
        "wvT_e": _t(ins["L1_attn_wv"]),
        "woT": _t(ins["L1_attn_wo"]),
        "Wr": np.ascontiguousarray(np.asarray(ins["L1_ffn_Wr"], np.float32)),
        "br": np.ascontiguousarray(np.asarray(ins["L1_ffn_br"], np.float32)[:, None]),
        "W1": np.ascontiguousarray(np.asarray(ins["L1_ffn_W1"], np.float32)),
        "b1d": np.ascontiguousarray(np.asarray(ins["L1_ffn_b1"], np.float32)
                                    .reshape(NE, HF // 128, 128).transpose(0, 2, 1)),
        "W2": np.ascontiguousarray(np.asarray(ins["L1_ffn_W2"], np.float32)),
        "b2d": np.ascontiguousarray(np.asarray(ins["L1_ffn_b2"], np.float32)
                                    .reshape(NE, D // 128, 128).transpose(0, 2, 1)),
    }
    maps = [dict(shared, xT=np.ascontiguousarray(
        np.asarray(ins["L0_attn_hidden_states"][b], np.float32).T)) for b in range(B)]
    nc = build_fused()
    res = run_bass_kernel_spmd(nc, maps, core_ids=cores)
    out = np.stack([np.asarray(res.results[b]["outT"]).T for b in range(B)], 0)
    return np.ascontiguousarray(out.astype(np.float32))


STAGE_BUILDERS = {"A": build_stageA, "B": build_stageB, "C": build_stageC, "D": build_stageD, "E": build_stageE, "F": build_stageF}


# version 10
# speedup vs baseline: 1.2587x; 1.0124x over previous
import contextlib
import numpy as np
import concourse.bass as bass
import concourse.mybir as mybir
from concourse.bass_utils import run_bass_kernel_spmd

F32 = mybir.dt.float32
BF16 = mybir.dt.bfloat16
F16 = mybir.dt.float16
AF = mybir.ActivationFunctionType
ALU = mybir.AluOpType
AX = mybir.AxisListType

ENGS = ("sync", "gpsimd", "vector", "scalar", "tensor")

B, T, D = 8, 2048, 1024
DIN = 2048
NST = 16
DTR = 64
NH, NKV, HD = 16, 4, 64
NE, HF = 8, 4096
TC = 512
NCH = T // TC
KD = D // 128
NDI = DIN // 128


class Prog:

    def __init__(self, name):
        self.nc = bass.Bass("TRN2", target_bir_lowering=False)
        self.name = name
        self.queues = {e: [] for e in ENGS}
        self.tot = {e: 0 for e in ENGS}
        self.tot_dma = 0
        self.pending = {e: None for e in ENGS}
        self.cur_snapshot = ({e: 0 for e in ENGS}, 0, {}, [])
        self.waited = {e: None for e in ENGS}
        self.stack = contextlib.ExitStack()
        self._stacks = [self.stack]
        self._names = {}
        self._dma_hist = [0]
        self.tot_dma1 = 0
        self._tot_hist = [dict(self.tot)]
        self.eng_lag = {}

    def _uniq(self, name):
        n = self._names.get(name, 0)
        self._names[name] = n + 1
        return name if n == 0 else f"{name}_{n}"

    def push_region(self):
        self._stacks.append(contextlib.ExitStack())

    def pop_region(self):
        self.barrier()
        self._stacks.pop().close()

    def dram(self, name, shape, dtype, kind):
        return self.nc.dram_tensor(name, list(shape), dtype, kind=kind).ap()

    def sb(self, name, shape, dtype):
        return self._stacks[-1].enter_context(
            self.nc.sbuf_tensor(self._uniq(name), list(shape), dtype))

    def ps(self, name, shape, dtype=F32):
        return self._stacks[-1].enter_context(
            self.nc.psum_tensor(self._uniq(name), list(shape), dtype))

    def _q(self, eng, item):
        snap = self.cur_snapshot
        if self.waited[eng] is not snap:
            self.queues[eng].append(("wait", snap, eng))
            self.waited[eng] = snap
        self.queues[eng].append(item)

    def op(self, eng, fn):
        cell = [0]
        self._q(eng, ("call", fn, cell))
        self.pending[eng] = cell

    def dma(self, eng, out, in_, lane=0):
        if lane == 0:
            cell = ["dma"]
            self._q(eng, ("call", lambda e: e.dma_start(out=out, in_=in_), cell))
            self.tot_dma += 16
        else:
            cell = ["dma1"]
            self._q(eng, ("call", lambda e: e.dma_start(out=out, in_=in_), cell))
            self.tot_dma1 += 16

    def fence_lane1(self, eng):
        v = self.tot_dma1
        if v > 0:
            self._q(eng, ("wait1", v, eng))

    def barrier(self, lazy=False):
        for e in ENGS:
            c = self.pending[e]
            if c is not None:
                c[0] = 1
                self.tot[e] += 1
                self.pending[e] = None
        dval = self._dma_hist[-1] if lazy else self.tot_dma
        self._dma_hist.append(self.tot_dma)
        self._tot_hist.append(dict(self.tot))
        self.cur_snapshot = (dict(self.tot), dval, dict(self.eng_lag),
                             [dict(h) for h in self._tot_hist[-4:]])

    def build(self):
        self.barrier()
        self.queues["sync"].append(("wait", self.cur_snapshot, "__final__"))
        if self.tot_dma1 > 0:
            self.queues["sync"].append(("wait1", self.tot_dma1, "__final__"))
        nc = self.nc
        bars = {e: self.stack.enter_context(nc.semaphore(f"bar_{e}")) for e in ENGS}
        bar_dma = self.stack.enter_context(nc.semaphore("bar_dma"))
        bar_dma1 = self.stack.enter_context(nc.semaphore("bar_dma1"))
        with nc.Block() as block:
            def emit_stream(eng_name):
                def _(e):
                    for item in self.queues[eng_name]:
                        if item[0] == "wait1":
                            e.wait_ge(bar_dma1, item[1])
                            continue
                        if item[0] == "wait":
                            snap, me = item[1], item[2]
                            counts, dmas, lags, hist = snap
                            mylag = lags.get(me, {})
                            for oe in ENGS:
                                v = counts[oe]
                                lag = mylag.get(oe, 0)
                                if lag > 0 and len(hist) > lag:
                                    v = hist[-1 - lag][oe]
                                if v > 0:
                                    e.wait_ge(bars[oe], v)
                            if dmas > 0:
                                e.wait_ge(bar_dma, dmas)
                        else:
                            _, fn, cell = item
                            inst = fn(e)
                            if cell[0] == "dma":
                                inst.then_inc(bar_dma, 16)
                            elif cell[0] == "dma1":
                                inst.then_inc(bar_dma1, 16)
                            elif cell[0] == 1:
                                inst.then_inc(bars[eng_name], 1)
                return _
            block.sync(emit_stream("sync"))
            block.gpsimd(emit_stream("gpsimd"))
            block.vector(emit_stream("vector"))
            block.scalar(emit_stream("scalar"))
            block.tensor(emit_stream("tensor"))
        self.stack.close()
        return nc


def _mk_dram(p, dr):
    def _D(name, shape, dtype, kind):
        if dr is not None:
            return dr[name]
        return p.dram(name, shape, dtype, kind)
    return _D


def build_stageA(p=None, dr=None):
    standalone = p is None
    if standalone:
        p = Prog("A")
    _D = _mk_dram(p, dr)
    p.push_region()
    xT = _D("xT", [D, T], F32, "ExternalInput")
    w_inT = _D("w_inT", [D, 2 * DIN], F32, "ExternalInput")
    conv_w = _D("conv_w", [DIN, 4], F32, "ExternalInput")
    conv_b = _D("conv_b", [128, NDI], F32, "ExternalInput")
    u_out = _D("u_bf", [DIN, T], BF16, "ExternalOutput")
    sz_out = _D("sz_bf", [DIN, T], BF16, "ExternalOutput")
    xn_out = _D("xn_bf", [D, T], BF16, "ExternalOutput")

    w_sb = p.sb("w_sb", [128, KD * 2 * DIN], BF16)
    cw_sb = p.sb("cw_sb", [128, NDI * 4], F32)
    cb_sb = p.sb("cb_sb", [128, NDI], F32)
    ones_col = p.sb("ones_col", [128, 1], BF16)
    ones_row = p.sb("ones_row", [1, 128], BF16)
    eps_sb = p.sb("eps_sb", [1, 1], F32)

    x_sb = p.sb("x_sb", [128, KD * TC], F32)
    xb_sb = p.sb("xb_sb", [128, KD * TC], BF16)
    xsq_sb = p.sb("xsq_sb", [128, KD * TC], BF16)
    xn_sb = p.sb("xn_sb", [128, KD * TC], BF16)
    tmp_sb = p.sb("tmp_sb", [128, TC], BF16)
    mrow = p.sb("mrow", [1, TC], F32)
    vrow = p.sb("vrow", [1, TC], F32)
    msq = p.sb("msq", [1, TC], F32)
    srow = p.sb("srow", [1, TC], F32)
    rrow = p.sb("rrow", [1, TC], F32)
    m_bf = p.sb("m_bf", [1, TC], BF16)
    r_bf = p.sb("r_bf", [1, TC], BF16)
    mb_sb = p.sb("mb_sb", [128, TC], BF16)
    rb_sb = p.sb("rb_sb", [128, TC], BF16)
    xc_sb = p.sb("xc_sb", [128, NDI * (TC + 3)], F32)
    cacc = p.sb("cacc", [128, 4 * TC], F32)
    u_sb = p.sb("u_sb", [128, NDI * TC], BF16)
    sz_sb = p.sb("sz_sb", [128, NDI * TC], BF16)

    ps_sx = p.ps("ps_sx", [1, TC])
    ps_sxx = p.ps("ps_sxx", [1, TC])
    ps_mb = p.ps("ps_mb", [128, TC])
    ps_rb = p.ps("ps_rb", [128, TC])
    ps_xz = [p.ps(f"ps_xz{i}", [128, TC]) for i in range(4)]

    for k in range(KD):
        p.dma("gpsimd", w_sb[:, k * 2 * DIN:(k + 1) * 2 * DIN],
              w_inT[k * 128:(k + 1) * 128, :])
    p.dma("sync", cw_sb.ap().rearrange("p (o c) -> p o c", c=4),
          conv_w.rearrange("(o p) c -> p o c", p=128))
    p.dma("sync", cb_sb[:], conv_b)
    p.op("vector", lambda e: e.memset(eps_sb[:], 1e-5))
    p.op("vector", lambda e: e.memset(ones_col[:], 1.0))
    p.op("vector", lambda e: e.memset(ones_row[:], 1.0))
    for o in range(NDI):
        p.op("vector", lambda e, o=o: e.memset(
            xc_sb[:, o * (TC + 3):o * (TC + 3) + 3], 0.0))
    p.barrier()

    for c in range(NCH):
        csl = slice(c * TC, (c + 1) * TC)
        src = xT[:, csl].rearrange("(k p) t -> p k t", p=128)
        p.dma("sync", x_sb.ap().rearrange("p (k t) -> p k t", k=KD), src)
        p.dma("gpsimd", xb_sb.ap().rearrange("p (k t) -> p k t", k=KD), src)
        p.barrier()
        p.op("scalar", lambda e: e.activation(out=xsq_sb[:], in_=x_sb[:],
                                              func=AF.Square))
        p.barrier()
        for k in range(KD):
            p.op("tensor", lambda e, k=k: e.matmul(
                ps_sx[:], ones_col[:], xb_sb[:, k * TC:(k + 1) * TC],
                start=(k == 0), stop=(k == KD - 1)))
        for k in range(KD):
            p.op("tensor", lambda e, k=k: e.matmul(
                ps_sxx[:], ones_col[:], xsq_sb[:, k * TC:(k + 1) * TC],
                start=(k == 0), stop=(k == KD - 1)))
        p.barrier()
        p.op("vector", lambda e: e.tensor_scalar_mul(out=mrow[:], in0=ps_sx[:],
                                                     scalar1=1.0 / D))
        p.op("vector", lambda e: e.tensor_scalar_mul(out=vrow[:], in0=ps_sxx[:],
                                                     scalar1=1.0 / D))
        p.op("vector", lambda e: e.tensor_mul(msq[:], mrow[:], mrow[:]))
        p.op("vector", lambda e: e.tensor_sub(vrow[:], vrow[:], msq[:]))
        p.barrier()
        p.op("scalar", lambda e: e.activation(out=srow[:], in_=vrow[:],
                                              func=AF.Sqrt, bias=eps_sb[:]))
        p.barrier()
        p.op("vector", lambda e: e.reciprocal(rrow[:], srow[:]))
        p.barrier()
        p.op("vector", lambda e: e.tensor_copy(m_bf[:], mrow[:]))
        p.op("vector", lambda e: e.tensor_copy(r_bf[:], rrow[:]))
        p.barrier()
        p.op("tensor", lambda e: e.matmul(ps_mb[:], ones_row[:], m_bf[:],
                                          start=True, stop=True))
        p.op("tensor", lambda e: e.matmul(ps_rb[:], ones_row[:], r_bf[:],
                                          start=True, stop=True))
        p.barrier()
        p.op("vector", lambda e: e.tensor_copy(mb_sb[:], ps_mb[:]))
        p.op("vector", lambda e: e.tensor_copy(rb_sb[:], ps_rb[:]))
        p.barrier()
        for k in range(KD):
            p.op("vector", lambda e, k=k: e.tensor_sub(
                tmp_sb[:], xb_sb[:, k * TC:(k + 1) * TC], mb_sb[:]))
            p.op("vector", lambda e, k=k: e.tensor_mul(
                xn_sb[:, k * TC:(k + 1) * TC], tmp_sb[:], rb_sb[:]))
        p.barrier()
        p.dma("gpsimd", xn_out[:, csl].rearrange("(k p) t -> p k t", p=128),
              xn_sb.ap().rearrange("p (k t) -> p k t", k=KD))
        p.eng_lag = {"tensor": {"vector": 2, "scalar": 2}}
        for g in range(35):
            if g < 32:
                o, pbx = g, g % 4
                for k in range(KD):
                    p.op("tensor", lambda e, o=o, pbx=pbx, k=k: e.matmul(
                        ps_xz[pbx][:],
                        w_sb[:, k * 2 * DIN + o * 128:k * 2 * DIN + (o + 1) * 128],
                        xn_sb[:, k * TC:(k + 1) * TC],
                        start=(k == 0), stop=(k == KD - 1)))
            gm = g - 1
            if 0 <= gm < 32:
                o, pbx = gm, gm % 4
                if o < 16:
                    base = o * (TC + 3)
                    p.op("vector", lambda e, pbx=pbx, base=base: e.tensor_copy(
                        xc_sb[:, base + 3:base + 3 + TC], ps_xz[pbx][:]))
                else:
                    zi = o - 16
                    p.op("scalar", lambda e, zi=zi, pbx=pbx: e.activation(
                        out=sz_sb[:, zi * TC:(zi + 1) * TC],
                        in_=ps_xz[pbx][:], func=AF.Silu))
            gm2 = g - 2
            if 0 <= gm2 < 16:
                o = gm2
                base = o * (TC + 3)
                w4 = cw_sb[:, o * 4:o * 4 + 4]
                cs = cacc[:, (gm2 % 4) * TC:(gm2 % 4 + 1) * TC]
                p.op("vector", lambda e, base=base, w4=w4, cs=cs: e.tensor_scalar_mul(
                    out=cs, in0=xc_sb[:, base + 3:base + 3 + TC],
                    scalar1=w4[:, 3:4]))
                for j in (2, 1, 0):
                    p.op("vector", lambda e, base=base, w4=w4, cs=cs, j=j:
                         e.scalar_tensor_tensor(
                             out=cs, in0=xc_sb[:, base + j:base + j + TC],
                             scalar=w4[:, j:j + 1], in1=cs,
                             op0=ALU.mult, op1=ALU.add))
                p.op("vector", lambda e, base=base: e.tensor_copy(
                    xc_sb[:, base:base + 3],
                    xc_sb[:, base + TC:base + TC + 3]))
            gm3 = g - 3
            if 0 <= gm3 < 16:
                o = gm3
                p.op("scalar", lambda e, o=o, gm3=gm3: e.activation(
                    out=u_sb[:, o * TC:(o + 1) * TC],
                    in_=cacc[:, (gm3 % 4) * TC:(gm3 % 4 + 1) * TC],
                    func=AF.Silu, bias=cb_sb[:, o:o + 1]))
            p.barrier(lazy=True)
        p.eng_lag = {}
        p.dma("gpsimd", u_out[:, csl].rearrange("(k p) t -> p k t", p=128),
              u_sb.ap().rearrange("p (k t) -> p k t", k=NDI))
        p.dma("gpsimd", sz_out[:, csl].rearrange("(k p) t -> p k t", p=128),
              sz_sb.ap().rearrange("p (k t) -> p k t", k=NDI))
        p.barrier()
    p.pop_region()
    if standalone:
        return p.build()


def build_stageB(p=None, dr=None):
    standalone = p is None
    if standalone:
        p = Prog("B")
    _D = _mk_dram(p, dr)
    p.push_region()
    u_in = _D("u_bf", [DIN, T], BF16, "ExternalInput")
    xp_wT = _D("xp_wT", [DIN, 96], F32, "ExternalInput")
    dt_wT = _D("dt_wT", [DTR, DIN], F32, "ExternalInput")
    dt_b = _D("dt_b", [128, NDI], F32, "ExternalInput")
    delta_out = _D("delta", [DIN, T], F32, "ExternalOutput")
    du_out = _D("du", [DIN, T], F16, "ExternalOutput")
    bc_out = _D("bc", [2 * NST, T], F32, "ExternalOutput")

    u_sb = p.sb("u_sb", [128, NDI * T], BF16)
    xpw_sb = p.sb("xpw_sb", [128, NDI * 96], BF16)
    dtw_sb = p.sb("dtw_sb", [DTR, DIN], BF16)
    dtb_sb = p.sb("dtb_sb", [128, NDI], F32)
    dt_sb = p.sb("dt_sb", [DTR, T], BF16)
    bc_sb = p.sb("bc_sb", [96, T], F32)
    delta_sb = p.sb("delta_sb", [128, 4 * TC], F32)
    du_sb = p.sb("du_sb", [128, 4 * TC], F16)

    ps_xd = p.ps("ps_xd", [96, TC])
    ps_dt = [p.ps(f"ps_dt{i}", [128, TC]) for i in range(4)]

    for k in range(NDI):
        p.dma("gpsimd", u_sb[:, k * T:(k + 1) * T], u_in[k * 128:(k + 1) * 128, :])
    p.dma("gpsimd", xpw_sb.ap().rearrange("p (k j) -> p k j", k=NDI),
          xp_wT.rearrange("(k p) j -> p k j", p=128))
    p.dma("gpsimd", dtw_sb[:], dt_wT)
    p.dma("sync", dtb_sb[:], dt_b)
    p.barrier()

    for c in range(NCH):
        csl = slice(c * TC, (c + 1) * TC)
        for k in range(NDI):
            p.op("tensor", lambda e, k=k, c=c: e.matmul(
                ps_xd[:], xpw_sb[:, k * 96:(k + 1) * 96],
                u_sb[:, k * T + c * TC:k * T + (c + 1) * TC],
                start=(k == 0), stop=(k == NDI - 1)))
        p.barrier()
        p.op("vector", lambda e, csl=csl: e.tensor_copy(dt_sb[:, csl], ps_xd[0:64, :]))
        p.op("vector", lambda e, csl=csl: e.tensor_copy(bc_sb[64:96, csl], ps_xd[64:96, :]))
        p.barrier()
        for mm in range(NDI + 3):
            if mm < NDI:
                p.op("tensor", lambda e, mm=mm, csl=csl: e.matmul(
                    ps_dt[mm % 4][:], dtw_sb[:, mm * 128:(mm + 1) * 128],
                    dt_sb[:, csl], start=True, stop=True))
            if 0 <= mm - 1 < NDI:
                m1 = mm - 1
                sl = slice((m1 % 4) * TC, (m1 % 4 + 1) * TC)
                p.op("scalar", lambda e, m1=m1, sl=sl: e.activation(
                    out=delta_sb[:, sl], in_=ps_dt[m1 % 4][:],
                    func=AF.Exp, bias=dtb_sb[:, m1:m1 + 1]))
                p.op("scalar", lambda e, sl=sl: e.activation(
                    out=delta_sb[:, sl], in_=delta_sb[:, sl], func=AF.Ln, bias=1.0))
            if 0 <= mm - 2 < NDI:
                m2 = mm - 2
                sl = slice((m2 % 4) * TC, (m2 % 4 + 1) * TC)
                p.op("vector", lambda e, m2=m2, sl=sl, c=c: e.tensor_mul(
                    du_sb[:, sl], delta_sb[:, sl],
                    u_sb[:, m2 * T + c * TC:m2 * T + (c + 1) * TC]))
            if 0 <= mm - 3 < NDI:
                m3 = mm - 3
                sl = slice((m3 % 4) * TC, (m3 % 4 + 1) * TC)
                p.dma("sync", delta_out[m3 * 128:(m3 + 1) * 128, csl],
                      delta_sb[:, sl])
                p.dma("gpsimd", du_out[m3 * 128:(m3 + 1) * 128, csl],
                      du_sb[:, sl])
            p.barrier()
    p.dma("sync", bc_out, bc_sb[64:96, :])
    p.barrier()
    p.pop_region()
    if standalone:
        return p.build()




def build_stageC(p=None, dr=None):
    CT = 1024
    NCC = T // CT
    standalone = p is None
    if standalone:
        p = Prog("C")
    _D = _mk_dram(p, dr)
    p.push_region()
    delta_in = _D("delta", [DIN, T], F32, "ExternalInput")
    du_in = _D("du", [DIN, T], F16, "ExternalInput")
    u_in = _D("u_bf", [DIN, T], BF16, "ExternalInput")
    sz_in = _D("sz_bf", [DIN, T], BF16, "ExternalInput")
    bc_in = _D("bc", [2 * NST, T], F32, "ExternalInput")
    A_in = _D("A_dev", [128, NDI * NST], F32, "ExternalInput")
    D_in = _D("D_dev", [128, NDI], F32, "ExternalInput")
    outw_in = _D("outw_T", [DIN, D], F32, "ExternalInput")
    mamba_out = _D("mamba", [D, T], F32, "ExternalOutput")

    Bb = p.sb("Bb", [128, NST * CT], F16)
    Cb = p.sb("Cb", [128, NST * CT], F16)
    delta_c = p.sb("delta_c", [128, 2 * CT], F32)
    du_c = p.sb("du_c", [128, 2 * CT], F16)
    u_c = p.sb("u_c", [128, 2 * CT], BF16)
    sz_c = p.sb("sz_c", [128, 2 * CT], BF16)
    dA = p.sb("dA", [128, 3 * CT], F16)
    dBu = p.sb("dBu", [128, 3 * CT], F16)
    h_b = p.sb("h_b", [128, 3 * CT], F16)
    ym = p.sb("ym", [128, 3 * CT], F16)
    y_b = p.sb("y_b", [128, 2 * CT], F16)
    a_gate = p.sb("a_gate", [128, 2 * CT], F16)
    b_gate = p.sb("b_gate", [128, 2 * CT], F16)
    yg = p.sb("yg", [128, NDI * CT], BF16)
    w_out = p.sb("w_out", [128, NDI * D], BF16)
    carry = p.sb("carry", [128, NDI * NST], F32)
    A_sb = p.sb("A_sb", [128, NDI * NST], F32)
    D_sb = p.sb("D_sb", [128, NDI], F32)
    mb_ev = p.sb("mb_ev", [128, 3 * CT], F32)
    ps_op = [p.ps(f"ps_op{i}", [128, CT]) for i in range(4)]

    for k in range(NDI):
        p.dma("gpsimd", w_out[:, k * D:(k + 1) * D],
              outw_in[k * 128:(k + 1) * 128, :])
    p.dma("sync", A_sb[:], A_in)
    p.dma("sync", D_sb[:], D_in)
    p.barrier()
    p.op("scalar", lambda e: e.activation(out=A_sb[:], in_=A_sb[:], func=AF.Exp))
    p.barrier()
    p.op("vector", lambda e: e.tensor_scalar_mul(out=A_sb[:], in0=A_sb[:],
                                                 scalar1=-1.0))
    p.barrier()

    def tile_loads(m, c):
        csl = slice(c * CT, (c + 1) * CT)
        buf = slice((m % 2) * CT, (m % 2 + 1) * CT)
        rows = slice(m * 128, (m + 1) * 128)
        p.dma("sync", delta_c[:, buf], delta_in[rows, csl])
        p.dma("sync", du_c[:, buf], du_in[rows, csl])
        p.dma("sync", u_c[:, buf], u_in[rows, csl])
        p.dma("sync", sz_c[:, buf], sz_in[rows, csl])

    NPH = NDI * NST + 8
    for c in range(NCC):
        csl = slice(c * CT, (c + 1) * CT)
        p.dma("gpsimd", Bb.ap().rearrange("p (n t) -> p n t", n=NST),
              bc_in[0:NST, csl].unsqueeze(0).to_broadcast((128, NST, CT)))
        p.dma("gpsimd", Cb.ap().rearrange("p (n t) -> p n t", n=NST),
              bc_in[NST:2 * NST, csl].unsqueeze(0).to_broadcast((128, NST, CT)))
        tile_loads(0, c)
        p.op("vector", lambda e: e.memset(y_b[:, 0:CT], 0.0))
        p.barrier()
        for t in range(NPH):
            if t < NDI * NST:
                i = t
                m, n = i // NST, i % NST
                j = t % 3
                p.op("scalar", lambda e, j=j, m=m, i=i: e.activation(
                    out=dA[:, j * CT:(j + 1) * CT],
                    in_=delta_c[:, (m % 2) * CT:(m % 2 + 1) * CT],
                    func=AF.Exp, scale=A_sb[:, i:i + 1]))
                p.op("vector", lambda e, j=j, m=m, n=n: e.tensor_tensor(
                    out=dBu[:, j * CT:(j + 1) * CT],
                    in0=du_c[:, (m % 2) * CT:(m % 2 + 1) * CT],
                    in1=Bb[:, n * CT:(n + 1) * CT], op=ALU.mult))
            if 0 <= t - 1 < NDI * NST:
                i = t - 1
                j = i % 3
                init = 0.0 if c == 0 else carry[:, i:i + 1]
                p.op("vector", lambda e, j=j, init=init: e.tensor_tensor_scan(
                    out=h_b[:, j * CT:(j + 1) * CT],
                    data0=dA[:, j * CT:(j + 1) * CT],
                    data1=dBu[:, j * CT:(j + 1) * CT],
                    initial=init, op0=ALU.mult, op1=ALU.add))
            if c == 0 and 0 <= t - 2 < NDI * NST:
                i2 = t - 2
                j2 = i2 % 3
                p.op("vector", lambda e, j2=j2, i2=i2: e.tensor_copy(
                    carry[:, i2:i2 + 1], h_b[:, (j2 + 1) * CT - 1:(j2 + 1) * CT]))
            if 0 <= t - 2 < NDI * NST:
                i = t - 2
                j = i % 3
                n = i % NST
                p.op("vector", lambda e, j=j, n=n: e.tensor_tensor(
                    out=ym[:, j * CT:(j + 1) * CT],
                    in0=h_b[:, j * CT:(j + 1) * CT],
                    in1=Cb[:, n * CT:(n + 1) * CT], op=ALU.mult))
            if 0 <= t - 3 < NDI * NST:
                i = t - 3
                j = i % 3
                m = i // NST
                ysl = slice((m % 2) * CT, (m % 2 + 1) * CT)
                p.op("vector", lambda e, j=j, ysl=ysl: e.tensor_tensor(
                    out=y_b[:, ysl], in0=y_b[:, ysl],
                    in1=ym[:, j * CT:(j + 1) * CT], op=ALU.add))
            if t % NST == 0 and 0 < t < NDI * NST:
                mnew = t // NST
                p.op("vector", lambda e, mnew=mnew: e.memset(
                    y_b[:, (mnew % 2) * CT:(mnew % 2 + 1) * CT], 0.0))
            if t % NST == 8 and t // NST < NDI - 1:
                tile_loads(t // NST + 1, c)
            if t >= 19:
                mg = (t - 3) // NST - 1
                off = (t - 3) % NST
                if 0 <= mg < NDI:
                    gb = slice((mg % 2) * CT, (mg % 2 + 1) * CT)
                    if off == 0:
                        p.op("vector", lambda e, gb=gb, mg=mg: e.tensor_scalar_mul(
                            out=a_gate[:, gb], in0=u_c[:, gb],
                            scalar1=D_sb[:, mg:mg + 1]))
                    elif off == 1:
                        p.op("vector", lambda e, gb=gb: e.tensor_tensor(
                            out=b_gate[:, gb], in0=y_b[:, gb],
                            in1=a_gate[:, gb], op=ALU.add))
                    elif off == 2:
                        p.op("vector", lambda e, gb=gb, mg=mg: e.tensor_tensor(
                            out=yg[:, mg * CT:(mg + 1) * CT], in0=b_gate[:, gb],
                            in1=sz_c[:, gb], op=ALU.mult))
            p.barrier(lazy=True)
        p.eng_lag = {"tensor": {"vector": 2, "scalar": 2}}
        for o in range(KD + 2):
            if o < KD:
                pb = o % 4
                for nn in range(CT // 512):
                    for k in range(NDI):
                        p.op("tensor", lambda e, o=o, pb=pb, nn=nn, k=k: e.matmul(
                            ps_op[pb][:, nn * 512:(nn + 1) * 512],
                            w_out[:, k * D + o * 128:k * D + (o + 1) * 128],
                            yg[:, k * CT + nn * 512:k * CT + nn * 512 + 512],
                            start=(k == 0), stop=(k == NDI - 1)))
            if 0 <= o - 1 < KD:
                pe3 = (o - 1) % 3
                pb = (o - 1) % 4
                p.op("scalar", lambda e, pe3=pe3, pb=pb: e.copy(
                    mb_ev[:, pe3 * CT:(pe3 + 1) * CT], ps_op[pb][:]))
            if 0 <= o - 2 < KD:
                pe3 = (o - 2) % 3
                p.dma("sync", mamba_out[(o - 2) * 128:(o - 1) * 128, csl],
                      mb_ev[:, pe3 * CT:(pe3 + 1) * CT])
            p.barrier(lazy=True)
        p.eng_lag = {}
    p.pop_region()
    if standalone:
        return p.build()



def build_stageD(p=None, dr=None):
    standalone = p is None
    if standalone:
        p = Prog("D")
    _D = _mk_dram(p, dr)
    p.push_region()
    xT = _D("xT", [D, T], F32, "ExternalInput")
    xn_in = _D("xn_bf", [D, T], BF16, "ExternalInput")
    mamba_in = _D("mamba", [D, T], F32, "ExternalInput")
    wrT = _D("wrT", [D, D], F32, "ExternalInput")
    wkT = _D("wkT", [D, D], F32, "ExternalInput")
    wvT = _D("wvT", [D, D], F32, "ExternalInput")
    mur_in = _D("mu_r", [128, KD], F32, "ExternalInput")
    muk_in = _D("mu_k", [128, KD], F32, "ExternalInput")
    x1_out = _D("x1", [D, T], F32, "ExternalOutput")
    xn1_out = _D("xn1_bf", [D, T], BF16, "ExternalOutput")

    wr_sb = p.sb("wr_sb", [128, KD * D], BF16)
    wk_sb = p.sb("wk_sb", [128, KD * D], BF16)
    wv_sb = p.sb("wv_sb", [128, KD * D], BF16)
    mur = p.sb("mur", [128, KD], F32)
    muk = p.sb("muk", [128, KD], F32)
    mur1 = p.sb("mur1", [128, KD], F32)
    muk1 = p.sb("muk1", [128, KD], F32)
    ones_col = p.sb("ones_colD", [128, 1], BF16)
    ones_row = p.sb("ones_rowD", [1, 128], BF16)
    eps_sb = p.sb("eps_sbD", [1, 1], F32)

    xs = p.sb("xs", [128, KD * (TC + 1)], BF16)
    xr_sb = p.sb("xr_sb", [128, KD * TC], BF16)
    xk_sb = p.sb("xk_sb", [128, KD * TC], BF16)
    tmpmix = p.sb("tmpmix", [128, TC], BF16)
    r_sb = p.sb("r_sb", [128, KD * TC], BF16)
    kk_sb = p.sb("kk_sb", [128, KD * TC], BF16)
    krelu = p.sb("krelu", [128, 2 * TC], BF16)
    x_sb = p.sb("x_sbD", [128, KD * TC], F32)
    mam_sb = p.sb("mam_sb", [128, KD * TC], F32)
    ffn_sb = p.sb("ffn_sb", [128, 2 * TC], F32)
    x1_sb = p.sb("x1_sb", [128, KD * TC], F32)
    xb1 = p.sb("xb1", [128, KD * TC], BF16)
    xsq1 = p.sb("xsq1", [128, KD * TC], BF16)
    xn1_sb = p.sb("xn1_sb", [128, KD * TC], BF16)
    tmp_ln = p.sb("tmp_lnD", [128, TC], BF16)
    mrow = p.sb("mrowD", [1, TC], F32)
    vrow = p.sb("vrowD", [1, TC], F32)
    msq = p.sb("msqD", [1, TC], F32)
    srow = p.sb("srowD", [1, TC], F32)
    rrow = p.sb("rrowD", [1, TC], F32)
    m_bf = p.sb("m_bfD", [1, TC], BF16)
    r_bf = p.sb("r_bfD", [1, TC], BF16)
    mb_sb = p.sb("mb_sbD", [128, TC], BF16)
    rb_sb = p.sb("rb_sbD", [128, TC], BF16)

    ps_a = [p.ps(f"ps_Da{i}", [128, TC]) for i in range(2)]
    ps_b = [p.ps(f"ps_Db{i}", [128, TC]) for i in range(2)]
    ps_sx = p.ps("ps_sxD", [1, TC])
    ps_sxx = p.ps("ps_sxxD", [1, TC])
    ps_mb = p.ps("ps_mbD", [128, TC])
    ps_rb = p.ps("ps_rbD", [128, TC])

    for k in range(KD):
        p.dma("gpsimd", wr_sb[:, k * D:(k + 1) * D], wrT[k * 128:(k + 1) * 128, :])
        p.dma("gpsimd", wk_sb[:, k * D:(k + 1) * D], wkT[k * 128:(k + 1) * 128, :])
        p.dma("gpsimd", wv_sb[:, k * D:(k + 1) * D], wvT[k * 128:(k + 1) * 128, :])
    p.dma("sync", mur[:], mur_in)
    p.dma("sync", muk[:], muk_in)
    p.op("vector", lambda e: e.memset(eps_sb[:], 1e-5))
    p.op("vector", lambda e: e.memset(ones_col[:], 1.0))
    p.op("vector", lambda e: e.memset(ones_row[:], 1.0))
    for k in range(KD):
        p.op("vector", lambda e, k=k: e.memset(xs[:, k * (TC + 1):k * (TC + 1) + 1], 0.0))
    p.barrier()
    p.op("vector", lambda e: e.tensor_scalar(out=mur1[:], in0=mur[:], scalar1=-1.0,
                                             scalar2=1.0, op0=ALU.mult, op1=ALU.add))
    p.op("vector", lambda e: e.tensor_scalar(out=muk1[:], in0=muk[:], scalar1=-1.0,
                                             scalar2=1.0, op0=ALU.mult, op1=ALU.add))
    p.barrier()

    for c in range(NCH):
        csl = slice(c * TC, (c + 1) * TC)
        p.dma("sync", xs.ap().rearrange("p (k t) -> p k t", k=KD)[:, :, 1:TC + 1],
              xn_in[:, csl].rearrange("(k p) t -> p k t", p=128))
        p.dma("sync", x_sb.ap().rearrange("p (k t) -> p k t", k=KD),
              xT[:, csl].rearrange("(k p) t -> p k t", p=128))
        p.dma("sync", mam_sb.ap().rearrange("p (k t) -> p k t", k=KD),
              mamba_in[:, csl].rearrange("(k p) t -> p k t", p=128))
        p.barrier()
        for k in range(KD):
            base = k * (TC + 1)
            cur = xs[:, base + 1:base + 1 + TC]
            prv = xs[:, base:base + TC]
            p.op("vector", lambda e, prv=prv, k=k: e.tensor_scalar_mul(
                out=tmpmix[:], in0=prv, scalar1=mur1[:, k:k + 1]))
            p.op("vector", lambda e, cur=cur, k=k: e.scalar_tensor_tensor(
                out=xr_sb[:, k * TC:(k + 1) * TC], in0=cur,
                scalar=mur[:, k:k + 1], in1=tmpmix[:], op0=ALU.mult, op1=ALU.add))
            p.op("vector", lambda e, prv=prv, k=k: e.tensor_scalar_mul(
                out=tmpmix[:], in0=prv, scalar1=muk1[:, k:k + 1]))
            p.op("vector", lambda e, cur=cur, k=k: e.scalar_tensor_tensor(
                out=xk_sb[:, k * TC:(k + 1) * TC], in0=cur,
                scalar=muk[:, k:k + 1], in1=tmpmix[:], op0=ALU.mult, op1=ALU.add))
        for k in range(KD):
            base = k * (TC + 1)
            p.op("vector", lambda e, base=base: e.tensor_copy(
                xs[:, base:base + 1], xs[:, base + TC:base + TC + 1]))
        p.barrier()
        for o in range(KD + 2):
            if o < KD:
                pb = o % 2
                for k in range(KD):
                    p.op("tensor", lambda e, o=o, k=k, pb=pb: e.matmul(
                        ps_a[pb][:], wr_sb[:, k * D + o * 128:k * D + (o + 1) * 128],
                        xr_sb[:, k * TC:(k + 1) * TC],
                        start=(k == 0), stop=(k == KD - 1)))
                for k in range(KD):
                    p.op("tensor", lambda e, o=o, k=k, pb=pb: e.matmul(
                        ps_b[pb][:], wk_sb[:, k * D + o * 128:k * D + (o + 1) * 128],
                        xk_sb[:, k * TC:(k + 1) * TC],
                        start=(k == 0), stop=(k == KD - 1)))
            if 0 <= o - 1 < KD:
                om, pb = o - 1, (o - 1) % 2
                p.op("scalar", lambda e, om=om, pb=pb: e.activation(
                    out=r_sb[:, om * TC:(om + 1) * TC], in_=ps_a[pb][:],
                    func=AF.Sigmoid))
                p.op("vector", lambda e, om=om, pb=pb: e.tensor_scalar_max(
                    out=krelu[:, pb * TC:(pb + 1) * TC], in0=ps_b[pb][:], scalar1=0.0))
            if 0 <= o - 2 < KD:
                om, pb = o - 2, (o - 2) % 2
                p.op("scalar", lambda e, om=om, pb=pb: e.activation(
                    out=kk_sb[:, om * TC:(om + 1) * TC],
                    in_=krelu[:, pb * TC:(pb + 1) * TC], func=AF.Square))
            p.barrier(lazy=True)
        for o in range(KD + 1):
            if o < KD:
                pb = o % 2
                for k in range(KD):
                    p.op("tensor", lambda e, o=o, k=k, pb=pb: e.matmul(
                        ps_a[pb][:], wv_sb[:, k * D + o * 128:k * D + (o + 1) * 128],
                        kk_sb[:, k * TC:(k + 1) * TC],
                        start=(k == 0), stop=(k == KD - 1)))
            if 0 <= o - 1 < KD:
                om, pb = o - 1, (o - 1) % 2
                p.op("vector", lambda e, om=om, pb=pb: e.tensor_tensor(
                    out=ffn_sb[:, pb * TC:(pb + 1) * TC], in0=ps_a[pb][:],
                    in1=r_sb[:, om * TC:(om + 1) * TC], op=ALU.mult))
                p.op("vector", lambda e, om=om: e.tensor_add(
                    x1_sb[:, om * TC:(om + 1) * TC], x_sb[:, om * TC:(om + 1) * TC],
                    mam_sb[:, om * TC:(om + 1) * TC]))
                p.op("vector", lambda e, om=om, pb=pb: e.tensor_add(
                    x1_sb[:, om * TC:(om + 1) * TC], x1_sb[:, om * TC:(om + 1) * TC],
                    ffn_sb[:, pb * TC:(pb + 1) * TC]))
            p.barrier(lazy=True)
        p.op("vector", lambda e: e.tensor_copy(xb1[:], x1_sb[:]))
        p.op("scalar", lambda e: e.activation(out=xsq1[:], in_=x1_sb[:],
                                              func=AF.Square))
        p.barrier()
        for k in range(KD):
            p.op("tensor", lambda e, k=k: e.matmul(
                ps_sx[:], ones_col[:], xb1[:, k * TC:(k + 1) * TC],
                start=(k == 0), stop=(k == KD - 1)))
        for k in range(KD):
            p.op("tensor", lambda e, k=k: e.matmul(
                ps_sxx[:], ones_col[:], xsq1[:, k * TC:(k + 1) * TC],
                start=(k == 0), stop=(k == KD - 1)))
        p.barrier()
        p.op("vector", lambda e: e.tensor_scalar_mul(out=mrow[:], in0=ps_sx[:],
                                                     scalar1=1.0 / D))
        p.op("vector", lambda e: e.tensor_scalar_mul(out=vrow[:], in0=ps_sxx[:],
                                                     scalar1=1.0 / D))
        p.op("vector", lambda e: e.tensor_mul(msq[:], mrow[:], mrow[:]))
        p.op("vector", lambda e: e.tensor_sub(vrow[:], vrow[:], msq[:]))
        p.barrier()
        p.op("scalar", lambda e: e.activation(out=srow[:], in_=vrow[:],
                                              func=AF.Sqrt, bias=eps_sb[:]))
        p.barrier()
        p.op("vector", lambda e: e.reciprocal(rrow[:], srow[:]))
        p.barrier()
        p.op("vector", lambda e: e.tensor_copy(m_bf[:], mrow[:]))
        p.op("vector", lambda e: e.tensor_copy(r_bf[:], rrow[:]))
        p.barrier()
        p.op("tensor", lambda e: e.matmul(ps_mb[:], ones_row[:], m_bf[:],
                                          start=True, stop=True))
        p.op("tensor", lambda e: e.matmul(ps_rb[:], ones_row[:], r_bf[:],
                                          start=True, stop=True))
        p.barrier()
        p.op("vector", lambda e: e.tensor_copy(mb_sb[:], ps_mb[:]))
        p.op("vector", lambda e: e.tensor_copy(rb_sb[:], ps_rb[:]))
        p.barrier()
        for k in range(KD):
            p.op("vector", lambda e, k=k: e.tensor_sub(
                tmp_ln[:], xb1[:, k * TC:(k + 1) * TC], mb_sb[:]))
            p.op("vector", lambda e, k=k: e.tensor_mul(
                xn1_sb[:, k * TC:(k + 1) * TC], tmp_ln[:], rb_sb[:]))
        p.barrier()
        p.dma("gpsimd", x1_out[:, csl].rearrange("(k p) t -> p k t", p=128),
              x1_sb.ap().rearrange("p (k t) -> p k t", k=KD))
        p.dma("gpsimd", xn1_out[:, csl].rearrange("(k p) t -> p k t", p=128),
              xn1_sb.ap().rearrange("p (k t) -> p k t", k=KD))
        p.barrier()
    p.pop_region()
    if standalone:
        return p.build()



def build_stageE(p=None, dr=None):
    standalone = p is None
    if standalone:
        p = Prog("E")
    _D = _mk_dram(p, dr)
    p.push_region()
    xn_in = _D("xn1_bf", [D, T], BF16, "ExternalInput")
    wqT = _D("wqT", [D, D], F32, "ExternalInput")
    wkT = _D("wkT", [D, NKV * HD], F32, "ExternalInput")
    wvT = _D("wvT", [D, NKV * HD], F32, "ExternalInput")
    woT = _D("woT", [D, D], F32, "ExternalInput")
    attn_out = _D("attn", [D, T], F32, "ExternalOutput")

    NB = T // 128
    wq_sb = p.sb("wq_sb", [128, KD * D], BF16)
    wk_sb = p.sb("wk_sb", [128, KD * NKV * HD], BF16)
    wv_sb = p.sb("wv_sb", [128, KD * NKV * HD], BF16)
    wo_sb = p.sb("wo_sb", [128, KD * D], BF16)
    xn_sb = p.sb("xn_sbE", [128, KD * T], BF16)
    q_sb = p.sb("q_sbE", [128, KD * T], BF16)
    k_sb = p.sb("k_sbE", [128, 2 * T], BF16)
    kd_sb = p.sb("kd_sbE", [128, NKV * T], BF16)
    v_sb = p.sb("v_sbE", [128, 2 * T], BF16)
    vtok = p.sb("vtok", [128, NB * NKV * HD], BF16)
    io = p.sb("ioE", [128, 128], F32)
    pcol = p.sb("pcolE", [128, 1], F32)
    ident = p.sb("identE", [128, 128], BF16)
    tri = p.sb("triE", [128, 128], BF16)
    eighth = p.sb("eighth", [128, 1], F32)
    p_buf = p.sb("p_buf", [128, 2 * T], BF16)
    pt_buf = p.sb("pt_buf", [128, 2 * T], BF16)
    nm_b = p.sb("nm_b", [128, 2], F32)
    l_b = p.sb("l_b", [128, 2], F32)
    rl_b = p.sb("rl_b", [128, 2], F32)
    ctx_sb = p.sb("ctx_sbE", [128, KD * T], BF16)
    at_ev = p.sb("at_ev", [128, 3 * 512], F32)

    ps_qkv = [p.ps(f"ps_Eq{i}", [128, 512]) for i in range(2)]
    ps_s = p.ps("ps_Es", [128, T])
    ps_t = p.ps("ps_Et", [128, T], BF16)
    ps_ctx = None

    for k in range(KD):
        p.dma("gpsimd", wq_sb[:, k * D:(k + 1) * D], wqT[k * 128:(k + 1) * 128, :])
        p.dma("gpsimd", wo_sb[:, k * D:(k + 1) * D], woT[k * 128:(k + 1) * 128, :])
        p.dma("gpsimd", wk_sb[:, k * NKV * HD:(k + 1) * NKV * HD],
              wkT[k * 128:(k + 1) * 128, :])
        p.dma("gpsimd", wv_sb[:, k * NKV * HD:(k + 1) * NKV * HD],
              wvT[k * 128:(k + 1) * 128, :])
        p.dma("sync", xn_sb[:, k * T:(k + 1) * T], xn_in[k * 128:(k + 1) * 128, :])
    p.op("gpsimd", lambda e: e.iota(io[:], [[1, 128]], channel_multiplier=0,
                                    allow_small_or_imprecise_dtypes=True))
    p.op("gpsimd", lambda e: e.iota(pcol[:], [[0, 1]], channel_multiplier=1,
                                    allow_small_or_imprecise_dtypes=True))
    p.op("vector", lambda e: e.memset(eighth[:], 0.125))
    p.barrier()
    p.op("vector", lambda e: e.tensor_single_scalar(out=ident[:], in_=io[:],
                                                    scalar=pcol[:], op=ALU.is_equal))
    p.op("vector", lambda e: e.tensor_single_scalar(out=tri[:], in_=io[:],
                                                    scalar=pcol[:], op=ALU.is_le))
    p.barrier()

    def proj(w_sb, nout, dst_sb, scale=None):
        for o in range(nout):
            for nn in range(T // 512):
                pb = (o * (T // 512) + nn) % 2
                for k in range(KD):
                    p.op("tensor", lambda e, o=o, nn=nn, k=k, pb=pb,
                         w_sb=w_sb, nout=nout: e.matmul(
                        ps_qkv[pb][:],
                        w_sb[:, k * nout * 128 + o * 128:k * nout * 128 + (o + 1) * 128],
                        xn_sb[:, k * T + nn * 512:k * T + nn * 512 + 512],
                        start=(k == 0), stop=(k == KD - 1)))
                p.barrier()
                if scale is None:
                    p.op("vector", lambda e, o=o, nn=nn, pb=pb, dst_sb=dst_sb: e.tensor_copy(
                        dst_sb[:, o * T + nn * 512:o * T + nn * 512 + 512], ps_qkv[pb][:]))
                else:
                    p.op("vector", lambda e, o=o, nn=nn, pb=pb, dst_sb=dst_sb,
                         scale=scale: e.tensor_scalar_mul(
                        out=dst_sb[:, o * T + nn * 512:o * T + nn * 512 + 512],
                        in0=ps_qkv[pb][:], scalar1=scale))
        p.barrier()
    proj(wq_sb, KD, q_sb, scale=0.125)
    proj(wk_sb, 2, k_sb)
    proj(wv_sb, 2, v_sb)
    p.barrier()
    for g in range(NKV):
        vt = g // 2
        vo = (g % 2) * 64
        for blk in range(NB):
            p.op("tensor", lambda e, vt=vt, vo=vo, blk=blk: e.transpose(
                ps_t[:, 0:64] if False else ps_t[:, (blk % 8) * 64:(blk % 8) * 64 + 64],
                v_sb[vo:vo + 64, vt * T + blk * 128:vt * T + (blk + 1) * 128],
                ident[vo:vo + 64, vo:vo + 64]))
            if blk % 8 == 7 or blk == NB - 1:
                p.barrier()
                for b2 in range(blk - blk % 8, blk + 1):
                    p.op("vector", lambda e, g=g, b2=b2, blk=blk: e.tensor_copy(
                        vtok[:, b2 * NKV * HD + g * HD:b2 * NKV * HD + (g + 1) * HD],
                        ps_t[:, (b2 % 8) * 64:(b2 % 8) * 64 + 64]))
                p.barrier()
    p.barrier()
    for g in range(NKV):
        kt, ko = g // 2, (g % 2) * 64
        p.dma("sync", kd_sb[0:64, g * T:(g + 1) * T],
              k_sb[ko:ko + 64, kt * T:(kt + 1) * T])
        p.dma("gpsimd", kd_sb[64:128, g * T:(g + 1) * T],
              k_sb[ko:ko + 64, kt * T:(kt + 1) * T])
    p.barrier()

    IT = NH * NB
    p.op("tensor", lambda e: e.matmul(
        ps_s[:, 0:128], q_sb[0:64, 0:128], kd_sb[0:64, 0:128],
        start=True, stop=True))
    p.barrier()
    for i in range(IT + 4):
        if i < IT:
            h, qb = divmod(i, NB)
            j = i % 2
            ncol = (qb + 1) * 128
            p.op("vector", lambda e, ncol=ncol, j=j: e.reduce_max(
                nm_b[:, j:j + 1], ps_s[:, 0:ncol], axis=AX.X, negate=True))
        if 0 <= i - 1 < IT:
            j = (i - 1) % 2
            p.op("vector", lambda e, j=j: e.reciprocal(rl_b[:, j:j + 1],
                                                       l_b[:, j:j + 1]))
        if 0 <= i - 2 < IT:
            h, qb = divmod(i - 2, NB)
            j = (i - 2) % 2
            ncol = (qb + 1) * 128
            p.op("vector", lambda e, ncol=ncol, j=j: e.tensor_copy(
                pt_buf[:, j * T:j * T + ncol], ps_t[:, 0:ncol]))
        p.barrier(lazy=True)
        if i < IT:
            h, qb = divmod(i, NB)
            j = i % 2
            ncol = (qb + 1) * 128
            p.op("scalar", lambda e, ncol=ncol, j=j: e.activation(
                out=p_buf[:, j * T:j * T + ncol], in_=ps_s[:, 0:ncol],
                func=AF.Exp, bias=nm_b[:, j:j + 1]))
        if 0 <= i - 1 < IT:
            h, qb = divmod(i - 1, NB)
            j = (i - 1) % 2
            ncol = (qb + 1) * 128
            p.op("vector", lambda e, ncol=ncol, j=j: e.tensor_scalar_mul(
                out=p_buf[:, j * T:j * T + ncol], in0=p_buf[:, j * T:j * T + ncol],
                scalar1=rl_b[:, j:j + 1]))
        if 0 <= i - 2 < IT:
            h, qb = divmod(i - 2, NB)
            j = (i - 2) % 2
            g = h // NKV
            for blk in range(qb + 1):
                p.op("tensor", lambda e, j=j, g=g, blk=blk, qb=qb, i=i: e.matmul(
                    ps_qkv[(i - 2) % 2][0:64, 0:128],
                    vtok[:, blk * NKV * HD + g * HD:blk * NKV * HD + (g + 1) * HD],
                    pt_buf[:, j * T + blk * 128:j * T + (blk + 1) * 128],
                    start=(blk == 0), stop=(blk == qb)))
        p.barrier(lazy=True)
        if i < IT:
            h, qb = divmod(i, NB)
            j = i % 2
            ncol = (qb + 1) * 128
            p.op("vector", lambda e, j=j, qb=qb: e.tensor_mul(
                p_buf[:, j * T + qb * 128:j * T + (qb + 1) * 128],
                p_buf[:, j * T + qb * 128:j * T + (qb + 1) * 128], tri[:]))
            p.op("vector", lambda e, ncol=ncol, j=j: e.reduce_sum(
                l_b[:, j:j + 1], p_buf[:, j * T:j * T + ncol], axis=AX.X))
        if 0 <= i - 1 < IT:
            h, qb = divmod(i - 1, NB)
            j = (i - 1) % 2
            for blk in range(qb + 1):
                p.op("tensor", lambda e, j=j, blk=blk: e.transpose(
                    ps_t[:, blk * 128:(blk + 1) * 128],
                    p_buf[:, j * T + blk * 128:j * T + (blk + 1) * 128], ident[:]))
        if i + 1 < IT:
            h, qb = divmod(i + 1, NB)
            qt, qo = h // 2, (h % 2) * 64
            g = h // NKV
            ncol = (qb + 1) * 128
            nmm = (ncol + 511) // 512
            for mmi in range(nmm):
                w = min(512, ncol - mmi * 512)
                p.op("tensor", lambda e, qt=qt, qo=qo, g=g, qb=qb,
                     mmi=mmi, w=w: e.matmul(
                    ps_s[:, mmi * 512:mmi * 512 + w],
                    q_sb[qo:qo + 64, qt * T + qb * 128:qt * T + (qb + 1) * 128],
                    kd_sb[qo:qo + 64, g * T + mmi * 512:g * T + mmi * 512 + w],
                    start=True, stop=True))
        if 0 <= i - 2 < IT:
            h, qb = divmod(i - 2, NB)
            j = (i - 2) % 2
            qt, qo = h // 2, (h % 2) * 64
            p.op("scalar", lambda e, qt=qt, qo=qo, qb=qb, j=j: e.copy(
                ctx_sb[qo:qo + 64, qt * T + qb * 128:qt * T + (qb + 1) * 128],
                ps_qkv[j][0:64, 0:128]))
        p.barrier(lazy=True)

    NOP = KD * (T // 512)
    for it in range(NOP + 3):
        if it < NOP:
            o, nn = divmod(it, T // 512)
            pb = it % 2
            for k in range(KD):
                p.op("tensor", lambda e, o=o, nn=nn, k=k, pb=pb: e.matmul(
                    ps_qkv[pb][:], wo_sb[:, k * D + o * 128:k * D + (o + 1) * 128],
                    ctx_sb[:, k * T + nn * 512:k * T + nn * 512 + 512],
                    start=(k == 0), stop=(k == KD - 1)))
        if 0 <= it - 1 < NOP:
            pb = (it - 1) % 2
            p3 = (it - 1) % 3
            p.op("vector", lambda e, pb=pb, p3=p3: e.tensor_copy(
                at_ev[:, p3 * 512:(p3 + 1) * 512], ps_qkv[pb][:]))
        if 0 <= it - 2 < NOP:
            o, nn = divmod(it - 2, T // 512)
            p3 = (it - 2) % 3
            p.dma("sync", attn_out[o * 128:(o + 1) * 128, nn * 512:(nn + 1) * 512],
                  at_ev[:, p3 * 512:(p3 + 1) * 512])
        p.barrier(lazy=True)
    p.pop_region()
    if standalone:
        return p.build()



def build_stageF(p=None, dr=None):
    standalone = p is None
    if standalone:
        p = Prog("F")
    _D = _mk_dram(p, dr)
    p.push_region()
    xn_in = _D("xn1_bf", [D, T], BF16, "ExternalInput")
    x1_in = _D("x1", [D, T], F32, "ExternalInput")
    attn_in = _D("attn", [D, T], F32, "ExternalInput")
    Wr_in = _D("Wr", [D, NE], F32, "ExternalInput")
    br_in = _D("br", [NE, 1], F32, "ExternalInput")
    W1_in = _D("W1", [NE, D, HF], F32, "ExternalInput")
    b1_in = _D("b1d", [NE, 128, HF // 128], F32, "ExternalInput")
    W2_in = _D("W2", [NE, HF, D], F32, "ExternalInput")
    b2_in = _D("b2d", [NE, 128, D // 128], F32, "ExternalInput")
    out_T = _D("outT", [D, T], F32, "ExternalOutput")
    w_dram = p.dram("w_scratch", [NE, T], F32, "Internal")
    acc_dram = p.dram("acc_scratch", [D, T], F32, "Internal")

    NB = T // 128
    NO1 = HF // 128
    xn_sb = p.sb("xn_sbF", [128, KD * T], BF16)
    wr_sb = p.sb("wr_sbF", [128, KD * NE], BF16)
    br_sb = p.sb("br_sbF", [NE, 1], F32)
    w1_sb = p.sb("w1_sbF", [128, KD * HF], BF16)
    w2_sb = p.sb("w2_sbF", [128, NO1 * D], BF16)
    b1_sb = p.sb("b1_sbF", [128, NE * NO1], F32)
    b2_sb = p.sb("b2_sbF", [128, NE * KD], F32)
    io = p.sb("ioF", [128, 128], F32)
    pcol = p.sb("pcolF", [128, 1], F32)
    id32 = p.sb("id32F", [128, 128], F32)
    lg_sb = p.sb("lg_sbF", [128, T], F32)
    lgt = p.sb("lgtF", [128, 2 * NE], F32)
    nm1 = p.sb("nm1F", [128, 1], F32)
    e1 = p.sb("e1F", [128, NE], F32)
    t1 = p.sb("t1F", [128, NE], F32)
    m2 = p.sb("m2F", [128, 1], F32)
    pe = p.sb("peF", [128, NE], F32)
    msk = p.sb("mskF", [128, NE], F32)
    zz = p.sb("zzF", [128, 1], F32)
    rz = p.sb("rzF", [128, 1], F32)
    w_tok = p.sb("w_tokF", [128, NE], F32)
    wb_e = p.sb("wb_eF", [128, 2 * TC], F32)
    h_sb = p.sb("h_sbF", [128, NO1 * 512], BF16)


    p.push_region()
    ps_r = p.ps("ps_rF", [NE, 512])
    ps_t8 = p.ps("ps_t8F", [128, NE])
    ps_w8 = p.ps("ps_w8F", [NE, 128])

    for k in range(KD):
        p.dma("sync", xn_sb[:, k * T:(k + 1) * T], xn_in[k * 128:(k + 1) * 128, :])
    p.dma("gpsimd", wr_sb.ap().rearrange("p (k j) -> p k j", k=KD),
          Wr_in.rearrange("(k p) j -> p k j", p=128))
    p.dma("sync", br_sb[:], br_in)
    p.dma("sync", b1_sb.ap().rearrange("p (e o) -> p e o", e=NE),
          b1_in.transpose((1, 0, 2)))
    p.dma("sync", b2_sb.ap().rearrange("p (e o) -> p e o", e=NE),
          b2_in.transpose((1, 0, 2)))
    p.op("gpsimd", lambda e: e.iota(io[:], [[1, 128]], channel_multiplier=0,
                                    allow_small_or_imprecise_dtypes=True))
    p.op("gpsimd", lambda e: e.iota(pcol[:], [[0, 1]], channel_multiplier=1,
                                    allow_small_or_imprecise_dtypes=True))
    p.barrier()
    p.op("vector", lambda e: e.tensor_single_scalar(out=id32[:], in_=io[:],
                                                    scalar=pcol[:], op=ALU.is_equal))
    p.barrier()

    for c in range(NCH):
        for k in range(KD):
            p.op("tensor", lambda e, k=k, c=c: e.matmul(
                ps_r[:], wr_sb[:, k * NE:(k + 1) * NE],
                xn_sb[:, k * T + c * TC:k * T + (c + 1) * TC],
                start=(k == 0), stop=(k == KD - 1)))
        p.barrier()
        p.op("vector", lambda e, c=c: e.tensor_scalar_add(
            out=lg_sb[0:NE, c * TC:(c + 1) * TC], in0=ps_r[:], scalar1=br_sb[:]))
        p.barrier()
    for blk in range(NB):
        pb = blk % 2
        p.op("tensor", lambda e, blk=blk: e.transpose(
            ps_t8[:], lg_sb[0:NE, blk * 128:(blk + 1) * 128], id32[0:NE, 0:NE]))
        p.barrier()
        p.op("vector", lambda e, pb=pb: e.tensor_copy(lgt[:, pb * NE:(pb + 1) * NE],
                                                      ps_t8[:]))
        p.barrier()
        sl = slice(pb * NE, (pb + 1) * NE)
        p.op("vector", lambda e, sl=sl: e.reduce_max(nm1[:], lgt[:, sl], axis=AX.X,
                                                     negate=True))
        p.barrier()
        p.op("vector", lambda e, sl=sl: e.tensor_scalar(
            out=e1[:], in0=lgt[:, sl], scalar1=nm1[:], scalar2=None,
            op0=ALU.add, op1=ALU.bypass))
        p.barrier()
        p.op("vector", lambda e: e.tensor_single_scalar(out=e1[:], in_=e1[:],
                                                        scalar=0.0, op=ALU.is_equal))
        p.barrier()
        p.op("vector", lambda e, sl=sl: e.scalar_tensor_tensor(
            out=t1[:], in0=e1[:], scalar=-1e9, in1=lgt[:, sl],
            op0=ALU.mult, op1=ALU.add))
        p.barrier()
        p.op("vector", lambda e: e.reduce_max(m2[:], t1[:], axis=AX.X))
        p.op("scalar", lambda e, sl=sl: e.activation(out=pe[:], in_=lgt[:, sl],
                                                     func=AF.Exp, bias=nm1[:]))
        p.barrier()
        p.op("vector", lambda e, sl=sl: e.tensor_single_scalar(
            out=msk[:], in_=lgt[:, sl], scalar=m2[:], op=ALU.is_ge))
        p.barrier()
        p.op("vector", lambda e: e.tensor_mul(pe[:], pe[:], msk[:]))
        p.barrier()
        p.op("vector", lambda e: e.reduce_sum(zz[:], pe[:], axis=AX.X))
        p.barrier()
        p.op("vector", lambda e: e.reciprocal(rz[:], zz[:]))
        p.barrier()
        p.op("vector", lambda e: e.tensor_scalar_mul(out=w_tok[:], in0=pe[:],
                                                     scalar1=rz[:]))
        p.barrier()
        p.op("tensor", lambda e: e.transpose(ps_w8[:], w_tok[:], id32[:]))
        p.barrier()
        p.op("vector", lambda e, blk=blk: e.tensor_copy(
            lg_sb[0:NE, blk * 128:(blk + 1) * 128], ps_w8[:]))
        p.barrier()
    p.dma("sync", w_dram, lg_sb[0:NE, :])
    p.barrier()
    p.pop_region()
    ps_m = [p.ps(f"ps_Fm{i}", [128, 512]) for i in range(8)]

    t2p = lg_sb[:, 0:1536]
    for k in range(KD):
        p.dma("gpsimd", w1_sb[:, k * HF:(k + 1) * HF], W1_in[0, k * 128:(k + 1) * 128, :],
              lane=1)
    p.dma("gpsimd", w2_sb.ap().rearrange("p (k dd) -> p k dd", k=NO1),
          W2_in[0].rearrange("(k p) dd -> p k dd", p=128), lane=1)
    p.dma("gpsimd", wb_e[:, 0:TC], w_dram[0, 0:TC].partition_broadcast(128), lane=1)
    p.fence_lane1("tensor")
    p.fence_lane1("vector")
    p.barrier()
    p.eng_lag = {}
    NP1 = NO1 // 4
    NG = NP1 + 1 + KD + 2
    for ex in range(NE):
        for c in range(NCH):
            csl = slice(c * TC, (c + 1) * TC)
            par = (ex * NCH + c) % 2
            wb_cur = wb_e[:, par * TC:(par + 1) * TC]
            cn, exn = (c + 1, ex) if c + 1 < NCH else (0, ex + 1)
            for g in range(NG):
                if g == 0 and c == 0 and ex >= 1:
                    p.fence_lane1("tensor")
                if g == 0 and exn < NE:
                    p.dma("gpsimd", wb_e[:, (1 - par) * TC:(2 - par) * TC],
                          w_dram[exn, cn * TC:(cn + 1) * TC].partition_broadcast(128),
                          lane=1)
                if g == 0 and c == 0 and ex >= 1:
                    p.dma("gpsimd", w2_sb.ap().rearrange("p (k dd) -> p k dd", k=NO1),
                          W2_in[ex].rearrange("(k p) dd -> p k dd", p=128), lane=1)
                if g == NP1:
                    if c == 0:
                        p.fence_lane1("tensor")
                    p.fence_lane1("vector")
                if g == NP1 + 1 and c == NCH - 1 and ex + 1 < NE:
                    for k in range(KD):
                        p.dma("gpsimd", w1_sb[:, k * HF:(k + 1) * HF],
                              W1_in[ex + 1, k * 128:(k + 1) * 128, :], lane=1)
                if g < NP1:
                    for oo in range(4):
                        o = g * 4 + oo
                        pb = (g % 2) * 4 + oo
                        for k in range(KD):
                            p.op("tensor", lambda e, o=o, k=k, pb=pb, c=c: e.matmul(
                                ps_m[pb][:],
                                w1_sb[:, k * HF + o * 128:k * HF + (o + 1) * 128],
                                xn_sb[:, k * T + c * TC:k * T + (c + 1) * TC],
                                start=(k == 0), stop=(k == KD - 1)))
                elif NP1 + 1 <= g < NP1 + 1 + KD:
                    dd = g - NP1 - 1
                    for k in range(NO1):
                        p.op("tensor", lambda e, dd=dd, k=k: e.matmul(
                            ps_m[dd][:], w2_sb[:, k * D + dd * 128:k * D + (dd + 1) * 128],
                            h_sb[:, k * TC:(k + 1) * TC],
                            start=(k == 0), stop=(k == NO1 - 1)))
                gm = g - 1
                if 0 <= gm < NP1:
                    for oo in range(4):
                        o = gm * 4 + oo
                        pb = (gm % 2) * 4 + oo
                        if oo % 2 == 0:
                            p.op("vector", lambda e, o=o, pb=pb, ex=ex: e.tensor_scalar(
                                out=h_sb[:, o * TC:(o + 1) * TC], in0=ps_m[pb][:],
                                scalar1=b1_sb[:, ex * NO1 + o:ex * NO1 + o + 1],
                                scalar2=0.0, op0=ALU.add, op1=ALU.max))
                        else:
                            p.op("scalar", lambda e, o=o, pb=pb, ex=ex: e.activation(
                                out=h_sb[:, o * TC:(o + 1) * TC], in_=ps_m[pb][:],
                                func=AF.Relu, bias=b1_sb[:, ex * NO1 + o:ex * NO1 + o + 1]))
                elif NP1 + 1 <= gm < NP1 + 1 + KD:
                    dd = gm - NP1 - 1
                    p3 = dd % 3
                    p.op("vector", lambda e, dd=dd, p3=p3, ex=ex, wb_cur=wb_cur:
                         e.scalar_tensor_tensor(
                        out=t2p[:, p3 * 512:(p3 + 1) * 512], in0=ps_m[dd][:],
                        scalar=b2_sb[:, ex * KD + dd:ex * KD + dd + 1],
                        in1=wb_cur, op0=ALU.add, op1=ALU.mult))
                gm2 = g - 2
                if NP1 + 1 <= gm2 < NP1 + 1 + KD:
                    dd = gm2 - NP1 - 1
                    p3 = dd % 3
                    if ex == 0:
                        p.dma("gpsimd", acc_dram[dd * 128:(dd + 1) * 128, csl],
                              t2p[:, p3 * 512:(p3 + 1) * 512])
                    else:
                        def accdma(e, dd=dd, csl=csl, p3=p3):
                            return e.dma_start(
                                out=acc_dram[dd * 128:(dd + 1) * 128, csl],
                                in_=t2p[:, p3 * 512:(p3 + 1) * 512], accum_op=ALU.add)
                        cell = ["dma"]
                        p._q("gpsimd", ("call", accdma, cell))
                        p.tot_dma += 16
                p.barrier(lazy=True)
    p.eng_lag = {}
    res_sb = h_sb[:, 0:3072].bitcast(F32)
    out_sb = h_sb[:, 3072:6144].bitcast(F32)
    acc_sb = h_sb[:, 6144:9216].bitcast(F32)
    NIT = KD * NCH
    for it in range(NIT + 3):
        if it < NIT:
            dd, c = divmod(it, NCH)
            pb = it % 3
            csl = slice(c * TC, (c + 1) * TC)
            rows = slice(dd * 128, (dd + 1) * 128)
            p.dma("sync", res_sb[:, pb * TC:(pb + 1) * TC], x1_in[rows, csl])
            p.dma("sync", out_sb[:, pb * TC:(pb + 1) * TC], attn_in[rows, csl])
            p.dma("sync", acc_sb[:, pb * TC:(pb + 1) * TC], acc_dram[rows, csl])
        if 0 <= it - 2 < NIT:
            pb = (it - 2) % 3
            rsl = res_sb[:, pb * TC:(pb + 1) * TC]
            osl = out_sb[:, pb * TC:(pb + 1) * TC]
            asl = acc_sb[:, pb * TC:(pb + 1) * TC]
            p.op("vector", lambda e, rsl=rsl, osl=osl: e.tensor_add(rsl, rsl, osl))
            p.op("vector", lambda e, rsl=rsl, osl=osl, asl=asl: e.tensor_add(
                osl, rsl, asl))
        if 0 <= it - 3 < NIT:
            dd, c = divmod(it - 3, NCH)
            pb = (it - 3) % 3
            csl = slice(c * TC, (c + 1) * TC)
            rows = slice(dd * 128, (dd + 1) * 128)
            p.dma("gpsimd", out_T[rows, csl], out_sb[:, pb * TC:(pb + 1) * TC])
        p.barrier()
    p.pop_region()
    if standalone:
        return p.build()

def _t(a):
    return np.ascontiguousarray(np.asarray(a).T)


def build_fused():
    p = Prog("fused")
    EI = lambda n, s, d: p.dram(n, list(s), d, "ExternalInput")
    IN = lambda n, s, d: p.dram(n, list(s), d, "Internal")
    xT = EI("xT", [D, T], F32)
    w_inT = EI("w_inT", [D, 2 * DIN], F32)
    conv_w = EI("conv_w", [DIN, 4], F32)
    conv_b = EI("conv_b", [128, NDI], F32)
    xp_wT = EI("xp_wT", [DIN, 96], F32)
    dt_wT = EI("dt_wT", [DTR, DIN], F32)
    dt_b = EI("dt_b", [128, NDI], F32)
    A_dev = EI("A_dev", [128, NDI * NST], F32)
    D_dev = EI("D_dev", [128, NDI], F32)
    outw_T = EI("outw_T", [DIN, D], F32)
    wrT = EI("wrT", [D, D], F32)
    wkT_d = EI("wkT_d", [D, D], F32)
    wvT_d = EI("wvT_d", [D, D], F32)
    mu_r = EI("mu_r", [128, KD], F32)
    mu_k = EI("mu_k", [128, KD], F32)
    wqT = EI("wqT", [D, D], F32)
    wkT_e = EI("wkT_e", [D, NKV * HD], F32)
    wvT_e = EI("wvT_e", [D, NKV * HD], F32)
    woT = EI("woT", [D, D], F32)
    Wr = EI("Wr", [D, NE], F32)
    br = EI("br", [NE, 1], F32)
    W1 = EI("W1", [NE, D, HF], F32)
    b1d = EI("b1d", [NE, 128, HF // 128], F32)
    W2 = EI("W2", [NE, HF, D], F32)
    b2d = EI("b2d", [NE, 128, D // 128], F32)
    outT = p.dram("outT", [D, T], F32, "ExternalOutput")

    u_bf = IN("u_bf", [DIN, T], BF16)
    sz_bf = IN("sz_bf", [DIN, T], BF16)
    xn_bf = IN("xn_bf", [D, T], BF16)
    delta = IN("delta", [DIN, T], F32)
    du = IN("du", [DIN, T], F16)
    bc = IN("bc", [2 * NST, T], F32)
    mamba = IN("mamba", [D, T], F32)
    x1 = IN("x1", [D, T], F32)
    xn1_bf = IN("xn1_bf", [D, T], BF16)
    attn = IN("attn", [D, T], F32)

    build_stageA(p, dict(xT=xT, w_inT=w_inT, conv_w=conv_w, conv_b=conv_b,
                         u_bf=u_bf, sz_bf=sz_bf, xn_bf=xn_bf))
    build_stageB(p, dict(u_bf=u_bf, xp_wT=xp_wT, dt_wT=dt_wT, dt_b=dt_b,
                         delta=delta, du=du, bc=bc))
    build_stageC(p, dict(delta=delta, du=du, u_bf=u_bf, sz_bf=sz_bf, bc=bc,
                         A_dev=A_dev, D_dev=D_dev, outw_T=outw_T, mamba=mamba))
    build_stageD(p, dict(xT=xT, xn_bf=xn_bf, mamba=mamba, wrT=wrT, wkT=wkT_d,
                         wvT=wvT_d, mu_r=mu_r, mu_k=mu_k, x1=x1, xn1_bf=xn1_bf))
    build_stageE(p, dict(xn1_bf=xn1_bf, wqT=wqT, wkT=wkT_e, wvT=wvT_e, woT=woT,
                         attn=attn))
    build_stageF(p, dict(xn1_bf=xn1_bf, x1=x1, attn=attn, Wr=Wr, br=br, W1=W1,
                         b1d=b1d, W2=W2, b2d=b2d, outT=outT))
    return p.build()


def kernel(**ins):
    cores = list(range(B))
    shared = {
        "w_inT": _t(ins["L0_attn_in_proj_w"]),
        "conv_w": np.ascontiguousarray(np.asarray(ins["L0_attn_conv_w"])[:, 0, :]),
        "conv_b": np.ascontiguousarray(np.asarray(ins["L0_attn_conv_b"]).reshape(NDI, 128).T),
        "xp_wT": _t(ins["L0_attn_x_proj_w"]),
        "dt_wT": _t(ins["L0_attn_dt_proj_w"]),
        "dt_b": np.ascontiguousarray(np.asarray(ins["L0_attn_dt_proj_b"]).reshape(NDI, 128).T),
        "A_dev": np.ascontiguousarray(np.asarray(ins["L0_attn_A_log"])
                                      .reshape(NDI, 128, NST).transpose(1, 0, 2)
                                      .reshape(128, NDI * NST)),
        "D_dev": np.ascontiguousarray(np.asarray(ins["L0_attn_D"]).reshape(NDI, 128).T),
        "outw_T": _t(ins["L0_attn_out_proj_w"]),
        "wrT": _t(ins["L0_ffn_W_r"]),
        "wkT_d": _t(ins["L0_ffn_W_k"]),
        "wvT_d": _t(ins["L0_ffn_W_v"]),
        "mu_r": np.ascontiguousarray(np.asarray(ins["L0_ffn_mu_r"]).reshape(KD, 128).T),
        "mu_k": np.ascontiguousarray(np.asarray(ins["L0_ffn_mu_k"]).reshape(KD, 128).T),
        "wqT": _t(ins["L1_attn_wq"]),
        "wkT_e": _t(ins["L1_attn_wk"]),
        "wvT_e": _t(ins["L1_attn_wv"]),
        "woT": _t(ins["L1_attn_wo"]),
        "Wr": np.ascontiguousarray(np.asarray(ins["L1_ffn_Wr"], np.float32)),
        "br": np.ascontiguousarray(np.asarray(ins["L1_ffn_br"], np.float32)[:, None]),
        "W1": np.ascontiguousarray(np.asarray(ins["L1_ffn_W1"], np.float32)),
        "b1d": np.ascontiguousarray(np.asarray(ins["L1_ffn_b1"], np.float32)
                                    .reshape(NE, HF // 128, 128).transpose(0, 2, 1)),
        "W2": np.ascontiguousarray(np.asarray(ins["L1_ffn_W2"], np.float32)),
        "b2d": np.ascontiguousarray(np.asarray(ins["L1_ffn_b2"], np.float32)
                                    .reshape(NE, D // 128, 128).transpose(0, 2, 1)),
    }
    maps = [dict(shared, xT=np.ascontiguousarray(
        np.asarray(ins["L0_attn_hidden_states"][b], np.float32).T)) for b in range(B)]
    nc = build_fused()
    res = run_bass_kernel_spmd(nc, maps, core_ids=cores)
    out = np.stack([np.asarray(res.results[b]["outT"]).T for b in range(B)], 0)
    return np.ascontiguousarray(out.astype(np.float32))


STAGE_BUILDERS = {"A": build_stageA, "B": build_stageB, "C": build_stageC, "D": build_stageD, "E": build_stageE, "F": build_stageF}


# version 11 (final)
# speedup vs baseline: 1.2674x; 1.0069x over previous
import contextlib
import numpy as np
import concourse.bass as bass
import concourse.mybir as mybir
from concourse.bass_utils import run_bass_kernel_spmd

F32 = mybir.dt.float32
BF16 = mybir.dt.bfloat16
F16 = mybir.dt.float16
AF = mybir.ActivationFunctionType
ALU = mybir.AluOpType
AX = mybir.AxisListType

ENGS = ("sync", "gpsimd", "vector", "scalar", "tensor")

B, T, D = 8, 2048, 1024
DIN = 2048
NST = 16
DTR = 64
NH, NKV, HD = 16, 4, 64
NE, HF = 8, 4096
TC = 512
NCH = T // TC
KD = D // 128
NDI = DIN // 128


class Prog:

    def __init__(self, name):
        self.nc = bass.Bass("TRN2", target_bir_lowering=False)
        self.name = name
        self.queues = {e: [] for e in ENGS}
        self.tot = {e: 0 for e in ENGS}
        self.tot_dma = 0
        self.pending = {e: None for e in ENGS}
        self.cur_snapshot = ({e: 0 for e in ENGS}, 0, {}, [])
        self.waited = {e: None for e in ENGS}
        self.stack = contextlib.ExitStack()
        self._stacks = [self.stack]
        self._names = {}
        self._dma_hist = [0]
        self.tot_dma1 = 0
        self._tot_hist = [dict(self.tot)]
        self.eng_lag = {}

    def _uniq(self, name):
        n = self._names.get(name, 0)
        self._names[name] = n + 1
        return name if n == 0 else f"{name}_{n}"

    def push_region(self):
        self._stacks.append(contextlib.ExitStack())

    def pop_region(self):
        self.barrier()
        self._stacks.pop().close()

    def dram(self, name, shape, dtype, kind):
        return self.nc.dram_tensor(name, list(shape), dtype, kind=kind).ap()

    def sb(self, name, shape, dtype):
        return self._stacks[-1].enter_context(
            self.nc.sbuf_tensor(self._uniq(name), list(shape), dtype))

    def ps(self, name, shape, dtype=F32):
        return self._stacks[-1].enter_context(
            self.nc.psum_tensor(self._uniq(name), list(shape), dtype))

    def _q(self, eng, item):
        snap = self.cur_snapshot
        if self.waited[eng] is not snap:
            self.queues[eng].append(("wait", snap, eng))
            self.waited[eng] = snap
        self.queues[eng].append(item)

    def op(self, eng, fn):
        cell = [0]
        self._q(eng, ("call", fn, cell))
        self.pending[eng] = cell

    def dma(self, eng, out, in_, lane=0):
        if lane == 0:
            cell = ["dma"]
            self._q(eng, ("call", lambda e: e.dma_start(out=out, in_=in_), cell))
            self.tot_dma += 16
        else:
            cell = ["dma1"]
            self._q(eng, ("call", lambda e: e.dma_start(out=out, in_=in_), cell))
            self.tot_dma1 += 16

    def fence_lane1(self, eng):
        v = self.tot_dma1
        if v > 0:
            self._q(eng, ("wait1", v, eng))

    def barrier(self, lazy=False):
        for e in ENGS:
            c = self.pending[e]
            if c is not None:
                c[0] = 1
                self.tot[e] += 1
                self.pending[e] = None
        dval = self._dma_hist[-1] if lazy else self.tot_dma
        self._dma_hist.append(self.tot_dma)
        self._tot_hist.append(dict(self.tot))
        self.cur_snapshot = (dict(self.tot), dval, dict(self.eng_lag),
                             [dict(h) for h in self._tot_hist[-4:]])

    def build(self):
        self.barrier()
        self.queues["sync"].append(("wait", self.cur_snapshot, "__final__"))
        if self.tot_dma1 > 0:
            self.queues["sync"].append(("wait1", self.tot_dma1, "__final__"))
        nc = self.nc
        bars = {e: self.stack.enter_context(nc.semaphore(f"bar_{e}")) for e in ENGS}
        bar_dma = self.stack.enter_context(nc.semaphore("bar_dma"))
        bar_dma1 = self.stack.enter_context(nc.semaphore("bar_dma1"))
        with nc.Block() as block:
            def emit_stream(eng_name):
                def _(e):
                    for item in self.queues[eng_name]:
                        if item[0] == "wait1":
                            e.wait_ge(bar_dma1, item[1])
                            continue
                        if item[0] == "wait":
                            snap, me = item[1], item[2]
                            counts, dmas, lags, hist = snap
                            mylag = lags.get(me, {})
                            for oe in ENGS:
                                v = counts[oe]
                                lag = mylag.get(oe, 0)
                                if lag > 0 and len(hist) > lag:
                                    v = hist[-1 - lag][oe]
                                if v > 0:
                                    e.wait_ge(bars[oe], v)
                            if dmas > 0:
                                e.wait_ge(bar_dma, dmas)
                        else:
                            _, fn, cell = item
                            inst = fn(e)
                            if cell[0] == "dma":
                                inst.then_inc(bar_dma, 16)
                            elif cell[0] == "dma1":
                                inst.then_inc(bar_dma1, 16)
                            elif cell[0] == 1:
                                inst.then_inc(bars[eng_name], 1)
                return _
            block.sync(emit_stream("sync"))
            block.gpsimd(emit_stream("gpsimd"))
            block.vector(emit_stream("vector"))
            block.scalar(emit_stream("scalar"))
            block.tensor(emit_stream("tensor"))
        self.stack.close()
        return nc


def _mk_dram(p, dr):
    def _D(name, shape, dtype, kind):
        if dr is not None:
            return dr[name]
        return p.dram(name, shape, dtype, kind)
    return _D


def build_stageA(p=None, dr=None):
    standalone = p is None
    if standalone:
        p = Prog("A")
    _D = _mk_dram(p, dr)
    p.push_region()
    xT = _D("xT", [D, T], F32, "ExternalInput")
    w_inT = _D("w_inT", [D, 2 * DIN], F32, "ExternalInput")
    conv_w = _D("conv_w", [DIN, 4], F32, "ExternalInput")
    conv_b = _D("conv_b", [128, NDI], F32, "ExternalInput")
    u_out = _D("u_bf", [DIN, T], BF16, "ExternalOutput")
    sz_out = _D("sz_bf", [DIN, T], BF16, "ExternalOutput")
    xn_out = _D("xn_bf", [D, T], BF16, "ExternalOutput")

    w_sb = p.sb("w_sb", [128, KD * 2 * DIN], BF16)
    cw_sb = p.sb("cw_sb", [128, NDI * 4], F32)
    cb_sb = p.sb("cb_sb", [128, NDI], F32)
    ones_col = p.sb("ones_col", [128, 1], BF16)
    ones_row = p.sb("ones_row", [1, 128], BF16)
    eps_sb = p.sb("eps_sb", [1, 1], F32)

    x_sb = p.sb("x_sb", [128, KD * TC], F32)
    xb_sb = p.sb("xb_sb", [128, KD * TC], BF16)
    xsq_sb = p.sb("xsq_sb", [128, KD * TC], BF16)
    xn_sb = p.sb("xn_sb", [128, KD * TC], BF16)
    tmp_sb = p.sb("tmp_sb", [128, TC], BF16)
    mrow = p.sb("mrow", [1, TC], F32)
    vrow = p.sb("vrow", [1, TC], F32)
    msq = p.sb("msq", [1, TC], F32)
    srow = p.sb("srow", [1, TC], F32)
    rrow = p.sb("rrow", [1, TC], F32)
    m_bf = p.sb("m_bf", [1, TC], BF16)
    r_bf = p.sb("r_bf", [1, TC], BF16)
    mb_sb = p.sb("mb_sb", [128, TC], BF16)
    rb_sb = p.sb("rb_sb", [128, TC], BF16)
    xc_sb = p.sb("xc_sb", [128, NDI * (TC + 3)], F32)
    cacc = p.sb("cacc", [128, 4 * TC], F32)
    u_sb = p.sb("u_sb", [128, NDI * TC], BF16)
    sz_sb = p.sb("sz_sb", [128, NDI * TC], BF16)

    ps_sx = p.ps("ps_sx", [1, TC])
    ps_sxx = p.ps("ps_sxx", [1, TC])
    ps_mb = p.ps("ps_mb", [128, TC])
    ps_rb = p.ps("ps_rb", [128, TC])
    ps_xz = [p.ps(f"ps_xz{i}", [128, TC]) for i in range(4)]

    for k in range(KD):
        p.dma("gpsimd", w_sb[:, k * 2 * DIN:(k + 1) * 2 * DIN],
              w_inT[k * 128:(k + 1) * 128, :])
    p.dma("sync", cw_sb.ap().rearrange("p (o c) -> p o c", c=4),
          conv_w.rearrange("(o p) c -> p o c", p=128))
    p.dma("sync", cb_sb[:], conv_b)
    p.op("vector", lambda e: e.memset(eps_sb[:], 1e-5))
    p.op("vector", lambda e: e.memset(ones_col[:], 1.0))
    p.op("vector", lambda e: e.memset(ones_row[:], 1.0))
    for o in range(NDI):
        p.op("vector", lambda e, o=o: e.memset(
            xc_sb[:, o * (TC + 3):o * (TC + 3) + 3], 0.0))
    p.barrier()

    for c in range(NCH):
        csl = slice(c * TC, (c + 1) * TC)
        src = xT[:, csl].rearrange("(k p) t -> p k t", p=128)
        p.dma("sync", x_sb.ap().rearrange("p (k t) -> p k t", k=KD), src)
        p.dma("gpsimd", xb_sb.ap().rearrange("p (k t) -> p k t", k=KD), src)
        p.barrier()
        p.op("scalar", lambda e: e.activation(out=xsq_sb[:], in_=x_sb[:],
                                              func=AF.Square))
        p.barrier()
        for k in range(KD):
            p.op("tensor", lambda e, k=k: e.matmul(
                ps_sx[:], ones_col[:], xb_sb[:, k * TC:(k + 1) * TC],
                start=(k == 0), stop=(k == KD - 1)))
        for k in range(KD):
            p.op("tensor", lambda e, k=k: e.matmul(
                ps_sxx[:], ones_col[:], xsq_sb[:, k * TC:(k + 1) * TC],
                start=(k == 0), stop=(k == KD - 1)))
        p.barrier()
        p.op("vector", lambda e: e.tensor_scalar_mul(out=mrow[:], in0=ps_sx[:],
                                                     scalar1=1.0 / D))
        p.op("vector", lambda e: e.tensor_scalar_mul(out=vrow[:], in0=ps_sxx[:],
                                                     scalar1=1.0 / D))
        p.op("vector", lambda e: e.tensor_mul(msq[:], mrow[:], mrow[:]))
        p.op("vector", lambda e: e.tensor_sub(vrow[:], vrow[:], msq[:]))
        p.barrier()
        p.op("scalar", lambda e: e.activation(out=srow[:], in_=vrow[:],
                                              func=AF.Sqrt, bias=eps_sb[:]))
        p.barrier()
        p.op("vector", lambda e: e.reciprocal(rrow[:], srow[:]))
        p.barrier()
        p.op("vector", lambda e: e.tensor_copy(m_bf[:], mrow[:]))
        p.op("vector", lambda e: e.tensor_copy(r_bf[:], rrow[:]))
        p.barrier()
        p.op("tensor", lambda e: e.matmul(ps_mb[:], ones_row[:], m_bf[:],
                                          start=True, stop=True))
        p.op("tensor", lambda e: e.matmul(ps_rb[:], ones_row[:], r_bf[:],
                                          start=True, stop=True))
        p.barrier()
        p.op("vector", lambda e: e.tensor_copy(mb_sb[:], ps_mb[:]))
        p.op("vector", lambda e: e.tensor_copy(rb_sb[:], ps_rb[:]))
        p.barrier()
        for k in range(KD):
            p.op("vector", lambda e, k=k: e.tensor_sub(
                tmp_sb[:], xb_sb[:, k * TC:(k + 1) * TC], mb_sb[:]))
            p.op("vector", lambda e, k=k: e.tensor_mul(
                xn_sb[:, k * TC:(k + 1) * TC], tmp_sb[:], rb_sb[:]))
        p.barrier()
        p.dma("gpsimd", xn_out[:, csl].rearrange("(k p) t -> p k t", p=128),
              xn_sb.ap().rearrange("p (k t) -> p k t", k=KD))
        p.eng_lag = {"tensor": {"vector": 2, "scalar": 2}}
        for g in range(35):
            if g < 32:
                o, pbx = g, g % 4
                for k in range(KD):
                    p.op("tensor", lambda e, o=o, pbx=pbx, k=k: e.matmul(
                        ps_xz[pbx][:],
                        w_sb[:, k * 2 * DIN + o * 128:k * 2 * DIN + (o + 1) * 128],
                        xn_sb[:, k * TC:(k + 1) * TC],
                        start=(k == 0), stop=(k == KD - 1)))
            gm = g - 1
            if 0 <= gm < 32:
                o, pbx = gm, gm % 4
                if o < 16:
                    base = o * (TC + 3)
                    p.op("vector", lambda e, pbx=pbx, base=base: e.tensor_copy(
                        xc_sb[:, base + 3:base + 3 + TC], ps_xz[pbx][:]))
                else:
                    zi = o - 16
                    p.op("scalar", lambda e, zi=zi, pbx=pbx: e.activation(
                        out=sz_sb[:, zi * TC:(zi + 1) * TC],
                        in_=ps_xz[pbx][:], func=AF.Silu))
            gm2 = g - 2
            if 0 <= gm2 < 16:
                o = gm2
                base = o * (TC + 3)
                w4 = cw_sb[:, o * 4:o * 4 + 4]
                cs = cacc[:, (gm2 % 4) * TC:(gm2 % 4 + 1) * TC]
                p.op("vector", lambda e, base=base, w4=w4, cs=cs: e.tensor_scalar_mul(
                    out=cs, in0=xc_sb[:, base + 3:base + 3 + TC],
                    scalar1=w4[:, 3:4]))
                for j in (2, 1, 0):
                    p.op("vector", lambda e, base=base, w4=w4, cs=cs, j=j:
                         e.scalar_tensor_tensor(
                             out=cs, in0=xc_sb[:, base + j:base + j + TC],
                             scalar=w4[:, j:j + 1], in1=cs,
                             op0=ALU.mult, op1=ALU.add))
                p.op("vector", lambda e, base=base: e.tensor_copy(
                    xc_sb[:, base:base + 3],
                    xc_sb[:, base + TC:base + TC + 3]))
            gm3 = g - 3
            if 0 <= gm3 < 16:
                o = gm3
                p.op("scalar", lambda e, o=o, gm3=gm3: e.activation(
                    out=u_sb[:, o * TC:(o + 1) * TC],
                    in_=cacc[:, (gm3 % 4) * TC:(gm3 % 4 + 1) * TC],
                    func=AF.Silu, bias=cb_sb[:, o:o + 1]))
            p.barrier(lazy=True)
        p.eng_lag = {}
        p.dma("gpsimd", u_out[:, csl].rearrange("(k p) t -> p k t", p=128),
              u_sb.ap().rearrange("p (k t) -> p k t", k=NDI))
        p.dma("gpsimd", sz_out[:, csl].rearrange("(k p) t -> p k t", p=128),
              sz_sb.ap().rearrange("p (k t) -> p k t", k=NDI))
        p.barrier()
    p.pop_region()
    if standalone:
        return p.build()


def build_stageB(p=None, dr=None):
    standalone = p is None
    if standalone:
        p = Prog("B")
    _D = _mk_dram(p, dr)
    p.push_region()
    u_in = _D("u_bf", [DIN, T], BF16, "ExternalInput")
    xp_wT = _D("xp_wT", [DIN, 96], F32, "ExternalInput")
    dt_wT = _D("dt_wT", [DTR, DIN], F32, "ExternalInput")
    dt_b = _D("dt_b", [128, NDI], F32, "ExternalInput")
    delta_out = _D("delta", [DIN, T], F32, "ExternalOutput")
    du_out = _D("du", [DIN, T], F16, "ExternalOutput")
    bc_out = _D("bc", [2 * NST, T], F32, "ExternalOutput")

    u_sb = p.sb("u_sb", [128, NDI * T], BF16)
    xpw_sb = p.sb("xpw_sb", [128, NDI * 96], BF16)
    dtw_sb = p.sb("dtw_sb", [DTR, DIN], BF16)
    dtb_sb = p.sb("dtb_sb", [128, NDI], F32)
    dt_sb = p.sb("dt_sb", [DTR, T], BF16)
    bc_sb = p.sb("bc_sb", [96, T], F32)
    delta_sb = p.sb("delta_sb", [128, 4 * TC], F32)
    du_sb = p.sb("du_sb", [128, 4 * TC], F16)

    ps_xd = p.ps("ps_xd", [96, TC])
    ps_dt = [p.ps(f"ps_dt{i}", [128, TC]) for i in range(4)]

    for k in range(NDI):
        p.dma("gpsimd", u_sb[:, k * T:(k + 1) * T], u_in[k * 128:(k + 1) * 128, :])
    p.dma("gpsimd", xpw_sb.ap().rearrange("p (k j) -> p k j", k=NDI),
          xp_wT.rearrange("(k p) j -> p k j", p=128))
    p.dma("gpsimd", dtw_sb[:], dt_wT)
    p.dma("sync", dtb_sb[:], dt_b)
    p.barrier()

    for c in range(NCH):
        csl = slice(c * TC, (c + 1) * TC)
        for k in range(NDI):
            p.op("tensor", lambda e, k=k, c=c: e.matmul(
                ps_xd[:], xpw_sb[:, k * 96:(k + 1) * 96],
                u_sb[:, k * T + c * TC:k * T + (c + 1) * TC],
                start=(k == 0), stop=(k == NDI - 1)))
        p.barrier()
        p.op("vector", lambda e, csl=csl: e.tensor_copy(dt_sb[:, csl], ps_xd[0:64, :]))
        p.op("vector", lambda e, csl=csl: e.tensor_copy(bc_sb[64:96, csl], ps_xd[64:96, :]))
        p.barrier()
        for mm in range(NDI + 3):
            if mm < NDI:
                p.op("tensor", lambda e, mm=mm, csl=csl: e.matmul(
                    ps_dt[mm % 4][:], dtw_sb[:, mm * 128:(mm + 1) * 128],
                    dt_sb[:, csl], start=True, stop=True))
            if 0 <= mm - 1 < NDI:
                m1 = mm - 1
                sl = slice((m1 % 4) * TC, (m1 % 4 + 1) * TC)
                p.op("scalar", lambda e, m1=m1, sl=sl: e.activation(
                    out=delta_sb[:, sl], in_=ps_dt[m1 % 4][:],
                    func=AF.Exp, bias=dtb_sb[:, m1:m1 + 1]))
                p.op("scalar", lambda e, sl=sl: e.activation(
                    out=delta_sb[:, sl], in_=delta_sb[:, sl], func=AF.Ln, bias=1.0))
            if 0 <= mm - 2 < NDI:
                m2 = mm - 2
                sl = slice((m2 % 4) * TC, (m2 % 4 + 1) * TC)
                p.op("vector", lambda e, m2=m2, sl=sl, c=c: e.tensor_mul(
                    du_sb[:, sl], delta_sb[:, sl],
                    u_sb[:, m2 * T + c * TC:m2 * T + (c + 1) * TC]))
            if 0 <= mm - 3 < NDI:
                m3 = mm - 3
                sl = slice((m3 % 4) * TC, (m3 % 4 + 1) * TC)
                p.dma("sync", delta_out[m3 * 128:(m3 + 1) * 128, csl],
                      delta_sb[:, sl])
                p.dma("gpsimd", du_out[m3 * 128:(m3 + 1) * 128, csl],
                      du_sb[:, sl])
            p.barrier()
    p.dma("sync", bc_out, bc_sb[64:96, :])
    p.barrier()
    p.pop_region()
    if standalone:
        return p.build()




def build_stageC(p=None, dr=None):
    CT = 1024
    NCC = T // CT
    standalone = p is None
    if standalone:
        p = Prog("C")
    _D = _mk_dram(p, dr)
    p.push_region()
    delta_in = _D("delta", [DIN, T], F32, "ExternalInput")
    du_in = _D("du", [DIN, T], F16, "ExternalInput")
    u_in = _D("u_bf", [DIN, T], BF16, "ExternalInput")
    sz_in = _D("sz_bf", [DIN, T], BF16, "ExternalInput")
    bc_in = _D("bc", [2 * NST, T], F32, "ExternalInput")
    A_in = _D("A_dev", [128, NDI * NST], F32, "ExternalInput")
    D_in = _D("D_dev", [128, NDI], F32, "ExternalInput")
    outw_in = _D("outw_T", [DIN, D], F32, "ExternalInput")
    mamba_out = _D("mamba", [D, T], F32, "ExternalOutput")

    Bb = p.sb("Bb", [128, NST * CT], F16)
    Cb = p.sb("Cb", [128, NST * CT], F16)
    delta_c = p.sb("delta_c", [128, 2 * CT], F32)
    du_c = p.sb("du_c", [128, 2 * CT], F16)
    u_c = p.sb("u_c", [128, 2 * CT], BF16)
    sz_c = p.sb("sz_c", [128, 2 * CT], BF16)
    dA = p.sb("dA", [128, 3 * CT], F16)
    dBu = p.sb("dBu", [128, 3 * CT], F16)
    h_b = p.sb("h_b", [128, 3 * CT], F16)
    ym = p.sb("ym", [128, 3 * CT], F16)
    y_b = p.sb("y_b", [128, 2 * CT], F16)
    a_gate = p.sb("a_gate", [128, 2 * CT], F16)
    b_gate = p.sb("b_gate", [128, 2 * CT], F16)
    yg = p.sb("yg", [128, NDI * CT], BF16)
    w_out = p.sb("w_out", [128, NDI * D], BF16)
    carry = p.sb("carry", [128, NDI * NST], F32)
    A_sb = p.sb("A_sb", [128, NDI * NST], F32)
    D_sb = p.sb("D_sb", [128, NDI], F32)
    mb_ev = p.sb("mb_ev", [128, 3 * CT], F32)
    ps_op = [p.ps(f"ps_op{i}", [128, CT]) for i in range(4)]

    for k in range(NDI):
        p.dma("gpsimd", w_out[:, k * D:(k + 1) * D],
              outw_in[k * 128:(k + 1) * 128, :])
    p.dma("sync", A_sb[:], A_in)
    p.dma("sync", D_sb[:], D_in)
    p.barrier()
    p.op("scalar", lambda e: e.activation(out=A_sb[:], in_=A_sb[:], func=AF.Exp))
    p.barrier()
    p.op("vector", lambda e: e.tensor_scalar_mul(out=A_sb[:], in0=A_sb[:],
                                                 scalar1=-1.0))
    p.barrier()

    def tile_loads(m, c):
        csl = slice(c * CT, (c + 1) * CT)
        buf = slice((m % 2) * CT, (m % 2 + 1) * CT)
        rows = slice(m * 128, (m + 1) * 128)
        p.dma("sync", delta_c[:, buf], delta_in[rows, csl])
        p.dma("sync", du_c[:, buf], du_in[rows, csl])
        p.dma("sync", u_c[:, buf], u_in[rows, csl])
        p.dma("sync", sz_c[:, buf], sz_in[rows, csl])

    NPH = NDI * NST + 8
    for c in range(NCC):
        csl = slice(c * CT, (c + 1) * CT)
        p.dma("gpsimd", Bb.ap().rearrange("p (n t) -> p n t", n=NST),
              bc_in[0:NST, csl].unsqueeze(0).to_broadcast((128, NST, CT)))
        p.dma("gpsimd", Cb.ap().rearrange("p (n t) -> p n t", n=NST),
              bc_in[NST:2 * NST, csl].unsqueeze(0).to_broadcast((128, NST, CT)))
        tile_loads(0, c)
        p.op("vector", lambda e: e.memset(y_b[:, 0:CT], 0.0))
        p.barrier()
        for t in range(NPH):
            if t < NDI * NST:
                i = t
                m, n = i // NST, i % NST
                j = t % 3
                p.op("scalar", lambda e, j=j, m=m, i=i: e.activation(
                    out=dA[:, j * CT:(j + 1) * CT],
                    in_=delta_c[:, (m % 2) * CT:(m % 2 + 1) * CT],
                    func=AF.Exp, scale=A_sb[:, i:i + 1]))
                p.op("vector", lambda e, j=j, m=m, n=n: e.tensor_tensor(
                    out=dBu[:, j * CT:(j + 1) * CT],
                    in0=du_c[:, (m % 2) * CT:(m % 2 + 1) * CT],
                    in1=Bb[:, n * CT:(n + 1) * CT], op=ALU.mult))
            if 0 <= t - 1 < NDI * NST:
                i = t - 1
                j = i % 3
                init = 0.0 if c == 0 else carry[:, i:i + 1]
                p.op("vector", lambda e, j=j, init=init: e.tensor_tensor_scan(
                    out=h_b[:, j * CT:(j + 1) * CT],
                    data0=dA[:, j * CT:(j + 1) * CT],
                    data1=dBu[:, j * CT:(j + 1) * CT],
                    initial=init, op0=ALU.mult, op1=ALU.add))
            if c == 0 and 0 <= t - 2 < NDI * NST:
                i2 = t - 2
                j2 = i2 % 3
                p.op("vector", lambda e, j2=j2, i2=i2: e.tensor_copy(
                    carry[:, i2:i2 + 1], h_b[:, (j2 + 1) * CT - 1:(j2 + 1) * CT]))
            if 0 <= t - 2 < NDI * NST:
                i = t - 2
                j = i % 3
                n = i % NST
                p.op("vector", lambda e, j=j, n=n: e.tensor_tensor(
                    out=ym[:, j * CT:(j + 1) * CT],
                    in0=h_b[:, j * CT:(j + 1) * CT],
                    in1=Cb[:, n * CT:(n + 1) * CT], op=ALU.mult))
            if 0 <= t - 3 < NDI * NST:
                i = t - 3
                j = i % 3
                m = i // NST
                ysl = slice((m % 2) * CT, (m % 2 + 1) * CT)
                p.op("vector", lambda e, j=j, ysl=ysl: e.tensor_tensor(
                    out=y_b[:, ysl], in0=y_b[:, ysl],
                    in1=ym[:, j * CT:(j + 1) * CT], op=ALU.add))
            if t % NST == 0 and 0 < t < NDI * NST:
                mnew = t // NST
                p.op("vector", lambda e, mnew=mnew: e.memset(
                    y_b[:, (mnew % 2) * CT:(mnew % 2 + 1) * CT], 0.0))
            if t % NST == 8 and t // NST < NDI - 1:
                tile_loads(t // NST + 1, c)
            if t >= 19:
                mg = (t - 3) // NST - 1
                off = (t - 3) % NST
                if 0 <= mg < NDI:
                    gb = slice((mg % 2) * CT, (mg % 2 + 1) * CT)
                    if off == 0:
                        p.op("vector", lambda e, gb=gb, mg=mg: e.tensor_scalar_mul(
                            out=a_gate[:, gb], in0=u_c[:, gb],
                            scalar1=D_sb[:, mg:mg + 1]))
                    elif off == 1:
                        p.op("vector", lambda e, gb=gb: e.tensor_tensor(
                            out=b_gate[:, gb], in0=y_b[:, gb],
                            in1=a_gate[:, gb], op=ALU.add))
                    elif off == 2:
                        p.op("vector", lambda e, gb=gb, mg=mg: e.tensor_tensor(
                            out=yg[:, mg * CT:(mg + 1) * CT], in0=b_gate[:, gb],
                            in1=sz_c[:, gb], op=ALU.mult))
            p.barrier(lazy=True)
        p.eng_lag = {"tensor": {"vector": 2, "scalar": 2}}
        for o in range(KD + 2):
            if o < KD:
                pb = o % 4
                for nn in range(CT // 512):
                    for k in range(NDI):
                        p.op("tensor", lambda e, o=o, pb=pb, nn=nn, k=k: e.matmul(
                            ps_op[pb][:, nn * 512:(nn + 1) * 512],
                            w_out[:, k * D + o * 128:k * D + (o + 1) * 128],
                            yg[:, k * CT + nn * 512:k * CT + nn * 512 + 512],
                            start=(k == 0), stop=(k == NDI - 1)))
            if 0 <= o - 1 < KD:
                pe3 = (o - 1) % 3
                pb = (o - 1) % 4
                p.op("scalar", lambda e, pe3=pe3, pb=pb: e.copy(
                    mb_ev[:, pe3 * CT:(pe3 + 1) * CT], ps_op[pb][:]))
            if 0 <= o - 2 < KD:
                pe3 = (o - 2) % 3
                p.dma("sync", mamba_out[(o - 2) * 128:(o - 1) * 128, csl],
                      mb_ev[:, pe3 * CT:(pe3 + 1) * CT])
            p.barrier(lazy=True)
        p.eng_lag = {}
    p.pop_region()
    if standalone:
        return p.build()



def build_stageD(p=None, dr=None):
    standalone = p is None
    if standalone:
        p = Prog("D")
    _D = _mk_dram(p, dr)
    p.push_region()
    xT = _D("xT", [D, T], F32, "ExternalInput")
    xn_in = _D("xn_bf", [D, T], BF16, "ExternalInput")
    mamba_in = _D("mamba", [D, T], F32, "ExternalInput")
    wrT = _D("wrT", [D, D], F32, "ExternalInput")
    wkT = _D("wkT", [D, D], F32, "ExternalInput")
    wvT = _D("wvT", [D, D], F32, "ExternalInput")
    mur_in = _D("mu_r", [128, KD], F32, "ExternalInput")
    muk_in = _D("mu_k", [128, KD], F32, "ExternalInput")
    x1_out = _D("x1", [D, T], F32, "ExternalOutput")
    xn1_out = _D("xn1_bf", [D, T], BF16, "ExternalOutput")

    wr_sb = p.sb("wr_sb", [128, KD * D], BF16)
    wk_sb = p.sb("wk_sb", [128, KD * D], BF16)
    wv_sb = p.sb("wv_sb", [128, KD * D], BF16)
    mur = p.sb("mur", [128, KD], F32)
    muk = p.sb("muk", [128, KD], F32)
    mur1 = p.sb("mur1", [128, KD], F32)
    muk1 = p.sb("muk1", [128, KD], F32)
    ones_col = p.sb("ones_colD", [128, 1], BF16)
    ones_row = p.sb("ones_rowD", [1, 128], BF16)
    eps_sb = p.sb("eps_sbD", [1, 1], F32)

    xs = p.sb("xs", [128, KD * (TC + 1)], BF16)
    xr_sb = p.sb("xr_sb", [128, KD * TC], BF16)
    xk_sb = p.sb("xk_sb", [128, KD * TC], BF16)
    tmpmix = p.sb("tmpmix", [128, TC], BF16)
    r_sb = p.sb("r_sb", [128, KD * TC], BF16)
    kk_sb = p.sb("kk_sb", [128, KD * TC], BF16)
    krelu = p.sb("krelu", [128, 2 * TC], BF16)
    x_sb = p.sb("x_sbD", [128, KD * TC], F32)
    mam_sb = p.sb("mam_sb", [128, KD * TC], F32)
    ffn_sb = p.sb("ffn_sb", [128, 2 * TC], F32)
    x1_sb = p.sb("x1_sb", [128, KD * TC], F32)
    xb1 = p.sb("xb1", [128, KD * TC], BF16)
    xsq1 = p.sb("xsq1", [128, KD * TC], BF16)
    xn1_sb = p.sb("xn1_sb", [128, KD * TC], BF16)
    tmp_ln = p.sb("tmp_lnD", [128, TC], BF16)
    mrow = p.sb("mrowD", [1, TC], F32)
    vrow = p.sb("vrowD", [1, TC], F32)
    msq = p.sb("msqD", [1, TC], F32)
    srow = p.sb("srowD", [1, TC], F32)
    rrow = p.sb("rrowD", [1, TC], F32)
    m_bf = p.sb("m_bfD", [1, TC], BF16)
    r_bf = p.sb("r_bfD", [1, TC], BF16)
    mb_sb = p.sb("mb_sbD", [128, TC], BF16)
    rb_sb = p.sb("rb_sbD", [128, TC], BF16)

    ps_a = [p.ps(f"ps_Da{i}", [128, TC]) for i in range(2)]
    ps_b = [p.ps(f"ps_Db{i}", [128, TC]) for i in range(2)]
    ps_sx = p.ps("ps_sxD", [1, TC])
    ps_sxx = p.ps("ps_sxxD", [1, TC])
    ps_mb = p.ps("ps_mbD", [128, TC])
    ps_rb = p.ps("ps_rbD", [128, TC])

    for k in range(KD):
        p.dma("gpsimd", wr_sb[:, k * D:(k + 1) * D], wrT[k * 128:(k + 1) * 128, :])
        p.dma("gpsimd", wk_sb[:, k * D:(k + 1) * D], wkT[k * 128:(k + 1) * 128, :])
        p.dma("gpsimd", wv_sb[:, k * D:(k + 1) * D], wvT[k * 128:(k + 1) * 128, :])
    p.dma("sync", mur[:], mur_in)
    p.dma("sync", muk[:], muk_in)
    p.op("vector", lambda e: e.memset(eps_sb[:], 1e-5))
    p.op("vector", lambda e: e.memset(ones_col[:], 1.0))
    p.op("vector", lambda e: e.memset(ones_row[:], 1.0))
    for k in range(KD):
        p.op("vector", lambda e, k=k: e.memset(xs[:, k * (TC + 1):k * (TC + 1) + 1], 0.0))
    p.barrier()
    p.op("vector", lambda e: e.tensor_scalar(out=mur1[:], in0=mur[:], scalar1=-1.0,
                                             scalar2=1.0, op0=ALU.mult, op1=ALU.add))
    p.op("vector", lambda e: e.tensor_scalar(out=muk1[:], in0=muk[:], scalar1=-1.0,
                                             scalar2=1.0, op0=ALU.mult, op1=ALU.add))
    p.barrier()

    for c in range(NCH):
        csl = slice(c * TC, (c + 1) * TC)
        p.dma("sync", xs.ap().rearrange("p (k t) -> p k t", k=KD)[:, :, 1:TC + 1],
              xn_in[:, csl].rearrange("(k p) t -> p k t", p=128))
        p.dma("sync", x_sb.ap().rearrange("p (k t) -> p k t", k=KD),
              xT[:, csl].rearrange("(k p) t -> p k t", p=128))
        p.dma("sync", mam_sb.ap().rearrange("p (k t) -> p k t", k=KD),
              mamba_in[:, csl].rearrange("(k p) t -> p k t", p=128))
        p.barrier()
        for k in range(KD):
            base = k * (TC + 1)
            cur = xs[:, base + 1:base + 1 + TC]
            prv = xs[:, base:base + TC]
            p.op("vector", lambda e, prv=prv, k=k: e.tensor_scalar_mul(
                out=tmpmix[:], in0=prv, scalar1=mur1[:, k:k + 1]))
            p.op("vector", lambda e, cur=cur, k=k: e.scalar_tensor_tensor(
                out=xr_sb[:, k * TC:(k + 1) * TC], in0=cur,
                scalar=mur[:, k:k + 1], in1=tmpmix[:], op0=ALU.mult, op1=ALU.add))
            p.op("vector", lambda e, prv=prv, k=k: e.tensor_scalar_mul(
                out=tmpmix[:], in0=prv, scalar1=muk1[:, k:k + 1]))
            p.op("vector", lambda e, cur=cur, k=k: e.scalar_tensor_tensor(
                out=xk_sb[:, k * TC:(k + 1) * TC], in0=cur,
                scalar=muk[:, k:k + 1], in1=tmpmix[:], op0=ALU.mult, op1=ALU.add))
        for k in range(KD):
            base = k * (TC + 1)
            p.op("vector", lambda e, base=base: e.tensor_copy(
                xs[:, base:base + 1], xs[:, base + TC:base + TC + 1]))
        p.barrier()
        for o in range(KD + 2):
            if o < KD:
                pb = o % 2
                for k in range(KD):
                    p.op("tensor", lambda e, o=o, k=k, pb=pb: e.matmul(
                        ps_a[pb][:], wr_sb[:, k * D + o * 128:k * D + (o + 1) * 128],
                        xr_sb[:, k * TC:(k + 1) * TC],
                        start=(k == 0), stop=(k == KD - 1)))
                for k in range(KD):
                    p.op("tensor", lambda e, o=o, k=k, pb=pb: e.matmul(
                        ps_b[pb][:], wk_sb[:, k * D + o * 128:k * D + (o + 1) * 128],
                        xk_sb[:, k * TC:(k + 1) * TC],
                        start=(k == 0), stop=(k == KD - 1)))
            if 0 <= o - 1 < KD:
                om, pb = o - 1, (o - 1) % 2
                p.op("scalar", lambda e, om=om, pb=pb: e.activation(
                    out=r_sb[:, om * TC:(om + 1) * TC], in_=ps_a[pb][:],
                    func=AF.Sigmoid))
                p.op("vector", lambda e, om=om, pb=pb: e.tensor_scalar_max(
                    out=krelu[:, pb * TC:(pb + 1) * TC], in0=ps_b[pb][:], scalar1=0.0))
            if 0 <= o - 2 < KD:
                om, pb = o - 2, (o - 2) % 2
                p.op("scalar", lambda e, om=om, pb=pb: e.activation(
                    out=kk_sb[:, om * TC:(om + 1) * TC],
                    in_=krelu[:, pb * TC:(pb + 1) * TC], func=AF.Square))
            p.barrier(lazy=True)
        for o in range(KD + 1):
            if o < KD:
                pb = o % 2
                for k in range(KD):
                    p.op("tensor", lambda e, o=o, k=k, pb=pb: e.matmul(
                        ps_a[pb][:], wv_sb[:, k * D + o * 128:k * D + (o + 1) * 128],
                        kk_sb[:, k * TC:(k + 1) * TC],
                        start=(k == 0), stop=(k == KD - 1)))
            if 0 <= o - 1 < KD:
                om, pb = o - 1, (o - 1) % 2
                p.op("vector", lambda e, om=om, pb=pb: e.tensor_tensor(
                    out=ffn_sb[:, pb * TC:(pb + 1) * TC], in0=ps_a[pb][:],
                    in1=r_sb[:, om * TC:(om + 1) * TC], op=ALU.mult))
                p.op("vector", lambda e, om=om: e.tensor_add(
                    x1_sb[:, om * TC:(om + 1) * TC], x_sb[:, om * TC:(om + 1) * TC],
                    mam_sb[:, om * TC:(om + 1) * TC]))
                p.op("vector", lambda e, om=om, pb=pb: e.tensor_add(
                    x1_sb[:, om * TC:(om + 1) * TC], x1_sb[:, om * TC:(om + 1) * TC],
                    ffn_sb[:, pb * TC:(pb + 1) * TC]))
            p.barrier(lazy=True)
        p.op("vector", lambda e: e.tensor_copy(xb1[:], x1_sb[:]))
        p.op("scalar", lambda e: e.activation(out=xsq1[:], in_=x1_sb[:],
                                              func=AF.Square))
        p.barrier()
        for k in range(KD):
            p.op("tensor", lambda e, k=k: e.matmul(
                ps_sx[:], ones_col[:], xb1[:, k * TC:(k + 1) * TC],
                start=(k == 0), stop=(k == KD - 1)))
        for k in range(KD):
            p.op("tensor", lambda e, k=k: e.matmul(
                ps_sxx[:], ones_col[:], xsq1[:, k * TC:(k + 1) * TC],
                start=(k == 0), stop=(k == KD - 1)))
        p.barrier()
        p.op("vector", lambda e: e.tensor_scalar_mul(out=mrow[:], in0=ps_sx[:],
                                                     scalar1=1.0 / D))
        p.op("vector", lambda e: e.tensor_scalar_mul(out=vrow[:], in0=ps_sxx[:],
                                                     scalar1=1.0 / D))
        p.op("vector", lambda e: e.tensor_mul(msq[:], mrow[:], mrow[:]))
        p.op("vector", lambda e: e.tensor_sub(vrow[:], vrow[:], msq[:]))
        p.barrier()
        p.op("scalar", lambda e: e.activation(out=srow[:], in_=vrow[:],
                                              func=AF.Sqrt, bias=eps_sb[:]))
        p.barrier()
        p.op("vector", lambda e: e.reciprocal(rrow[:], srow[:]))
        p.barrier()
        p.op("vector", lambda e: e.tensor_copy(m_bf[:], mrow[:]))
        p.op("vector", lambda e: e.tensor_copy(r_bf[:], rrow[:]))
        p.barrier()
        p.op("tensor", lambda e: e.matmul(ps_mb[:], ones_row[:], m_bf[:],
                                          start=True, stop=True))
        p.op("tensor", lambda e: e.matmul(ps_rb[:], ones_row[:], r_bf[:],
                                          start=True, stop=True))
        p.barrier()
        p.op("vector", lambda e: e.tensor_copy(mb_sb[:], ps_mb[:]))
        p.op("vector", lambda e: e.tensor_copy(rb_sb[:], ps_rb[:]))
        p.barrier()
        for k in range(KD):
            p.op("vector", lambda e, k=k: e.tensor_sub(
                tmp_ln[:], xb1[:, k * TC:(k + 1) * TC], mb_sb[:]))
            p.op("vector", lambda e, k=k: e.tensor_mul(
                xn1_sb[:, k * TC:(k + 1) * TC], tmp_ln[:], rb_sb[:]))
        p.barrier()
        p.dma("gpsimd", x1_out[:, csl].rearrange("(k p) t -> p k t", p=128),
              x1_sb.ap().rearrange("p (k t) -> p k t", k=KD))
        p.dma("gpsimd", xn1_out[:, csl].rearrange("(k p) t -> p k t", p=128),
              xn1_sb.ap().rearrange("p (k t) -> p k t", k=KD))
        p.barrier()
    p.pop_region()
    if standalone:
        return p.build()



def build_stageE(p=None, dr=None):
    standalone = p is None
    if standalone:
        p = Prog("E")
    _D = _mk_dram(p, dr)
    p.push_region()
    xn_in = _D("xn1_bf", [D, T], BF16, "ExternalInput")
    wqT = _D("wqT", [D, D], F32, "ExternalInput")
    wkT = _D("wkT", [D, NKV * HD], F32, "ExternalInput")
    wvT = _D("wvT", [D, NKV * HD], F32, "ExternalInput")
    woT = _D("woT", [D, D], F32, "ExternalInput")
    attn_out = _D("attn", [D, T], F32, "ExternalOutput")

    NB = T // 128
    wq_sb = p.sb("wq_sb", [128, KD * D], BF16)
    wk_sb = p.sb("wk_sb", [128, KD * NKV * HD], BF16)
    wv_sb = p.sb("wv_sb", [128, KD * NKV * HD], BF16)
    wo_sb = p.sb("wo_sb", [128, KD * D], BF16)
    xn_sb = p.sb("xn_sbE", [128, KD * T], BF16)
    q_sb = p.sb("q_sbE", [128, KD * T], BF16)
    k_sb = p.sb("k_sbE", [128, 2 * T], BF16)
    kd_sb = p.sb("kd_sbE", [128, NKV * T], BF16)
    v_sb = p.sb("v_sbE", [128, 2 * T], BF16)
    vtok = p.sb("vtok", [128, NB * NKV * HD], BF16)
    io = p.sb("ioE", [128, 128], F32)
    pcol = p.sb("pcolE", [128, 1], F32)
    ident = p.sb("identE", [128, 128], BF16)
    tri = p.sb("triE", [128, 128], BF16)
    eighth = p.sb("eighth", [128, 1], F32)
    p_buf = p.sb("p_buf", [128, 2 * T], BF16)
    pt_buf = p.sb("pt_buf", [128, 2 * T], BF16)
    nm_b = p.sb("nm_b", [128, 2], F32)
    l_b = p.sb("l_b", [128, 2], F32)
    rl_b = p.sb("rl_b", [128, 2], F32)
    ctx_sb = p.sb("ctx_sbE", [128, KD * T], BF16)
    at_ev = p.sb("at_ev", [128, 3 * 512], F32)

    ps_qkv = [p.ps(f"ps_Eq{i}", [128, 512]) for i in range(2)]
    ps_s = p.ps("ps_Es", [128, T])
    ps_t = p.ps("ps_Et", [128, T], BF16)
    ps_ctx = None

    for k in range(KD):
        p.dma("gpsimd", wq_sb[:, k * D:(k + 1) * D], wqT[k * 128:(k + 1) * 128, :])
        p.dma("gpsimd", wo_sb[:, k * D:(k + 1) * D], woT[k * 128:(k + 1) * 128, :])
        p.dma("gpsimd", wk_sb[:, k * NKV * HD:(k + 1) * NKV * HD],
              wkT[k * 128:(k + 1) * 128, :])
        p.dma("gpsimd", wv_sb[:, k * NKV * HD:(k + 1) * NKV * HD],
              wvT[k * 128:(k + 1) * 128, :])
        p.dma("sync", xn_sb[:, k * T:(k + 1) * T], xn_in[k * 128:(k + 1) * 128, :])
    p.op("gpsimd", lambda e: e.iota(io[:], [[1, 128]], channel_multiplier=0,
                                    allow_small_or_imprecise_dtypes=True))
    p.op("gpsimd", lambda e: e.iota(pcol[:], [[0, 1]], channel_multiplier=1,
                                    allow_small_or_imprecise_dtypes=True))
    p.op("vector", lambda e: e.memset(eighth[:], 0.125))
    p.barrier()
    p.op("vector", lambda e: e.tensor_single_scalar(out=ident[:], in_=io[:],
                                                    scalar=pcol[:], op=ALU.is_equal))
    p.op("vector", lambda e: e.tensor_single_scalar(out=tri[:], in_=io[:],
                                                    scalar=pcol[:], op=ALU.is_le))
    p.barrier()

    def proj(w_sb, nout, dst_sb, scale=None):
        for o in range(nout):
            for nn in range(T // 512):
                pb = (o * (T // 512) + nn) % 2
                for k in range(KD):
                    p.op("tensor", lambda e, o=o, nn=nn, k=k, pb=pb,
                         w_sb=w_sb, nout=nout: e.matmul(
                        ps_qkv[pb][:],
                        w_sb[:, k * nout * 128 + o * 128:k * nout * 128 + (o + 1) * 128],
                        xn_sb[:, k * T + nn * 512:k * T + nn * 512 + 512],
                        start=(k == 0), stop=(k == KD - 1)))
                p.barrier()
                if scale is None:
                    p.op("vector", lambda e, o=o, nn=nn, pb=pb, dst_sb=dst_sb: e.tensor_copy(
                        dst_sb[:, o * T + nn * 512:o * T + nn * 512 + 512], ps_qkv[pb][:]))
                else:
                    p.op("vector", lambda e, o=o, nn=nn, pb=pb, dst_sb=dst_sb,
                         scale=scale: e.tensor_scalar_mul(
                        out=dst_sb[:, o * T + nn * 512:o * T + nn * 512 + 512],
                        in0=ps_qkv[pb][:], scalar1=scale))
        p.barrier()
    proj(wq_sb, KD, q_sb, scale=0.125)
    proj(wk_sb, 2, k_sb)
    proj(wv_sb, 2, v_sb)
    p.barrier()
    for g in range(NKV):
        vt = g // 2
        vo = (g % 2) * 64
        for blk in range(NB):
            p.op("tensor", lambda e, vt=vt, vo=vo, blk=blk: e.transpose(
                ps_t[:, 0:64] if False else ps_t[:, (blk % 8) * 64:(blk % 8) * 64 + 64],
                v_sb[vo:vo + 64, vt * T + blk * 128:vt * T + (blk + 1) * 128],
                ident[vo:vo + 64, vo:vo + 64]))
            if blk % 8 == 7 or blk == NB - 1:
                p.barrier()
                for b2 in range(blk - blk % 8, blk + 1):
                    p.op("vector", lambda e, g=g, b2=b2, blk=blk: e.tensor_copy(
                        vtok[:, b2 * NKV * HD + g * HD:b2 * NKV * HD + (g + 1) * HD],
                        ps_t[:, (b2 % 8) * 64:(b2 % 8) * 64 + 64]))
                p.barrier()
    p.barrier()
    for g in range(NKV):
        kt, ko = g // 2, (g % 2) * 64
        p.dma("sync", kd_sb[0:64, g * T:(g + 1) * T],
              k_sb[ko:ko + 64, kt * T:(kt + 1) * T])
        p.dma("gpsimd", kd_sb[64:128, g * T:(g + 1) * T],
              k_sb[ko:ko + 64, kt * T:(kt + 1) * T])
    p.barrier()

    IT = NH * NB
    p.op("tensor", lambda e: e.matmul(
        ps_s[:, 0:128], q_sb[0:64, 0:128], kd_sb[0:64, 0:128],
        start=True, stop=True))
    p.barrier()
    for i in range(IT + 4):
        if i < IT:
            h, qb = divmod(i, NB)
            j = i % 2
            ncol = (qb + 1) * 128
            p.op("vector", lambda e, ncol=ncol, j=j: e.reduce_max(
                nm_b[:, j:j + 1], ps_s[:, 0:ncol], axis=AX.X, negate=True))
        if 0 <= i - 1 < IT:
            j = (i - 1) % 2
            p.op("vector", lambda e, j=j: e.reciprocal(rl_b[:, j:j + 1],
                                                       l_b[:, j:j + 1]))
        if 0 <= i - 2 < IT:
            h, qb = divmod(i - 2, NB)
            j = (i - 2) % 2
            ncol = (qb + 1) * 128
            p.op("vector", lambda e, ncol=ncol, j=j: e.tensor_copy(
                pt_buf[:, j * T:j * T + ncol], ps_t[:, 0:ncol]))
        p.barrier(lazy=True)
        if i < IT:
            h, qb = divmod(i, NB)
            j = i % 2
            ncol = (qb + 1) * 128
            p.op("scalar", lambda e, ncol=ncol, j=j: e.activation(
                out=p_buf[:, j * T:j * T + ncol], in_=ps_s[:, 0:ncol],
                func=AF.Exp, bias=nm_b[:, j:j + 1]))
        if 0 <= i - 1 < IT:
            h, qb = divmod(i - 1, NB)
            j = (i - 1) % 2
            ncol = (qb + 1) * 128
            p.op("vector", lambda e, ncol=ncol, j=j: e.tensor_scalar_mul(
                out=p_buf[:, j * T:j * T + ncol], in0=p_buf[:, j * T:j * T + ncol],
                scalar1=rl_b[:, j:j + 1]))
        if 0 <= i - 2 < IT:
            h, qb = divmod(i - 2, NB)
            j = (i - 2) % 2
            g = h // NKV
            for blk in range(qb + 1):
                p.op("tensor", lambda e, j=j, g=g, blk=blk, qb=qb, i=i: e.matmul(
                    ps_qkv[(i - 2) % 2][0:64, 0:128],
                    vtok[:, blk * NKV * HD + g * HD:blk * NKV * HD + (g + 1) * HD],
                    pt_buf[:, j * T + blk * 128:j * T + (blk + 1) * 128],
                    start=(blk == 0), stop=(blk == qb)))
        p.barrier(lazy=True)
        if i < IT:
            h, qb = divmod(i, NB)
            j = i % 2
            ncol = (qb + 1) * 128
            p.op("vector", lambda e, j=j, qb=qb: e.tensor_mul(
                p_buf[:, j * T + qb * 128:j * T + (qb + 1) * 128],
                p_buf[:, j * T + qb * 128:j * T + (qb + 1) * 128], tri[:]))
            p.op("vector", lambda e, ncol=ncol, j=j: e.reduce_sum(
                l_b[:, j:j + 1], p_buf[:, j * T:j * T + ncol], axis=AX.X))
        if 0 <= i - 1 < IT:
            h, qb = divmod(i - 1, NB)
            j = (i - 1) % 2
            for blk in range(qb + 1):
                p.op("tensor", lambda e, j=j, blk=blk: e.transpose(
                    ps_t[:, blk * 128:(blk + 1) * 128],
                    p_buf[:, j * T + blk * 128:j * T + (blk + 1) * 128], ident[:]))
        if i + 1 < IT:
            h, qb = divmod(i + 1, NB)
            qt, qo = h // 2, (h % 2) * 64
            g = h // NKV
            ncol = (qb + 1) * 128
            nmm = (ncol + 511) // 512
            for mmi in range(nmm):
                w = min(512, ncol - mmi * 512)
                p.op("tensor", lambda e, qt=qt, qo=qo, g=g, qb=qb,
                     mmi=mmi, w=w: e.matmul(
                    ps_s[:, mmi * 512:mmi * 512 + w],
                    q_sb[qo:qo + 64, qt * T + qb * 128:qt * T + (qb + 1) * 128],
                    kd_sb[qo:qo + 64, g * T + mmi * 512:g * T + mmi * 512 + w],
                    start=True, stop=True))
        if 0 <= i - 2 < IT:
            h, qb = divmod(i - 2, NB)
            j = (i - 2) % 2
            qt, qo = h // 2, (h % 2) * 64
            p.op("scalar", lambda e, qt=qt, qo=qo, qb=qb, j=j: e.copy(
                ctx_sb[qo:qo + 64, qt * T + qb * 128:qt * T + (qb + 1) * 128],
                ps_qkv[j][0:64, 0:128]))
        p.barrier(lazy=True)

    NOP = KD * (T // 512)
    for it in range(NOP + 3):
        if it < NOP:
            o, nn = divmod(it, T // 512)
            pb = it % 2
            for k in range(KD):
                p.op("tensor", lambda e, o=o, nn=nn, k=k, pb=pb: e.matmul(
                    ps_qkv[pb][:], wo_sb[:, k * D + o * 128:k * D + (o + 1) * 128],
                    ctx_sb[:, k * T + nn * 512:k * T + nn * 512 + 512],
                    start=(k == 0), stop=(k == KD - 1)))
        if 0 <= it - 1 < NOP:
            pb = (it - 1) % 2
            p3 = (it - 1) % 3
            p.op("vector", lambda e, pb=pb, p3=p3: e.tensor_copy(
                at_ev[:, p3 * 512:(p3 + 1) * 512], ps_qkv[pb][:]))
        if 0 <= it - 2 < NOP:
            o, nn = divmod(it - 2, T // 512)
            p3 = (it - 2) % 3
            p.dma("sync", attn_out[o * 128:(o + 1) * 128, nn * 512:(nn + 1) * 512],
                  at_ev[:, p3 * 512:(p3 + 1) * 512])
        p.barrier(lazy=True)
    p.pop_region()
    if standalone:
        return p.build()



def build_stageF(p=None, dr=None):
    standalone = p is None
    if standalone:
        p = Prog("F")
    _D = _mk_dram(p, dr)
    p.push_region()
    xn_in = _D("xn1_bf", [D, T], BF16, "ExternalInput")
    x1_in = _D("x1", [D, T], F32, "ExternalInput")
    attn_in = _D("attn", [D, T], F32, "ExternalInput")
    Wr_in = _D("Wr", [D, NE], F32, "ExternalInput")
    br_in = _D("br", [NE, 1], F32, "ExternalInput")
    W1_in = _D("W1", [NE, D, HF], F32, "ExternalInput")
    b1_in = _D("b1d", [NE, 128, HF // 128], F32, "ExternalInput")
    W2_in = _D("W2", [NE, HF, D], F32, "ExternalInput")
    b2_in = _D("b2d", [NE, 128, D // 128], F32, "ExternalInput")
    out_T = _D("outT", [D, T], F32, "ExternalOutput")
    w_dram = p.dram("w_scratch", [NE, T], F32, "Internal")
    acc_dram = p.dram("acc_scratch", [D, T], F32, "Internal")

    NB = T // 128
    NO1 = HF // 128
    xn_sb = p.sb("xn_sbF", [128, KD * T], BF16)
    wr_sb = p.sb("wr_sbF", [128, KD * NE], BF16)
    br_sb = p.sb("br_sbF", [NE, 1], F32)
    w1_sb = p.sb("w1_sbF", [128, KD * HF], BF16)
    w2_sb = p.sb("w2_sbF", [128, NO1 * D], BF16)
    b1_sb = p.sb("b1_sbF", [128, NE * NO1], F32)
    b2_sb = p.sb("b2_sbF", [128, NE * KD], F32)
    io = p.sb("ioF", [128, 128], F32)
    pcol = p.sb("pcolF", [128, 1], F32)
    id32 = p.sb("id32F", [128, 128], F32)
    lg_sb = p.sb("lg_sbF", [128, T], F32)
    lgt = p.sb("lgtF", [128, 2 * NE], F32)
    nm1 = p.sb("nm1F", [128, 1], F32)
    e1 = p.sb("e1F", [128, NE], F32)
    t1 = p.sb("t1F", [128, NE], F32)
    m2 = p.sb("m2F", [128, 1], F32)
    pe = p.sb("peF", [128, NE], F32)
    msk = p.sb("mskF", [128, NE], F32)
    zz = p.sb("zzF", [128, 1], F32)
    rz = p.sb("rzF", [128, 1], F32)
    w_tok = p.sb("w_tokF", [128, NE], F32)
    wb_e = p.sb("wb_eF", [128, 2 * TC], F32)
    h_sb = p.sb("h_sbF", [128, NO1 * 512], BF16)


    p.push_region()
    ps_r = p.ps("ps_rF", [NE, 512])
    ps_t8 = p.ps("ps_t8F", [128, NE])
    ps_w8 = p.ps("ps_w8F", [NE, 128])

    for k in range(KD):
        p.dma("sync", xn_sb[:, k * T:(k + 1) * T], xn_in[k * 128:(k + 1) * 128, :])
    p.dma("gpsimd", wr_sb.ap().rearrange("p (k j) -> p k j", k=KD),
          Wr_in.rearrange("(k p) j -> p k j", p=128))
    p.dma("sync", br_sb[:], br_in)
    p.dma("sync", b1_sb.ap().rearrange("p (e o) -> p e o", e=NE),
          b1_in.transpose((1, 0, 2)))
    p.dma("sync", b2_sb.ap().rearrange("p (e o) -> p e o", e=NE),
          b2_in.transpose((1, 0, 2)))
    p.op("gpsimd", lambda e: e.iota(io[:], [[1, 128]], channel_multiplier=0,
                                    allow_small_or_imprecise_dtypes=True))
    p.op("gpsimd", lambda e: e.iota(pcol[:], [[0, 1]], channel_multiplier=1,
                                    allow_small_or_imprecise_dtypes=True))
    p.barrier()
    p.op("vector", lambda e: e.tensor_single_scalar(out=id32[:], in_=io[:],
                                                    scalar=pcol[:], op=ALU.is_equal))
    p.barrier()

    for c in range(NCH):
        for k in range(KD):
            p.op("tensor", lambda e, k=k, c=c: e.matmul(
                ps_r[:], wr_sb[:, k * NE:(k + 1) * NE],
                xn_sb[:, k * T + c * TC:k * T + (c + 1) * TC],
                start=(k == 0), stop=(k == KD - 1)))
        p.barrier()
        p.op("vector", lambda e, c=c: e.tensor_scalar_add(
            out=lg_sb[0:NE, c * TC:(c + 1) * TC], in0=ps_r[:], scalar1=br_sb[:]))
        p.barrier()
    for blk in range(NB):
        pb = blk % 2
        p.op("tensor", lambda e, blk=blk: e.transpose(
            ps_t8[:], lg_sb[0:NE, blk * 128:(blk + 1) * 128], id32[0:NE, 0:NE]))
        p.barrier()
        p.op("vector", lambda e, pb=pb: e.tensor_copy(lgt[:, pb * NE:(pb + 1) * NE],
                                                      ps_t8[:]))
        p.barrier()
        sl = slice(pb * NE, (pb + 1) * NE)
        p.op("vector", lambda e, sl=sl: e.reduce_max(nm1[:], lgt[:, sl], axis=AX.X,
                                                     negate=True))
        p.barrier()
        p.op("vector", lambda e, sl=sl: e.tensor_scalar(
            out=e1[:], in0=lgt[:, sl], scalar1=nm1[:], scalar2=None,
            op0=ALU.add, op1=ALU.bypass))
        p.barrier()
        p.op("vector", lambda e: e.tensor_single_scalar(out=e1[:], in_=e1[:],
                                                        scalar=0.0, op=ALU.is_equal))
        p.barrier()
        p.op("vector", lambda e, sl=sl: e.scalar_tensor_tensor(
            out=t1[:], in0=e1[:], scalar=-1e9, in1=lgt[:, sl],
            op0=ALU.mult, op1=ALU.add))
        p.barrier()
        p.op("vector", lambda e: e.reduce_max(m2[:], t1[:], axis=AX.X))
        p.op("scalar", lambda e, sl=sl: e.activation(out=pe[:], in_=lgt[:, sl],
                                                     func=AF.Exp, bias=nm1[:]))
        p.barrier()
        p.op("vector", lambda e, sl=sl: e.tensor_single_scalar(
            out=msk[:], in_=lgt[:, sl], scalar=m2[:], op=ALU.is_ge))
        p.barrier()
        p.op("vector", lambda e: e.tensor_mul(pe[:], pe[:], msk[:]))
        p.barrier()
        p.op("vector", lambda e: e.reduce_sum(zz[:], pe[:], axis=AX.X))
        p.barrier()
        p.op("vector", lambda e: e.reciprocal(rz[:], zz[:]))
        p.barrier()
        p.op("vector", lambda e: e.tensor_scalar_mul(out=w_tok[:], in0=pe[:],
                                                     scalar1=rz[:]))
        p.barrier()
        p.op("tensor", lambda e: e.transpose(ps_w8[:], w_tok[:], id32[:]))
        p.barrier()
        p.op("vector", lambda e, blk=blk: e.tensor_copy(
            lg_sb[0:NE, blk * 128:(blk + 1) * 128], ps_w8[:]))
        p.barrier()
    p.dma("sync", w_dram, lg_sb[0:NE, :])
    p.barrier()
    p.pop_region()
    ps_m = [p.ps(f"ps_Fm{i}", [128, 512]) for i in range(8)]

    t2p = lg_sb[:, 0:2048]
    for k in range(KD):
        p.dma("gpsimd", w1_sb[:, k * HF:(k + 1) * HF], W1_in[0, k * 128:(k + 1) * 128, :],
              lane=1)
    p.dma("gpsimd", w2_sb.ap().rearrange("p (k dd) -> p k dd", k=NO1),
          W2_in[0].rearrange("(k p) dd -> p k dd", p=128), lane=1)
    p.dma("gpsimd", wb_e[:, 0:TC], w_dram[0, 0:TC].partition_broadcast(128), lane=1)
    p.fence_lane1("tensor")
    p.fence_lane1("vector")
    p.barrier()
    p.eng_lag = {}
    t2b = h_sb[:, 0:4096].bitcast(F32)
    NP1 = NO1 // 4
    NG = NP1 + 1 + 2 + 2
    for ex in range(NE):
        for c in range(NCH):
            csl = slice(c * TC, (c + 1) * TC)
            par = (ex * NCH + c) % 2
            wb_cur = wb_e[:, par * TC:(par + 1) * TC]
            cn, exn = (c + 1, ex) if c + 1 < NCH else (0, ex + 1)
            for g in range(NG):
                if g == 0 and c == 0 and ex >= 1:
                    p.fence_lane1("tensor")
                if g == 0 and exn < NE:
                    p.dma("gpsimd", wb_e[:, (1 - par) * TC:(2 - par) * TC],
                          w_dram[exn, cn * TC:(cn + 1) * TC].partition_broadcast(128),
                          lane=1)
                if g == 0 and c == 0 and ex >= 1:
                    p.dma("gpsimd", w2_sb.ap().rearrange("p (k dd) -> p k dd", k=NO1),
                          W2_in[ex].rearrange("(k p) dd -> p k dd", p=128), lane=1)
                if g == NP1:
                    if c == 0:
                        p.fence_lane1("tensor")
                    p.fence_lane1("vector")
                if g == NP1 + 1 and c == NCH - 1 and ex + 1 < NE:
                    for k in range(KD):
                        p.dma("gpsimd", w1_sb[:, k * HF:(k + 1) * HF],
                              W1_in[ex + 1, k * 128:(k + 1) * 128, :], lane=1)
                if g < NP1:
                    for oo in range(4):
                        o = g * 4 + oo
                        pb = (g % 2) * 4 + oo
                        for k in range(KD):
                            p.op("tensor", lambda e, o=o, k=k, pb=pb, c=c: e.matmul(
                                ps_m[pb][:],
                                w1_sb[:, k * HF + o * 128:k * HF + (o + 1) * 128],
                                xn_sb[:, k * T + c * TC:k * T + (c + 1) * TC],
                                start=(k == 0), stop=(k == KD - 1)))
                elif g in (NP1 + 1, NP1 + 2):
                    for dd in range((g - NP1 - 1) * 4, (g - NP1) * 4):
                        for k in range(NO1):
                            p.op("tensor", lambda e, dd=dd, k=k: e.matmul(
                                ps_m[dd][:], w2_sb[:, k * D + dd * 128:k * D + (dd + 1) * 128],
                                h_sb[:, k * TC:(k + 1) * TC],
                                start=(k == 0), stop=(k == NO1 - 1)))
                gm = g - 1
                if 0 <= gm < NP1:
                    for oo in range(4):
                        o = gm * 4 + oo
                        pb = (gm % 2) * 4 + oo
                        if oo % 2 == 0:
                            p.op("vector", lambda e, o=o, pb=pb, ex=ex: e.tensor_scalar(
                                out=h_sb[:, o * TC:(o + 1) * TC], in0=ps_m[pb][:],
                                scalar1=b1_sb[:, ex * NO1 + o:ex * NO1 + o + 1],
                                scalar2=0.0, op0=ALU.add, op1=ALU.max))
                        else:
                            p.op("scalar", lambda e, o=o, pb=pb, ex=ex: e.activation(
                                out=h_sb[:, o * TC:(o + 1) * TC], in_=ps_m[pb][:],
                                func=AF.Relu, bias=b1_sb[:, ex * NO1 + o:ex * NO1 + o + 1]))
                elif gm in (NP1 + 1, NP1 + 2):
                    for dd in range((gm - NP1 - 1) * 4, (gm - NP1) * 4):
                        half = dd // 4
                        tsl = (t2p[:, dd * 512:(dd + 1) * 512] if half == 0
                               else t2b[:, (dd - 4) * 512:(dd - 3) * 512])
                        p.op("vector", lambda e, dd=dd, tsl=tsl, ex=ex, wb_cur=wb_cur:
                             e.scalar_tensor_tensor(
                            out=tsl, in0=ps_m[dd][:],
                            scalar=b2_sb[:, ex * KD + dd:ex * KD + dd + 1],
                            in1=wb_cur, op0=ALU.add, op1=ALU.mult))
                gm2 = g - 2
                if gm2 in (NP1 + 1, NP1 + 2):
                    for dd in range((gm2 - NP1 - 1) * 4, (gm2 - NP1) * 4):
                        half = dd // 4
                        tsl = (t2p[:, dd * 512:(dd + 1) * 512] if half == 0
                               else t2b[:, (dd - 4) * 512:(dd - 3) * 512])
                        if ex == 0:
                            p.dma("gpsimd", acc_dram[dd * 128:(dd + 1) * 128, csl],
                                  tsl)
                        else:
                            def accdma(e, dd=dd, csl=csl, tsl=tsl):
                                return e.dma_start(
                                    out=acc_dram[dd * 128:(dd + 1) * 128, csl],
                                    in_=tsl, accum_op=ALU.add)
                            cell = ["dma"]
                            p._q("gpsimd", ("call", accdma, cell))
                            p.tot_dma += 16
                p.barrier(lazy=True)
    p.eng_lag = {}
    res_sb = h_sb[:, 0:3072].bitcast(F32)
    out_sb = h_sb[:, 3072:6144].bitcast(F32)
    acc_sb = h_sb[:, 6144:9216].bitcast(F32)
    NIT = KD * NCH
    for it in range(NIT + 3):
        if it < NIT:
            dd, c = divmod(it, NCH)
            pb = it % 3
            csl = slice(c * TC, (c + 1) * TC)
            rows = slice(dd * 128, (dd + 1) * 128)
            p.dma("sync", res_sb[:, pb * TC:(pb + 1) * TC], x1_in[rows, csl])
            p.dma("sync", out_sb[:, pb * TC:(pb + 1) * TC], attn_in[rows, csl])
            p.dma("sync", acc_sb[:, pb * TC:(pb + 1) * TC], acc_dram[rows, csl])
        if 0 <= it - 2 < NIT:
            pb = (it - 2) % 3
            rsl = res_sb[:, pb * TC:(pb + 1) * TC]
            osl = out_sb[:, pb * TC:(pb + 1) * TC]
            asl = acc_sb[:, pb * TC:(pb + 1) * TC]
            p.op("vector", lambda e, rsl=rsl, osl=osl: e.tensor_add(rsl, rsl, osl))
            p.op("vector", lambda e, rsl=rsl, osl=osl, asl=asl: e.tensor_add(
                osl, rsl, asl))
        if 0 <= it - 3 < NIT:
            dd, c = divmod(it - 3, NCH)
            pb = (it - 3) % 3
            csl = slice(c * TC, (c + 1) * TC)
            rows = slice(dd * 128, (dd + 1) * 128)
            p.dma("gpsimd", out_T[rows, csl], out_sb[:, pb * TC:(pb + 1) * TC])
        p.barrier()
    p.pop_region()
    if standalone:
        return p.build()

def _t(a):
    return np.ascontiguousarray(np.asarray(a).T)


def build_fused():
    p = Prog("fused")
    EI = lambda n, s, d: p.dram(n, list(s), d, "ExternalInput")
    IN = lambda n, s, d: p.dram(n, list(s), d, "Internal")
    xT = EI("xT", [D, T], F32)
    w_inT = EI("w_inT", [D, 2 * DIN], F32)
    conv_w = EI("conv_w", [DIN, 4], F32)
    conv_b = EI("conv_b", [128, NDI], F32)
    xp_wT = EI("xp_wT", [DIN, 96], F32)
    dt_wT = EI("dt_wT", [DTR, DIN], F32)
    dt_b = EI("dt_b", [128, NDI], F32)
    A_dev = EI("A_dev", [128, NDI * NST], F32)
    D_dev = EI("D_dev", [128, NDI], F32)
    outw_T = EI("outw_T", [DIN, D], F32)
    wrT = EI("wrT", [D, D], F32)
    wkT_d = EI("wkT_d", [D, D], F32)
    wvT_d = EI("wvT_d", [D, D], F32)
    mu_r = EI("mu_r", [128, KD], F32)
    mu_k = EI("mu_k", [128, KD], F32)
    wqT = EI("wqT", [D, D], F32)
    wkT_e = EI("wkT_e", [D, NKV * HD], F32)
    wvT_e = EI("wvT_e", [D, NKV * HD], F32)
    woT = EI("woT", [D, D], F32)
    Wr = EI("Wr", [D, NE], F32)
    br = EI("br", [NE, 1], F32)
    W1 = EI("W1", [NE, D, HF], F32)
    b1d = EI("b1d", [NE, 128, HF // 128], F32)
    W2 = EI("W2", [NE, HF, D], F32)
    b2d = EI("b2d", [NE, 128, D // 128], F32)
    outT = p.dram("outT", [D, T], F32, "ExternalOutput")

    u_bf = IN("u_bf", [DIN, T], BF16)
    sz_bf = IN("sz_bf", [DIN, T], BF16)
    xn_bf = IN("xn_bf", [D, T], BF16)
    delta = IN("delta", [DIN, T], F32)
    du = IN("du", [DIN, T], F16)
    bc = IN("bc", [2 * NST, T], F32)
    mamba = IN("mamba", [D, T], F32)
    x1 = IN("x1", [D, T], F32)
    xn1_bf = IN("xn1_bf", [D, T], BF16)
    attn = IN("attn", [D, T], F32)

    build_stageA(p, dict(xT=xT, w_inT=w_inT, conv_w=conv_w, conv_b=conv_b,
                         u_bf=u_bf, sz_bf=sz_bf, xn_bf=xn_bf))
    build_stageB(p, dict(u_bf=u_bf, xp_wT=xp_wT, dt_wT=dt_wT, dt_b=dt_b,
                         delta=delta, du=du, bc=bc))
    build_stageC(p, dict(delta=delta, du=du, u_bf=u_bf, sz_bf=sz_bf, bc=bc,
                         A_dev=A_dev, D_dev=D_dev, outw_T=outw_T, mamba=mamba))
    build_stageD(p, dict(xT=xT, xn_bf=xn_bf, mamba=mamba, wrT=wrT, wkT=wkT_d,
                         wvT=wvT_d, mu_r=mu_r, mu_k=mu_k, x1=x1, xn1_bf=xn1_bf))
    build_stageE(p, dict(xn1_bf=xn1_bf, wqT=wqT, wkT=wkT_e, wvT=wvT_e, woT=woT,
                         attn=attn))
    build_stageF(p, dict(xn1_bf=xn1_bf, x1=x1, attn=attn, Wr=Wr, br=br, W1=W1,
                         b1d=b1d, W2=W2, b2d=b2d, outT=outT))
    return p.build()


def kernel(**ins):
    cores = list(range(B))
    shared = {
        "w_inT": _t(ins["L0_attn_in_proj_w"]),
        "conv_w": np.ascontiguousarray(np.asarray(ins["L0_attn_conv_w"])[:, 0, :]),
        "conv_b": np.ascontiguousarray(np.asarray(ins["L0_attn_conv_b"]).reshape(NDI, 128).T),
        "xp_wT": _t(ins["L0_attn_x_proj_w"]),
        "dt_wT": _t(ins["L0_attn_dt_proj_w"]),
        "dt_b": np.ascontiguousarray(np.asarray(ins["L0_attn_dt_proj_b"]).reshape(NDI, 128).T),
        "A_dev": np.ascontiguousarray(np.asarray(ins["L0_attn_A_log"])
                                      .reshape(NDI, 128, NST).transpose(1, 0, 2)
                                      .reshape(128, NDI * NST)),
        "D_dev": np.ascontiguousarray(np.asarray(ins["L0_attn_D"]).reshape(NDI, 128).T),
        "outw_T": _t(ins["L0_attn_out_proj_w"]),
        "wrT": _t(ins["L0_ffn_W_r"]),
        "wkT_d": _t(ins["L0_ffn_W_k"]),
        "wvT_d": _t(ins["L0_ffn_W_v"]),
        "mu_r": np.ascontiguousarray(np.asarray(ins["L0_ffn_mu_r"]).reshape(KD, 128).T),
        "mu_k": np.ascontiguousarray(np.asarray(ins["L0_ffn_mu_k"]).reshape(KD, 128).T),
        "wqT": _t(ins["L1_attn_wq"]),
        "wkT_e": _t(ins["L1_attn_wk"]),
        "wvT_e": _t(ins["L1_attn_wv"]),
        "woT": _t(ins["L1_attn_wo"]),
        "Wr": np.ascontiguousarray(np.asarray(ins["L1_ffn_Wr"], np.float32)),
        "br": np.ascontiguousarray(np.asarray(ins["L1_ffn_br"], np.float32)[:, None]),
        "W1": np.ascontiguousarray(np.asarray(ins["L1_ffn_W1"], np.float32)),
        "b1d": np.ascontiguousarray(np.asarray(ins["L1_ffn_b1"], np.float32)
                                    .reshape(NE, HF // 128, 128).transpose(0, 2, 1)),
        "W2": np.ascontiguousarray(np.asarray(ins["L1_ffn_W2"], np.float32)),
        "b2d": np.ascontiguousarray(np.asarray(ins["L1_ffn_b2"], np.float32)
                                    .reshape(NE, D // 128, 128).transpose(0, 2, 1)),
    }
    maps = [dict(shared, xT=np.ascontiguousarray(
        np.asarray(ins["L0_attn_hidden_states"][b], np.float32).T)) for b in range(B)]
    nc = build_fused()
    res = run_bass_kernel_spmd(nc, maps, core_ids=cores)
    out = np.stack([np.asarray(res.results[b]["outT"]).T for b in range(B)], 0)
    return np.ascontiguousarray(out.astype(np.float32))


STAGE_BUILDERS = {"A": build_stageA, "B": build_stageB, "C": build_stageC, "D": build_stageD, "E": build_stageE, "F": build_stageF}
